# version 07
# speedup vs baseline: 1.6237x; 1.1495x over previous
import numpy as np
import ml_dtypes
from contextlib import ExitStack
import concourse.bass as bass
import concourse.mybir as mybir
from concourse.bass_utils import run_bass_kernel_spmd

F32 = mybir.dt.float32
BF16 = mybir.dt.bfloat16
U32 = mybir.dt.uint32
AF = mybir.ActivationFunctionType
ALU = mybir.AluOpType
AX = mybir.AxisListType
BFNP = ml_dtypes.bfloat16

SWAP16 = [(i + 16) % 32 for i in range(32)]
AB_COLS = 60416


def _l0_perm():
    return [*range(0, 16), *range(32, 48), *range(16, 32), *range(48, 64)]

def _mla_perm():
    return [*range(32), *[32 + p for p in (*range(0, 32, 2), *range(1, 32, 2))]]

def host_consts(T):
    t = np.arange(T, dtype=np.float64)
    invf = 1.0 / (10000.0 ** (np.arange(32) / 32.0))
    ang = t[None, :] * invf[:, None]
    Tc0 = np.zeros((128, T)); Ts0 = np.zeros((128, T))
    for r in range(128):
        qd, idx = r // 32, r % 32
        f = (qd % 2) * 16 + (idx % 16)
        Tc0[r] = np.cos(ang[f])
        Ts0[r] = np.sin(ang[f]) * (1.0 if idx < 16 else -1.0)
    invf2 = 1.0 / (100000.0 ** (np.arange(0, 32, 2) / 32.0))
    ang2 = t[None, :] * invf2[:, None]
    TcM = np.zeros((128, T)); TsM = np.zeros((128, T))
    for r in range(128):
        qd, idx = r // 32, r % 32
        if qd % 2 == 0:
            TcM[r] = 1.0
        else:
            f = idx % 16
            TcM[r] = np.cos(ang2[f])
            TsM[r] = np.sin(ang2[f]) * (1.0 if idx < 16 else -1.0)
    masks = np.zeros((128, 4, 512))
    for v_, o in enumerate((0, 128, 256, 384)):
        masks[:, v_, :] = (np.arange(128)[:, None] + o <= np.arange(512)[None, :])
    idx = np.pi / 2 * (t + 1) / T
    TT = T // 128
    sc = np.zeros((128, 2 * TT), np.float32)
    for tt in range(TT):
        sc[:, tt] = np.sin(idx[tt * 128:(tt + 1) * 128])
        sc[:, TT + tt] = np.cos(idx[tt * 128:(tt + 1) * 128])
    return {
        "c_ident": np.eye(128).astype(BFNP),
        "c_onesb": np.ones((128, 128), BFNP),
        "c_ltri": np.triu(np.ones((128, 128)), 1).astype(BFNP),
        "c_tc0": Tc0.astype(BFNP), "c_ts0": Ts0.astype(BFNP),
        "c_tcm": TcM.astype(BFNP), "c_tsm": TsM.astype(BFNP),
        "c_masks": masks.astype(BFNP).reshape(128, 4 * 512),
        "c_sincos": sc,
    }

def pack_l0_attn(w_qkv, w_o):
    p = np.asarray(_l0_perm())
    q = w_qkv[:, :1024].reshape(1024, 16, 64)[:, :, p].reshape(1024, 1024)
    k = w_qkv[:, 1024:1280].reshape(1024, 4, 64)[:, :, p].reshape(1024, 256)
    v = w_qkv[:, 1280:1536]
    return {"a0_wqkv": np.ascontiguousarray(np.concatenate([q, k, v], axis=1), np.float32),
            "a0_wo": np.ascontiguousarray(w_o, np.float32)}

def pack_l0_ffn(Wq, Wkva, Wkvb, Wo, Wg, Wu, Wd, sWg, sWu, sWd, Wgate, expert_bias,
                w_ln1, w_ln2):
    p64 = np.asarray(_mla_perm())
    p32 = np.asarray([*range(0, 32, 2), *range(1, 32, 2)])
    wq = Wq.reshape(1024, 16, 64)[:, :, p64].reshape(1024, 1024)
    wkva = np.concatenate([Wkva[:, :128], Wkva[:, 128:][:, p32]], axis=1)
    c = np.ascontiguousarray
    woe = Wo.reshape(16, 64, 1024)[:, :32, :].reshape(512, 1024)
    return {"f0_wq": c(wq, np.float32), "f0_wkva": c(wkva, np.float32),
            "f0_wkvb": c(Wkvb, np.float32), "f0_woe": c(woe, np.float32),
            "f0_wg": c(np.concatenate([sWg, Wg], 0), np.float32),
            "f0_wu": c(np.concatenate([sWu, Wu], 0), np.float32),
            "f0_wd": c(np.concatenate([sWd, Wd], 0), np.float32),
            "f0_wgate": c(Wgate, np.float32),
            "f0_ebias": c(expert_bias.reshape(1, 8), np.float32),
            "f0_g1": c(w_ln1.reshape(1, 1024), np.float32),
            "f0_g2": c(w_ln2.reshape(1, 1024), np.float32)}



class Arena:
    def __init__(self, tensor, total):
        self.t, self.total, self.blocks = tensor, total, {}
        self.hold = False
        self.freed = []
        self.pending = {}

    def alloc(self, name, size):
        size = int(size)
        off = 0
        for (o, s) in sorted(self.blocks.values()):
            if off + size <= o:
                break
            off = max(off, o + s)
        assert off + size <= self.total, \
            f"arena overflow: {name} sz={size} off={off} held={sorted(self.blocks.items(), key=lambda kv: kv[1])}"
        self.blocks[name] = (off, size)
        waits = {}
        keep = []
        for (o, s, rd) in self.freed:
            if o < off + size and off < o + s:
                for k, v in rd.items():
                    waits[k] = max(waits.get(k, 0), v)
                if o < off:
                    keep.append((o, off - o, rd))
                if o + s > off + size:
                    keep.append((off + size, o + s - (off + size), rd))
            else:
                keep.append((o, s, rd))
        self.freed = keep
        self.pending[name] = waits
        return off

    def free(self, *names, readers=None):
        if self.hold:
            return
        for n in names:
            off, size = self.blocks[n]
            self.freed.append((off, size, dict(readers or {})))
            del self.blocks[n]

    def ap(self, name):
        off, size = self.blocks[name]
        return self.t[:, off:off + size]


def apply_pending(P, eng, *names):
    for n in names:
        for snm, v_ in P.AB.pending.get(n, {}).items():
            eng.wait_ge(P.sems[snm], v_)


def v3(ap, inner):
    return ap.rearrange("p (c t) -> p c t", t=inner)


class Prog:
    def __init__(self, T=2048, n_seq=4):
        self.T, self.TT, self.NQ = T, T // 128, T // 512
        self.n_seq = n_seq
        self.nc = bass.Bass("TRN2", target_bir_lowering=False, detect_race_conditions=False)
        self.es = ExitStack()
        self.cnt, self.sems = {}, {}
        self.dram = {}
        self.ready = {}

    def sem(self, name):
        if name not in self.sems:
            self.sems[name] = self.es.enter_context(self.nc.semaphore(name))
            self.cnt[name] = 0
        return self.sems[name]

    def res(self, name, n, step=1):
        self.sem(name)
        base = self.cnt[name]
        self.cnt[name] += n * step
        return [base + (i + 1) * step for i in range(n)]

    def setup(self):
        nc = self.nc
        for s in ("ld", "lg", "pe", "dv", "ac", "gp", "st"):
            self.sem(s)
        self.X = self.es.enter_context(nc.sbuf_tensor("X", [128, self.TT * 1024], F32))
        ab_t = self.es.enter_context(nc.sbuf_tensor("ABt", [128, AB_COLS], BF16))
        self.AB = Arena(ab_t, AB_COLS)
        self.SC = self.es.enter_context(nc.sbuf_tensor("SCt", [128, 4096], F32))
        self.AF = self.es.enter_context(nc.sbuf_tensor("AFt", [128, 2304], F32))
        self.ident = self.es.enter_context(nc.sbuf_tensor("ident", [128, 128], BF16))
        self.banks = [self.es.enter_context(nc.psum_tensor(f"pb{i}", [128, 512], F32))
                      for i in range(7)]
        self.bt = self.es.enter_context(nc.psum_tensor("pbt", [128, 1024], BF16))
        self.onesb = self.es.enter_context(nc.sbuf_tensor("onesb", [128, 128], BF16))
        self.ltri = self.es.enter_context(nc.sbuf_tensor("ltri", [128, 128], BF16))
        self.mi8 = self.es.enter_context(nc.sbuf_tensor("mi8", [128, 8], U32))

    def din(self, name, shape, dt=F32):
        if name not in self.dram:
            self.dram[name] = self.nc.dram_tensor(name, shape, dt, kind="ExternalInput").ap()
        return self.dram[name]

    def dout(self, name, shape, dt=F32):
        self.dram[name] = self.nc.dram_tensor(name, shape, dt, kind="ExternalOutput").ap()
        return self.dram[name]



def emit_consts_load(P):
    idt = P.din("c_ident", [128, 128], BF16)
    ob = P.din("c_onesb", [128, 128], BF16)
    lt = P.din("c_ltri", [128, 128], BF16)
    ldv = P.res("ld", 3, 16)

    @P.block.sync
    def _(sy):
        sy.dma_start(out=P.ident[:], in_=idt).then_inc(P.sems["ld"], 16)
        sy.dma_start(out=P.onesb[:], in_=ob).then_inc(P.sems["ld"], 16)
        sy.dma_start(out=P.ltri[:], in_=lt).then_inc(P.sems["ld"], 16)
    P.const_ready = ldv[-1]


def dbg_dump(P, name, ap2d, wait, dt=BF16):
    if not getattr(P, "debug", False):
        return
    shp = [128, int(np.prod(ap2d.shape[1:]))]
    full = "dbg_" + name
    while full in P.dram:
        full += "_"
    d = P.dout(full, shp, dt)
    P.res("st", 1, 16)

    @P.block.sync
    def _(sy, ap2d=ap2d, wait=dict(wait), d=d):
        for snm, v_ in wait.items():
            sy.wait_ge(P.sems[snm], v_)
        sy.dma_start(out=d, in_=ap2d).then_inc(P.sems["st"], 16)


def emit_toT(P, tag, wait=None, dst="xT", cast_emit=None, pe_extra_wait=None):
    T, TT = P.T, P.TT
    X = P.X
    if dst not in P.AB.blocks:
        P.AB.alloc(dst, 8 * T)
    dsv = v3(P.AB.ap(dst), T)
    cast_off = P.AB.alloc(f"{tag}_cb", 2048)
    cb = [P.AB.t[:, cast_off:cast_off + 1024], P.AB.t[:, cast_off + 1024:cast_off + 2048]]
    if cast_emit is None:
        def cast_emit(vv, tt, out_ap):
            return vv.tensor_copy(out_ap, X[:, tt * 1024:(tt + 1) * 1024])

    t_v = P.res("pe", TT)
    order = [("cast", 0)]
    for tt in range(1, TT):
        order += [("cast", tt), ("evac", tt - 1)]
    order.append(("evac", TT - 1))
    dvv = P.res("dv", len(order))
    c_v, e_v = {}, {}
    for i, (kind, tt) in enumerate(order):
        (c_v if kind == "cast" else e_v)[tt] = dvv[i]

    @P.block.vector
    def _(vv):
        if wait:
            for snm, v_ in wait.items():
                vv.wait_ge(P.sems[snm], v_)
        apply_pending(P, vv, dst, f"{tag}_cb")
        for kind, tt in order:
            if kind == "cast":
                if tt >= 2:
                    vv.wait_ge(P.sems["pe"], t_v[tt - 2])
                cast_emit(vv, tt, cb[tt % 2]).then_inc(P.sems["dv"], 1)
            else:
                vv.wait_ge(P.sems["pe"], t_v[tt])
                vv.tensor_copy(dsv[:, :, tt * 128:(tt + 1) * 128],
                               P.bt[:].rearrange("p (c t) -> p c t", t=128)
                               ).then_inc(P.sems["dv"], 1)

    @P.block.tensor
    def _(t):
        t.wait_ge(P.sems["ld"], P.const_ready)
        if pe_extra_wait:
            for snm, v_ in pe_extra_wait.items():
                t.wait_ge(P.sems[snm], v_)
        for tt in range(TT):
            t.wait_ge(P.sems["dv"], c_v[tt])
            if tt >= 1:
                t.wait_ge(P.sems["dv"], e_v[tt - 1])
            for c in range(8):
                mm = t.transpose(P.bt[:, c * 128:(c + 1) * 128],
                                 cb[tt % 2][:, c * 128:(c + 1) * 128], P.ident[:])
            mm.then_inc(P.sems["pe"], 1)
    P.AB.free(f"{tag}_cb", readers={"pe": t_v[-1]})
    return {"dv": dvv[-1], "pe": t_v[-1]}


def emit_xT_refresh(P, tag, wait=None, dst="xT"):
    return emit_toT(P, tag, wait=wait, dst=dst)


def emit_seq_barrier(P):
    snap = {k: v for k, v in P.cnt.items() if v > 0}

    def mk(eng):
        for k, v in snap.items():
            eng.wait_ge(P.sems[k], v)
    P.block.tensor(mk)
    P.block.vector(mk)
    P.block.scalar(mk)
    P.block.gpsimd(mk)
    P.block.sync(mk)


def emit_load_x(P, s, refresh=True):
    ldv = P.res("ld", 1, 16)[0]
    xin = P.din("xin", [P.n_seq, P.T, 1024])
    st_prev = P.cnt.get("st", 0)

    @P.block.sync
    def _(sy):
        if st_prev:
            sy.wait_ge(P.sems["st"], st_prev)
        sy.dma_start(out=P.X[:, :].rearrange("p (tt c) -> p tt c", c=1024),
                     in_=xin[s].rearrange("(tt p) c -> p tt c", p=128)
                     ).then_inc(P.sems["ld"], 16)
    if refresh:
        P.ready = emit_xT_refresh(P, f"lx{s}", wait={"ld": ldv})
    else:
        P.ready = {"ld": ldv}


def emit_store_x(P, s, final=False):
    P.res("st", 1, 16)
    xout = P.dram["xout"]

    @P.block.sync
    def _(sy, ready=dict(P.ready)):
        for snm, v_ in ready.items():
            sy.wait_ge(P.sems[snm], v_)
        sy.dma_start(out=xout[s].rearrange("(tt p) c -> p tt c", p=128),
                     in_=P.X[:, :].rearrange("p (tt c) -> p tt c", c=1024)
                     ).then_inc(P.sems["st"], 16)
        if final:
            sy.wait_ge(P.sems["st"], P.cnt["st"])


def emit_postln(P, tag, xready, refresh_xt=True):
    TT = P.TT
    SC = P.SC
    stats = SC[:, 0:12 * TT].rearrange("p (t k) -> p t k", k=12)
    agg = SC[:, 12 * TT:16 * TT].rearrange("p (t k) -> p t k", k=4)
    aux = SC[:, 16 * TT:20 * TT].rearrange("p (t k) -> p t k", k=4)

    g1, g2, g3, g4 = P.res("dv", 4)
    g5 = P.res("ac", 1)[0]
    g6, g7 = P.res("dv", 2)
    g8 = P.res("ac", TT)

    @P.block.vector
    def _(vv):
        for snm, v_ in xready.items():
            vv.wait_ge(P.sems[snm], v_)
        for tt in range(TT):
            vv.bn_stats(stats[:, tt, 0:6], P.X[:, tt * 1024:tt * 1024 + 512])
            ins = vv.bn_stats(stats[:, tt, 6:12], P.X[:, tt * 1024 + 512:(tt + 1) * 1024])
        ins.then_inc(P.sems["dv"], 1)
        vv.wait_ge(P.sems["dv"], g1)
        for tt in range(TT):
            ins = vv.bn_aggr(agg[:, tt, 0:2], stats[:, tt, :])
        ins.then_inc(P.sems["dv"], 1)
        vv.wait_ge(P.sems["dv"], g2)
        vv.tensor_scalar_add(aux[:, :, 0], agg[:, :, 1], 1e-5).then_inc(P.sems["dv"], 1)
        vv.wait_ge(P.sems["dv"], g3)
        vv.reciprocal(aux[:, :, 1], aux[:, :, 0]).then_inc(P.sems["dv"], 1)

    @P.block.scalar
    def _(sa):
        sa.wait_ge(P.sems["dv"], g4)
        sa.activation(aux[:, :, 1], aux[:, :, 1], AF.Sqrt).then_inc(P.sems["ac"], 1)

    @P.block.vector
    def _(vv):
        vv.wait_ge(P.sems["ac"], g5)
        vv.tensor_mul(aux[:, :, 2], agg[:, :, 0], aux[:, :, 1]).then_inc(P.sems["dv"], 1)
        vv.wait_ge(P.sems["dv"], g6)
        vv.tensor_scalar_mul(aux[:, :, 2], aux[:, :, 2], -1.0).then_inc(P.sems["dv"], 1)

    @P.block.scalar
    def _(sa):
        sa.wait_ge(P.sems["dv"], g7)
        for tt in range(TT):
            sa.activation(P.X[:, tt * 1024:(tt + 1) * 1024], P.X[:, tt * 1024:(tt + 1) * 1024],
                          AF.Identity, bias=aux[:, tt, 2:3], scale=aux[:, tt, 1:2]
                          ).then_inc(P.sems["ac"], 1)

    ready = {"ac": g8[-1]}
    if refresh_xt:
        ready = emit_xT_refresh(P, f"{tag}_rx", wait={"ac": g8[-1]})
        ready["ac"] = g8[-1]
    return ready


def emit_rope(P, tile_ap, rsc, tbl, done_sems, tbl_ready, rsc_name=None):
    r1 = P.res("dv", 1)[0]
    r2 = P.res("gp", 1)[0]
    r3 = P.res("dv", 1)[0]
    r4 = P.res("gp", 1)[0]

    @P.block.vector
    def _(vv):
        for snm, v_ in done_sems.items():
            vv.wait_ge(P.sems[snm], v_)
        if rsc_name:
            apply_pending(P, vv, rsc_name)
        vv.wait_ge(P.sems["ld"], tbl_ready)
        vv.tensor_mul(rsc[:, 0, :], tile_ap, tbl[:, 0, :]).then_inc(P.sems["dv"], 1)

    @P.block.gpsimd
    def _(g):
        for snm, v_ in done_sems.items():
            g.wait_ge(P.sems[snm], v_)
        if rsc_name:
            apply_pending(P, g, rsc_name)
        g.wait_ge(P.sems["ld"], tbl_ready)
        g.tensor_mul(rsc[:, 1, :], tile_ap, tbl[:, 1, :]).then_inc(P.sems["gp"], 1)

    @P.block.vector
    def _(vv):
        vv.wait_ge(P.sems["gp"], r2)
        vv.stream_shuffle(rsc[:, 1, :], rsc[:, 1, :], SWAP16).then_inc(P.sems["dv"], 1)

    @P.block.gpsimd
    def _(g):
        g.wait_ge(P.sems["dv"], r1)
        g.wait_ge(P.sems["dv"], r3)
        g.tensor_add(tile_ap, rsc[:, 0, :], rsc[:, 1, :]).then_inc(P.sems["gp"], 1)
    return r4



def emit_l0_attn(P, s, upto=None, refresh_out=True):
    T, TT, NQ = P.T, P.TT, P.NQ
    AB, SC, X, b, bt = P.AB, P.SC, P.X, P.banks, P.bt
    tag = f"a0s{s}"
    nc = P.nc

    wqkv_d = P.din("a0_wqkv", [1024, 1536])
    wo_d = P.din("a0_wo", [1024, 1024])
    tc0_d = P.din("c_tc0", [128, T], BF16)
    ts0_d = P.din("c_ts0", [128, T], BF16)
    masks_d = P.din("c_masks", [128, 4 * 512], BF16)
    xT = v3(AB.ap("xT"), T)

    AB.alloc(f"{tag}_wqkv", 8 * 1536)
    AB.alloc(f"{tag}_qn", TT * 1024)
    AB.alloc(f"{tag}_kn", TT * 256)
    AB.alloc(f"{tag}_v", TT * 4 * 66)
    wq = v3(AB.ap(f"{tag}_wqkv"), 1536)
    qn = v3(AB.ap(f"{tag}_qn"), 1024)
    kn = v3(AB.ap(f"{tag}_kn"), 256)
    vv_ = AB.ap(f"{tag}_v").rearrange("p (t g e) -> p t g e", g=4, e=66)
    SSQ = SC[:, 0:20 * TT].rearrange("p (t h) -> p t h", h=20)
    SQS = SC[:, 20 * TT:20 * TT + 1280]

    lw = P.res("lg", 1, 16)[0]
    gpo = P.res("gp", 1)[0]
    xready = dict(P.ready)

    @P.block.gpsimd
    def _(g):
        apply_pending(P, g, f"{tag}_wqkv", f"{tag}_v", f"{tag}_qn", f"{tag}_kn")
        g.dma_start(out=wq, in_=wqkv_d.rearrange("(c p) n -> p c n", p=128)
                    ).then_inc(P.sems["lg"], 16)
        g.memset(vv_[:, :, :, 64:66], 1.0).then_inc(P.sems["gp"], 1)

    pe_a = P.res("pe", TT)
    ac_a = P.res("ac", TT)
    dv_a = P.res("dv", TT)

    @P.block.tensor
    def _(t):
        t.wait_ge(P.sems["lg"], lw)
        for snm, v_ in xready.items():
            t.wait_ge(P.sems[snm], v_)
        for tt in range(TT):
            trio = (0, 1, 2) if tt % 2 == 0 else (3, 4, 5)
            if tt >= 2:
                t.wait_ge(P.sems["dv"], dv_a[tt - 2])
            for nj in range(3):
                for c in range(8):
                    mm = t.matmul(b[trio[nj]][:], xT[:, c, tt * 128:(tt + 1) * 128],
                                  wq[:, c, nj * 512:(nj + 1) * 512],
                                  start=(c == 0), stop=(c == 7))
            mm.then_inc(P.sems["pe"], 1)

    @P.block.scalar
    def _(sa):
        for tt in range(TT):
            trio = (0, 1, 2) if tt % 2 == 0 else (3, 4, 5)
            sa.wait_ge(P.sems["pe"], pe_a[tt])
            if tt >= 1:
                sa.wait_ge(P.sems["dv"], dv_a[tt - 1])
            sa.activation(SQS[:, 0:512], b[trio[0]][:], AF.Square)
            sa.activation(SQS[:, 512:1024], b[trio[1]][:], AF.Square)
            sa.activation(SQS[:, 1024:1280], b[trio[2]][:, 0:256], AF.Square
                          ).then_inc(P.sems["ac"], 1)

    @P.block.vector
    def _(vv):
        vv.wait_ge(P.sems["gp"], gpo)
        apply_pending(P, vv, f"{tag}_qn", f"{tag}_kn", f"{tag}_v")
        for tt in range(TT):
            trio = (0, 1, 2) if tt % 2 == 0 else (3, 4, 5)
            vv.wait_ge(P.sems["ac"], ac_a[tt])
            vv.tensor_copy(qn[:, tt, 0:512], b[trio[0]][:])
            vv.tensor_copy(qn[:, tt, 512:1024], b[trio[1]][:])
            vv.tensor_copy(kn[:, tt, :], b[trio[2]][:, 0:256])
            for g_ in range(4):
                vv.tensor_copy(vv_[:, tt, g_, 0:64], b[trio[2]][:, 256 + g_ * 64:320 + g_ * 64])
            vv.tensor_reduce(SSQ[:, tt, 0:16], SQS[:, 0:1024].rearrange("p (h e) -> p h e", e=64),
                             AX.X, ALU.add)
            vv.tensor_reduce(SSQ[:, tt, 16:20], SQS[:, 1024:1280].rearrange("p (h e) -> p h e", e=64),
                             AX.X, ALU.add).then_inc(P.sems["dv"], 1)

    AB.free(f"{tag}_wqkv", "xT", readers={"pe": pe_a[TT - 1]})

    n1 = P.res("ac", 1)[0]
    n2, n3, n4 = P.res("dv", 3)
    n5 = P.res("gp", 1)[0]
    SSQf = SSQ.rearrange("p t h -> p (t h)")

    @P.block.scalar
    def _(sa):
        sa.wait_ge(P.sems["dv"], dv_a[TT - 1])
        sa.activation(SSQf, SSQf, AF.Sqrt).then_inc(P.sems["ac"], 1)

    @P.block.vector
    def _(vv):
        vv.wait_ge(P.sems["ac"], n1)
        vv.tensor_scalar_add(SSQf, SSQf, 1e-10).then_inc(P.sems["dv"], 1)
        vv.wait_ge(P.sems["dv"], n2)
        vv.reciprocal(SSQf, SSQf).then_inc(P.sems["dv"], 1)
        vv.wait_ge(P.sems["dv"], n3)
        for tt in range(TT):
            for h in range(0, 16, 2):
                ins = vv.tensor_scalar_mul(qn[:, tt, h * 64:(h + 1) * 64],
                                           qn[:, tt, h * 64:(h + 1) * 64], SSQ[:, tt, h:h + 1])
        ins.then_inc(P.sems["dv"], 1)

    @P.block.gpsimd
    def _(g):
        g.wait_ge(P.sems["dv"], n3)
        for tt in range(TT):
            for h in range(1, 16, 2):
                g.tensor_scalar_mul(qn[:, tt, h * 64:(h + 1) * 64],
                                    qn[:, tt, h * 64:(h + 1) * 64], SSQ[:, tt, h:h + 1])
            for g_ in range(4):
                ins = g.tensor_scalar_mul(kn[:, tt, g_ * 64:(g_ + 1) * 64],
                                          kn[:, tt, g_ * 64:(g_ + 1) * 64],
                                          SSQ[:, tt, 16 + g_:17 + g_])
        ins.then_inc(P.sems["gp"], 1)

    if upto == "A":
        dbg_dump(P, "qn", AB.ap(f"{tag}_qn"), {"dv": n4, "gp": n5})
        dbg_dump(P, "kn", AB.ap(f"{tag}_kn"), {"dv": n4, "gp": n5})
        dbg_dump(P, "vv", AB.ap(f"{tag}_v"), {"dv": dv_a[TT - 1], "gp": gpo})
        return

    AB.alloc(f"{tag}_qT", 8 * T)
    qT = v3(AB.ap(f"{tag}_qT"), T)
    AB.alloc(f"{tag}_tbl", 2 * T)
    tbl = v3(AB.ap(f"{tag}_tbl"), T)
    AB.alloc(f"{tag}_kts", 2 * T)
    kts = v3(AB.ap(f"{tag}_kts"), T)
    AB.alloc(f"{tag}_rsc", 2 * T)
    rsc = v3(AB.ap(f"{tag}_rsc"), T)

    lt1, lt2 = P.res("ld", 2, 16)

    @P.block.sync
    def _(sy):
        apply_pending(P, sy, f"{tag}_tbl")
        sy.dma_start(out=tbl[:, 0, :], in_=tc0_d).then_inc(P.sems["ld"], 16)
        sy.dma_start(out=tbl[:, 1, :], in_=ts0_d).then_inc(P.sems["ld"], 16)

    kt_pe = P.res("pe", TT)
    kt_dv = P.res("dv", TT)

    @P.block.tensor
    def _(t):
        t.wait_ge(P.sems["gp"], n5)
        t.wait_ge(P.sems["dv"], n4)
        for tt in range(TT):
            if tt >= 1:
                t.wait_ge(P.sems["dv"], kt_dv[tt - 1])
            t.transpose(bt[:, 0:128], kn[:, tt, 0:128], P.ident[:])
            t.transpose(bt[:, 128:256], kn[:, tt, 128:256], P.ident[:]).then_inc(P.sems["pe"], 1)

    @P.block.vector
    def _(vv):
        apply_pending(P, vv, f"{tag}_kts")
        for tt in range(TT):
            vv.wait_ge(P.sems["pe"], kt_pe[tt])
            vv.tensor_copy(kts[:, 0, tt * 128:(tt + 1) * 128], bt[:, 0:128])
            vv.tensor_copy(kts[:, 1, tt * 128:(tt + 1) * 128], bt[:, 128:256]
                           ).then_inc(P.sems["dv"], 1)
    AB.free(f"{tag}_kn", readers={"pe": kt_pe[TT - 1]})
    AB.alloc(f"{tag}_ktr01", 2 * T)
    AB.alloc(f"{tag}_ktr23", 2 * T)
    ktr = [v3(AB.ap(f"{tag}_ktr01"), T), v3(AB.ap(f"{tag}_ktr23"), T)]

    kr0 = emit_rope(P, kts[:, 0, :], rsc, tbl, {"dv": kt_dv[TT - 1]}, lt2,
                    rsc_name=f"{tag}_rsc")
    kr1 = emit_rope(P, kts[:, 1, :], rsc, tbl, {"gp": kr0}, lt2)
    krep = P.res("dv", 1)[0]

    @P.block.vector
    def _(vv):
        apply_pending(P, vv, f"{tag}_ktr01", f"{tag}_ktr23")
        vv.wait_ge(P.sems["gp"], kr1)
        for g_ in range(4):
            src = kts[:, g_ // 2, :]
            half = (g_ % 2) * 64
            vv.tensor_copy(ktr[g_ // 2][0:64, g_ % 2, :], src[half:half + 64, :])
            ins = vv.tensor_copy(ktr[g_ // 2][64:128, g_ % 2, :], src[half:half + 64, :])
        ins.then_inc(P.sems["dv"], 1)

    qt_pe = P.res("pe", TT)
    qt_dv = P.res("dv", TT)

    @P.block.tensor
    def _(t):
        t.wait_ge(P.sems["dv"], kt_dv[TT - 1])
        for tt in range(TT):
            if tt >= 1:
                t.wait_ge(P.sems["dv"], qt_dv[tt - 1])
            for c in range(8):
                mm = t.transpose(bt[:, c * 128:(c + 1) * 128], qn[:, tt, c * 128:(c + 1) * 128],
                                 P.ident[:])
            mm.then_inc(P.sems["pe"], 1)

    @P.block.vector
    def _(vv):
        apply_pending(P, vv, f"{tag}_qT")
        vv.wait_ge(P.sems["dv"], krep)
        for tt in range(TT):
            vv.wait_ge(P.sems["pe"], qt_pe[tt])
            vv.tensor_copy(qT[:, :, tt * 128:(tt + 1) * 128],
                           bt[:].rearrange("p (c t) -> p c t", t=128)).then_inc(P.sems["dv"], 1)

    prev = {"dv": qt_dv[TT - 1]}
    for c in range(8):
        r = emit_rope(P, qT[:, c, :], rsc, tbl, prev, lt2)
        prev = {"gp": r}
    rope_done = r
    if upto == "B":
        dbg_dump(P, "qT", AB.ap(f"{tag}_qT"), {"gp": rope_done})
        dbg_dump(P, "ktr01", AB.ap(f"{tag}_ktr01"), {"dv": krep})
        dbg_dump(P, "ktr23", AB.ap(f"{tag}_ktr23"), {"dv": krep})
        return
    AB.free(f"{tag}_qn", readers={"pe": qt_pe[TT - 1]})
    AB.free(f"{tag}_kts", readers={"dv": krep})
    AB.free(f"{tag}_rsc", f"{tag}_tbl", readers={"gp": rope_done})

    AB.alloc(f"{tag}_y", TT * 1024)
    y = v3(AB.ap(f"{tag}_y"), 1024)
    AB.alloc(f"{tag}_wo", 8 * 1024)
    wo = v3(AB.ap(f"{tag}_wo"), 1024)
    AB.alloc(f"{tag}_mask", 4 * 512)
    msk = v3(AB.ap(f"{tag}_mask"), 512)
    AB.alloc(f"{tag}_strip", 4 * 512)
    strip = v3(AB.ap(f"{tag}_strip"), 512)

    lm = P.res("ld", 1, 16)[0]
    lwo = P.res("lg", 1, 16)[0]

    @P.block.sync
    def _(sy):
        apply_pending(P, sy, f"{tag}_mask")
        sy.dma_start(out=AB.ap(f"{tag}_mask"), in_=masks_d).then_inc(P.sems["ld"], 16)

    @P.block.gpsimd
    def _(g):
        apply_pending(P, g, f"{tag}_wo")
        g.dma_start(out=wo, in_=wo_d.rearrange("(c p) n -> p c n", p=128)
                    ).then_inc(P.sems["lg"], 16)

    rd = SC[:, 1664:1672]
    prev_ep = None
    for p_ in range(8):
        g_ = p_ // 2
        for qj in range(NQ):
            kmax = min(TT, 4 * (qj + 1))
            diag = sorted(set(range(4 * qj, kmax)))
            pe_v = P.res("pe", 2 * kmax)
            s_idx, a_idx = {}, {}
            n_ = 0
            for ki in range(kmax):
                s_idx[ki] = pe_v[n_]; n_ += 1
                if ki >= 1:
                    a_idx[ki - 1] = pe_v[n_]; n_ += 1
            a_idx[kmax - 1] = pe_v[n_]
            ac_v = P.res("ac", kmax)
            dvl = P.res("dv", len(diag) + 2)
            dv_m = {ki: dvl[i] for i, ki in enumerate(diag)}
            ep1, ep2 = dvl[-2], dvl[-1]

            def emit_av(t, ki, kmax, diag, ac_v, dv_m, first_out):
                t.wait_ge(P.sems["ac"], ac_v[ki])
                if ki in diag:
                    t.wait_ge(P.sems["dv"], dv_m[ki])
                sidx = (ki % 2) * 2
                g_l = g_
                for j in range(4):
                    t.matmul(b[0][:, j * 128:j * 128 + 65],
                             strip[:, sidx, j * 128:(j + 1) * 128],
                             vv_[:, ki, g_l, 0:65], start=first_out,
                             stop=(ki == kmax - 1 and j == 3))
                    mm = t.matmul(b[1][:, j * 128:j * 128 + 65],
                                  strip[:, sidx + 1, j * 128:(j + 1) * 128],
                                  vv_[:, ki, g_l, 0:65], start=first_out,
                                  stop=(ki == kmax - 1 and j == 3))
                    first_out = False
                mm.then_inc(P.sems["pe"], 1)

            @P.block.tensor
            def _(t, p_=p_, qj=qj, kmax=kmax, diag=diag, ac_v=ac_v,
                  dv_m=dv_m, prev_ep=prev_ep, g_=g_):
                if p_ == 0 and qj == 0:
                    t.wait_ge(P.sems["dv"], krep)
                    t.wait_ge(P.sems["gp"], rope_done)
                    t.wait_ge(P.sems["ld"], lm)
                if prev_ep is not None:
                    t.wait_ge(P.sems["dv"], prev_ep)
                for ki in range(kmax):
                    sb_, sb2 = (b[2], b[3]) if ki % 2 == 0 else (b[4], b[5])
                    if ki >= 2:
                        t.wait_ge(P.sems["ac"], ac_v[ki - 2])
                    kt_t = ktr[g_ // 2]
                    gg = g_ % 2
                    t.matmul(sb_[:], kt_t[0:64, gg, ki * 128:(ki + 1) * 128],
                             qT[0:64, p_, qj * 512:(qj + 1) * 512], start=True, stop=True)
                    t.matmul(sb2[:], kt_t[64:128, gg, ki * 128:(ki + 1) * 128],
                             qT[64:128, p_, qj * 512:(qj + 1) * 512], start=True, stop=True
                             ).then_inc(P.sems["pe"], 1)
                    if ki >= 1:
                        emit_av(t, ki - 1, kmax, diag, ac_v, dv_m, ki == 1)
                emit_av(t, kmax - 1, kmax, diag, ac_v, dv_m, kmax == 1)

            @P.block.scalar
            def _(sa, kmax=kmax, s_idx=s_idx, a_idx=a_idx, p_=p_, qj=qj):
                if p_ == 0 and qj == 0:
                    apply_pending(P, sa, f"{tag}_strip")
                for ki in range(kmax):
                    sidx = (ki % 2) * 2
                    sa.wait_ge(P.sems["pe"], s_idx[ki])
                    if ki >= 2:
                        sa.wait_ge(P.sems["pe"], a_idx[ki - 2])
                    sb_, sb2 = (b[2], b[3]) if ki % 2 == 0 else (b[4], b[5])
                    sa.activation(strip[:, sidx, :], sb_[:], AF.Exp, scale=0.125)
                    sa.activation(strip[:, sidx + 1, :], sb2[:], AF.Exp, scale=0.125
                                  ).then_inc(P.sems["ac"], 1)

            @P.block.vector
            def _(vv, kmax=kmax, qj=qj, diag=diag, ac_v=ac_v, a_idx=a_idx,
                  ep1=ep1, ep2=ep2, p_=p_):
                if p_ == 0 and qj == 0:
                    apply_pending(P, vv, f"{tag}_y")
                    vv.wait_ge(P.sems["ld"], lm)
                for ki in diag:
                    sidx = (ki % 2) * 2
                    o_var = ki - 4 * qj
                    vv.wait_ge(P.sems["ac"], ac_v[ki])
                    vv.tensor_mul(strip[:, sidx, :], strip[:, sidx, :], msk[:, o_var, :])
                    vv.tensor_mul(strip[:, sidx + 1, :], strip[:, sidx + 1, :],
                                  msk[:, o_var, :]).then_inc(P.sems["dv"], 1)
                vv.wait_ge(P.sems["pe"], a_idx[kmax - 1])
                vv.reciprocal(rd[:, 0:4], bass.AP(b[0], 64, [[512, 128], [128, 4]]))
                vv.reciprocal(rd[:, 4:8], bass.AP(b[1], 64, [[512, 128], [128, 4]])
                              ).then_inc(P.sems["dv"], 1)
                vv.wait_ge(P.sems["dv"], ep1)
                for j in range(4):
                    vv.tensor_scalar_mul(y[:, qj * 4 + j, (2 * p_) * 64:(2 * p_ + 1) * 64],
                                         b[0][:, j * 128:j * 128 + 64], rd[:, j:j + 1])
                    ins = vv.tensor_scalar_mul(y[:, qj * 4 + j, (2 * p_ + 1) * 64:(2 * p_ + 2) * 64],
                                               b[1][:, j * 128:j * 128 + 64], rd[:, 4 + j:5 + j])
                ins.then_inc(P.sems["dv"], 1)
            prev_ep = ep2

    AB.free(f"{tag}_qT", f"{tag}_ktr01", f"{tag}_ktr23", f"{tag}_strip", f"{tag}_v",
            readers={"pe": P.cnt["pe"]})
    AB.free(f"{tag}_mask", readers={"dv": prev_ep})

    if upto == "C":
        dbg_dump(P, "y", AB.ap(f"{tag}_y"), {"dv": prev_ep})
        return

    AB.alloc(f"{tag}_yT", 8 * T)
    yT = v3(AB.ap(f"{tag}_yT"), T)
    yt_pe = P.res("pe", TT)
    yt_dv = P.res("dv", TT)

    @P.block.tensor
    def _(t):
        t.wait_ge(P.sems["dv"], prev_ep)
        for tt in range(TT):
            if tt >= 1:
                t.wait_ge(P.sems["dv"], yt_dv[tt - 1])
            for c in range(8):
                mm = t.transpose(bt[:, c * 128:(c + 1) * 128], y[:, tt, c * 128:(c + 1) * 128],
                                 P.ident[:])
            mm.then_inc(P.sems["pe"], 1)

    @P.block.vector
    def _(vv):
        apply_pending(P, vv, f"{tag}_yT")
        for tt in range(TT):
            vv.wait_ge(P.sems["pe"], yt_pe[tt])
            vv.tensor_copy(yT[:, :, tt * 128:(tt + 1) * 128],
                           bt[:].rearrange("p (c t) -> p c t", t=128)).then_inc(P.sems["dv"], 1)

    op_pe = P.res("pe", TT)
    op_dv = P.res("dv", TT)

    @P.block.tensor
    def _(t):
        t.wait_ge(P.sems["lg"], lwo)
        for tt in range(TT):
            pr = (b[2], b[3]) if tt % 2 == 0 else (b[4], b[5])
            if tt >= 2:
                t.wait_ge(P.sems["dv"], op_dv[tt - 2])
            for nj in range(2):
                for c in range(8):
                    mm = t.matmul(pr[nj][:], yT[:, c, tt * 128:(tt + 1) * 128],
                                  wo[:, c, nj * 512:(nj + 1) * 512],
                                  start=(c == 0), stop=(c == 7))
            mm.then_inc(P.sems["pe"], 1)

    @P.block.vector
    def _(vv):
        for tt in range(TT):
            pr = (b[2], b[3]) if tt % 2 == 0 else (b[4], b[5])
            vv.wait_ge(P.sems["pe"], op_pe[tt])
            vv.tensor_add(X[:, tt * 1024:tt * 1024 + 512], pr[0][:],
                          X[:, tt * 1024:tt * 1024 + 512])
            vv.tensor_add(X[:, tt * 1024 + 512:(tt + 1) * 1024], pr[1][:],
                          X[:, tt * 1024 + 512:(tt + 1) * 1024]).then_inc(P.sems["dv"], 1)

    dbg_dump(P, "xpre", P.X[:, :], {"dv": op_dv[TT - 1]}, dt=F32)
    AB.free(f"{tag}_y", readers={"pe": yt_pe[TT - 1]})
    AB.free(f"{tag}_yT", f"{tag}_wo", readers={"pe": op_pe[TT - 1]})
    P.ready = emit_postln(P, tag, {"dv": op_dv[TT - 1]}, refresh_xt=refresh_out)



def emit_rms_toT(P, tag, gain_ap, dst, xready):
    TT = P.TT
    SC = P.SC
    stats = SC[:, 2048:2048 + 12 * TT].rearrange("p (t k) -> p t k", k=12)
    agg = SC[:, 2048 + 12 * TT:2048 + 16 * TT].rearrange("p (t k) -> p t k", k=4)
    g1, g2, g3, g3b, g3c, g4 = P.res("dv", 6)
    g5 = P.res("ac", 1)[0]

    @P.block.vector
    def _(vv):
        for snm, v_ in xready.items():
            vv.wait_ge(P.sems[snm], v_)
        for tt in range(TT):
            vv.bn_stats(stats[:, tt, 0:6], P.X[:, tt * 1024:tt * 1024 + 512])
            ins = vv.bn_stats(stats[:, tt, 6:12], P.X[:, tt * 1024 + 512:(tt + 1) * 1024])
        ins.then_inc(P.sems["dv"], 1)
        vv.wait_ge(P.sems["dv"], g1)
        for tt in range(TT):
            ins = vv.bn_aggr(agg[:, tt, 0:2], stats[:, tt, :])
        ins.then_inc(P.sems["dv"], 1)
        vv.wait_ge(P.sems["dv"], g2)
        vv.tensor_mul(agg[:, :, 2], agg[:, :, 0], agg[:, :, 0]).then_inc(P.sems["dv"], 1)
        vv.wait_ge(P.sems["dv"], g3)
        vv.tensor_add(agg[:, :, 2], agg[:, :, 2], agg[:, :, 1]).then_inc(P.sems["dv"], 1)
        vv.wait_ge(P.sems["dv"], g3b)
        vv.tensor_scalar_add(agg[:, :, 2], agg[:, :, 2], 1e-5).then_inc(P.sems["dv"], 1)
        vv.wait_ge(P.sems["dv"], g3c)
        vv.reciprocal(agg[:, :, 3], agg[:, :, 2]).then_inc(P.sems["dv"], 1)

    @P.block.scalar
    def _(sa):
        sa.wait_ge(P.sems["dv"], g4)
        sa.activation(agg[:, :, 3], agg[:, :, 3], AF.Sqrt).then_inc(P.sems["ac"], 1)

    def cast(vv, tt, out_ap):
        return vv.scalar_tensor_tensor(out_ap, P.X[:, tt * 1024:(tt + 1) * 1024],
                                       agg[:, tt, 3:4], gain_ap, ALU.mult, ALU.mult)
    return emit_toT(P, f"{tag}_h", wait={"ac": g5}, dst=dst, cast_emit=cast)


def emit_l0_ffn(P, s, upto=None, refresh_out=True):
    T, TT, NQ = P.T, P.TT, P.NQ
    AB, SC, X, b, bt = P.AB, P.SC, P.X, P.banks, P.bt
    nc = P.nc
    tag = f"f0s{s}"

    wq_d = P.din("f0_wq", [1024, 1024])
    wkva_d = P.din("f0_wkva", [1024, 160])
    wkvb_d = P.din("f0_wkvb", [128, 512])
    woe_d = P.din("f0_woe", [512, 1024])
    wg_d = P.din("f0_wg", [9, 1024, 512])
    wu_d = P.din("f0_wu", [9, 1024, 512])
    wd_d = P.din("f0_wd", [9, 512, 1024])
    wgate_d = P.din("f0_wgate", [1024, 8])
    ebias_d = P.din("f0_ebias", [1, 8])
    tcm_d = P.din("c_tcm", [128, T], BF16)
    tsm_d = P.din("c_tsm", [128, T], BF16)
    masks_d = P.din("c_masks", [128, 4 * 512], BF16)

    if not getattr(P, "f0_gains", False):
        P.f0_gains = True
        g1_d = P.din("f0_g1", [1, 1024])
        g2_d = P.din("f0_g2", [1, 1024])
        gl = P.res("ld", 2, 16)

        @P.block.sync
        def _(sy):
            sy.dma_start(out=P.AF[:, 0:1024], in_=g1_d.broadcast_to([128, 1024])
                         ).then_inc(P.sems["ld"], 16)
            sy.dma_start(out=P.AF[:, 1024:2048], in_=g2_d.broadcast_to([128, 1024])
                         ).then_inc(P.sems["ld"], 16)
        P.f0_gl = gl[-1]

        P.f0_ones = None

    xready = dict(P.ready)

    AB.alloc(f"{tag}_hT", 8 * T)
    hT = v3(AB.ap(f"{tag}_hT"), T)
    hr = emit_rms_toT(P, tag, P.AF[:, 0:1024], f"{tag}_hT",
                      {**xready, "ld": P.f0_gl})

    AB.alloc(f"{tag}_qT", 8 * T)
    qT = v3(AB.ap(f"{tag}_qT"), T)
    AB.alloc(f"{tag}_wq", 8 * 1024)
    wqv = v3(AB.ap(f"{tag}_wq"), 1024)
    AB.alloc(f"{tag}_tbl", 2 * T)
    tbl = v3(AB.ap(f"{tag}_tbl"), T)

    lw = P.res("lg", 1, 16)[0]

    @P.block.gpsimd
    def _(g):
        apply_pending(P, g, f"{tag}_wq")
        g.dma_start(out=wqv, in_=wq_d.rearrange("(c p) n -> p c n", p=128)
                    ).then_inc(P.sems["lg"], 16)

    ltl = P.res("ld", 2, 16)
    lt = ltl[-1]

    @P.block.sync
    def _(sy):
        apply_pending(P, sy, f"{tag}_tbl")
        sy.dma_start(out=tbl[:, 0, :], in_=tcm_d).then_inc(P.sems["ld"], 16)
        sy.dma_start(out=tbl[:, 1, :], in_=tsm_d).then_inc(P.sems["ld"], 16)

    qp_pe = P.res("pe", 8 * NQ)
    qp_dv = P.res("dv", 8 * NQ)

    @P.block.tensor
    def _(t):
        t.wait_ge(P.sems["lg"], lw)
        t.wait_ge(P.sems["dv"], hr["dv"])
        i = 0
        for pt in range(8):
            for tq in range(NQ):
                bank = b[i % 2]
                if i >= 2:
                    t.wait_ge(P.sems["dv"], qp_dv[i - 2])
                for c in range(8):
                    mm = t.matmul(bank[:], wqv[:, c, pt * 128:(pt + 1) * 128],
                                  hT[:, c, tq * 512:(tq + 1) * 512],
                                  start=(c == 0), stop=(c == 7))
                mm.then_inc(P.sems["pe"], 1)
                i += 1

    @P.block.vector
    def _(vv):
        apply_pending(P, vv, f"{tag}_qT")
        i = 0
        for pt in range(8):
            for tq in range(NQ):
                vv.wait_ge(P.sems["pe"], qp_pe[i])
                vv.tensor_copy(qT[:, pt, tq * 512:(tq + 1) * 512], b[i % 2][:]
                               ).then_inc(P.sems["dv"], 1)
                i += 1

    AB.free(f"{tag}_wq", readers={"pe": qp_pe[-1]})

    AB.alloc(f"{tag}_rsc", 2 * T)
    rsc = v3(AB.ap(f"{tag}_rsc"), T)
    prev = {"dv": qp_dv[-1]}
    first = True
    for c in range(8):
        r = emit_rope(P, qT[:, c, :], rsc, tbl, prev, lt,
                      rsc_name=(f"{tag}_rsc" if first else None))
        prev = {"gp": r}
        first = False
    q_rope_done = r

    AB.alloc(f"{tag}_kvl", T)
    kvl = v3(AB.ap(f"{tag}_kvl"), T)
    AB.alloc(f"{tag}_krs", T)
    KRS = AB.ap(f"{tag}_krs")
    AB.alloc(f"{tag}_wkva", 8 * 160)
    wkvav = v3(AB.ap(f"{tag}_wkva"), 160)
    AB.alloc(f"{tag}_wkvb", 512)
    wkvbv = AB.ap(f"{tag}_wkvb")

    lkvl = P.res("lg", 2, 16)
    lkv = lkvl[-1]

    @P.block.gpsimd
    def _(g):
        apply_pending(P, g, f"{tag}_wkva", f"{tag}_wkvb")
        g.dma_start(out=wkvav, in_=wkva_d.rearrange("(c p) n -> p c n", p=128)
                    ).then_inc(P.sems["lg"], 16)
        g.dma_start(out=wkvbv, in_=wkvb_d).then_inc(P.sems["lg"], 16)

    ck_pe = P.res("pe", NQ + NQ)
    ck_dv = P.res("dv", NQ + NQ + 1)

    @P.block.tensor
    def _(t):
        t.wait_ge(P.sems["lg"], lkv)
        for tq in range(NQ):
            bank = b[2 + tq % 2]
            if tq >= 2:
                t.wait_ge(P.sems["dv"], ck_dv[tq - 2])
            for c in range(8):
                mm = t.matmul(bank[:], wkvav[:, c, 0:128], hT[:, c, tq * 512:(tq + 1) * 512],
                              start=(c == 0), stop=(c == 7))
            mm.then_inc(P.sems["pe"], 1)
        for tq in range(NQ):
            bank = b[4 + tq % 2]
            if tq >= 2:
                t.wait_ge(P.sems["dv"], ck_dv[NQ + tq - 2])
            for c in range(8):
                mm = t.matmul(bank[0:32, :], wkvav[:, c, 128:160], hT[:, c, tq * 512:(tq + 1) * 512],
                              start=(c == 0), stop=(c == 7))
            mm.then_inc(P.sems["pe"], 1)

    @P.block.vector
    def _(vv):
        apply_pending(P, vv, f"{tag}_kvl", f"{tag}_krs")
        for tq in range(NQ):
            vv.wait_ge(P.sems["pe"], ck_pe[tq])
            vv.tensor_copy(kvl[:, 0, tq * 512:(tq + 1) * 512], b[2 + tq % 2][:]
                           ).then_inc(P.sems["dv"], 1)
        for tq in range(NQ):
            vv.wait_ge(P.sems["pe"], ck_pe[NQ + tq])
            vv.tensor_copy(KRS[32:64, tq * 512:(tq + 1) * 512], b[4 + tq % 2][0:32, :]
                           ).then_inc(P.sems["dv"], 1)
        vv.memset(KRS[0:32, :], 0.0)
        vv.memset(KRS[64:128, :], 0.0).then_inc(P.sems["dv"], 1)

    kro = emit_rope(P, KRS, rsc, tbl, {"dv": ck_dv[-1], "gp": q_rope_done}, lt)
    AB.free(f"{tag}_rsc", f"{tag}_tbl", readers={"gp": kro})
    AB.free(f"{tag}_wkva", readers={"pe": ck_pe[-1]})

    AB.alloc(f"{tag}_kva", 4 * T)
    kva = v3(AB.ap(f"{tag}_kva"), T)
    TTH = TT // 2
    AB.alloc(f"{tag}_v0", TTH * 512)
    AB.alloc(f"{tag}_v1", TTH * 512)
    _vh = [v3(AB.ap(f"{tag}_v0"), 512), v3(AB.ap(f"{tag}_v1"), 512)]
    def vsl(ki, sl):
        return _vh[ki // TTH][:, ki % TTH, sl]

    kb_pe = P.res("pe", 4 * NQ)
    kb_dv = P.res("dv", 4 * NQ)

    @P.block.tensor
    def _(t):
        t.wait_ge(P.sems["dv"], ck_dv[NQ - 1])
        t.wait_ge(P.sems["dv"], qp_dv[-1])
        i = 0
        for st_ in range(4):
            for tq in range(NQ):
                bank = b[i % 2]
                if i >= 2:
                    t.wait_ge(P.sems["dv"], kb_dv[i - 2])
                t.matmul(bank[:], wkvbv[:, st_ * 128:(st_ + 1) * 128],
                         kvl[:, 0, tq * 512:(tq + 1) * 512], start=True, stop=True
                         ).then_inc(P.sems["pe"], 1)
                i += 1

    @P.block.vector
    def _(vv):
        apply_pending(P, vv, f"{tag}_kva")
        i = 0
        for st_ in range(4):
            for tq in range(NQ):
                vv.wait_ge(P.sems["pe"], kb_pe[i])
                vv.tensor_copy(kva[:, st_, tq * 512:(tq + 1) * 512], b[i % 2][:]
                               ).then_inc(P.sems["dv"], 1)
                i += 1

    vt_pe = P.res("pe", TT)
    vt_dv = P.res("dv", TT)

    @P.block.tensor
    def _(t):
        t.wait_ge(P.sems["dv"], ck_dv[-1])
        for tt in range(TT):
            bank = b[2 + tt % 2]
            if tt >= 2:
                t.wait_ge(P.sems["dv"], vt_dv[tt - 2])
            t.matmul(bank[:], kvl[:, 0, tt * 128:(tt + 1) * 128], wkvbv[:],
                     start=True, stop=True).then_inc(P.sems["pe"], 1)

    @P.block.vector
    def _(vv):
        apply_pending(P, vv, f"{tag}_v0", f"{tag}_v1")
        for tt in range(TT):
            vv.wait_ge(P.sems["pe"], vt_pe[tt])
            vv.tensor_copy(vsl(tt, slice(0, 512)), b[2 + tt % 2][:]).then_inc(P.sems["dv"], 1)

    AB.free(f"{tag}_hT", readers={"pe": vt_pe[-1]})
    AB.free(f"{tag}_kvl", f"{tag}_wkvb", readers={"pe": vt_pe[-1]})

    AB.alloc(f"{tag}_y", TT * 512)
    y = v3(AB.ap(f"{tag}_y"), 512)
    AB.alloc(f"{tag}_woe", 4 * 1024)
    woev = v3(AB.ap(f"{tag}_woe"), 1024)
    AB.alloc(f"{tag}_mask", 4 * 512)
    msk = v3(AB.ap(f"{tag}_mask"), 512)
    AB.alloc(f"{tag}_strip", 4 * 512)
    strip = v3(AB.ap(f"{tag}_strip"), 512)
    AB.alloc(f"{tag}_ktp", T)
    ktp = AB.ap(f"{tag}_ktp")

    lm = P.res("ld", 1, 16)[0]
    lwo = P.res("lg", 1, 16)[0]

    @P.block.sync
    def _(sy):
        apply_pending(P, sy, f"{tag}_mask")
        sy.dma_start(out=AB.ap(f"{tag}_mask"), in_=masks_d).then_inc(P.sems["ld"], 16)

    @P.block.gpsimd
    def _(g):
        apply_pending(P, g, f"{tag}_woe")
        g.dma_start(out=woev, in_=woe_d.rearrange("(c p) n -> p c n", p=128)
                    ).then_inc(P.sems["lg"], 16)

    rd = SC[:, 1664:1672]
    prev_ep = None
    prev_sc = None
    for p_ in range(8):
        stA, rA = p_ // 2, (2 * p_ % 4) * 32
        stB, rB = p_ // 2, ((2 * p_ + 1) % 4) * 32
        ktb = P.res("dv", 1)[0]

        @P.block.vector
        def _(vv, p_=p_, stA=stA, rA=rA, stB=stB, rB=rB, prev_sc=prev_sc):
            if p_ == 0:
                apply_pending(P, vv, f"{tag}_ktp")
                vv.wait_ge(P.sems["dv"], kb_dv[-1])
                vv.wait_ge(P.sems["gp"], kro)
            if prev_sc is not None:
                vv.wait_ge(P.sems["pe"], prev_sc)
            vv.tensor_copy(ktp[0:32, :], kva[rA:rA + 32, stA, :])
            vv.tensor_copy(ktp[32:64, :], KRS[32:64, :])
            vv.tensor_copy(ktp[64:96, :], kva[rB:rB + 32, stB, :])
            vv.tensor_copy(ktp[96:128, :], KRS[32:64, :]).then_inc(P.sems["dv"], 1)

        for qj in range(NQ):
            kmax = min(TT, 4 * (qj + 1))
            diag = sorted(set(range(4 * qj, kmax)))
            pe_v = P.res("pe", 2 * kmax)
            s_idx, a_idx = {}, {}
            n_ = 0
            for ki in range(kmax):
                s_idx[ki] = pe_v[n_]; n_ += 1
                if ki >= 1:
                    a_idx[ki - 1] = pe_v[n_]; n_ += 1
            a_idx[kmax - 1] = pe_v[n_]
            ac_v = P.res("ac", kmax)
            dvl = P.res("dv", len(diag) + 2)
            dv_m = {ki: dvl[i] for i, ki in enumerate(diag)}
            ep1, ep2 = dvl[-2], dvl[-1]

            def emit_avm(t, ki, kmax, diag, ac_v, dv_m, first_out, p_l):
                t.wait_ge(P.sems["ac"], ac_v[ki])
                if ki in diag:
                    t.wait_ge(P.sems["dv"], dv_m[ki])
                sidx = (ki % 2) * 2
                for j in range(4):
                    t.matmul(b[0][:, j * 64:j * 64 + 32],
                             strip[:, sidx, j * 128:(j + 1) * 128],
                             vsl(ki, slice(2 * p_l * 32, (2 * p_l + 1) * 32)),
                             start=first_out, stop=False)
                    t.matmul(b[0][:, j * 64 + 32:j * 64 + 33],
                             strip[:, sidx, j * 128:(j + 1) * 128],
                             P.onesb[:, 0:1], start=False,
                             stop=(ki == kmax - 1 and j == 3))
                    t.matmul(b[1][:, j * 64:j * 64 + 32],
                             strip[:, sidx + 1, j * 128:(j + 1) * 128],
                             vsl(ki, slice((2 * p_l + 1) * 32, (2 * p_l + 2) * 32)),
                             start=first_out, stop=False)
                    mm = t.matmul(b[1][:, j * 64 + 32:j * 64 + 33],
                                  strip[:, sidx + 1, j * 128:(j + 1) * 128],
                                  P.onesb[:, 0:1], start=False,
                                  stop=(ki == kmax - 1 and j == 3))
                    first_out = False
                mm.then_inc(P.sems["pe"], 1)

            @P.block.tensor
            def _(t, p_=p_, qj=qj, kmax=kmax, diag=diag, ac_v=ac_v,
                  dv_m=dv_m, prev_ep=prev_ep, ktb=ktb):
                if p_ == 0 and qj == 0:
                    t.wait_ge(P.sems["gp"], q_rope_done)
                    t.wait_ge(P.sems["ld"], lm)
                    t.wait_ge(P.sems["ld"], P.const_ready)
                    t.wait_ge(P.sems["dv"], vt_dv[TT - 1])
                if qj == 0:
                    t.wait_ge(P.sems["dv"], ktb)
                if prev_ep is not None:
                    t.wait_ge(P.sems["dv"], prev_ep)
                for ki in range(kmax):
                    sb_, sb2 = (b[2], b[3]) if ki % 2 == 0 else (b[4], b[5])
                    if ki >= 2:
                        t.wait_ge(P.sems["ac"], ac_v[ki - 2])
                    t.matmul(sb_[:], ktp[0:64, ki * 128:(ki + 1) * 128],
                             qT[0:64, p_, qj * 512:(qj + 1) * 512], start=True, stop=True)
                    t.matmul(sb2[:], ktp[64:128, ki * 128:(ki + 1) * 128],
                             qT[64:128, p_, qj * 512:(qj + 1) * 512], start=True, stop=True
                             ).then_inc(P.sems["pe"], 1)
                    if ki >= 1:
                        emit_avm(t, ki - 1, kmax, diag, ac_v, dv_m, ki == 1, p_)
                emit_avm(t, kmax - 1, kmax, diag, ac_v, dv_m, kmax == 1, p_)

            @P.block.scalar
            def _(sa, kmax=kmax, s_idx=s_idx, a_idx=a_idx, p_=p_, qj=qj):
                if p_ == 0 and qj == 0:
                    apply_pending(P, sa, f"{tag}_strip")
                for ki in range(kmax):
                    sidx = (ki % 2) * 2
                    sa.wait_ge(P.sems["pe"], s_idx[ki])
                    if ki >= 2:
                        sa.wait_ge(P.sems["pe"], a_idx[ki - 2])
                    sb_, sb2 = (b[2], b[3]) if ki % 2 == 0 else (b[4], b[5])
                    sa.activation(strip[:, sidx, :], sb_[:], AF.Exp, scale=0.125)
                    sa.activation(strip[:, sidx + 1, :], sb2[:], AF.Exp, scale=0.125
                                  ).then_inc(P.sems["ac"], 1)

            @P.block.vector
            def _(vv, kmax=kmax, qj=qj, diag=diag, ac_v=ac_v, a_idx=a_idx,
                  ep1=ep1, ep2=ep2, p_=p_):
                if p_ == 0 and qj == 0:
                    apply_pending(P, vv, f"{tag}_y")
                    vv.wait_ge(P.sems["ld"], lm)
                for ki in diag:
                    sidx = (ki % 2) * 2
                    o_var = ki - 4 * qj
                    vv.wait_ge(P.sems["ac"], ac_v[ki])
                    vv.tensor_mul(strip[:, sidx, :], strip[:, sidx, :], msk[:, o_var, :])
                    vv.tensor_mul(strip[:, sidx + 1, :], strip[:, sidx + 1, :],
                                  msk[:, o_var, :]).then_inc(P.sems["dv"], 1)
                vv.wait_ge(P.sems["pe"], a_idx[kmax - 1])
                vv.reciprocal(rd[:, 0:4], bass.AP(b[0], 32, [[512, 128], [64, 4]]))
                vv.reciprocal(rd[:, 4:8], bass.AP(b[1], 32, [[512, 128], [64, 4]])
                              ).then_inc(P.sems["dv"], 1)
                vv.wait_ge(P.sems["dv"], ep1)
                for j in range(4):
                    vv.tensor_scalar_mul(y[:, qj * 4 + j, (2 * p_) * 32:(2 * p_ + 1) * 32],
                                         b[0][:, j * 64:j * 64 + 32], rd[:, j:j + 1])
                    ins = vv.tensor_scalar_mul(y[:, qj * 4 + j, (2 * p_ + 1) * 32:(2 * p_ + 2) * 32],
                                               b[1][:, j * 64:j * 64 + 32], rd[:, 4 + j:5 + j])
                ins.then_inc(P.sems["dv"], 1)
            prev_ep = ep2
        prev_sc = s_idx[kmax - 1]

    AB.free(f"{tag}_qT", f"{tag}_strip", f"{tag}_ktp", f"{tag}_kva", f"{tag}_krs",
            f"{tag}_v0", f"{tag}_v1", readers={"pe": P.cnt["pe"]})
    AB.free(f"{tag}_mask", readers={"dv": prev_ep})

    if upto == "C2":
        dbg_dump(P, "ymla", AB.ap(f"{tag}_y"), {"dv": prev_ep})
        dbg_dump(P, "qTm", AB.ap(f"{tag}_qT"), {"gp": q_rope_done})
        dbg_dump(P, "krsm", AB.ap(f"{tag}_krs"), {"gp": kro})
        dbg_dump(P, "kvam", AB.ap(f"{tag}_kva"), {"dv": kb_dv[-1]})
        return

    AB.alloc(f"{tag}_ymT", 4 * T)
    ymT = v3(AB.ap(f"{tag}_ymT"), T)
    yt_pe = P.res("pe", TT)
    yt_dv = P.res("dv", TT)

    @P.block.tensor
    def _(t):
        t.wait_ge(P.sems["dv"], prev_ep)
        for tt in range(TT):
            if tt >= 1:
                t.wait_ge(P.sems["dv"], yt_dv[tt - 1])
            for c in range(4):
                mm = t.transpose(bt[:, c * 128:(c + 1) * 128], y[:, tt, c * 128:(c + 1) * 128],
                                 P.ident[:])
            mm.then_inc(P.sems["pe"], 1)

    @P.block.vector
    def _(vv):
        apply_pending(P, vv, f"{tag}_ymT")
        for tt in range(TT):
            vv.wait_ge(P.sems["pe"], yt_pe[tt])
            vv.tensor_copy(ymT[:, :, tt * 128:(tt + 1) * 128],
                           bt[:, 0:512].rearrange("p (c t) -> p c t", t=128)
                           ).then_inc(P.sems["dv"], 1)

    AB.alloc(f"{tag}_hnT", 8 * T)
    hnT = v3(AB.ap(f"{tag}_hnT"), T)
    AB.alloc(f"{tag}_hcb", 2048)
    hcb_off = AB.blocks[f"{tag}_hcb"][0]
    hcb = [AB.t[:, hcb_off:hcb_off + 1024], AB.t[:, hcb_off + 1024:hcb_off + 2048]]
    xm = [SC[:, 2048:3072], SC[:, 3072:4096]]
    est = SC[:, 1672:1672 + 16 * TT].rearrange("p (t k) -> p t k", k=16)

    opv = P.res("pe", 2 * TT)
    op_pe = {tt: opv[2 * tt] for tt in range(TT)}
    ht_pe = {tt: opv[2 * tt + 1] for tt in range(TT)}
    dvv = P.res("dv", 8 * TT)
    e_v = {(tt, k): dvv[8 * tt + k] for tt in range(TT) for k in range(8)}
    xm_dv = {tt: e_v[(tt, 0)] for tt in range(TT)}
    ms_dv = {tt: e_v[(tt, 5)] for tt in range(TT)}
    hn_dv = {tt: e_v[(tt, 6)] for tt in range(TT)}
    he_dv = {tt: e_v[(tt, 7)] for tt in range(TT)}
    rr_ac = P.res("ac", TT)
    sq_ac = {tt: rr_ac[tt] for tt in range(TT)}

    @P.block.tensor
    def _(t):
        t.wait_ge(P.sems["lg"], lwo)
        t.wait_ge(P.sems["dv"], yt_dv[TT - 1])
        for tt in range(TT):
            pr = (b[2], b[3]) if tt % 2 == 0 else (b[4], b[5])
            if tt >= 2:
                t.wait_ge(P.sems["dv"], xm_dv[tt - 2])
            for nj in range(2):
                for c in range(4):
                    mm = t.matmul(pr[nj][:], ymT[:, c, tt * 128:(tt + 1) * 128],
                                  woev[:, c, nj * 512:(nj + 1) * 512],
                                  start=(c == 0), stop=(c == 3))
            mm.then_inc(P.sems["pe"], 1)
            t.wait_ge(P.sems["dv"], hn_dv[tt])
            if tt >= 1:
                t.wait_ge(P.sems["dv"], he_dv[tt - 1])
            for c in range(8):
                mm = t.transpose(bt[:, c * 128:(c + 1) * 128],
                                 hcb[tt % 2][:, c * 128:(c + 1) * 128], P.ident[:])
            mm.then_inc(P.sems["pe"], 1)

    @P.block.vector
    def _(vv):
        apply_pending(P, vv, f"{tag}_hnT", f"{tag}_hcb")
        for tt in range(TT):
            pr = (b[2], b[3]) if tt % 2 == 0 else (b[4], b[5])
            vv.wait_ge(P.sems["pe"], op_pe[tt])
            x_t = X[:, tt * 1024:(tt + 1) * 1024]
            vv.tensor_add(xm[tt % 2][:, 0:512], pr[0][:], x_t[:, 0:512])
            vv.tensor_add(xm[tt % 2][:, 512:1024], pr[1][:], x_t[:, 512:1024]
                          ).then_inc(P.sems["dv"], 1)
            vv.wait_ge(P.sems["dv"], e_v[(tt, 0)])
            vv.bn_stats(est[:, tt, 0:6], xm[tt % 2][:, 0:512])
            vv.bn_stats(est[:, tt, 6:12], xm[tt % 2][:, 512:1024]).then_inc(P.sems["dv"], 1)
            vv.wait_ge(P.sems["dv"], e_v[(tt, 1)])
            vv.bn_aggr(est[:, tt, 12:14], est[:, tt, 0:12]).then_inc(P.sems["dv"], 1)
            vv.wait_ge(P.sems["dv"], e_v[(tt, 2)])
            vv.scalar_tensor_tensor(est[:, tt, 14:15], est[:, tt, 12:13], est[:, tt, 12:13],
                                    est[:, tt, 13:14], ALU.mult, ALU.add).then_inc(P.sems["dv"], 1)
            vv.wait_ge(P.sems["dv"], e_v[(tt, 3)])
            vv.tensor_scalar_add(est[:, tt, 14:15], est[:, tt, 14:15], 1e-5
                                 ).then_inc(P.sems["dv"], 1)
            vv.wait_ge(P.sems["dv"], e_v[(tt, 4)])
            vv.reciprocal(est[:, tt, 15:16], est[:, tt, 14:15]).then_inc(P.sems["dv"], 1)
            vv.wait_ge(P.sems["ac"], sq_ac[tt])
            vv.scalar_tensor_tensor(hcb[tt % 2], xm[tt % 2], est[:, tt, 15:16],
                                    P.AF[:, 1024:2048], ALU.mult, ALU.mult)
            vv.tensor_add(X[:, tt * 1024:(tt + 1) * 1024],
                          xm[tt % 2], X[:, tt * 1024:(tt + 1) * 1024]).then_inc(P.sems["dv"], 1)
            vv.wait_ge(P.sems["pe"], ht_pe[tt])
            vv.tensor_copy(hnT[:, :, tt * 128:(tt + 1) * 128],
                           bt[:].rearrange("p (c t) -> p c t", t=128)).then_inc(P.sems["dv"], 1)

    @P.block.scalar
    def _(sa):
        for tt in range(TT):
            sa.wait_ge(P.sems["dv"], ms_dv[tt])
            sa.activation(est[:, tt, 15:16], est[:, tt, 15:16], AF.Sqrt
                          ).then_inc(P.sems["ac"], 1)

    AB.free(f"{tag}_y", readers={"pe": yt_pe[TT - 1]})
    AB.free(f"{tag}_ymT", f"{tag}_woe", readers={"pe": op_pe[TT - 1]})
    AB.free(f"{tag}_hcb", readers={"pe": ht_pe[TT - 1]})
    moe_in_ready = {"dv": he_dv[TT - 1]}

    if upto == "M":
        dbg_dump(P, "hnT", AB.ap(f"{tag}_hnT"), moe_in_ready)
        dbg_dump(P, "xmid2", X[:, :], moe_in_ready, dt=F32)
        return

    AB.alloc(f"{tag}_wgu", 2 * 8 * 512)
    wguv = AB.ap(f"{tag}_wgu").rearrange("p (w c n) -> p w c n", w=2, n=512)
    AB.alloc(f"{tag}_wd", 4 * 1024)
    wdv = v3(AB.ap(f"{tag}_wd"), 1024)
    AB.alloc(f"{tag}_gub", 3 * 4 * 512)
    gub = AB.ap(f"{tag}_gub").rearrange("p (k i n) -> p k i n", k=3, n=512)
    AB.alloc(f"{tag}_wgate", 8 * 8)
    wgv = v3(AB.ap(f"{tag}_wgate"), 8)
    combw = SC[:, 0:8 * TT].rearrange("p (t e) -> p t e", e=8)
    mx8 = SC[:, 8 * TT:8 * TT + 8]
    rt8 = SC[:, 8 * TT + 8:8 * TT + 24]
    iota8 = SC[:, 8 * TT + 24:8 * TT + 32]
    lg8 = SC[:, 8 * TT + 32:8 * TT + 40]
    ebc = P.AF[:, 2048:2056]

    lgt = P.res("lg", 1, 16)[0]
    gpi = P.res("gp", 1)[0]
    lde = P.res("ld", 1, 16)[0]

    @P.block.sync
    def _(sy):
        sy.dma_start(out=ebc, in_=ebias_d.broadcast_to([128, 8])).then_inc(P.sems["ld"], 16)

    @P.block.gpsimd
    def _(g):
        apply_pending(P, g, f"{tag}_wgate")
        g.dma_start(out=wgv, in_=wgate_d.rearrange("(c p) n -> p c n", p=128)
                    ).then_inc(P.sems["lg"], 16)
        g.iota(iota8, [[1, 8]], base=0, channel_multiplier=0,
               allow_small_or_imprecise_dtypes=True).then_inc(P.sems["gp"], 1)

    rt_pe = P.res("pe", TT)
    rt_dv = P.res("dv", 7 * TT)
    rt_ac = P.res("ac", TT)

    @P.block.tensor
    def _(t):
        t.wait_ge(P.sems["lg"], lgt)
        for tt in range(TT):
            bank = b[tt % 2]
            if tt >= 2:
                t.wait_ge(P.sems["dv"], rt_dv[7 * (tt - 2) + 2])
            t.wait_ge(P.sems["dv"], he_dv[tt])
            for c in range(8):
                mm = t.matmul(bank[:, 0:8], hnT[:, c, tt * 128:(tt + 1) * 128],
                              wgv[:, c, :], start=(c == 0), stop=(c == 7))
            mm.then_inc(P.sems["pe"], 1)

    @P.block.vector
    def _(vv):
        vv.wait_ge(P.sems["gp"], gpi)
        for tt in range(TT):
            bank = b[tt % 2]
            vv.wait_ge(P.sems["pe"], rt_pe[tt])
            if tt == 0:
                vv.wait_ge(P.sems["ld"], lde)
            if tt >= 1:
                vv.wait_ge(P.sems["dv"], rt_dv[7 * (tt - 1) + 6])
            vv.tensor_add(lg8, bank[:, 0:8], ebc).then_inc(P.sems["dv"], 1)
            vv.wait_ge(P.sems["dv"], rt_dv[7 * tt])
            vv.max(mx8, lg8).then_inc(P.sems["dv"], 1)
            vv.wait_ge(P.sems["dv"], rt_dv[7 * tt + 1])
            vv.max_index(P.mi8[:], mx8, lg8)
            vv.tensor_sub(rt8[:, 0:1], mx8[:, 1:2], mx8[:, 0:1]).then_inc(P.sems["dv"], 1)
            vv.wait_ge(P.sems["ac"], rt_ac[tt])
            vv.tensor_scalar(rt8[:, 2:3], rt8[:, 1:2], -1.0, 1.0, ALU.mult, ALU.add)
            vv.tensor_copy(rt8[:, 3:4], P.mi8[:, 0:1])
            vv.tensor_copy(rt8[:, 4:5], P.mi8[:, 1:2]).then_inc(P.sems["dv"], 1)
            vv.wait_ge(P.sems["dv"], rt_dv[7 * tt + 3])
            vv.tensor_scalar(combw[:, tt, :], iota8, rt8[:, 3:4], None, ALU.is_equal)
            vv.tensor_scalar(rt8[:, 8:16], iota8, rt8[:, 4:5], None, ALU.is_equal
                             ).then_inc(P.sems["dv"], 1)
            vv.wait_ge(P.sems["dv"], rt_dv[7 * tt + 4])
            vv.tensor_scalar_mul(combw[:, tt, :], combw[:, tt, :], rt8[:, 2:3]
                                 ).then_inc(P.sems["dv"], 1)
            vv.wait_ge(P.sems["dv"], rt_dv[7 * tt + 5])
            vv.scalar_tensor_tensor(combw[:, tt, :], rt8[:, 8:16], rt8[:, 1:2],
                                    combw[:, tt, :], ALU.mult, ALU.add
                                    ).then_inc(P.sems["dv"], 1)

    @P.block.scalar
    def _(sa):
        for tt in range(TT):
            sa.wait_ge(P.sems["dv"], rt_dv[7 * tt + 2])
            sa.activation(rt8[:, 1:2], rt8[:, 0:1], AF.Sigmoid).then_inc(P.sems["ac"], 1)
    cw_dv = P.cnt["dv"]

    lgw = P.res("lg", 3 * 9, 16)
    prev_gu_last = None
    prev_dn_last = None
    prev_dn_dv = None
    for e in range(9):
        @P.block.gpsimd
        def _(g, e=e, prev_gu_last=prev_gu_last, prev_dn_last=prev_dn_last):
            if e == 0:
                apply_pending(P, g, f"{tag}_wgu", f"{tag}_wd", f"{tag}_gub")
            if prev_gu_last is not None:
                g.wait_ge(P.sems["pe"], prev_gu_last)
            g.dma_start(out=wguv[:, 0, :, :], in_=wg_d[e].rearrange("(c p) n -> p c n", p=128)
                        ).then_inc(P.sems["lg"], 16)
            g.dma_start(out=wguv[:, 1, :, :], in_=wu_d[e].rearrange("(c p) n -> p c n", p=128)
                        ).then_inc(P.sems["lg"], 16)
            if prev_dn_last is not None:
                g.wait_ge(P.sems["pe"], prev_dn_last)
            g.dma_start(out=wdv, in_=wd_d[e].rearrange("(c p) n -> p c n", p=128)
                        ).then_inc(P.sems["lg"], 16)

        gu_ac = P.res("ac", 4 * NQ)
        gu_dv = P.res("dv", 2 * 4 * NQ)
        dn_dv = P.res("dv", 4 * NQ)
        P.cnt["dv"] -= 3 * 4 * NQ
        dvo = P.res("dv", 3 * 4 * NQ)
        gu_dv, dn_dv = {}, {}
        k_ = 0
        for tq in range(NQ):
            for it in range(4):
                gu_dv[2 * (tq * 4 + it)] = dvo[k_]; k_ += 1
                gu_dv[2 * (tq * 4 + it) + 1] = dvo[k_]; k_ += 1
            for j in range(4):
                dn_dv[tq * 4 + j] = dvo[k_]; k_ += 1
        pev = P.res("pe", 8 * NQ)
        gu_pe, dn_pe = {}, {}
        k_ = 0
        for tq in range(NQ):
            for it in range(4):
                gu_pe[tq * 4 + it] = pev[k_]; k_ += 1
            for j in range(4):
                dn_pe[tq * 4 + j] = pev[k_]; k_ += 1

        @P.block.tensor
        def _(t, e=e, gu_pe=gu_pe, dn_pe=dn_pe, gu_ac=gu_ac, gu_dv=gu_dv, dn_dv=dn_dv,
              prev_dn_dv=prev_dn_dv):
            t.wait_ge(P.sems["lg"], lgw[3 * e + 1])
            if e == 0:
                t.wait_ge(P.sems["dv"], cw_dv)
            for tq in range(NQ):
                for it in range(4):
                    i = tq * 4 + it
                    bank, bank2 = (b[2], b[3]) if i % 2 == 0 else (b[4], b[5])
                    if i >= 2:
                        t.wait_ge(P.sems["ac"], gu_ac[i - 2])
                        t.wait_ge(P.sems["dv"], gu_dv[2 * (i - 2)])
                    for c in range(8):
                        t.matmul(bank[:], wguv[:, 0, c, it * 128:(it + 1) * 128],
                                 hnT[:, c, tq * 512:(tq + 1) * 512],
                                 start=(c == 0), stop=(c == 7))
                    for c in range(8):
                        mm = t.matmul(bank2[:], wguv[:, 1, c, it * 128:(it + 1) * 128],
                                      hnT[:, c, tq * 512:(tq + 1) * 512],
                                      start=(c == 0), stop=(c == 7))
                    mm.then_inc(P.sems["pe"], 1)
                t.wait_ge(P.sems["lg"], lgw[3 * e + 2])
                t.wait_ge(P.sems["dv"], gu_dv[2 * (tq * 4 + 3) + 1])
                for j in range(4):
                    tt = tq * 4 + j
                    if tt >= 1:
                        t.wait_ge(P.sems["dv"], dn_dv[tt - 1])
                    elif prev_dn_dv is not None:
                        t.wait_ge(P.sems["dv"], prev_dn_dv)
                    for i2 in range(4):
                        t.matmul(b[0][:], gub[:, 2, i2, j * 128:(j + 1) * 128],
                                 wdv[:, i2, 0:512], start=(i2 == 0), stop=(i2 == 3))
                    for i2 in range(4):
                        mm = t.matmul(b[1][:], gub[:, 2, i2, j * 128:(j + 1) * 128],
                                      wdv[:, i2, 512:1024], start=(i2 == 0), stop=(i2 == 3))
                    mm.then_inc(P.sems["pe"], 1)

        @P.block.scalar
        def _(sa, e=e, gu_pe=gu_pe, dn_pe=dn_pe, prev_dn_last=prev_dn_last):
            if e == 0:
                apply_pending(P, sa, f"{tag}_gub")
            for tq in range(NQ):
                for it in range(4):
                    i = tq * 4 + it
                    sa.wait_ge(P.sems["pe"], gu_pe[i])
                    if tq >= 1:
                        sa.wait_ge(P.sems["pe"], dn_pe[tq * 4 - 1])
                    elif prev_dn_last is not None:
                        sa.wait_ge(P.sems["pe"], prev_dn_last)
                    bank = b[2] if i % 2 == 0 else b[4]
                    sa.activation(gub[:, 0, it, :], bank[:], AF.Silu).then_inc(P.sems["ac"], 1)

        @P.block.vector
        def _(vv, e=e, gu_pe=gu_pe, dn_pe=dn_pe, gu_ac=gu_ac, gu_dv=gu_dv, dn_dv=dn_dv,
              prev_dn_last=prev_dn_last):
            if e == 0:
                apply_pending(P, vv, f"{tag}_gub")
            for tq in range(NQ):
                for it in range(4):
                    i = tq * 4 + it
                    vv.wait_ge(P.sems["pe"], gu_pe[i])
                    if tq == 0 and prev_dn_last is not None:
                        vv.wait_ge(P.sems["pe"], prev_dn_last)
                    if tq >= 1:
                        vv.wait_ge(P.sems["pe"], dn_pe[tq * 4 - 1])
                    bank2 = b[3] if i % 2 == 0 else b[5]
                    vv.tensor_copy(gub[:, 1, it, :], bank2[:]).then_inc(P.sems["dv"], 1)
                    vv.wait_ge(P.sems["dv"], gu_dv[2 * i])
                    vv.wait_ge(P.sems["ac"], gu_ac[i])
                    vv.tensor_mul(gub[:, 2, it, :], gub[:, 0, it, :], gub[:, 1, it, :]
                                  ).then_inc(P.sems["dv"], 1)
                for j in range(4):
                    tt = tq * 4 + j
                    vv.wait_ge(P.sems["pe"], dn_pe[tt])
                    x_t = X[:, tt * 1024:(tt + 1) * 1024]
                    cwc = 1.0 if e == 0 else combw[:, tt, e - 1:e]
                    vv.scalar_tensor_tensor(x_t[:, 0:512], b[0][:], cwc, x_t[:, 0:512],
                                            ALU.mult, ALU.add)
                    vv.scalar_tensor_tensor(x_t[:, 512:1024], b[1][:], cwc, x_t[:, 512:1024],
                                            ALU.mult, ALU.add).then_inc(P.sems["dv"], 1)

        prev_gu_last = gu_pe[4 * NQ - 1]
        prev_dn_last = dn_pe[4 * NQ - 1]
        prev_dn_dv = dn_dv[4 * NQ - 1]

    AB.free(f"{tag}_hnT", f"{tag}_wgu", f"{tag}_wd", f"{tag}_gub",
            readers={"pe": P.cnt["pe"]})
    AB.free(f"{tag}_wgate", readers={"pe": rt_pe[TT - 1]})
    P.ready = emit_postln(P, tag, {"dv": prev_dn_dv}, refresh_xt=refresh_out)




def pack_l1_attn(Wq, bq, Wk, bk, Wv, bv, Wo, bo):
    c = np.ascontiguousarray
    return {"a1_wq": c(Wq.T, np.float32), "a1_wk": c(Wk.T, np.float32),
            "a1_wv": c(Wv.T, np.float32), "a1_wo": c(Wo.T, np.float32),
            "a1_bq": c(bq.reshape(8, 128).T, np.float32),
            "a1_brow": c(np.stack([bk, bv, bo]), np.float32)}


def emit_l1_attn(P, s, upto=None):
    T, TT, NQ = P.T, P.TT, P.NQ
    AB, SC, X, b, bt = P.AB, P.SC, P.X, P.banks, P.bt
    tag = f"a1s{s}"

    wq_d = P.din("a1_wq", [1024, 1024])
    wk_d = P.din("a1_wk", [1024, 1024])
    wv_d = P.din("a1_wv", [1024, 1024])
    wo_d = P.din("a1_wo", [1024, 1024])
    bq_d = P.din("a1_bq", [128, 8])
    brow_d = P.din("a1_brow", [3, 1024])
    sc_d = P.din("c_sincos", [128, 2 * TT])

    xready = dict(P.ready)
    xT = v3(AB.ap("xT"), T)

    scc = P.AF[:, 2056:2056 + 2 * TT]
    bqc = SC[:, 3968:3976]
    AB.alloc(f"{tag}_brow", 3072)
    brow_ap = AB.ap(f"{tag}_brow")
    brow = {i: brow_ap[0:1, i * 1024:(i + 1) * 1024] for i in range(3)}
    lc = P.res("ld", 2, 16)
    lbr = P.res("lg", 3, 16)[-1]

    @P.block.sync
    def _(sy):
        for snm, v_ in xready.items():
            sy.wait_ge(P.sems[snm], v_)
        sy.dma_start(out=scc, in_=sc_d).then_inc(P.sems["ld"], 16)
        sy.dma_start(out=bqc, in_=bq_d).then_inc(P.sems["ld"], 16)

    @P.block.gpsimd
    def _(g):
        apply_pending(P, g, f"{tag}_brow")
        g.dma_start(out=brow[0], in_=brow_d[0:1, :]).then_inc(P.sems["lg"], 16)
        g.dma_start(out=brow[1], in_=brow_d[1:2, :]).then_inc(P.sems["lg"], 16)
        g.dma_start(out=brow[2], in_=brow_d[2:3, :]).then_inc(P.sems["lg"], 16)

    AB.alloc(f"{tag}_qT", 8 * T)
    qT = v3(AB.ap(f"{tag}_qT"), T)
    AB.alloc(f"{tag}_wq", 8 * 1024)
    wqv = v3(AB.ap(f"{tag}_wq"), 1024)
    lwq = P.res("lg", 1, 16)[0]

    @P.block.gpsimd
    def _(g):
        apply_pending(P, g, f"{tag}_wq")
        g.dma_start(out=wqv, in_=wq_d.rearrange("(c p) n -> p c n", p=128)
                    ).then_inc(P.sems["lg"], 16)

    qp_pe = P.res("pe", 8 * NQ)
    qp_ac = P.res("ac", 8 * NQ)

    @P.block.tensor
    def _(t):
        t.wait_ge(P.sems["lg"], lwq)
        for snm, v_ in xready.items():
            t.wait_ge(P.sems[snm], v_)
        i = 0
        for ct in range(8):
            for tq in range(NQ):
                bank = b[i % 2]
                if i >= 2:
                    t.wait_ge(P.sems["ac"], qp_ac[i - 2])
                for c in range(8):
                    mm = t.matmul(bank[:], wqv[:, c, ct * 128:(ct + 1) * 128],
                                  xT[:, c, tq * 512:(tq + 1) * 512],
                                  start=(c == 0), stop=(c == 7))
                mm.then_inc(P.sems["pe"], 1)
                i += 1

    @P.block.scalar
    def _(sa):
        apply_pending(P, sa, f"{tag}_qT")
        sa.wait_ge(P.sems["ld"], lc[1])
        i = 0
        for ct in range(8):
            for tq in range(NQ):
                sa.wait_ge(P.sems["pe"], qp_pe[i])
                sa.activation(qT[:, ct, tq * 512:(tq + 1) * 512], b[i % 2][:],
                              AF.Relu, bias=bqc[:, ct:ct + 1]).then_inc(P.sems["ac"], 1)
                i += 1
    AB.free(f"{tag}_wq", readers={"ac": qp_ac[-1]})
    if upto == "Q":
        dbg_dump(P, "a1qT", AB.ap(f"{tag}_qT"), {"ac": qp_ac[-1]})
        return

    AB.alloc(f"{tag}_wkv", 2 * 8 * 1024)
    wkv = AB.ap(f"{tag}_wkv").rearrange("p (w c n) -> p w c n", w=2, n=1024)
    AB.alloc(f"{tag}_ks", 16 * 128)
    ksb0 = AB.ap(f"{tag}_ks").rearrange("p (h n) -> p h n", n=128)
    AB.alloc(f"{tag}_vv", 2 * 1024)
    vvb = v3(AB.ap(f"{tag}_vv"), 1024)
    lkv = P.res("lg", 2, 16)

    @P.block.gpsimd
    def _(g):
        apply_pending(P, g, f"{tag}_wkv", f"{tag}_ks", f"{tag}_vv")
        g.dma_start(out=wkv[:, 0, :, :], in_=wk_d.rearrange("(c p) n -> p c n", p=128)
                    ).then_inc(P.sems["lg"], 16)
        g.dma_start(out=wkv[:, 1, :, :], in_=wv_d.rearrange("(c p) n -> p c n", p=128)
                    ).then_inc(P.sems["lg"], 16)

    pev = P.res("pe", 3 * TT)
    kp_pe = {tt: pev[3 * tt] for tt in range(TT)}
    vp_pe = {tt: pev[3 * tt + 1] for tt in range(TT)}
    kv_pe = {tt: pev[3 * tt + 2] for tt in range(TT)}
    dvv = P.res("dv", 3 * TT)
    ke1 = {tt: dvv[3 * tt] for tt in range(TT)}
    ke2 = {tt: dvv[3 * tt + 1] for tt in range(TT)}
    vev = {tt: dvv[3 * tt + 2] for tt in range(TT)}
    kv_banks = (b[2], b[3])
    ks_bank = b[6]
    scr = [SC[:, 0:1024], SC[:, 1024:2048]]

    @P.block.tensor
    def _(t):
        t.wait_ge(P.sems["lg"], lkv[-1])
        t.wait_ge(P.sems["lg"], lbr)
        t.wait_ge(P.sems["ld"], P.const_ready)
        t.wait_ge(P.sems["ac"], qp_ac[-1])
        for tt in range(TT):
            kb = (b[0], b[1]) if tt % 2 == 0 else (b[4], b[5])
            if tt >= 1:
                t.wait_ge(P.sems["dv"], ke1[tt - 1])
                t.wait_ge(P.sems["dv"], vev[tt - 1])
            for nj in range(2):
                for c in range(8):
                    t.matmul(kb[nj][:], xT[:, c, tt * 128:(tt + 1) * 128],
                             wkv[:, 0, c, nj * 512:(nj + 1) * 512],
                             start=(c == 0), stop=False)
                mm = t.matmul(kb[nj][:], P.onesb[0:1, 0:128],
                              brow[0][:, nj * 512:(nj + 1) * 512], start=False, stop=True)
            mm.then_inc(P.sems["pe"], 1)
            vb = (b[4], b[5]) if tt % 2 == 0 else (b[0], b[1])
            for nj in range(2):
                for c in range(8):
                    t.matmul(vb[nj][:], xT[:, c, tt * 128:(tt + 1) * 128],
                             wkv[:, 1, c, nj * 512:(nj + 1) * 512],
                             start=(c == 0), stop=False)
                mm = t.matmul(vb[nj][:], P.onesb[0:1, 0:128],
                              brow[1][:, nj * 512:(nj + 1) * 512], start=False, stop=True)
            mm.then_inc(P.sems["pe"], 1)
            t.wait_ge(P.sems["dv"], ke2[tt])
            t.wait_ge(P.sems["dv"], vev[tt])
            for h in range(16):
                t.matmul(kv_banks[h // 8][:, (h % 8) * 64:(h % 8) * 64 + 64],
                         ksb0[:, h, :], vvb[:, tt % 2, h * 64:(h + 1) * 64],
                         start=(tt == 0 and h % 8 == 0), stop=(tt == TT - 1 and h % 8 == 7))
            for h in range(16):
                mm = t.matmul(ks_bank[:, h:h + 1], ksb0[:, h, :], P.onesb[:, 0:1],
                              start=(tt == 0 and h == 0), stop=(tt == TT - 1 and h == 15))
            mm.then_inc(P.sems["pe"], 1)

    @P.block.vector
    def _(vv):
        apply_pending(P, vv, f"{tag}_ks", f"{tag}_vv")
        vv.wait_ge(P.sems["ld"], lc[0])
        for tt in range(TT):
            kb = (b[0], b[1]) if tt % 2 == 0 else (b[4], b[5])
            vb = (b[4], b[5]) if tt % 2 == 0 else (b[0], b[1])
            vv.wait_ge(P.sems["pe"], kp_pe[tt])
            if tt >= 1:
                vv.wait_ge(P.sems["pe"], kv_pe[tt - 1])
            vv.tensor_scalar_max(scr[tt % 2][:, 0:512], kb[0][:], 0.0)
            vv.tensor_scalar_max(scr[tt % 2][:, 512:1024], kb[1][:], 0.0
                                 ).then_inc(P.sems["dv"], 1)
            vv.wait_ge(P.sems["dv"], ke1[tt])
            sview = scr[tt % 2].rearrange("p (h n) -> p h n", n=64)
            vv.tensor_scalar_mul(ksb0[:, :, 0:64], sview, scc[:, tt:tt + 1])
            vv.tensor_scalar_mul(ksb0[:, :, 64:128], sview, scc[:, TT + tt:TT + tt + 1]
                                 ).then_inc(P.sems["dv"], 1)
            vv.wait_ge(P.sems["pe"], vp_pe[tt])
            vv.tensor_copy(vvb[:, tt % 2, 0:512], vb[0][:])
            vv.tensor_copy(vvb[:, tt % 2, 512:1024], vb[1][:]).then_inc(P.sems["dv"], 1)

    AB.alloc(f"{tag}_kvr", 16 * 136)
    kvr = AB.ap(f"{tag}_kvr").rearrange("p (h n) -> p h n", n=136)
    kve = P.res("dv", 1)[0]

    @P.block.vector
    def _(vv):
        apply_pending(P, vv, f"{tag}_kvr")
        vv.wait_ge(P.sems["pe"], kv_pe[TT - 1])
        for h in range(16):
            bank = kv_banks[h // 8]
            sl = (h % 8) * 64
            vv.tensor_copy(kvr[0:64, h, 0:64], bank[0:64, sl:sl + 64])
            vv.tensor_copy(kvr[64:128, h, 0:64], bank[0:64, sl:sl + 64])
            vv.tensor_copy(kvr[0:64, h, 68:132], bank[64:128, sl:sl + 64])
            vv.tensor_copy(kvr[64:128, h, 68:132], bank[64:128, sl:sl + 64])
            vv.tensor_copy(kvr[0:64, h, 64:65], ks_bank[0:64, h:h + 1])
            vv.tensor_copy(kvr[64:128, h, 64:65], ks_bank[0:64, h:h + 1])
            vv.tensor_copy(kvr[0:64, h, 132:133], ks_bank[64:128, h:h + 1])
            ins = vv.tensor_copy(kvr[64:128, h, 132:133], ks_bank[64:128, h:h + 1])
        ins.then_inc(P.sems["dv"], 1)
    AB.free(f"{tag}_wkv", f"{tag}_ks", f"{tag}_vv", readers={"pe": kv_pe[TT - 1], "dv": kve})
    AB.free("xT", readers={"pe": kv_pe[TT - 1]})
    if upto == "KV":
        dbg_dump(P, "a1kvr", AB.ap(f"{tag}_kvr"), {"dv": kve})
        return

    AB.alloc(f"{tag}_y", TT * 1024)
    y = v3(AB.ap(f"{tag}_y"), 1024)
    GRP = [(0, 2, 4, 6), (8, 10, 12, 14), (1, 3, 5, 7), (9, 11, 13, 15)]
    at_pe = P.res("pe", TT * len(GRP))
    at_dv = P.res("dv", TT * len(GRP) * 6)

    @P.block.tensor
    def _(t):
        t.wait_ge(P.sems["dv"], kve)
        t.wait_ge(P.sems["ac"], qp_ac[-1])
        for tt in range(TT):
            for g_, heads in enumerate(GRP):
                pr = (b[0], b[1]) if (tt * len(GRP) + g_) % 2 == 0 else (b[4], b[5])
                idx = tt * len(GRP) + g_
                nh = len(heads)
                base = (heads[0] % 2) * 64
                if idx >= 2:
                    t.wait_ge(P.sems["dv"], at_dv[6 * (idx - 2)])
                for j, h in enumerate(heads):
                    t.matmul(pr[0][:, j * 128:j * 128 + 65],
                             qT[base:base + 64, h // 2, tt * 128:(tt + 1) * 128],
                             kvr[base:base + 64, h, 0:65], start=(j == 0), stop=(j == nh - 1))
                    mm = t.matmul(pr[1][:, j * 128:j * 128 + 65],
                                  qT[base:base + 64, h // 2, tt * 128:(tt + 1) * 128],
                                  kvr[base:base + 64, h, 68:133], start=(j == 0), stop=(j == nh - 1))
                mm.then_inc(P.sems["pe"], 1)

    @P.block.vector
    def _(vv):
        apply_pending(P, vv, f"{tag}_y")
        tps = SC[:, 2048:2048 + 512]
        bts_ = SC[:, 2560:2560 + 512]
        att = SC[:, 3072:3072 + 512]
        zz = SC[:, 3584:3584 + 16]
        for tt in range(TT):
            for g_, heads in enumerate(GRP):
                idx = tt * len(GRP) + g_
                nh = len(heads)
                pr = (b[0], b[1]) if idx % 2 == 0 else (b[4], b[5])
                vv.wait_ge(P.sems["pe"], at_pe[idx])
                vv.tensor_copy(tps[:, 0:nh * 128], pr[0][:, 0:nh * 128])
                vv.tensor_copy(bts_[:, 0:nh * 128], pr[1][:, 0:nh * 128]).then_inc(P.sems["dv"], 1)
                vv.wait_ge(P.sems["dv"], at_dv[6 * idx])
                attv = att[:, 0:nh * 128].rearrange("p (j n) -> p j n", n=128)[:, :, 0:65]
                topv = tps[:, 0:nh * 128].rearrange("p (j n) -> p j n", n=128)[:, :, 0:65]
                botv = bts_[:, 0:nh * 128].rearrange("p (j n) -> p j n", n=128)[:, :, 0:65]
                vv.tensor_scalar_mul(attv, topv, scc[:, tt:tt + 1]).then_inc(P.sems["dv"], 1)
                vv.wait_ge(P.sems["dv"], at_dv[6 * idx + 1])
                vv.scalar_tensor_tensor(attv, botv, scc[:, TT + tt:TT + tt + 1], attv,
                                        ALU.mult, ALU.add).then_inc(P.sems["dv"], 1)
                vv.wait_ge(P.sems["dv"], at_dv[6 * idx + 2])
                dview = att[:, 0:nh * 128].rearrange("p (j n) -> p j n", n=128)[:, :, 64:65]
                vv.tensor_scalar_max(zz[:, 0:nh], dview, 1e-4).then_inc(P.sems["dv"], 1)
                vv.wait_ge(P.sems["dv"], at_dv[6 * idx + 3])
                vv.reciprocal(zz[:, 8:8 + nh], zz[:, 0:nh]).then_inc(P.sems["dv"], 1)
                vv.wait_ge(P.sems["dv"], at_dv[6 * idx + 4])
                for j, h in enumerate(heads):
                    ins = vv.tensor_scalar_mul(y[:, tt, h * 64:(h + 1) * 64],
                                               att[:, j * 128:j * 128 + 64],
                                               zz[:, 8 + j:9 + j])
                ins.then_inc(P.sems["dv"], 1)
    y_done = P.cnt["dv"]
    if upto == "AT":
        dbg_dump(P, "a1y", AB.ap(f"{tag}_y"), {"dv": y_done})
        return
    AB.free(f"{tag}_qT", f"{tag}_kvr", readers={"pe": at_pe[-1]})

    AB.alloc(f"{tag}_yT", 8 * T)
    yT = v3(AB.ap(f"{tag}_yT"), T)
    AB.alloc(f"{tag}_wo", 8 * 1024)
    wov = v3(AB.ap(f"{tag}_wo"), 1024)
    lwo = P.res("lg", 1, 16)[0]

    @P.block.gpsimd
    def _(g):
        apply_pending(P, g, f"{tag}_wo")
        g.dma_start(out=wov, in_=wo_d.rearrange("(c p) n -> p c n", p=128)
                    ).then_inc(P.sems["lg"], 16)

    yt_pe = P.res("pe", TT)
    yt_dv = P.res("dv", TT)

    @P.block.tensor
    def _(t):
        t.wait_ge(P.sems["dv"], y_done)
        for tt in range(TT):
            if tt >= 1:
                t.wait_ge(P.sems["dv"], yt_dv[tt - 1])
            for c in range(8):
                mm = t.transpose(bt[:, c * 128:(c + 1) * 128], y[:, tt, c * 128:(c + 1) * 128],
                                 P.ident[:])
            mm.then_inc(P.sems["pe"], 1)

    @P.block.vector
    def _(vv):
        apply_pending(P, vv, f"{tag}_yT")
        for tt in range(TT):
            vv.wait_ge(P.sems["pe"], yt_pe[tt])
            vv.tensor_copy(yT[:, :, tt * 128:(tt + 1) * 128],
                           bt[:].rearrange("p (c t) -> p c t", t=128)).then_inc(P.sems["dv"], 1)

    op_pe = P.res("pe", TT)
    op_dv = P.res("dv", TT)

    @P.block.tensor
    def _(t):
        t.wait_ge(P.sems["lg"], lwo)
        t.wait_ge(P.sems["dv"], yt_dv[TT - 1])
        for tt in range(TT):
            pr = (b[2], b[3]) if tt % 2 == 0 else (b[4], b[5])
            if tt >= 2:
                t.wait_ge(P.sems["dv"], op_dv[tt - 2])
            for nj in range(2):
                for c in range(8):
                    t.matmul(pr[nj][:], yT[:, c, tt * 128:(tt + 1) * 128],
                             wov[:, c, nj * 512:(nj + 1) * 512],
                             start=(c == 0), stop=False)
                mm = t.matmul(pr[nj][:], P.onesb[0:1, 0:128],
                              brow[2][:, nj * 512:(nj + 1) * 512], start=False, stop=True)
            mm.then_inc(P.sems["pe"], 1)

    @P.block.vector
    def _(vv):
        for tt in range(TT):
            pr = (b[2], b[3]) if tt % 2 == 0 else (b[4], b[5])
            vv.wait_ge(P.sems["pe"], op_pe[tt])
            vv.tensor_add(X[:, tt * 1024:tt * 1024 + 512], pr[0][:],
                          X[:, tt * 1024:tt * 1024 + 512])
            vv.tensor_add(X[:, tt * 1024 + 512:(tt + 1) * 1024], pr[1][:],
                          X[:, tt * 1024 + 512:(tt + 1) * 1024]).then_inc(P.sems["dv"], 1)

    AB.free(f"{tag}_y", readers={"pe": yt_pe[TT - 1]})
    AB.free(f"{tag}_yT", f"{tag}_wo", f"{tag}_brow", readers={"pe": op_pe[TT - 1]})
    P.ready = emit_postln(P, tag, {"dv": op_dv[TT - 1]})



def pack_l1_ffn(router_w, w1, b1, w2, b2):
    c = np.ascontiguousarray
    return {"f1_rw": c(router_w, np.float32), "f1_w1": c(w1, np.float32),
            "f1_b1": c(b1.reshape(14, 16, 128).transpose(0, 2, 1), np.float32),
            "f1_w2": c(w2, np.float32), "f1_b2": c(b2, np.float32)}


def emit_l1_ffn(P, s, upto=None, guard=()):
    T, TT, NQ = P.T, P.TT, P.NQ
    AB, SC, X, b, bt = P.AB, P.SC, P.X, P.banks, P.bt
    tag = f"f1s{s}"

    rw_d = P.din("f1_rw", [1024, 14])
    w1_d = P.din("f1_w1", [14, 1024, 2048])
    b1_d = P.din("f1_b1", [14, 128, 16])
    w2_d = P.din("f1_w2", [14, 2048, 1024])
    b2_d = P.din("f1_b2", [14, 1024])

    xready = dict(P.ready)
    xT = v3(AB.ap("xT"), T)

    AB.alloc(f"{tag}_rw", 8 * 14)
    rwv = v3(AB.ap(f"{tag}_rw"), 14)
    AB.alloc(f"{tag}_rwe", 8 * 14)
    rwe = v3(AB.ap(f"{tag}_rwe"), 14)
    AB.alloc(f"{tag}_ecb", 2 * 1024)
    ecb_off = AB.blocks[f"{tag}_ecb"][0]
    ecb = [AB.t[:, ecb_off + i * 1024:ecb_off + (i + 1) * 1024] for i in range(2)]
    AB.alloc(f"{tag}_eT", 2 * 1024)
    eT_off = AB.blocks[f"{tag}_eT"][0]
    eT = [v3(AB.t[:, eT_off + i * 1024:eT_off + (i + 1) * 1024], 128) for i in range(2)]
    combw = SC[:, 0:14 * TT].rearrange("p (t e) -> p t e", e=14)
    mx8 = SC[:, 14 * TT + 0:14 * TT + 8]
    rt = SC[:, 14 * TT + 8:14 * TT + 24]
    iota14 = SC[:, 14 * TT + 24:14 * TT + 40]
    exps = SC[:, 14 * TT + 40:14 * TT + 56]
    rwf = SC[:, 3856:3968].rearrange("p (c n) -> p c n", n=14)

    lrw = P.res("ld", 1, 16)[0]
    gpi = P.res("gp", 1)[0]
    rp_dv = P.res("dv", 2)

    @P.block.sync
    def _(sy):
        for snm, v_ in xready.items():
            sy.wait_ge(P.sems[snm], v_)
        sy.dma_start(out=rwf, in_=rw_d.rearrange("(c p) n -> p c n", p=128)
                     ).then_inc(P.sems["ld"], 16)

    @P.block.gpsimd
    def _(g):
        g.iota(iota14[:, 0:14], [[1, 14]], base=0, channel_multiplier=0,
               allow_small_or_imprecise_dtypes=True).then_inc(P.sems["gp"], 1)

    @P.block.vector
    def _(vv):
        apply_pending(P, vv, f"{tag}_rw", f"{tag}_rwe", f"{tag}_ecb", f"{tag}_eT")
        vv.wait_ge(P.sems["ld"], lrw)
        vv.tensor_copy(rwv.rearrange("p c n -> p (c n)"),
                       rwf.rearrange("p c n -> p (c n)")).then_inc(P.sems["dv"], 1)
        vv.wait_ge(P.sems["dv"], rp_dv[0])
        vv.tensor_sub(rwe.rearrange("p c n -> p (c n)"), rwf.rearrange("p c n -> p (c n)"),
                      rwv.rearrange("p c n -> p (c n)")).then_inc(P.sems["dv"], 1)

    rt_pe = P.res("pe", 2 * TT)
    rt_dv = P.res("dv", 9 * TT)
    rt_ac = P.res("ac", TT)

    @P.block.tensor
    def _(t):
        t.wait_ge(P.sems["dv"], rp_dv[1])
        for snm, v_ in xready.items():
            t.wait_ge(P.sems[snm], v_)
        for tt in range(TT):
            bank = b[tt % 2]
            t.wait_ge(P.sems["dv"], rt_dv[9 * tt + 1])
            if tt >= 1:
                t.wait_ge(P.sems["dv"], rt_dv[9 * (tt - 1) + 2])
            for c in range(8):
                mm = t.transpose(bt[:, c * 128:(c + 1) * 128],
                                 ecb[tt % 2][:, c * 128:(c + 1) * 128], P.ident[:])
            mm.then_inc(P.sems["pe"], 1)
            if tt >= 2:
                t.wait_ge(P.sems["dv"], rt_dv[9 * (tt - 2) + 4])
            t.wait_ge(P.sems["dv"], rt_dv[9 * tt + 2])
            for c in range(8):
                t.matmul(bank[:, 0:14], xT[:, c, tt * 128:(tt + 1) * 128],
                         rwv[:, c, :], start=(c == 0), stop=False)
            for c in range(8):
                t.matmul(bank[:, 0:14], eT[tt % 2][:, c, :], rwv[:, c, :],
                         start=False, stop=False)
            for c in range(8):
                mm = t.matmul(bank[:, 0:14], xT[:, c, tt * 128:(tt + 1) * 128],
                              rwe[:, c, :], start=False, stop=(c == 7))
            mm.then_inc(P.sems["pe"], 1)

    @P.block.vector
    def _(vv):
        vv.wait_ge(P.sems["gp"], gpi)
        for snm, v_ in xready.items():
            vv.wait_ge(P.sems[snm], v_)
        for tt in range(TT):
            bank = b[tt % 2]
            if tt >= 2:
                vv.wait_ge(P.sems["pe"], rt_pe[2 * (tt - 2)])
            vv.tensor_copy(ecb[tt % 2], X[:, tt * 1024:(tt + 1) * 1024]
                           ).then_inc(P.sems["dv"], 1)
            vv.wait_ge(P.sems["dv"], rt_dv[9 * tt])
            vv.tensor_sub(ecb[tt % 2], X[:, tt * 1024:(tt + 1) * 1024], ecb[tt % 2]
                          ).then_inc(P.sems["dv"], 1)
            vv.wait_ge(P.sems["pe"], rt_pe[2 * tt])
            vv.tensor_copy(eT[tt % 2].rearrange("p c n -> p (c n)"), bt[:]
                           ).then_inc(P.sems["dv"], 1)
            vv.wait_ge(P.sems["pe"], rt_pe[2 * tt + 1])
            if tt >= 1:
                vv.wait_ge(P.sems["dv"], rt_dv[9 * (tt - 1) + 8])
            vv.max(mx8, bank[:, 0:14]).then_inc(P.sems["dv"], 1)
            vv.wait_ge(P.sems["dv"], rt_dv[9 * tt + 3])
            vv.max_index(P.mi8[:], mx8, bank[:, 0:14])
            vv.tensor_scalar_mul(rt[:, 0:1], mx8[:, 0:1], -1.0).then_inc(P.sems["dv"], 1)
            vv.wait_ge(P.sems["ac"], rt_ac[tt])
            vv.reciprocal(rt[:, 2:3], rt[:, 1:2])
            vv.tensor_copy(rt[:, 3:4], P.mi8[:, 0:1]).then_inc(P.sems["dv"], 1)
            vv.wait_ge(P.sems["dv"], rt_dv[9 * tt + 5])
            vv.tensor_scalar(combw[:, tt, :], iota14[:, 0:14], rt[:, 3:4], None,
                             ALU.is_equal).then_inc(P.sems["dv"], 1)
            vv.wait_ge(P.sems["dv"], rt_dv[9 * tt + 6])
            vv.tensor_scalar_mul(combw[:, tt, :], combw[:, tt, :], rt[:, 2:3]
                                 ).then_inc(P.sems["dv"], 1)
            vv.engine_nop().then_inc(P.sems["dv"], 1)

    @P.block.scalar
    def _(sa):
        for tt in range(TT):
            bank = b[tt % 2]
            sa.wait_ge(P.sems["dv"], rt_dv[9 * tt + 4])
            sa.activation(exps[:, 0:14], bank[:, 0:14], AF.Exp, bias=rt[:, 0:1],
                          accum_out=rt[:, 1:2]).then_inc(P.sems["ac"], 1)
    cw_dv = P.cnt["dv"]
    AB.free(f"{tag}_rw", f"{tag}_rwe", f"{tag}_ecb", f"{tag}_eT",
            readers={"pe": rt_pe[-1]})
    if "rb" in guard:
        emit_seq_barrier(P)

    AB.alloc(f"{tag}_w1", 8 * 2048)
    w1v = v3(AB.ap(f"{tag}_w1"), 2048)
    AB.alloc(f"{tag}_w2", 16 * 1024)
    w2v = v3(AB.ap(f"{tag}_w2"), 1024)
    AB.alloc(f"{tag}_hid", 16 * 512)
    hid = v3(AB.ap(f"{tag}_hid"), 512)
    b1s = SC[:, 3840:3856]
    AB.alloc(f"{tag}_b2", 1024)
    b2r = AB.ap(f"{tag}_b2")

    lge = P.res("lg", 3 * 14, 16)
    lb1 = P.res("ld", 14, 16)
    prev_h_last = None
    prev_hac_last = None
    prev_dn_last = None
    prev_dn_dv = None
    for e in range(14):
        if "eb" in guard and e >= 1:
            emit_seq_barrier(P)

        @P.block.gpsimd
        def _(g, e=e, prev_h_last=prev_h_last, prev_dn_last=prev_dn_last):
            if e == 0:
                apply_pending(P, g, f"{tag}_w1", f"{tag}_w2", f"{tag}_hid", f"{tag}_b2")
            if prev_h_last is not None:
                g.wait_ge(P.sems["pe"], prev_h_last)
            g.dma_start(out=w1v, in_=w1_d[e].rearrange("(c p) n -> p c n", p=128)
                        ).then_inc(P.sems["lg"], 16)
            if prev_dn_last is not None:
                g.wait_ge(P.sems["pe"], prev_dn_last)
            g.dma_start(out=w2v, in_=w2_d[e].rearrange("(c p) n -> p c n", p=128)
                        ).then_inc(P.sems["lg"], 16)
            g.dma_start(out=b2r[0:1, :], in_=b2_d[e:e + 1, :]).then_inc(P.sems["lg"], 16)

        @P.block.sync
        def _(sy, e=e, prev_hac_last=(None if e == 0 else prev_hac_last)):
            if prev_hac_last is not None:
                sy.wait_ge(P.sems["ac"], prev_hac_last)
            sy.dma_start(out=b1s, in_=b1_d[e]).then_inc(P.sems["ld"], 16)

        pev = P.res("pe", (16 + 4) * NQ)
        h_pe, dn_pe = {}, {}
        k_ = 0
        for tq in range(NQ):
            for it in range(16):
                h_pe[tq * 16 + it] = pev[k_]; k_ += 1
            for j in range(4):
                dn_pe[tq * 4 + j] = pev[k_]; k_ += 1
        h_ac = P.res("ac", 16 * NQ)
        dn_dv = P.res("dv", 4 * NQ)

        @P.block.tensor
        def _(t, e=e, h_pe=h_pe, dn_pe=dn_pe, h_ac=h_ac, dn_dv=dn_dv,
              prev_dn_dv=prev_dn_dv, lge=lge, lb1=lb1):
            t.wait_ge(P.sems["lg"], lge[3 * e])
            if e == 0:
                t.wait_ge(P.sems["dv"], cw_dv)
            for tq in range(NQ):
                for it in range(16):
                    i = tq * 16 + it
                    bank = b[2 + i % 4]
                    if i >= 4:
                        t.wait_ge(P.sems["ac"], h_ac[i - 4])
                    for c in range(8):
                        mm = t.matmul(bank[:], w1v[:, c, it * 128:(it + 1) * 128],
                                      xT[:, c, tq * 512:(tq + 1) * 512],
                                      start=(c == 0), stop=(c == 7))
                    mm.then_inc(P.sems["pe"], 1)
                t.wait_ge(P.sems["lg"], lge[3 * e + 2])
                t.wait_ge(P.sems["ac"], h_ac[tq * 16 + 15])
                for j in range(4):
                    tt = tq * 4 + j
                    if tt >= 1:
                        t.wait_ge(P.sems["dv"], dn_dv[tt - 1])
                    elif prev_dn_dv is not None:
                        t.wait_ge(P.sems["dv"], prev_dn_dv)
                    for nj in range(2):
                        for i2 in range(16):
                            t.matmul(b[0 if nj == 0 else 1][:],
                                     hid[:, i2, j * 128:(j + 1) * 128],
                                     w2v[:, i2, nj * 512:(nj + 1) * 512],
                                     start=(i2 == 0), stop=False)
                        mm = t.matmul(b[0 if nj == 0 else 1][:], P.onesb[0:1, 0:128],
                                      b2r[0:1, nj * 512:(nj + 1) * 512],
                                      start=False, stop=True)
                    mm.then_inc(P.sems["pe"], 1)

        @P.block.scalar
        def _(sa, e=e, h_pe=h_pe, dn_pe=dn_pe, prev_dn_last=prev_dn_last, lb1=lb1):
            if e == 0:
                apply_pending(P, sa, f"{tag}_hid")
            sa.wait_ge(P.sems["ld"], lb1[e])
            for tq in range(NQ):
                if tq >= 1:
                    sa.wait_ge(P.sems["pe"], dn_pe[tq * 4 - 1])
                elif prev_dn_last is not None:
                    sa.wait_ge(P.sems["pe"], prev_dn_last)
                for it in range(16):
                    i = tq * 16 + it
                    sa.wait_ge(P.sems["pe"], h_pe[i])
                    sa.activation(hid[:, it, :], b[2 + i % 4][:], AF.Gelu,
                                  bias=b1s[:, it:it + 1]).then_inc(P.sems["ac"], 1)

        @P.block.vector
        def _(vv, e=e, dn_pe=dn_pe, dn_dv=dn_dv):
            for tq in range(NQ):
                for j in range(4):
                    tt = tq * 4 + j
                    vv.wait_ge(P.sems["pe"], dn_pe[tt])
                    x_t = X[:, tt * 1024:(tt + 1) * 1024]
                    cwc = combw[:, tt, e:e + 1]
                    vv.scalar_tensor_tensor(x_t[:, 0:512], b[0][:], cwc, x_t[:, 0:512],
                                            ALU.mult, ALU.add)
                    vv.scalar_tensor_tensor(x_t[:, 512:1024], b[1][:], cwc, x_t[:, 512:1024],
                                            ALU.mult, ALU.add).then_inc(P.sems["dv"], 1)

        prev_h_last = h_pe[16 * NQ - 1]
        prev_hac_last = h_ac[16 * NQ - 1]
        prev_dn_last = dn_pe[4 * NQ - 1]
        prev_dn_dv = dn_dv[4 * NQ - 1]

    AB.free(f"{tag}_w1", f"{tag}_w2", f"{tag}_hid", f"{tag}_b2",
            readers={"pe": P.cnt["pe"]})
    if "pb" in guard:
        emit_seq_barrier(P)
    P.ready = emit_postln(P, tag, {"dv": prev_dn_dv}, refresh_xt=False)


def emit_l1_ffn_sp(P, s, guard=("pb",)):
    T, TT, NQ = P.T, P.TT, P.NQ
    AB, SC, X, b, bt = P.AB, P.SC, P.X, P.banks, P.bt
    tag = f"f1s{s}"

    rw_d = P.din("f1_rw", [1024, 14])
    w1_d = P.din("f1_w1", [14, 1024, 2048])
    b1_d = P.din("f1_b1", [14, 128, 16])
    w2_d = P.din("f1_w2", [14, 2048, 1024])
    b2_d = P.din("f1_b2", [14, 1024])

    xready = dict(P.ready)
    xT = v3(AB.ap("xT"), T)

    AB.alloc(f"{tag}_rw", 8 * 14)
    rwv = v3(AB.ap(f"{tag}_rw"), 14)
    AB.alloc(f"{tag}_rwe", 8 * 14)
    rwe = v3(AB.ap(f"{tag}_rwe"), 14)
    AB.alloc(f"{tag}_ecb", 2 * 1024)
    ecb_off = AB.blocks[f"{tag}_ecb"][0]
    ecb = [AB.t[:, ecb_off + i * 1024:ecb_off + (i + 1) * 1024] for i in range(2)]
    AB.alloc(f"{tag}_eT", 2 * 1024)
    eT_off = AB.blocks[f"{tag}_eT"][0]
    eT = [v3(AB.t[:, eT_off + i * 1024:eT_off + (i + 1) * 1024], 128) for i in range(2)]
    AB.alloc(f"{tag}_ohb", TT * 14)
    ohb = v3(AB.ap(f"{tag}_ohb"), 14)

    mx8 = SC[:, 0:8]
    rt = SC[:, 8:24]
    iota14 = SC[:, 24:40]
    exps = SC[:, 40:56]
    oh0 = 56
    oneh = SC[:, oh0:oh0 + 14 * TT].rearrange("p (t e) -> p t e", e=14)
    i0 = oh0 + 14 * TT
    iota128 = SC[:, i0:i0 + 128]
    pr0 = i0 + 128
    prefix = SC[:, pr0:pr0 + 14 * TT]
    pm0 = pr0 + 14 * TT
    posm = [SC[:, pm0 + k * 14 * TT:pm0 + (k + 1) * 14 * TT] for k in range(2)]
    wg0 = pm0 + 2 * 14 * TT
    assert wg0 + TT <= 3840
    rwf = SC[:, 3856:3968].rearrange("p (c n) -> p c n", n=14)
    b1s = SC[:, 3840:3856]

    lrw = P.res("ld", 1, 16)[0]
    gpi = P.res("gp", 1)[0]
    rp_dv = P.res("dv", 2)

    @P.block.sync
    def _(sy):
        for snm, v_ in xready.items():
            sy.wait_ge(P.sems[snm], v_)
        sy.dma_start(out=rwf, in_=rw_d.rearrange("(c p) n -> p c n", p=128)
                     ).then_inc(P.sems["ld"], 16)

    @P.block.gpsimd
    def _(g):
        g.iota(iota14[:, 0:14], [[1, 14]], base=0, channel_multiplier=0,
               allow_small_or_imprecise_dtypes=True)
        g.iota(iota128, [[1, 128]], base=0, channel_multiplier=0,
               allow_small_or_imprecise_dtypes=True).then_inc(P.sems["gp"], 1)

    @P.block.vector
    def _(vv):
        apply_pending(P, vv, f"{tag}_rw", f"{tag}_rwe", f"{tag}_ecb", f"{tag}_eT",
                      f"{tag}_ohb")
        vv.wait_ge(P.sems["ld"], lrw)
        vv.tensor_copy(rwv.rearrange("p c n -> p (c n)"),
                       rwf.rearrange("p c n -> p (c n)")).then_inc(P.sems["dv"], 1)
        vv.wait_ge(P.sems["dv"], rp_dv[0])
        vv.tensor_sub(rwe.rearrange("p c n -> p (c n)"), rwf.rearrange("p c n -> p (c n)"),
                      rwv.rearrange("p c n -> p (c n)")).then_inc(P.sems["dv"], 1)

    rt_pe = P.res("pe", 2 * TT)
    rt_dv = P.res("dv", 9 * TT)
    rt_ac = P.res("ac", TT)

    @P.block.tensor
    def _(t):
        t.wait_ge(P.sems["dv"], rp_dv[1])
        for snm, v_ in xready.items():
            t.wait_ge(P.sems[snm], v_)
        for tt in range(TT):
            bank = b[tt % 2]
            t.wait_ge(P.sems["dv"], rt_dv[9 * tt + 1])
            if tt >= 1:
                t.wait_ge(P.sems["dv"], rt_dv[9 * (tt - 1) + 2])
            for c in range(8):
                mm = t.transpose(bt[:, c * 128:(c + 1) * 128],
                                 ecb[tt % 2][:, c * 128:(c + 1) * 128], P.ident[:])
            mm.then_inc(P.sems["pe"], 1)
            if tt >= 2:
                t.wait_ge(P.sems["dv"], rt_dv[9 * (tt - 2) + 4])
            t.wait_ge(P.sems["dv"], rt_dv[9 * tt + 2])
            for c in range(8):
                t.matmul(bank[:, 0:14], xT[:, c, tt * 128:(tt + 1) * 128],
                         rwv[:, c, :], start=(c == 0), stop=False)
            for c in range(8):
                t.matmul(bank[:, 0:14], eT[tt % 2][:, c, :], rwv[:, c, :],
                         start=False, stop=False)
            for c in range(8):
                mm = t.matmul(bank[:, 0:14], xT[:, c, tt * 128:(tt + 1) * 128],
                              rwe[:, c, :], start=False, stop=(c == 7))
            mm.then_inc(P.sems["pe"], 1)

    @P.block.vector
    def _(vv):
        vv.wait_ge(P.sems["gp"], gpi)
        for snm, v_ in xready.items():
            vv.wait_ge(P.sems[snm], v_)
        for tt in range(TT):
            bank = b[tt % 2]
            if tt >= 2:
                vv.wait_ge(P.sems["pe"], rt_pe[2 * (tt - 2)])
            vv.tensor_copy(ecb[tt % 2], X[:, tt * 1024:(tt + 1) * 1024]
                           ).then_inc(P.sems["dv"], 1)
            vv.wait_ge(P.sems["dv"], rt_dv[9 * tt])
            vv.tensor_sub(ecb[tt % 2], X[:, tt * 1024:(tt + 1) * 1024], ecb[tt % 2]
                          ).then_inc(P.sems["dv"], 1)
            vv.wait_ge(P.sems["pe"], rt_pe[2 * tt])
            vv.tensor_copy(eT[tt % 2].rearrange("p c n -> p (c n)"), bt[:]
                           ).then_inc(P.sems["dv"], 1)
            vv.wait_ge(P.sems["pe"], rt_pe[2 * tt + 1])
            if tt >= 1:
                vv.wait_ge(P.sems["dv"], rt_dv[9 * (tt - 1) + 8])
            vv.max(mx8, bank[:, 0:14]).then_inc(P.sems["dv"], 1)
            vv.wait_ge(P.sems["dv"], rt_dv[9 * tt + 3])
            vv.max_index(P.mi8[:], mx8, bank[:, 0:14])
            vv.tensor_scalar_mul(rt[:, 0:1], mx8[:, 0:1], -1.0).then_inc(P.sems["dv"], 1)
            vv.wait_ge(P.sems["ac"], rt_ac[tt])
            vv.reciprocal(rt[:, 2:3], rt[:, 1:2])
            vv.tensor_copy(rt[:, 3:4], P.mi8[:, 0:1]).then_inc(P.sems["dv"], 1)
            vv.wait_ge(P.sems["dv"], rt_dv[9 * tt + 5])
            vv.tensor_scalar(oneh[:, tt, :], iota14[:, 0:14], rt[:, 3:4], None,
                             ALU.is_equal).then_inc(P.sems["dv"], 1)
            vv.wait_ge(P.sems["dv"], rt_dv[9 * tt + 6])
            vv.tensor_copy(ohb[:, tt, :], oneh[:, tt, :])
            vv.tensor_copy(SC[:, wg0 + tt:wg0 + tt + 1], rt[:, 2:3]
                           ).then_inc(P.sems["dv"], 1)
            vv.engine_nop().then_inc(P.sems["dv"], 1)

    @P.block.scalar
    def _(sa):
        for tt in range(TT):
            bank = b[tt % 2]
            sa.wait_ge(P.sems["dv"], rt_dv[9 * tt + 4])
            sa.activation(exps[:, 0:14], bank[:, 0:14], AF.Exp, bias=rt[:, 0:1],
                          accum_out=rt[:, 1:2]).then_inc(P.sems["ac"], 1)

    AB.free(f"{tag}_rw", f"{tag}_rwe", f"{tag}_ecb", f"{tag}_eT", "xT",
            readers={"pe": rt_pe[-1]})
    emit_seq_barrier(P)

    pf_pe = P.res("pe", 1)
    pf_dv = P.res("dv", 6)

    @P.block.tensor
    def _(t):
        t.wait_ge(P.sems["ld"], P.const_ready)
        n = 0
        for tt in range(TT):
            for tt2 in range(tt + 1):
                n += 1
                mm = t.matmul(b[6][:, tt * 14:(tt + 1) * 14],
                              (P.ltri if tt2 == tt else P.onesb)[:],
                              ohb[:, tt2, :], start=(tt == 0 and tt2 == 0),
                              stop=(tt == TT - 1 and tt2 == tt))
        mm.then_inc(P.sems["pe"], 1)

    AB.alloc(f"{tag}_Xb", TT * 1024)
    Xbv = v3(AB.ap(f"{tag}_Xb"), 1024)

    @P.block.vector
    def _(vv):
        apply_pending(P, vv, f"{tag}_Xb")
        vv.wait_ge(P.sems["pe"], pf_pe[0])
        vv.tensor_copy(prefix, b[6][:, 0:14 * TT]).then_inc(P.sems["dv"], 1)
        vv.wait_ge(P.sems["dv"], pf_dv[0])
        vv.tensor_scalar_add(posm[0], prefix, 1.0).then_inc(P.sems["dv"], 1)
        vv.wait_ge(P.sems["dv"], pf_dv[1])
        vv.tensor_mul(posm[0], posm[0], SC[:, oh0:oh0 + 14 * TT]
                      ).then_inc(P.sems["dv"], 1)
        vv.wait_ge(P.sems["dv"], pf_dv[2])
        vv.tensor_scalar_add(posm[0], posm[0], -1.0).then_inc(P.sems["dv"], 1)
        vv.wait_ge(P.sems["dv"], pf_dv[3])
        vv.tensor_scalar_add(posm[1], posm[0], -128.0).then_inc(P.sems["dv"], 1)
        for tt in range(TT):
            ins = vv.tensor_copy(Xbv[:, tt, :], X[:, tt * 1024:(tt + 1) * 1024])
        ins.then_inc(P.sems["dv"], 1)
    AB.free(f"{tag}_ohb", readers={"pe": pf_pe[0]})

    AB.alloc(f"{tag}_w1", 8 * 2048)
    w1v = v3(AB.ap(f"{tag}_w1"), 2048)
    AB.alloc(f"{tag}_w2", 8 * 1024)
    w2v = v3(AB.ap(f"{tag}_w2"), 1024)
    AB.alloc(f"{tag}_b2", 1024)
    b2r = AB.ap(f"{tag}_b2")
    AB.alloc(f"{tag}_Sg", TT * 128)
    Sgv = v3(AB.ap(f"{tag}_Sg"), 128)
    AB.alloc(f"{tag}_St", 2 * TT * 128)
    Stv = AB.ap(f"{tag}_St").rearrange("p (k t s) -> p k t s", k=2, s=128)
    AB.alloc(f"{tag}_xgsb", 1024)
    xgsb = AB.ap(f"{tag}_xgsb")
    AB.alloc(f"{tag}_xgT", 8 * 256)
    xgT = v3(AB.ap(f"{tag}_xgT"), 256)
    AB.alloc(f"{tag}_hid", 16 * 256)
    hid = v3(AB.ap(f"{tag}_hid"), 256)
    AB.alloc(f"{tag}_yo", 2 * 1024)
    yov = v3(AB.ap(f"{tag}_yo"), 1024)

    rounds = (TT + 7) // 8
    lge = P.res("lg", 4 * 14, 16)
    lb1 = P.res("ld", 14, 16)
    prev = None
    for e in range(14):
        peL = P.res("pe", 2 * (2 + rounds) + 16 + 1 + 4 + 2 * TT)
        g_pe = [peL[k * (2 + rounds)] for k in range(2)]
        xgt_pe = [peL[k * (2 + rounds) + 1] for k in range(2)]
        st_pe = [peL[(kr // rounds) * (2 + rounds) + 2 + kr % rounds]
                 for kr in range(2 * rounds)]
        o_ = 2 * (2 + rounds)
        h_pe = peL[o_:o_ + 16]
        dnh0_pe = peL[o_ + 16]
        dn_pe = peL[o_ + 17:o_ + 21]
        sc_pe = peL[o_ + 21:o_ + 21 + 2 * TT]
        dvL = P.res("dv", 2 * (3 + rounds) + 4 + 2 * TT)
        sg_dv = [dvL[k * (3 + rounds)] for k in range(2)]
        xgs_dv = [dvL[k * (3 + rounds) + 1] for k in range(2)]
        xgt_dv = [dvL[k * (3 + rounds) + 2] for k in range(2)]
        st_dv = [dvL[(kr // rounds) * (3 + rounds) + 3 + kr % rounds]
                 for kr in range(2 * rounds)]
        o_ = 2 * (3 + rounds)
        yo_dv = dvL[o_:o_ + 4]
        xa_dv = dvL[o_ + 4:o_ + 4 + 2 * TT]
        h_ac = P.res("ac", 16)

        @P.block.gpsimd
        def _(g, e=e, prev=prev, dnh0_pe=dnh0_pe):
            if e == 0:
                apply_pending(P, g, f"{tag}_w1", f"{tag}_w2", f"{tag}_b2")
            if prev is not None:
                g.wait_ge(P.sems["pe"], prev["h"])
            g.dma_start(out=w1v, in_=w1_d[e].rearrange("(c p) n -> p c n", p=128)
                        ).then_inc(P.sems["lg"], 16)
            if prev is not None:
                g.wait_ge(P.sems["pe"], prev["dn"])
            g.dma_start(out=w2v, in_=w2_d[e, 0:1024].rearrange("(c p) n -> p c n", p=128)
                        ).then_inc(P.sems["lg"], 16)
            g.dma_start(out=b2r[0:1, :], in_=b2_d[e:e + 1, :]).then_inc(P.sems["lg"], 16)
            g.wait_ge(P.sems["pe"], dnh0_pe)
            g.dma_start(out=w2v, in_=w2_d[e, 1024:2048].rearrange("(c p) n -> p c n", p=128)
                        ).then_inc(P.sems["lg"], 16)

        @P.block.sync
        def _(sy, e=e, prev=prev):
            if prev is not None:
                sy.wait_ge(P.sems["ac"], prev["gelu"])
            sy.dma_start(out=b1s, in_=b1_d[e]).then_inc(P.sems["ld"], 16)

        @P.block.vector
        def _(vv, e=e, prev=prev, sg_dv=sg_dv, g_pe=g_pe, xgs_dv=xgs_dv,
              xgt_pe=xgt_pe, xgt_dv=xgt_dv, st_pe=st_pe, st_dv=st_dv,
              dn_pe=dn_pe, yo_dv=yo_dv):
            if e == 0:
                apply_pending(P, vv, f"{tag}_Sg", f"{tag}_St", f"{tag}_xgsb",
                              f"{tag}_xgT", f"{tag}_yo")
                vv.wait_ge(P.sems["dv"], pf_dv[4])
            if prev is not None:
                vv.wait_ge(P.sems["pe"], prev["sc"])
            for k in range(2):
                if k == 1:
                    vv.wait_ge(P.sems["pe"], g_pe[0])
                    vv.wait_ge(P.sems["pe"], st_pe[rounds - 1])
                for tt in range(TT):
                    ins = vv.tensor_scalar(Sgv[:, tt, :], iota128,
                                           posm[k][:, tt * 14 + e:tt * 14 + e + 1],
                                           None, ALU.is_equal)
                ins.then_inc(P.sems["dv"], 1)
                vv.wait_ge(P.sems["pe"], g_pe[k])
                vv.tensor_copy(xgsb[:, 0:512], b[0][:])
                vv.tensor_copy(xgsb[:, 512:1024], b[1][:]).then_inc(P.sems["dv"], 1)
                vv.wait_ge(P.sems["pe"], xgt_pe[k])
                vv.tensor_copy(xgT[:, :, k * 128:(k + 1) * 128],
                               bt[:].rearrange("p (c t) -> p c t", t=128)
                               ).then_inc(P.sems["dv"], 1)
                for r in range(rounds):
                    nt = min(8, TT - r * 8)
                    vv.wait_ge(P.sems["pe"], st_pe[k * rounds + r])
                    vv.tensor_copy(Stv[:, k, r * 8:r * 8 + nt, :],
                                   bt[:, 0:nt * 128].rearrange("p (c t) -> p c t", t=128)
                                   ).then_inc(P.sems["dv"], 1)
            for k in range(2):
                for nj in range(2):
                    vv.wait_ge(P.sems["pe"], dn_pe[2 * k + nj])
                    vv.tensor_copy(yov[:, k, nj * 512:(nj + 1) * 512],
                                   b[2 * k + nj][:]).then_inc(P.sems["dv"], 1)

        @P.block.tensor
        def _(t, e=e, prev=prev, sg_dv=sg_dv, g_pe=g_pe, xgs_dv=xgs_dv,
              xgt_pe=xgt_pe, xgt_dv=xgt_dv, st_pe=st_pe, st_dv=st_dv,
              h_pe=h_pe, h_ac=h_ac, dn_pe=dn_pe, yo_dv=yo_dv, sc_pe=sc_pe,
              xa_dv=xa_dv):
            bt_free = prev["bt"] if prev is not None else None
            for k in range(2):
                t.wait_ge(P.sems["dv"], sg_dv[k])
                if e == 0 and k == 0:
                    t.wait_ge(P.sems["dv"], pf_dv[5])
                if prev is not None and k == 0:
                    t.wait_ge(P.sems["dv"], prev["yo"])
                if k == 1:
                    t.wait_ge(P.sems["dv"], xgs_dv[0])
                for tt in range(TT):
                    t.matmul(b[0][:], Sgv[:, tt, :], Xbv[:, tt, 0:512],
                             start=(tt == 0), stop=(tt == TT - 1))
                    mm = t.matmul(b[1][:], Sgv[:, tt, :], Xbv[:, tt, 512:1024],
                                  start=(tt == 0), stop=(tt == TT - 1))
                mm.then_inc(P.sems["pe"], 1)
                t.wait_ge(P.sems["dv"], xgs_dv[k])
                if bt_free is not None:
                    t.wait_ge(P.sems["dv"], bt_free)
                for c in range(8):
                    mm = t.transpose(bt[:, c * 128:(c + 1) * 128],
                                     xgsb[:, c * 128:(c + 1) * 128], P.ident[:])
                mm.then_inc(P.sems["pe"], 1)
                prev_evac = xgt_dv[k]
                for r in range(rounds):
                    nt = min(8, TT - r * 8)
                    t.wait_ge(P.sems["dv"], prev_evac)
                    for i in range(nt):
                        mm = t.transpose(bt[:, i * 128:(i + 1) * 128],
                                         Sgv[:, r * 8 + i, :], P.ident[:])
                    mm.then_inc(P.sems["pe"], 1)
                    prev_evac = st_dv[k * rounds + r]
                bt_free = prev_evac
            t.wait_ge(P.sems["lg"], lge[4 * e])
            t.wait_ge(P.sems["dv"], xgt_dv[1])
            if prev is not None:
                t.wait_ge(P.sems["dv"], prev["xa"])
            for it in range(16):
                if it >= 4:
                    t.wait_ge(P.sems["ac"], h_ac[it - 4])
                for c in range(8):
                    mm = t.matmul(b[2 + it % 4][:, 0:256], w1v[:, c, it * 128:(it + 1) * 128],
                                  xgT[:, c, :], start=(c == 0), stop=(c == 7))
                mm.then_inc(P.sems["pe"], 1)
            t.wait_ge(P.sems["lg"], lge[4 * e + 2])
            t.wait_ge(P.sems["ac"], h_ac[15])
            for k in range(2):
                for nj in range(2):
                    bk = b[2 * k + nj]
                    for i2 in range(8):
                        mm = t.matmul(bk[:], hid[:, i2, k * 128:(k + 1) * 128],
                                      w2v[:, i2, nj * 512:(nj + 1) * 512],
                                      start=(i2 == 0), stop=False)
            mm.then_inc(P.sems["pe"], 1)
            t.wait_ge(P.sems["lg"], lge[4 * e + 3])
            for k in range(2):
                for nj in range(2):
                    bk = b[2 * k + nj]
                    for i2 in range(8):
                        t.matmul(bk[:], hid[:, 8 + i2, k * 128:(k + 1) * 128],
                                 w2v[:, i2, nj * 512:(nj + 1) * 512],
                                 start=False, stop=False)
                    mm = t.matmul(bk[:], P.onesb[0:1, 0:128],
                                  b2r[0:1, nj * 512:(nj + 1) * 512],
                                  start=False, stop=True)
                    mm.then_inc(P.sems["pe"], 1)
            t.wait_ge(P.sems["dv"], yo_dv[3])
            t.wait_ge(P.sems["dv"], st_dv[2 * rounds - 1])
            g2 = 0
            for tt in range(TT):
                for nj in range(2):
                    bk = b[4 + g2 % 3]
                    if g2 >= 3:
                        t.wait_ge(P.sems["dv"], xa_dv[g2 - 3])
                    for k in range(2):
                        mm = t.matmul(bk[:], Stv[:, k, tt, :],
                                      yov[:, k, nj * 512:(nj + 1) * 512],
                                      start=(k == 0), stop=(k == 1))
                    mm.then_inc(P.sems["pe"], 1)
                    g2 += 1

        @P.block.scalar
        def _(sa, e=e, prev=prev, h_pe=h_pe, h_ac=h_ac):
            if e == 0:
                apply_pending(P, sa, f"{tag}_hid")
            sa.wait_ge(P.sems["ld"], lb1[e])
            if prev is not None:
                sa.wait_ge(P.sems["pe"], prev["dn"])
            for it in range(16):
                sa.wait_ge(P.sems["pe"], h_pe[it])
                sa.activation(hid[:, it, :], b[2 + it % 4][:, 0:256], AF.Gelu,
                              bias=b1s[:, it:it + 1]).then_inc(P.sems["ac"], 1)

        @P.block.vector
        def _(vv, e=e, sc_pe=sc_pe, xa_dv=xa_dv):
            g2 = 0
            for tt in range(TT):
                for nj in range(2):
                    bk = b[4 + g2 % 3]
                    vv.wait_ge(P.sems["pe"], sc_pe[g2])
                    x_t = X[:, tt * 1024 + nj * 512:tt * 1024 + (nj + 1) * 512]
                    vv.scalar_tensor_tensor(x_t, bk[:], SC[:, wg0 + tt:wg0 + tt + 1],
                                            x_t, ALU.mult, ALU.add
                                            ).then_inc(P.sems["dv"], 1)
                    g2 += 1

        prev = dict(h=h_pe[15], dn=dn_pe[3], gelu=h_ac[15], sc=sc_pe[-1],
                    xa=xa_dv[-1], yo=yo_dv[3], bt=st_dv[2 * rounds - 1])

    AB.free(f"{tag}_Xb", f"{tag}_w1", f"{tag}_w2", f"{tag}_b2", f"{tag}_Sg",
            f"{tag}_St", f"{tag}_xgsb", f"{tag}_xgT", f"{tag}_hid", f"{tag}_yo",
            readers={"pe": P.cnt["pe"]})
    if "pb" in guard:
        emit_seq_barrier(P)
    P.ready = emit_postln(P, tag, {"dv": prev["xa"]}, refresh_xt=False)


def build_a0(T=2048, n_seq=4, debug=False, hold=None, upto=None):
    P = Prog(T, n_seq)
    P.debug = debug or (upto is not None)
    P.setup()
    P.AB.hold = debug if hold is None else hold
    with P.es, P.nc.Block() as block:
        P.block = block
        P.dout("xout", [n_seq, T, 1024])
        emit_consts_load(P)
        for s in range(n_seq):
            if s:
                emit_seq_barrier(P)
            emit_load_x(P, s)
            emit_l0_attn(P, s, upto=upto)
            if upto is None:
                emit_store_x(P, s, final=(s == n_seq - 1))
            else:
                @P.block.sync
                def _(sy):
                    sy.wait_ge(P.sems["st"], P.cnt["st"])
                break
    return P


def build_f0(T=2048, n_seq=4, debug=False, hold=None, upto=None):
    def em(P, s, upto=None):
        return emit_l0_ffn(P, s, upto=upto, refresh_out=False)
    return _build_one(em, T, n_seq, debug=debug, hold=hold, upto=upto, refresh=False)


def build_f0_old(T=2048, n_seq=4, debug=False, hold=None, upto=None):
    P = Prog(T, n_seq)
    P.debug = debug or (upto is not None)
    P.setup()
    P.AB.hold = debug if hold is None else hold
    with P.es, P.nc.Block() as block:
        P.block = block
        P.dout("xout", [n_seq, T, 1024])
        emit_consts_load(P)
        for s in range(n_seq):
            emit_load_x(P, s, refresh=False)
            emit_l0_ffn(P, s, upto=upto)
            if upto is None:
                emit_store_x(P, s, final=(s == n_seq - 1))
            else:
                @P.block.sync
                def _(sy):
                    sy.wait_ge(P.sems["st"], P.cnt["st"])
                break
    return P


def _build_one(emitter, T=2048, n_seq=4, debug=False, hold=None, upto=None,
               refresh=True):
    P = Prog(T, n_seq)
    P.debug = debug or (upto is not None)
    P.setup()
    P.AB.hold = debug if hold is None else hold
    with P.es, P.nc.Block() as block:
        P.block = block
        P.dout("xout", [n_seq, T, 1024])
        emit_consts_load(P)
        for s in range(n_seq):
            if s:
                emit_seq_barrier(P)
            emit_load_x(P, s, refresh=refresh)
            emitter(P, s, upto=upto)
            if upto is None:
                emit_store_x(P, s, final=(s == n_seq - 1))
            else:
                @P.block.sync
                def _(sy):
                    sy.wait_ge(P.sems["st"], P.cnt["st"])
                break
    return P


def build_a1(T=2048, n_seq=4, **kw):
    return _build_one(emit_l1_attn, T, n_seq, **kw)


def build_f1(T=2048, n_seq=4, **kw):
    return _build_one(emit_l1_ffn, T, n_seq, **kw)


def build_f1sp(T=2048, n_seq=4, **kw):
    def em(P, s, upto=None):
        return emit_l1_ffn_sp(P, s)
    return _build_one(em, T, n_seq, **kw)


def build_full(T=2048, n_seq=4, debug=False, stop=None, f1_guard=("rb", "pb"),
               barriers="full"):
    P = Prog(T, n_seq)
    P.debug = debug
    P.f1_guard = f1_guard
    P.setup()
    bar = (lambda: emit_seq_barrier(P)) if barriers == "full" else (lambda: None)
    with P.es, P.nc.Block() as block:
        P.block = block
        P.dout("xout", [n_seq, T, 1024])
        emit_consts_load(P)
        for s in range(n_seq):
            if s:
                bar()
            nb = 4 if stop is None or s < stop[0] else stop[1]
            emit_load_x(P, s, refresh=True)
            emit_l0_attn(P, s, refresh_out=False)
            if debug:
                dbg_dump(P, f"x0_{s}", P.X[:, :], dict(P.ready), dt=F32)
            if nb >= 2:
                bar()
                emit_l0_ffn(P, s, refresh_out=True)
                if debug:
                    dbg_dump(P, f"x1_{s}", P.X[:, :], dict(P.ready), dt=F32)
            if nb >= 3:
                bar()
                emit_l1_attn(P, s)
                if debug:
                    dbg_dump(P, f"x2_{s}", P.X[:, :], dict(P.ready), dt=F32)
            if nb >= 4:
                bar()
                if getattr(P, "f1_sparse", True):
                    emit_l1_ffn_sp(P, s)
                else:
                    emit_l1_ffn(P, s, guard=getattr(P, "f1_guard", ()))
            last = (s == n_seq - 1) or (stop is not None and s == stop[0])
            emit_store_x(P, s, final=last)
            if last and stop is not None and s == stop[0]:
                break
    return P


_PCACHE = {}


def _progs():
    if "p" not in _PCACHE:
        _PCACHE["p"] = [build_a0(2048, 4), build_f0(2048, 4),
                        build_a1(2048, 4), build_f1(2048, 4)]
    return _PCACHE["p"]


def _prog_full():
    if "f" not in _PCACHE:
        _PCACHE["f"] = build_full(2048, 4)
    return _PCACHE["f"]


def kernel(**inputs):
    T, NS = 2048, 4
    x = np.ascontiguousarray(inputs["L0_attn_x"], np.float32)
    consts = host_consts(T)
    packs = {}
    packs.update(pack_l0_attn(inputs["L0_attn_w_qkv"], inputs["L0_attn_w_o"]))
    packs.update(pack_l0_ffn(inputs["L0_ffn_Wq"], inputs["L0_ffn_Wkva"], inputs["L0_ffn_Wkvb"],
                             inputs["L0_ffn_Wo"], inputs["L0_ffn_Wg"], inputs["L0_ffn_Wu"],
                             inputs["L0_ffn_Wd"], inputs["L0_ffn_sWg"], inputs["L0_ffn_sWu"],
                             inputs["L0_ffn_sWd"], inputs["L0_ffn_Wgate"],
                             inputs["L0_ffn_expert_bias"], inputs["L0_ffn_w_ln1"],
                             inputs["L0_ffn_w_ln2"]))
    packs.update(pack_l1_attn(inputs["L1_attn_Wq"], inputs["L1_attn_bq"], inputs["L1_attn_Wk"],
                              inputs["L1_attn_bk"], inputs["L1_attn_Wv"], inputs["L1_attn_bv"],
                              inputs["L1_attn_Wo"], inputs["L1_attn_bo"]))
    packs.update(pack_l1_ffn(inputs["L1_ffn_router_w"], inputs["L1_ffn_w1"], inputs["L1_ffn_b1"],
                             inputs["L1_ffn_w2"], inputs["L1_ffn_b2"]))
    allin = {**consts, **packs}

    P = _prog_full()
    names = set(P.dram.keys())
    base = {k: np.ascontiguousarray(v) for k, v in allin.items() if k in names}
    in_maps = [{**base, "xin": np.ascontiguousarray(x[4 * c:4 * c + 4])} for c in range(8)]
    res = run_bass_kernel_spmd(P.nc, in_maps, core_ids=list(range(8)))
    return np.concatenate([np.asarray(res.results[c]["xout"], np.float32)
                           for c in range(8)], axis=0)


# version 08 (final)
# speedup vs baseline: 1.6327x; 1.0055x over previous
import numpy as np
import ml_dtypes
from contextlib import ExitStack
import concourse.bass as bass
import concourse.mybir as mybir
from concourse.bass_utils import run_bass_kernel_spmd

F32 = mybir.dt.float32
BF16 = mybir.dt.bfloat16
U32 = mybir.dt.uint32
AF = mybir.ActivationFunctionType
ALU = mybir.AluOpType
AX = mybir.AxisListType
BFNP = ml_dtypes.bfloat16

SWAP16 = [(i + 16) % 32 for i in range(32)]
AB_COLS = 60416


def _l0_perm():
    return [*range(0, 16), *range(32, 48), *range(16, 32), *range(48, 64)]

def _mla_perm():
    return [*range(32), *[32 + p for p in (*range(0, 32, 2), *range(1, 32, 2))]]

def host_consts(T):
    t = np.arange(T, dtype=np.float64)
    invf = 1.0 / (10000.0 ** (np.arange(32) / 32.0))
    ang = t[None, :] * invf[:, None]
    Tc0 = np.zeros((128, T)); Ts0 = np.zeros((128, T))
    for r in range(128):
        qd, idx = r // 32, r % 32
        f = (qd % 2) * 16 + (idx % 16)
        Tc0[r] = np.cos(ang[f])
        Ts0[r] = np.sin(ang[f]) * (1.0 if idx < 16 else -1.0)
    invf2 = 1.0 / (100000.0 ** (np.arange(0, 32, 2) / 32.0))
    ang2 = t[None, :] * invf2[:, None]
    TcM = np.zeros((128, T)); TsM = np.zeros((128, T))
    for r in range(128):
        qd, idx = r // 32, r % 32
        if qd % 2 == 0:
            TcM[r] = 1.0
        else:
            f = idx % 16
            TcM[r] = np.cos(ang2[f])
            TsM[r] = np.sin(ang2[f]) * (1.0 if idx < 16 else -1.0)
    masks = np.zeros((128, 4, 512))
    for v_, o in enumerate((0, 128, 256, 384)):
        masks[:, v_, :] = (np.arange(128)[:, None] + o <= np.arange(512)[None, :])
    idx = np.pi / 2 * (t + 1) / T
    TT = T // 128
    sc = np.zeros((128, 2 * TT), np.float32)
    for tt in range(TT):
        sc[:, tt] = np.sin(idx[tt * 128:(tt + 1) * 128])
        sc[:, TT + tt] = np.cos(idx[tt * 128:(tt + 1) * 128])
    return {
        "c_ident": np.eye(128).astype(BFNP),
        "c_onesb": np.ones((128, 128), BFNP),
        "c_ltri": np.triu(np.ones((128, 128)), 1).astype(BFNP),
        "c_tc0": Tc0.astype(BFNP), "c_ts0": Ts0.astype(BFNP),
        "c_tcm": TcM.astype(BFNP), "c_tsm": TsM.astype(BFNP),
        "c_masks": masks.astype(BFNP).reshape(128, 4 * 512),
        "c_sincos": sc,
    }

def pack_l0_attn(w_qkv, w_o):
    p = np.asarray(_l0_perm())
    q = w_qkv[:, :1024].reshape(1024, 16, 64)[:, :, p].reshape(1024, 1024)
    k = w_qkv[:, 1024:1280].reshape(1024, 4, 64)[:, :, p].reshape(1024, 256)
    v = w_qkv[:, 1280:1536]
    return {"a0_wqkv": np.ascontiguousarray(np.concatenate([q, k, v], axis=1), np.float32),
            "a0_wo": np.ascontiguousarray(w_o, np.float32)}

def pack_l0_ffn(Wq, Wkva, Wkvb, Wo, Wg, Wu, Wd, sWg, sWu, sWd, Wgate, expert_bias,
                w_ln1, w_ln2):
    p64 = np.asarray(_mla_perm())
    p32 = np.asarray([*range(0, 32, 2), *range(1, 32, 2)])
    wq = Wq.reshape(1024, 16, 64)[:, :, p64].reshape(1024, 1024)
    wkva = np.concatenate([Wkva[:, :128], Wkva[:, 128:][:, p32]], axis=1)
    c = np.ascontiguousarray
    woe = Wo.reshape(16, 64, 1024)[:, :32, :].reshape(512, 1024)
    return {"f0_wq": c(wq, np.float32), "f0_wkva": c(wkva, np.float32),
            "f0_wkvb": c(Wkvb, np.float32), "f0_woe": c(woe, np.float32),
            "f0_wg": c(np.concatenate([sWg, Wg], 0), np.float32),
            "f0_wu": c(np.concatenate([sWu, Wu], 0), np.float32),
            "f0_wd": c(np.concatenate([sWd, Wd], 0), np.float32),
            "f0_wgate": c(Wgate, np.float32),
            "f0_ebias": c(expert_bias.reshape(1, 8), np.float32),
            "f0_g1": c(w_ln1.reshape(1, 1024), np.float32),
            "f0_g2": c(w_ln2.reshape(1, 1024), np.float32)}



class Arena:
    def __init__(self, tensor, total):
        self.t, self.total, self.blocks = tensor, total, {}
        self.hold = False
        self.freed = []
        self.pending = {}

    def alloc(self, name, size):
        size = int(size)
        off = 0
        for (o, s) in sorted(self.blocks.values()):
            if off + size <= o:
                break
            off = max(off, o + s)
        assert off + size <= self.total, \
            f"arena overflow: {name} sz={size} off={off} held={sorted(self.blocks.items(), key=lambda kv: kv[1])}"
        self.blocks[name] = (off, size)
        waits = {}
        keep = []
        for (o, s, rd) in self.freed:
            if o < off + size and off < o + s:
                for k, v in rd.items():
                    waits[k] = max(waits.get(k, 0), v)
                if o < off:
                    keep.append((o, off - o, rd))
                if o + s > off + size:
                    keep.append((off + size, o + s - (off + size), rd))
            else:
                keep.append((o, s, rd))
        self.freed = keep
        self.pending[name] = waits
        return off

    def free(self, *names, readers=None):
        if self.hold:
            return
        for n in names:
            off, size = self.blocks[n]
            self.freed.append((off, size, dict(readers or {})))
            del self.blocks[n]

    def ap(self, name):
        off, size = self.blocks[name]
        return self.t[:, off:off + size]


def apply_pending(P, eng, *names):
    for n in names:
        for snm, v_ in P.AB.pending.get(n, {}).items():
            eng.wait_ge(P.sems[snm], v_)


def v3(ap, inner):
    return ap.rearrange("p (c t) -> p c t", t=inner)


class Prog:
    def __init__(self, T=2048, n_seq=4):
        self.T, self.TT, self.NQ = T, T // 128, T // 512
        self.n_seq = n_seq
        self.nc = bass.Bass("TRN2", target_bir_lowering=False, detect_race_conditions=False)
        self.es = ExitStack()
        self.cnt, self.sems = {}, {}
        self.dram = {}
        self.ready = {}

    def sem(self, name):
        if name not in self.sems:
            self.sems[name] = self.es.enter_context(self.nc.semaphore(name))
            self.cnt[name] = 0
        return self.sems[name]

    def res(self, name, n, step=1):
        self.sem(name)
        base = self.cnt[name]
        self.cnt[name] += n * step
        return [base + (i + 1) * step for i in range(n)]

    def setup(self):
        nc = self.nc
        for s in ("ld", "lg", "pe", "dv", "ac", "gp", "st"):
            self.sem(s)
        self.X = self.es.enter_context(nc.sbuf_tensor("X", [128, self.TT * 1024], F32))
        ab_t = self.es.enter_context(nc.sbuf_tensor("ABt", [128, AB_COLS], BF16))
        self.AB = Arena(ab_t, AB_COLS)
        self.SC = self.es.enter_context(nc.sbuf_tensor("SCt", [128, 4096], F32))
        self.AF = self.es.enter_context(nc.sbuf_tensor("AFt", [128, 2304], F32))
        self.ident = self.es.enter_context(nc.sbuf_tensor("ident", [128, 128], BF16))
        self.banks = [self.es.enter_context(nc.psum_tensor(f"pb{i}", [128, 512], F32))
                      for i in range(7)]
        self.bt = self.es.enter_context(nc.psum_tensor("pbt", [128, 1024], BF16))
        self.onesb = self.es.enter_context(nc.sbuf_tensor("onesb", [128, 128], BF16))
        self.ltri = self.es.enter_context(nc.sbuf_tensor("ltri", [128, 128], BF16))
        self.mi8 = self.es.enter_context(nc.sbuf_tensor("mi8", [128, 8], U32))

    def din(self, name, shape, dt=F32):
        if name not in self.dram:
            self.dram[name] = self.nc.dram_tensor(name, shape, dt, kind="ExternalInput").ap()
        return self.dram[name]

    def dout(self, name, shape, dt=F32):
        self.dram[name] = self.nc.dram_tensor(name, shape, dt, kind="ExternalOutput").ap()
        return self.dram[name]



def emit_consts_load(P):
    idt = P.din("c_ident", [128, 128], BF16)
    ob = P.din("c_onesb", [128, 128], BF16)
    lt = P.din("c_ltri", [128, 128], BF16)
    ldv = P.res("ld", 3, 16)

    @P.block.sync
    def _(sy):
        sy.dma_start(out=P.ident[:], in_=idt).then_inc(P.sems["ld"], 16)
        sy.dma_start(out=P.onesb[:], in_=ob).then_inc(P.sems["ld"], 16)
        sy.dma_start(out=P.ltri[:], in_=lt).then_inc(P.sems["ld"], 16)
    P.const_ready = ldv[-1]


def dbg_dump(P, name, ap2d, wait, dt=BF16):
    if not getattr(P, "debug", False):
        return
    shp = [128, int(np.prod(ap2d.shape[1:]))]
    full = "dbg_" + name
    while full in P.dram:
        full += "_"
    d = P.dout(full, shp, dt)
    P.res("st", 1, 16)

    @P.block.sync
    def _(sy, ap2d=ap2d, wait=dict(wait), d=d):
        for snm, v_ in wait.items():
            sy.wait_ge(P.sems[snm], v_)
        sy.dma_start(out=d, in_=ap2d).then_inc(P.sems["st"], 16)


def emit_toT(P, tag, wait=None, dst="xT", cast_emit=None, pe_extra_wait=None,
             tile_wait=None):
    T, TT = P.T, P.TT
    X = P.X
    if dst not in P.AB.blocks:
        P.AB.alloc(dst, 8 * T)
    dsv = v3(P.AB.ap(dst), T)
    cast_off = P.AB.alloc(f"{tag}_cb", 2048)
    cb = [P.AB.t[:, cast_off:cast_off + 1024], P.AB.t[:, cast_off + 1024:cast_off + 2048]]
    if cast_emit is None:
        def cast_emit(vv, tt, out_ap):
            return vv.tensor_copy(out_ap, X[:, tt * 1024:(tt + 1) * 1024])

    t_v = P.res("pe", TT)
    order = [("cast", 0)]
    for tt in range(1, TT):
        order += [("cast", tt), ("evac", tt - 1)]
    order.append(("evac", TT - 1))
    dvv = P.res("dv", len(order))
    c_v, e_v = {}, {}
    for i, (kind, tt) in enumerate(order):
        (c_v if kind == "cast" else e_v)[tt] = dvv[i]

    @P.block.vector
    def _(vv):
        if wait:
            for snm, v_ in wait.items():
                vv.wait_ge(P.sems[snm], v_)
        apply_pending(P, vv, dst, f"{tag}_cb")
        for kind, tt in order:
            if kind == "cast":
                if tt >= 2:
                    vv.wait_ge(P.sems["pe"], t_v[tt - 2])
                if tile_wait is not None:
                    for snm, v_ in tile_wait(tt).items():
                        vv.wait_ge(P.sems[snm], v_)
                cast_emit(vv, tt, cb[tt % 2]).then_inc(P.sems["dv"], 1)
            else:
                vv.wait_ge(P.sems["pe"], t_v[tt])
                vv.tensor_copy(dsv[:, :, tt * 128:(tt + 1) * 128],
                               P.bt[:].rearrange("p (c t) -> p c t", t=128)
                               ).then_inc(P.sems["dv"], 1)

    @P.block.tensor
    def _(t):
        t.wait_ge(P.sems["ld"], P.const_ready)
        if pe_extra_wait:
            for snm, v_ in pe_extra_wait.items():
                t.wait_ge(P.sems[snm], v_)
        for tt in range(TT):
            t.wait_ge(P.sems["dv"], c_v[tt])
            if tt >= 1:
                t.wait_ge(P.sems["dv"], e_v[tt - 1])
            for c in range(8):
                mm = t.transpose(P.bt[:, c * 128:(c + 1) * 128],
                                 cb[tt % 2][:, c * 128:(c + 1) * 128], P.ident[:])
            mm.then_inc(P.sems["pe"], 1)
    P.AB.free(f"{tag}_cb", readers={"pe": t_v[-1]})
    return {"dv": dvv[-1], "pe": t_v[-1]}


def emit_xT_refresh(P, tag, wait=None, dst="xT", tile_wait=None):
    return emit_toT(P, tag, wait=wait, dst=dst, tile_wait=tile_wait)


def emit_seq_barrier(P):
    snap = {k: v for k, v in P.cnt.items() if v > 0}

    def mk(eng):
        for k, v in snap.items():
            eng.wait_ge(P.sems[k], v)
    P.block.tensor(mk)
    P.block.vector(mk)
    P.block.scalar(mk)
    P.block.gpsimd(mk)
    P.block.sync(mk)


def emit_load_x(P, s, refresh=True):
    TT = P.TT
    ldv = P.res("ld", 4, 16)
    xin = P.din("xin", [P.n_seq, P.T, 1024])
    st_prev = P.cnt.get("st", 0)
    q4 = TT // 4

    @P.block.sync
    def _(sy):
        if st_prev:
            sy.wait_ge(P.sems["st"], st_prev)
        for qq in range(4):
            sy.dma_start(
                out=P.X[:, qq * q4 * 1024:(qq + 1) * q4 * 1024]
                    .rearrange("p (tt c) -> p tt c", c=1024),
                in_=xin[s, qq * q4 * 128:(qq + 1) * q4 * 128]
                    .rearrange("(tt p) c -> p tt c", p=128)
                ).then_inc(P.sems["ld"], 16)
    if refresh:
        P.ready = emit_xT_refresh(P, f"lx{s}",
                                  tile_wait=lambda tt: {"ld": ldv[tt // q4]})
        P.ready["ld"] = ldv[-1]
    else:
        P.ready = {"ld": ldv[-1]}


def emit_store_x(P, s, final=False):
    P.res("st", 1, 16)
    xout = P.dram["xout"]

    @P.block.sync
    def _(sy, ready=dict(P.ready)):
        for snm, v_ in ready.items():
            sy.wait_ge(P.sems[snm], v_)
        sy.dma_start(out=xout[s].rearrange("(tt p) c -> p tt c", p=128),
                     in_=P.X[:, :].rearrange("p (tt c) -> p tt c", c=1024)
                     ).then_inc(P.sems["st"], 16)
        if final:
            sy.wait_ge(P.sems["st"], P.cnt["st"])


def emit_postln(P, tag, xready, refresh_xt=True):
    TT = P.TT
    SC = P.SC
    stats = SC[:, 0:12 * TT].rearrange("p (t k) -> p t k", k=12)
    agg = SC[:, 12 * TT:16 * TT].rearrange("p (t k) -> p t k", k=4)
    aux = SC[:, 16 * TT:20 * TT].rearrange("p (t k) -> p t k", k=4)

    g1, g2, g3, g4 = P.res("dv", 4)
    g5 = P.res("ac", 1)[0]
    g6, g7 = P.res("dv", 2)
    g8 = P.res("ac", TT)

    @P.block.vector
    def _(vv):
        for snm, v_ in xready.items():
            vv.wait_ge(P.sems[snm], v_)
        for tt in range(TT):
            vv.bn_stats(stats[:, tt, 0:6], P.X[:, tt * 1024:tt * 1024 + 512])
            ins = vv.bn_stats(stats[:, tt, 6:12], P.X[:, tt * 1024 + 512:(tt + 1) * 1024])
        ins.then_inc(P.sems["dv"], 1)
        vv.wait_ge(P.sems["dv"], g1)
        for tt in range(TT):
            ins = vv.bn_aggr(agg[:, tt, 0:2], stats[:, tt, :])
        ins.then_inc(P.sems["dv"], 1)
        vv.wait_ge(P.sems["dv"], g2)
        vv.tensor_scalar_add(aux[:, :, 0], agg[:, :, 1], 1e-5).then_inc(P.sems["dv"], 1)
        vv.wait_ge(P.sems["dv"], g3)
        vv.reciprocal(aux[:, :, 1], aux[:, :, 0]).then_inc(P.sems["dv"], 1)

    @P.block.scalar
    def _(sa):
        sa.wait_ge(P.sems["dv"], g4)
        sa.activation(aux[:, :, 1], aux[:, :, 1], AF.Sqrt).then_inc(P.sems["ac"], 1)

    @P.block.vector
    def _(vv):
        vv.wait_ge(P.sems["ac"], g5)
        vv.tensor_mul(aux[:, :, 2], agg[:, :, 0], aux[:, :, 1]).then_inc(P.sems["dv"], 1)
        vv.wait_ge(P.sems["dv"], g6)
        vv.tensor_scalar_mul(aux[:, :, 2], aux[:, :, 2], -1.0).then_inc(P.sems["dv"], 1)

    @P.block.scalar
    def _(sa):
        sa.wait_ge(P.sems["dv"], g7)
        for tt in range(TT):
            sa.activation(P.X[:, tt * 1024:(tt + 1) * 1024], P.X[:, tt * 1024:(tt + 1) * 1024],
                          AF.Identity, bias=aux[:, tt, 2:3], scale=aux[:, tt, 1:2]
                          ).then_inc(P.sems["ac"], 1)

    ready = {"ac": g8[-1]}
    if refresh_xt:
        ready = emit_xT_refresh(P, f"{tag}_rx",
                                tile_wait=lambda tt: {"ac": g8[tt]})
        ready["ac"] = g8[-1]
    return ready


def emit_rope(P, tile_ap, rsc, tbl, done_sems, tbl_ready, rsc_name=None):
    r1 = P.res("dv", 1)[0]
    r2 = P.res("gp", 1)[0]
    r3 = P.res("dv", 1)[0]
    r4 = P.res("gp", 1)[0]

    @P.block.vector
    def _(vv):
        for snm, v_ in done_sems.items():
            vv.wait_ge(P.sems[snm], v_)
        if rsc_name:
            apply_pending(P, vv, rsc_name)
        vv.wait_ge(P.sems["ld"], tbl_ready)
        vv.tensor_mul(rsc[:, 0, :], tile_ap, tbl[:, 0, :]).then_inc(P.sems["dv"], 1)

    @P.block.gpsimd
    def _(g):
        for snm, v_ in done_sems.items():
            g.wait_ge(P.sems[snm], v_)
        if rsc_name:
            apply_pending(P, g, rsc_name)
        g.wait_ge(P.sems["ld"], tbl_ready)
        g.tensor_mul(rsc[:, 1, :], tile_ap, tbl[:, 1, :]).then_inc(P.sems["gp"], 1)

    @P.block.vector
    def _(vv):
        vv.wait_ge(P.sems["gp"], r2)
        vv.stream_shuffle(rsc[:, 1, :], rsc[:, 1, :], SWAP16).then_inc(P.sems["dv"], 1)

    @P.block.gpsimd
    def _(g):
        g.wait_ge(P.sems["dv"], r1)
        g.wait_ge(P.sems["dv"], r3)
        g.tensor_add(tile_ap, rsc[:, 0, :], rsc[:, 1, :]).then_inc(P.sems["gp"], 1)
    return r4



def emit_l0_attn(P, s, upto=None, refresh_out=True):
    T, TT, NQ = P.T, P.TT, P.NQ
    AB, SC, X, b, bt = P.AB, P.SC, P.X, P.banks, P.bt
    tag = f"a0s{s}"
    nc = P.nc

    wqkv_d = P.din("a0_wqkv", [1024, 1536])
    wo_d = P.din("a0_wo", [1024, 1024])
    tc0_d = P.din("c_tc0", [128, T], BF16)
    ts0_d = P.din("c_ts0", [128, T], BF16)
    masks_d = P.din("c_masks", [128, 4 * 512], BF16)
    xT = v3(AB.ap("xT"), T)

    AB.alloc(f"{tag}_wqkv", 8 * 1536)
    AB.alloc(f"{tag}_qn", TT * 1024)
    AB.alloc(f"{tag}_kn", TT * 256)
    AB.alloc(f"{tag}_v", TT * 4 * 66)
    wq = v3(AB.ap(f"{tag}_wqkv"), 1536)
    qn = v3(AB.ap(f"{tag}_qn"), 1024)
    kn = v3(AB.ap(f"{tag}_kn"), 256)
    vv_ = AB.ap(f"{tag}_v").rearrange("p (t g e) -> p t g e", g=4, e=66)
    SSQ = SC[:, 0:20 * TT].rearrange("p (t h) -> p t h", h=20)
    SQS = SC[:, 20 * TT:20 * TT + 1280]

    lw = P.res("lg", 1, 16)[0]
    gpo = P.res("gp", 1)[0]
    xready = dict(P.ready)

    @P.block.gpsimd
    def _(g):
        apply_pending(P, g, f"{tag}_wqkv", f"{tag}_v", f"{tag}_qn", f"{tag}_kn")
        g.dma_start(out=wq, in_=wqkv_d.rearrange("(c p) n -> p c n", p=128)
                    ).then_inc(P.sems["lg"], 16)
        g.memset(vv_[:, :, :, 64:66], 1.0).then_inc(P.sems["gp"], 1)

    pe_a = P.res("pe", TT)
    ac_a = P.res("ac", TT)
    dv_a = P.res("dv", TT)

    @P.block.tensor
    def _(t):
        t.wait_ge(P.sems["lg"], lw)
        for snm, v_ in xready.items():
            t.wait_ge(P.sems[snm], v_)
        for tt in range(TT):
            trio = (0, 1, 2) if tt % 2 == 0 else (3, 4, 5)
            if tt >= 2:
                t.wait_ge(P.sems["dv"], dv_a[tt - 2])
            for nj in range(3):
                for c in range(8):
                    mm = t.matmul(b[trio[nj]][:], xT[:, c, tt * 128:(tt + 1) * 128],
                                  wq[:, c, nj * 512:(nj + 1) * 512],
                                  start=(c == 0), stop=(c == 7))
            mm.then_inc(P.sems["pe"], 1)

    @P.block.scalar
    def _(sa):
        for tt in range(TT):
            trio = (0, 1, 2) if tt % 2 == 0 else (3, 4, 5)
            sa.wait_ge(P.sems["pe"], pe_a[tt])
            if tt >= 1:
                sa.wait_ge(P.sems["dv"], dv_a[tt - 1])
            sa.activation(SQS[:, 0:512], b[trio[0]][:], AF.Square)
            sa.activation(SQS[:, 512:1024], b[trio[1]][:], AF.Square)
            sa.activation(SQS[:, 1024:1280], b[trio[2]][:, 0:256], AF.Square
                          ).then_inc(P.sems["ac"], 1)

    @P.block.vector
    def _(vv):
        vv.wait_ge(P.sems["gp"], gpo)
        apply_pending(P, vv, f"{tag}_qn", f"{tag}_kn", f"{tag}_v")
        for tt in range(TT):
            trio = (0, 1, 2) if tt % 2 == 0 else (3, 4, 5)
            vv.wait_ge(P.sems["ac"], ac_a[tt])
            vv.tensor_copy(qn[:, tt, 0:512], b[trio[0]][:])
            vv.tensor_copy(qn[:, tt, 512:1024], b[trio[1]][:])
            vv.tensor_copy(kn[:, tt, :], b[trio[2]][:, 0:256])
            for g_ in range(4):
                vv.tensor_copy(vv_[:, tt, g_, 0:64], b[trio[2]][:, 256 + g_ * 64:320 + g_ * 64])
            vv.tensor_reduce(SSQ[:, tt, 0:16], SQS[:, 0:1024].rearrange("p (h e) -> p h e", e=64),
                             AX.X, ALU.add)
            vv.tensor_reduce(SSQ[:, tt, 16:20], SQS[:, 1024:1280].rearrange("p (h e) -> p h e", e=64),
                             AX.X, ALU.add).then_inc(P.sems["dv"], 1)

    AB.free(f"{tag}_wqkv", "xT", readers={"pe": pe_a[TT - 1]})

    n1 = P.res("ac", 1)[0]
    n2, n3, n4 = P.res("dv", 3)
    n5 = P.res("gp", 1)[0]
    SSQf = SSQ.rearrange("p t h -> p (t h)")

    @P.block.scalar
    def _(sa):
        sa.wait_ge(P.sems["dv"], dv_a[TT - 1])
        sa.activation(SSQf, SSQf, AF.Sqrt).then_inc(P.sems["ac"], 1)

    @P.block.vector
    def _(vv):
        vv.wait_ge(P.sems["ac"], n1)
        vv.tensor_scalar_add(SSQf, SSQf, 1e-10).then_inc(P.sems["dv"], 1)
        vv.wait_ge(P.sems["dv"], n2)
        vv.reciprocal(SSQf, SSQf).then_inc(P.sems["dv"], 1)
        vv.wait_ge(P.sems["dv"], n3)
        for tt in range(TT):
            for h in range(0, 16, 2):
                ins = vv.tensor_scalar_mul(qn[:, tt, h * 64:(h + 1) * 64],
                                           qn[:, tt, h * 64:(h + 1) * 64], SSQ[:, tt, h:h + 1])
        ins.then_inc(P.sems["dv"], 1)

    @P.block.gpsimd
    def _(g):
        g.wait_ge(P.sems["dv"], n3)
        for tt in range(TT):
            for h in range(1, 16, 2):
                g.tensor_scalar_mul(qn[:, tt, h * 64:(h + 1) * 64],
                                    qn[:, tt, h * 64:(h + 1) * 64], SSQ[:, tt, h:h + 1])
            for g_ in range(4):
                ins = g.tensor_scalar_mul(kn[:, tt, g_ * 64:(g_ + 1) * 64],
                                          kn[:, tt, g_ * 64:(g_ + 1) * 64],
                                          SSQ[:, tt, 16 + g_:17 + g_])
        ins.then_inc(P.sems["gp"], 1)

    if upto == "A":
        dbg_dump(P, "qn", AB.ap(f"{tag}_qn"), {"dv": n4, "gp": n5})
        dbg_dump(P, "kn", AB.ap(f"{tag}_kn"), {"dv": n4, "gp": n5})
        dbg_dump(P, "vv", AB.ap(f"{tag}_v"), {"dv": dv_a[TT - 1], "gp": gpo})
        return

    AB.alloc(f"{tag}_qT", 8 * T)
    qT = v3(AB.ap(f"{tag}_qT"), T)
    AB.alloc(f"{tag}_tbl", 2 * T)
    tbl = v3(AB.ap(f"{tag}_tbl"), T)
    AB.alloc(f"{tag}_kts", 2 * T)
    kts = v3(AB.ap(f"{tag}_kts"), T)
    AB.alloc(f"{tag}_rsc", 2 * T)
    rsc = v3(AB.ap(f"{tag}_rsc"), T)

    lt1, lt2 = P.res("ld", 2, 16)

    @P.block.sync
    def _(sy):
        apply_pending(P, sy, f"{tag}_tbl")
        sy.dma_start(out=tbl[:, 0, :], in_=tc0_d).then_inc(P.sems["ld"], 16)
        sy.dma_start(out=tbl[:, 1, :], in_=ts0_d).then_inc(P.sems["ld"], 16)

    kt_pe = P.res("pe", TT)
    kt_dv = P.res("dv", TT)

    @P.block.tensor
    def _(t):
        t.wait_ge(P.sems["gp"], n5)
        t.wait_ge(P.sems["dv"], n4)
        for tt in range(TT):
            if tt >= 1:
                t.wait_ge(P.sems["dv"], kt_dv[tt - 1])
            t.transpose(bt[:, 0:128], kn[:, tt, 0:128], P.ident[:])
            t.transpose(bt[:, 128:256], kn[:, tt, 128:256], P.ident[:]).then_inc(P.sems["pe"], 1)

    @P.block.vector
    def _(vv):
        apply_pending(P, vv, f"{tag}_kts")
        for tt in range(TT):
            vv.wait_ge(P.sems["pe"], kt_pe[tt])
            vv.tensor_copy(kts[:, 0, tt * 128:(tt + 1) * 128], bt[:, 0:128])
            vv.tensor_copy(kts[:, 1, tt * 128:(tt + 1) * 128], bt[:, 128:256]
                           ).then_inc(P.sems["dv"], 1)
    AB.free(f"{tag}_kn", readers={"pe": kt_pe[TT - 1]})
    AB.alloc(f"{tag}_ktr01", 2 * T)
    AB.alloc(f"{tag}_ktr23", 2 * T)
    ktr = [v3(AB.ap(f"{tag}_ktr01"), T), v3(AB.ap(f"{tag}_ktr23"), T)]

    kr0 = emit_rope(P, kts[:, 0, :], rsc, tbl, {"dv": kt_dv[TT - 1]}, lt2,
                    rsc_name=f"{tag}_rsc")
    kr1 = emit_rope(P, kts[:, 1, :], rsc, tbl, {"gp": kr0}, lt2)
    krep = P.res("dv", 1)[0]

    @P.block.vector
    def _(vv):
        apply_pending(P, vv, f"{tag}_ktr01", f"{tag}_ktr23")
        vv.wait_ge(P.sems["gp"], kr1)
        for g_ in range(4):
            src = kts[:, g_ // 2, :]
            half = (g_ % 2) * 64
            vv.tensor_copy(ktr[g_ // 2][0:64, g_ % 2, :], src[half:half + 64, :])
            ins = vv.tensor_copy(ktr[g_ // 2][64:128, g_ % 2, :], src[half:half + 64, :])
        ins.then_inc(P.sems["dv"], 1)

    qt_pe = P.res("pe", TT)
    qt_dv = P.res("dv", TT)

    @P.block.tensor
    def _(t):
        t.wait_ge(P.sems["dv"], kt_dv[TT - 1])
        for tt in range(TT):
            if tt >= 1:
                t.wait_ge(P.sems["dv"], qt_dv[tt - 1])
            for c in range(8):
                mm = t.transpose(bt[:, c * 128:(c + 1) * 128], qn[:, tt, c * 128:(c + 1) * 128],
                                 P.ident[:])
            mm.then_inc(P.sems["pe"], 1)

    @P.block.vector
    def _(vv):
        apply_pending(P, vv, f"{tag}_qT")
        vv.wait_ge(P.sems["dv"], krep)
        for tt in range(TT):
            vv.wait_ge(P.sems["pe"], qt_pe[tt])
            vv.tensor_copy(qT[:, :, tt * 128:(tt + 1) * 128],
                           bt[:].rearrange("p (c t) -> p c t", t=128)).then_inc(P.sems["dv"], 1)

    prev = {"dv": qt_dv[TT - 1]}
    for c in range(8):
        r = emit_rope(P, qT[:, c, :], rsc, tbl, prev, lt2)
        prev = {"gp": r}
    rope_done = r
    if upto == "B":
        dbg_dump(P, "qT", AB.ap(f"{tag}_qT"), {"gp": rope_done})
        dbg_dump(P, "ktr01", AB.ap(f"{tag}_ktr01"), {"dv": krep})
        dbg_dump(P, "ktr23", AB.ap(f"{tag}_ktr23"), {"dv": krep})
        return
    AB.free(f"{tag}_qn", readers={"pe": qt_pe[TT - 1]})
    AB.free(f"{tag}_kts", readers={"dv": krep})
    AB.free(f"{tag}_rsc", f"{tag}_tbl", readers={"gp": rope_done})

    AB.alloc(f"{tag}_y", TT * 1024)
    y = v3(AB.ap(f"{tag}_y"), 1024)
    AB.alloc(f"{tag}_wo", 8 * 1024)
    wo = v3(AB.ap(f"{tag}_wo"), 1024)
    AB.alloc(f"{tag}_mask", 4 * 512)
    msk = v3(AB.ap(f"{tag}_mask"), 512)
    AB.alloc(f"{tag}_strip", 4 * 512)
    strip = v3(AB.ap(f"{tag}_strip"), 512)

    lm = P.res("ld", 1, 16)[0]
    lwo = P.res("lg", 1, 16)[0]

    @P.block.sync
    def _(sy):
        apply_pending(P, sy, f"{tag}_mask")
        sy.dma_start(out=AB.ap(f"{tag}_mask"), in_=masks_d).then_inc(P.sems["ld"], 16)

    @P.block.gpsimd
    def _(g):
        apply_pending(P, g, f"{tag}_wo")
        g.dma_start(out=wo, in_=wo_d.rearrange("(c p) n -> p c n", p=128)
                    ).then_inc(P.sems["lg"], 16)

    rd = SC[:, 1664:1672]
    prev_ep = None
    for p_ in range(8):
        g_ = p_ // 2
        for qj in range(NQ):
            kmax = min(TT, 4 * (qj + 1))
            diag = sorted(set(range(4 * qj, kmax)))
            pe_v = P.res("pe", 2 * kmax)
            s_idx, a_idx = {}, {}
            n_ = 0
            for ki in range(kmax):
                s_idx[ki] = pe_v[n_]; n_ += 1
                if ki >= 1:
                    a_idx[ki - 1] = pe_v[n_]; n_ += 1
            a_idx[kmax - 1] = pe_v[n_]
            ac_v = P.res("ac", kmax)
            dvl = P.res("dv", len(diag) + 2)
            dv_m = {ki: dvl[i] for i, ki in enumerate(diag)}
            ep1, ep2 = dvl[-2], dvl[-1]

            def emit_av(t, ki, kmax, diag, ac_v, dv_m, first_out):
                t.wait_ge(P.sems["ac"], ac_v[ki])
                if ki in diag:
                    t.wait_ge(P.sems["dv"], dv_m[ki])
                sidx = (ki % 2) * 2
                g_l = g_
                for j in range(4):
                    t.matmul(b[0][:, j * 128:j * 128 + 65],
                             strip[:, sidx, j * 128:(j + 1) * 128],
                             vv_[:, ki, g_l, 0:65], start=first_out,
                             stop=(ki == kmax - 1 and j == 3))
                    mm = t.matmul(b[1][:, j * 128:j * 128 + 65],
                                  strip[:, sidx + 1, j * 128:(j + 1) * 128],
                                  vv_[:, ki, g_l, 0:65], start=first_out,
                                  stop=(ki == kmax - 1 and j == 3))
                    first_out = False
                mm.then_inc(P.sems["pe"], 1)

            @P.block.tensor
            def _(t, p_=p_, qj=qj, kmax=kmax, diag=diag, ac_v=ac_v,
                  dv_m=dv_m, prev_ep=prev_ep, g_=g_):
                if p_ == 0 and qj == 0:
                    t.wait_ge(P.sems["dv"], krep)
                    t.wait_ge(P.sems["gp"], rope_done)
                    t.wait_ge(P.sems["ld"], lm)
                if prev_ep is not None:
                    t.wait_ge(P.sems["dv"], prev_ep)
                for ki in range(kmax):
                    sb_, sb2 = (b[2], b[3]) if ki % 2 == 0 else (b[4], b[5])
                    if ki >= 2:
                        t.wait_ge(P.sems["ac"], ac_v[ki - 2])
                    kt_t = ktr[g_ // 2]
                    gg = g_ % 2
                    t.matmul(sb_[:], kt_t[0:64, gg, ki * 128:(ki + 1) * 128],
                             qT[0:64, p_, qj * 512:(qj + 1) * 512], start=True, stop=True)
                    t.matmul(sb2[:], kt_t[64:128, gg, ki * 128:(ki + 1) * 128],
                             qT[64:128, p_, qj * 512:(qj + 1) * 512], start=True, stop=True
                             ).then_inc(P.sems["pe"], 1)
                    if ki >= 1:
                        emit_av(t, ki - 1, kmax, diag, ac_v, dv_m, ki == 1)
                emit_av(t, kmax - 1, kmax, diag, ac_v, dv_m, kmax == 1)

            @P.block.scalar
            def _(sa, kmax=kmax, s_idx=s_idx, a_idx=a_idx, p_=p_, qj=qj):
                if p_ == 0 and qj == 0:
                    apply_pending(P, sa, f"{tag}_strip")
                for ki in range(kmax):
                    sidx = (ki % 2) * 2
                    sa.wait_ge(P.sems["pe"], s_idx[ki])
                    if ki >= 2:
                        sa.wait_ge(P.sems["pe"], a_idx[ki - 2])
                    sb_, sb2 = (b[2], b[3]) if ki % 2 == 0 else (b[4], b[5])
                    sa.activation(strip[:, sidx, :], sb_[:], AF.Exp, scale=0.125)
                    sa.activation(strip[:, sidx + 1, :], sb2[:], AF.Exp, scale=0.125
                                  ).then_inc(P.sems["ac"], 1)

            @P.block.vector
            def _(vv, kmax=kmax, qj=qj, diag=diag, ac_v=ac_v, a_idx=a_idx,
                  ep1=ep1, ep2=ep2, p_=p_):
                if p_ == 0 and qj == 0:
                    apply_pending(P, vv, f"{tag}_y")
                    vv.wait_ge(P.sems["ld"], lm)
                for ki in diag:
                    sidx = (ki % 2) * 2
                    o_var = ki - 4 * qj
                    vv.wait_ge(P.sems["ac"], ac_v[ki])
                    vv.tensor_mul(strip[:, sidx, :], strip[:, sidx, :], msk[:, o_var, :])
                    vv.tensor_mul(strip[:, sidx + 1, :], strip[:, sidx + 1, :],
                                  msk[:, o_var, :]).then_inc(P.sems["dv"], 1)
                vv.wait_ge(P.sems["pe"], a_idx[kmax - 1])
                vv.reciprocal(rd[:, 0:4], bass.AP(b[0], 64, [[512, 128], [128, 4]]))
                vv.reciprocal(rd[:, 4:8], bass.AP(b[1], 64, [[512, 128], [128, 4]])
                              ).then_inc(P.sems["dv"], 1)
                vv.wait_ge(P.sems["dv"], ep1)
                for j in range(4):
                    vv.tensor_scalar_mul(y[:, qj * 4 + j, (2 * p_) * 64:(2 * p_ + 1) * 64],
                                         b[0][:, j * 128:j * 128 + 64], rd[:, j:j + 1])
                    ins = vv.tensor_scalar_mul(y[:, qj * 4 + j, (2 * p_ + 1) * 64:(2 * p_ + 2) * 64],
                                               b[1][:, j * 128:j * 128 + 64], rd[:, 4 + j:5 + j])
                ins.then_inc(P.sems["dv"], 1)
            prev_ep = ep2

    AB.free(f"{tag}_qT", f"{tag}_ktr01", f"{tag}_ktr23", f"{tag}_strip", f"{tag}_v",
            readers={"pe": P.cnt["pe"]})
    AB.free(f"{tag}_mask", readers={"dv": prev_ep})

    if upto == "C":
        dbg_dump(P, "y", AB.ap(f"{tag}_y"), {"dv": prev_ep})
        return

    AB.alloc(f"{tag}_yT", 8 * T)
    yT = v3(AB.ap(f"{tag}_yT"), T)
    yt_pe = P.res("pe", TT)
    yt_dv = P.res("dv", TT)

    @P.block.tensor
    def _(t):
        t.wait_ge(P.sems["dv"], prev_ep)
        for tt in range(TT):
            if tt >= 1:
                t.wait_ge(P.sems["dv"], yt_dv[tt - 1])
            for c in range(8):
                mm = t.transpose(bt[:, c * 128:(c + 1) * 128], y[:, tt, c * 128:(c + 1) * 128],
                                 P.ident[:])
            mm.then_inc(P.sems["pe"], 1)

    @P.block.vector
    def _(vv):
        apply_pending(P, vv, f"{tag}_yT")
        for tt in range(TT):
            vv.wait_ge(P.sems["pe"], yt_pe[tt])
            vv.tensor_copy(yT[:, :, tt * 128:(tt + 1) * 128],
                           bt[:].rearrange("p (c t) -> p c t", t=128)).then_inc(P.sems["dv"], 1)

    op_pe = P.res("pe", TT)
    op_dv = P.res("dv", TT)

    @P.block.tensor
    def _(t):
        t.wait_ge(P.sems["lg"], lwo)
        for tt in range(TT):
            pr = (b[2], b[3]) if tt % 2 == 0 else (b[4], b[5])
            if tt >= 2:
                t.wait_ge(P.sems["dv"], op_dv[tt - 2])
            for nj in range(2):
                for c in range(8):
                    mm = t.matmul(pr[nj][:], yT[:, c, tt * 128:(tt + 1) * 128],
                                  wo[:, c, nj * 512:(nj + 1) * 512],
                                  start=(c == 0), stop=(c == 7))
            mm.then_inc(P.sems["pe"], 1)

    @P.block.vector
    def _(vv):
        for tt in range(TT):
            pr = (b[2], b[3]) if tt % 2 == 0 else (b[4], b[5])
            vv.wait_ge(P.sems["pe"], op_pe[tt])
            vv.tensor_add(X[:, tt * 1024:tt * 1024 + 512], pr[0][:],
                          X[:, tt * 1024:tt * 1024 + 512])
            vv.tensor_add(X[:, tt * 1024 + 512:(tt + 1) * 1024], pr[1][:],
                          X[:, tt * 1024 + 512:(tt + 1) * 1024]).then_inc(P.sems["dv"], 1)

    dbg_dump(P, "xpre", P.X[:, :], {"dv": op_dv[TT - 1]}, dt=F32)
    AB.free(f"{tag}_y", readers={"pe": yt_pe[TT - 1]})
    AB.free(f"{tag}_yT", f"{tag}_wo", readers={"pe": op_pe[TT - 1]})
    P.ready = emit_postln(P, tag, {"dv": op_dv[TT - 1]}, refresh_xt=refresh_out)



def emit_rms_toT(P, tag, gain_ap, dst, xready):
    TT = P.TT
    SC = P.SC
    stats = SC[:, 2048:2048 + 12 * TT].rearrange("p (t k) -> p t k", k=12)
    agg = SC[:, 2048 + 12 * TT:2048 + 16 * TT].rearrange("p (t k) -> p t k", k=4)
    g1, g2, g3, g3b, g3c, g4 = P.res("dv", 6)
    g5 = P.res("ac", 1)[0]

    @P.block.vector
    def _(vv):
        for snm, v_ in xready.items():
            vv.wait_ge(P.sems[snm], v_)
        for tt in range(TT):
            vv.bn_stats(stats[:, tt, 0:6], P.X[:, tt * 1024:tt * 1024 + 512])
            ins = vv.bn_stats(stats[:, tt, 6:12], P.X[:, tt * 1024 + 512:(tt + 1) * 1024])
        ins.then_inc(P.sems["dv"], 1)
        vv.wait_ge(P.sems["dv"], g1)
        for tt in range(TT):
            ins = vv.bn_aggr(agg[:, tt, 0:2], stats[:, tt, :])
        ins.then_inc(P.sems["dv"], 1)
        vv.wait_ge(P.sems["dv"], g2)
        vv.tensor_mul(agg[:, :, 2], agg[:, :, 0], agg[:, :, 0]).then_inc(P.sems["dv"], 1)
        vv.wait_ge(P.sems["dv"], g3)
        vv.tensor_add(agg[:, :, 2], agg[:, :, 2], agg[:, :, 1]).then_inc(P.sems["dv"], 1)
        vv.wait_ge(P.sems["dv"], g3b)
        vv.tensor_scalar_add(agg[:, :, 2], agg[:, :, 2], 1e-5).then_inc(P.sems["dv"], 1)
        vv.wait_ge(P.sems["dv"], g3c)
        vv.reciprocal(agg[:, :, 3], agg[:, :, 2]).then_inc(P.sems["dv"], 1)

    @P.block.scalar
    def _(sa):
        sa.wait_ge(P.sems["dv"], g4)
        sa.activation(agg[:, :, 3], agg[:, :, 3], AF.Sqrt).then_inc(P.sems["ac"], 1)

    def cast(vv, tt, out_ap):
        return vv.scalar_tensor_tensor(out_ap, P.X[:, tt * 1024:(tt + 1) * 1024],
                                       agg[:, tt, 3:4], gain_ap, ALU.mult, ALU.mult)
    return emit_toT(P, f"{tag}_h", wait={"ac": g5}, dst=dst, cast_emit=cast)


def emit_l0_ffn(P, s, upto=None, refresh_out=True):
    T, TT, NQ = P.T, P.TT, P.NQ
    AB, SC, X, b, bt = P.AB, P.SC, P.X, P.banks, P.bt
    nc = P.nc
    tag = f"f0s{s}"

    wq_d = P.din("f0_wq", [1024, 1024])
    wkva_d = P.din("f0_wkva", [1024, 160])
    wkvb_d = P.din("f0_wkvb", [128, 512])
    woe_d = P.din("f0_woe", [512, 1024])
    wg_d = P.din("f0_wg", [9, 1024, 512])
    wu_d = P.din("f0_wu", [9, 1024, 512])
    wd_d = P.din("f0_wd", [9, 512, 1024])
    wgate_d = P.din("f0_wgate", [1024, 8])
    ebias_d = P.din("f0_ebias", [1, 8])
    tcm_d = P.din("c_tcm", [128, T], BF16)
    tsm_d = P.din("c_tsm", [128, T], BF16)
    masks_d = P.din("c_masks", [128, 4 * 512], BF16)

    if not getattr(P, "f0_gains", False):
        P.f0_gains = True
        g1_d = P.din("f0_g1", [1, 1024])
        g2_d = P.din("f0_g2", [1, 1024])
        gl = P.res("ld", 2, 16)

        @P.block.sync
        def _(sy):
            sy.dma_start(out=P.AF[:, 0:1024], in_=g1_d.broadcast_to([128, 1024])
                         ).then_inc(P.sems["ld"], 16)
            sy.dma_start(out=P.AF[:, 1024:2048], in_=g2_d.broadcast_to([128, 1024])
                         ).then_inc(P.sems["ld"], 16)
        P.f0_gl = gl[-1]

        P.f0_ones = None

    xready = dict(P.ready)

    AB.alloc(f"{tag}_hT", 8 * T)
    hT = v3(AB.ap(f"{tag}_hT"), T)
    hr = emit_rms_toT(P, tag, P.AF[:, 0:1024], f"{tag}_hT",
                      {**xready, "ld": P.f0_gl})

    AB.alloc(f"{tag}_qT", 8 * T)
    qT = v3(AB.ap(f"{tag}_qT"), T)
    AB.alloc(f"{tag}_wq", 8 * 1024)
    wqv = v3(AB.ap(f"{tag}_wq"), 1024)
    AB.alloc(f"{tag}_tbl", 2 * T)
    tbl = v3(AB.ap(f"{tag}_tbl"), T)

    lw = P.res("lg", 1, 16)[0]

    @P.block.gpsimd
    def _(g):
        apply_pending(P, g, f"{tag}_wq")
        g.dma_start(out=wqv, in_=wq_d.rearrange("(c p) n -> p c n", p=128)
                    ).then_inc(P.sems["lg"], 16)

    ltl = P.res("ld", 2, 16)
    lt = ltl[-1]

    @P.block.sync
    def _(sy):
        apply_pending(P, sy, f"{tag}_tbl")
        sy.dma_start(out=tbl[:, 0, :], in_=tcm_d).then_inc(P.sems["ld"], 16)
        sy.dma_start(out=tbl[:, 1, :], in_=tsm_d).then_inc(P.sems["ld"], 16)

    qp_pe = P.res("pe", 8 * NQ)
    qp_dv = P.res("dv", 8 * NQ)

    @P.block.tensor
    def _(t):
        t.wait_ge(P.sems["lg"], lw)
        t.wait_ge(P.sems["dv"], hr["dv"])
        i = 0
        for pt in range(8):
            for tq in range(NQ):
                bank = b[i % 2]
                if i >= 2:
                    t.wait_ge(P.sems["dv"], qp_dv[i - 2])
                for c in range(8):
                    mm = t.matmul(bank[:], wqv[:, c, pt * 128:(pt + 1) * 128],
                                  hT[:, c, tq * 512:(tq + 1) * 512],
                                  start=(c == 0), stop=(c == 7))
                mm.then_inc(P.sems["pe"], 1)
                i += 1

    @P.block.vector
    def _(vv):
        apply_pending(P, vv, f"{tag}_qT")
        i = 0
        for pt in range(8):
            for tq in range(NQ):
                vv.wait_ge(P.sems["pe"], qp_pe[i])
                vv.tensor_copy(qT[:, pt, tq * 512:(tq + 1) * 512], b[i % 2][:]
                               ).then_inc(P.sems["dv"], 1)
                i += 1

    AB.free(f"{tag}_wq", readers={"pe": qp_pe[-1]})

    AB.alloc(f"{tag}_rsc", 2 * T)
    rsc = v3(AB.ap(f"{tag}_rsc"), T)
    prev = {"dv": qp_dv[-1]}
    first = True
    for c in range(8):
        r = emit_rope(P, qT[:, c, :], rsc, tbl, prev, lt,
                      rsc_name=(f"{tag}_rsc" if first else None))
        prev = {"gp": r}
        first = False
    q_rope_done = r

    AB.alloc(f"{tag}_kvl", T)
    kvl = v3(AB.ap(f"{tag}_kvl"), T)
    AB.alloc(f"{tag}_krs", T)
    KRS = AB.ap(f"{tag}_krs")
    AB.alloc(f"{tag}_wkva", 8 * 160)
    wkvav = v3(AB.ap(f"{tag}_wkva"), 160)
    AB.alloc(f"{tag}_wkvb", 512)
    wkvbv = AB.ap(f"{tag}_wkvb")

    lkvl = P.res("lg", 2, 16)
    lkv = lkvl[-1]

    @P.block.gpsimd
    def _(g):
        apply_pending(P, g, f"{tag}_wkva", f"{tag}_wkvb")
        g.dma_start(out=wkvav, in_=wkva_d.rearrange("(c p) n -> p c n", p=128)
                    ).then_inc(P.sems["lg"], 16)
        g.dma_start(out=wkvbv, in_=wkvb_d).then_inc(P.sems["lg"], 16)

    ck_pe = P.res("pe", NQ + NQ)
    ck_dv = P.res("dv", NQ + NQ + 1)

    @P.block.tensor
    def _(t):
        t.wait_ge(P.sems["lg"], lkv)
        for tq in range(NQ):
            bank = b[2 + tq % 2]
            if tq >= 2:
                t.wait_ge(P.sems["dv"], ck_dv[tq - 2])
            for c in range(8):
                mm = t.matmul(bank[:], wkvav[:, c, 0:128], hT[:, c, tq * 512:(tq + 1) * 512],
                              start=(c == 0), stop=(c == 7))
            mm.then_inc(P.sems["pe"], 1)
        for tq in range(NQ):
            bank = b[4 + tq % 2]
            if tq >= 2:
                t.wait_ge(P.sems["dv"], ck_dv[NQ + tq - 2])
            for c in range(8):
                mm = t.matmul(bank[0:32, :], wkvav[:, c, 128:160], hT[:, c, tq * 512:(tq + 1) * 512],
                              start=(c == 0), stop=(c == 7))
            mm.then_inc(P.sems["pe"], 1)

    @P.block.vector
    def _(vv):
        apply_pending(P, vv, f"{tag}_kvl", f"{tag}_krs")
        for tq in range(NQ):
            vv.wait_ge(P.sems["pe"], ck_pe[tq])
            vv.tensor_copy(kvl[:, 0, tq * 512:(tq + 1) * 512], b[2 + tq % 2][:]
                           ).then_inc(P.sems["dv"], 1)
        for tq in range(NQ):
            vv.wait_ge(P.sems["pe"], ck_pe[NQ + tq])
            vv.tensor_copy(KRS[32:64, tq * 512:(tq + 1) * 512], b[4 + tq % 2][0:32, :]
                           ).then_inc(P.sems["dv"], 1)
        vv.memset(KRS[0:32, :], 0.0)
        vv.memset(KRS[64:128, :], 0.0).then_inc(P.sems["dv"], 1)

    kro = emit_rope(P, KRS, rsc, tbl, {"dv": ck_dv[-1], "gp": q_rope_done}, lt)
    AB.free(f"{tag}_rsc", f"{tag}_tbl", readers={"gp": kro})
    AB.free(f"{tag}_wkva", readers={"pe": ck_pe[-1]})

    AB.alloc(f"{tag}_kva", 4 * T)
    kva = v3(AB.ap(f"{tag}_kva"), T)
    TTH = TT // 2
    AB.alloc(f"{tag}_v0", TTH * 512)
    AB.alloc(f"{tag}_v1", TTH * 512)
    _vh = [v3(AB.ap(f"{tag}_v0"), 512), v3(AB.ap(f"{tag}_v1"), 512)]
    def vsl(ki, sl):
        return _vh[ki // TTH][:, ki % TTH, sl]

    kb_pe = P.res("pe", 4 * NQ)
    kb_dv = P.res("dv", 4 * NQ)

    @P.block.tensor
    def _(t):
        t.wait_ge(P.sems["dv"], ck_dv[NQ - 1])
        t.wait_ge(P.sems["dv"], qp_dv[-1])
        i = 0
        for st_ in range(4):
            for tq in range(NQ):
                bank = b[i % 2]
                if i >= 2:
                    t.wait_ge(P.sems["dv"], kb_dv[i - 2])
                t.matmul(bank[:], wkvbv[:, st_ * 128:(st_ + 1) * 128],
                         kvl[:, 0, tq * 512:(tq + 1) * 512], start=True, stop=True
                         ).then_inc(P.sems["pe"], 1)
                i += 1

    @P.block.vector
    def _(vv):
        apply_pending(P, vv, f"{tag}_kva")
        i = 0
        for st_ in range(4):
            for tq in range(NQ):
                vv.wait_ge(P.sems["pe"], kb_pe[i])
                vv.tensor_copy(kva[:, st_, tq * 512:(tq + 1) * 512], b[i % 2][:]
                               ).then_inc(P.sems["dv"], 1)
                i += 1

    vt_pe = P.res("pe", TT)
    vt_dv = P.res("dv", TT)

    @P.block.tensor
    def _(t):
        t.wait_ge(P.sems["dv"], ck_dv[-1])
        for tt in range(TT):
            bank = b[2 + tt % 2]
            if tt >= 2:
                t.wait_ge(P.sems["dv"], vt_dv[tt - 2])
            t.matmul(bank[:], kvl[:, 0, tt * 128:(tt + 1) * 128], wkvbv[:],
                     start=True, stop=True).then_inc(P.sems["pe"], 1)

    @P.block.vector
    def _(vv):
        apply_pending(P, vv, f"{tag}_v0", f"{tag}_v1")
        for tt in range(TT):
            vv.wait_ge(P.sems["pe"], vt_pe[tt])
            vv.tensor_copy(vsl(tt, slice(0, 512)), b[2 + tt % 2][:]).then_inc(P.sems["dv"], 1)

    AB.free(f"{tag}_hT", readers={"pe": vt_pe[-1]})
    AB.free(f"{tag}_kvl", f"{tag}_wkvb", readers={"pe": vt_pe[-1]})

    AB.alloc(f"{tag}_y", TT * 512)
    y = v3(AB.ap(f"{tag}_y"), 512)
    AB.alloc(f"{tag}_woe", 4 * 1024)
    woev = v3(AB.ap(f"{tag}_woe"), 1024)
    AB.alloc(f"{tag}_mask", 4 * 512)
    msk = v3(AB.ap(f"{tag}_mask"), 512)
    AB.alloc(f"{tag}_strip", 4 * 512)
    strip = v3(AB.ap(f"{tag}_strip"), 512)
    AB.alloc(f"{tag}_ktp", T)
    ktp = AB.ap(f"{tag}_ktp")

    lm = P.res("ld", 1, 16)[0]
    lwo = P.res("lg", 1, 16)[0]

    @P.block.sync
    def _(sy):
        apply_pending(P, sy, f"{tag}_mask")
        sy.dma_start(out=AB.ap(f"{tag}_mask"), in_=masks_d).then_inc(P.sems["ld"], 16)

    @P.block.gpsimd
    def _(g):
        apply_pending(P, g, f"{tag}_woe")
        g.dma_start(out=woev, in_=woe_d.rearrange("(c p) n -> p c n", p=128)
                    ).then_inc(P.sems["lg"], 16)

    rd = SC[:, 1664:1672]
    prev_ep = None
    prev_sc = None
    for p_ in range(8):
        stA, rA = p_ // 2, (2 * p_ % 4) * 32
        stB, rB = p_ // 2, ((2 * p_ + 1) % 4) * 32
        ktb = P.res("dv", 1)[0]

        @P.block.vector
        def _(vv, p_=p_, stA=stA, rA=rA, stB=stB, rB=rB, prev_sc=prev_sc):
            if p_ == 0:
                apply_pending(P, vv, f"{tag}_ktp")
                vv.wait_ge(P.sems["dv"], kb_dv[-1])
                vv.wait_ge(P.sems["gp"], kro)
            if prev_sc is not None:
                vv.wait_ge(P.sems["pe"], prev_sc)
            vv.tensor_copy(ktp[0:32, :], kva[rA:rA + 32, stA, :])
            vv.tensor_copy(ktp[32:64, :], KRS[32:64, :])
            vv.tensor_copy(ktp[64:96, :], kva[rB:rB + 32, stB, :])
            vv.tensor_copy(ktp[96:128, :], KRS[32:64, :]).then_inc(P.sems["dv"], 1)

        for qj in range(NQ):
            kmax = min(TT, 4 * (qj + 1))
            diag = sorted(set(range(4 * qj, kmax)))
            pe_v = P.res("pe", 2 * kmax)
            s_idx, a_idx = {}, {}
            n_ = 0
            for ki in range(kmax):
                s_idx[ki] = pe_v[n_]; n_ += 1
                if ki >= 1:
                    a_idx[ki - 1] = pe_v[n_]; n_ += 1
            a_idx[kmax - 1] = pe_v[n_]
            ac_v = P.res("ac", kmax)
            dvl = P.res("dv", len(diag) + 2)
            dv_m = {ki: dvl[i] for i, ki in enumerate(diag)}
            ep1, ep2 = dvl[-2], dvl[-1]

            def emit_avm(t, ki, kmax, diag, ac_v, dv_m, first_out, p_l):
                t.wait_ge(P.sems["ac"], ac_v[ki])
                if ki in diag:
                    t.wait_ge(P.sems["dv"], dv_m[ki])
                sidx = (ki % 2) * 2
                for j in range(4):
                    t.matmul(b[0][:, j * 64:j * 64 + 32],
                             strip[:, sidx, j * 128:(j + 1) * 128],
                             vsl(ki, slice(2 * p_l * 32, (2 * p_l + 1) * 32)),
                             start=first_out, stop=False)
                    t.matmul(b[0][:, j * 64 + 32:j * 64 + 33],
                             strip[:, sidx, j * 128:(j + 1) * 128],
                             P.onesb[:, 0:1], start=False,
                             stop=(ki == kmax - 1 and j == 3))
                    t.matmul(b[1][:, j * 64:j * 64 + 32],
                             strip[:, sidx + 1, j * 128:(j + 1) * 128],
                             vsl(ki, slice((2 * p_l + 1) * 32, (2 * p_l + 2) * 32)),
                             start=first_out, stop=False)
                    mm = t.matmul(b[1][:, j * 64 + 32:j * 64 + 33],
                                  strip[:, sidx + 1, j * 128:(j + 1) * 128],
                                  P.onesb[:, 0:1], start=False,
                                  stop=(ki == kmax - 1 and j == 3))
                    first_out = False
                mm.then_inc(P.sems["pe"], 1)

            @P.block.tensor
            def _(t, p_=p_, qj=qj, kmax=kmax, diag=diag, ac_v=ac_v,
                  dv_m=dv_m, prev_ep=prev_ep, ktb=ktb):
                if p_ == 0 and qj == 0:
                    t.wait_ge(P.sems["gp"], q_rope_done)
                    t.wait_ge(P.sems["ld"], lm)
                    t.wait_ge(P.sems["ld"], P.const_ready)
                    t.wait_ge(P.sems["dv"], vt_dv[TT - 1])
                if qj == 0:
                    t.wait_ge(P.sems["dv"], ktb)
                if prev_ep is not None:
                    t.wait_ge(P.sems["dv"], prev_ep)
                for ki in range(kmax):
                    sb_, sb2 = (b[2], b[3]) if ki % 2 == 0 else (b[4], b[5])
                    if ki >= 2:
                        t.wait_ge(P.sems["ac"], ac_v[ki - 2])
                    t.matmul(sb_[:], ktp[0:64, ki * 128:(ki + 1) * 128],
                             qT[0:64, p_, qj * 512:(qj + 1) * 512], start=True, stop=True)
                    t.matmul(sb2[:], ktp[64:128, ki * 128:(ki + 1) * 128],
                             qT[64:128, p_, qj * 512:(qj + 1) * 512], start=True, stop=True
                             ).then_inc(P.sems["pe"], 1)
                    if ki >= 1:
                        emit_avm(t, ki - 1, kmax, diag, ac_v, dv_m, ki == 1, p_)
                emit_avm(t, kmax - 1, kmax, diag, ac_v, dv_m, kmax == 1, p_)

            @P.block.scalar
            def _(sa, kmax=kmax, s_idx=s_idx, a_idx=a_idx, p_=p_, qj=qj):
                if p_ == 0 and qj == 0:
                    apply_pending(P, sa, f"{tag}_strip")
                for ki in range(kmax):
                    sidx = (ki % 2) * 2
                    sa.wait_ge(P.sems["pe"], s_idx[ki])
                    if ki >= 2:
                        sa.wait_ge(P.sems["pe"], a_idx[ki - 2])
                    sb_, sb2 = (b[2], b[3]) if ki % 2 == 0 else (b[4], b[5])
                    sa.activation(strip[:, sidx, :], sb_[:], AF.Exp, scale=0.125)
                    sa.activation(strip[:, sidx + 1, :], sb2[:], AF.Exp, scale=0.125
                                  ).then_inc(P.sems["ac"], 1)

            @P.block.vector
            def _(vv, kmax=kmax, qj=qj, diag=diag, ac_v=ac_v, a_idx=a_idx,
                  ep1=ep1, ep2=ep2, p_=p_):
                if p_ == 0 and qj == 0:
                    apply_pending(P, vv, f"{tag}_y")
                    vv.wait_ge(P.sems["ld"], lm)
                for ki in diag:
                    sidx = (ki % 2) * 2
                    o_var = ki - 4 * qj
                    vv.wait_ge(P.sems["ac"], ac_v[ki])
                    vv.tensor_mul(strip[:, sidx, :], strip[:, sidx, :], msk[:, o_var, :])
                    vv.tensor_mul(strip[:, sidx + 1, :], strip[:, sidx + 1, :],
                                  msk[:, o_var, :]).then_inc(P.sems["dv"], 1)
                vv.wait_ge(P.sems["pe"], a_idx[kmax - 1])
                vv.reciprocal(rd[:, 0:4], bass.AP(b[0], 32, [[512, 128], [64, 4]]))
                vv.reciprocal(rd[:, 4:8], bass.AP(b[1], 32, [[512, 128], [64, 4]])
                              ).then_inc(P.sems["dv"], 1)
                vv.wait_ge(P.sems["dv"], ep1)
                for j in range(4):
                    vv.tensor_scalar_mul(y[:, qj * 4 + j, (2 * p_) * 32:(2 * p_ + 1) * 32],
                                         b[0][:, j * 64:j * 64 + 32], rd[:, j:j + 1])
                    ins = vv.tensor_scalar_mul(y[:, qj * 4 + j, (2 * p_ + 1) * 32:(2 * p_ + 2) * 32],
                                               b[1][:, j * 64:j * 64 + 32], rd[:, 4 + j:5 + j])
                ins.then_inc(P.sems["dv"], 1)
            prev_ep = ep2
        prev_sc = s_idx[kmax - 1]

    AB.free(f"{tag}_qT", f"{tag}_strip", f"{tag}_ktp", f"{tag}_kva", f"{tag}_krs",
            f"{tag}_v0", f"{tag}_v1", readers={"pe": P.cnt["pe"]})
    AB.free(f"{tag}_mask", readers={"dv": prev_ep})

    if upto == "C2":
        dbg_dump(P, "ymla", AB.ap(f"{tag}_y"), {"dv": prev_ep})
        dbg_dump(P, "qTm", AB.ap(f"{tag}_qT"), {"gp": q_rope_done})
        dbg_dump(P, "krsm", AB.ap(f"{tag}_krs"), {"gp": kro})
        dbg_dump(P, "kvam", AB.ap(f"{tag}_kva"), {"dv": kb_dv[-1]})
        return

    AB.alloc(f"{tag}_ymT", 4 * T)
    ymT = v3(AB.ap(f"{tag}_ymT"), T)
    yt_pe = P.res("pe", TT)
    yt_dv = P.res("dv", TT)

    @P.block.tensor
    def _(t):
        t.wait_ge(P.sems["dv"], prev_ep)
        for tt in range(TT):
            if tt >= 1:
                t.wait_ge(P.sems["dv"], yt_dv[tt - 1])
            for c in range(4):
                mm = t.transpose(bt[:, c * 128:(c + 1) * 128], y[:, tt, c * 128:(c + 1) * 128],
                                 P.ident[:])
            mm.then_inc(P.sems["pe"], 1)

    @P.block.vector
    def _(vv):
        apply_pending(P, vv, f"{tag}_ymT")
        for tt in range(TT):
            vv.wait_ge(P.sems["pe"], yt_pe[tt])
            vv.tensor_copy(ymT[:, :, tt * 128:(tt + 1) * 128],
                           bt[:, 0:512].rearrange("p (c t) -> p c t", t=128)
                           ).then_inc(P.sems["dv"], 1)

    AB.alloc(f"{tag}_hnT", 8 * T)
    hnT = v3(AB.ap(f"{tag}_hnT"), T)
    AB.alloc(f"{tag}_hcb", 2048)
    hcb_off = AB.blocks[f"{tag}_hcb"][0]
    hcb = [AB.t[:, hcb_off:hcb_off + 1024], AB.t[:, hcb_off + 1024:hcb_off + 2048]]
    xm = [SC[:, 2048:3072], SC[:, 3072:4096]]
    est = SC[:, 1672:1672 + 16 * TT].rearrange("p (t k) -> p t k", k=16)

    opv = P.res("pe", 2 * TT)
    op_pe = {tt: opv[2 * tt] for tt in range(TT)}
    ht_pe = {tt: opv[2 * tt + 1] for tt in range(TT)}
    dvv = P.res("dv", 8 * TT)
    e_v = {(tt, k): dvv[8 * tt + k] for tt in range(TT) for k in range(8)}
    xm_dv = {tt: e_v[(tt, 0)] for tt in range(TT)}
    ms_dv = {tt: e_v[(tt, 5)] for tt in range(TT)}
    hn_dv = {tt: e_v[(tt, 6)] for tt in range(TT)}
    he_dv = {tt: e_v[(tt, 7)] for tt in range(TT)}
    rr_ac = P.res("ac", TT)
    sq_ac = {tt: rr_ac[tt] for tt in range(TT)}

    @P.block.tensor
    def _(t):
        t.wait_ge(P.sems["lg"], lwo)
        t.wait_ge(P.sems["dv"], yt_dv[TT - 1])
        for tt in range(TT):
            pr = (b[2], b[3]) if tt % 2 == 0 else (b[4], b[5])
            if tt >= 2:
                t.wait_ge(P.sems["dv"], xm_dv[tt - 2])
            for nj in range(2):
                for c in range(4):
                    mm = t.matmul(pr[nj][:], ymT[:, c, tt * 128:(tt + 1) * 128],
                                  woev[:, c, nj * 512:(nj + 1) * 512],
                                  start=(c == 0), stop=(c == 3))
            mm.then_inc(P.sems["pe"], 1)
            t.wait_ge(P.sems["dv"], hn_dv[tt])
            if tt >= 1:
                t.wait_ge(P.sems["dv"], he_dv[tt - 1])
            for c in range(8):
                mm = t.transpose(bt[:, c * 128:(c + 1) * 128],
                                 hcb[tt % 2][:, c * 128:(c + 1) * 128], P.ident[:])
            mm.then_inc(P.sems["pe"], 1)

    @P.block.vector
    def _(vv):
        apply_pending(P, vv, f"{tag}_hnT", f"{tag}_hcb")
        for tt in range(TT):
            pr = (b[2], b[3]) if tt % 2 == 0 else (b[4], b[5])
            vv.wait_ge(P.sems["pe"], op_pe[tt])
            x_t = X[:, tt * 1024:(tt + 1) * 1024]
            vv.tensor_add(xm[tt % 2][:, 0:512], pr[0][:], x_t[:, 0:512])
            vv.tensor_add(xm[tt % 2][:, 512:1024], pr[1][:], x_t[:, 512:1024]
                          ).then_inc(P.sems["dv"], 1)
            vv.wait_ge(P.sems["dv"], e_v[(tt, 0)])
            vv.bn_stats(est[:, tt, 0:6], xm[tt % 2][:, 0:512])
            vv.bn_stats(est[:, tt, 6:12], xm[tt % 2][:, 512:1024]).then_inc(P.sems["dv"], 1)
            vv.wait_ge(P.sems["dv"], e_v[(tt, 1)])
            vv.bn_aggr(est[:, tt, 12:14], est[:, tt, 0:12]).then_inc(P.sems["dv"], 1)
            vv.wait_ge(P.sems["dv"], e_v[(tt, 2)])
            vv.scalar_tensor_tensor(est[:, tt, 14:15], est[:, tt, 12:13], est[:, tt, 12:13],
                                    est[:, tt, 13:14], ALU.mult, ALU.add).then_inc(P.sems["dv"], 1)
            vv.wait_ge(P.sems["dv"], e_v[(tt, 3)])
            vv.tensor_scalar_add(est[:, tt, 14:15], est[:, tt, 14:15], 1e-5
                                 ).then_inc(P.sems["dv"], 1)
            vv.wait_ge(P.sems["dv"], e_v[(tt, 4)])
            vv.reciprocal(est[:, tt, 15:16], est[:, tt, 14:15]).then_inc(P.sems["dv"], 1)
            vv.wait_ge(P.sems["ac"], sq_ac[tt])
            vv.scalar_tensor_tensor(hcb[tt % 2], xm[tt % 2], est[:, tt, 15:16],
                                    P.AF[:, 1024:2048], ALU.mult, ALU.mult)
            vv.tensor_add(X[:, tt * 1024:(tt + 1) * 1024],
                          xm[tt % 2], X[:, tt * 1024:(tt + 1) * 1024]).then_inc(P.sems["dv"], 1)
            vv.wait_ge(P.sems["pe"], ht_pe[tt])
            vv.tensor_copy(hnT[:, :, tt * 128:(tt + 1) * 128],
                           bt[:].rearrange("p (c t) -> p c t", t=128)).then_inc(P.sems["dv"], 1)

    @P.block.scalar
    def _(sa):
        for tt in range(TT):
            sa.wait_ge(P.sems["dv"], ms_dv[tt])
            sa.activation(est[:, tt, 15:16], est[:, tt, 15:16], AF.Sqrt
                          ).then_inc(P.sems["ac"], 1)

    AB.free(f"{tag}_y", readers={"pe": yt_pe[TT - 1]})
    AB.free(f"{tag}_ymT", f"{tag}_woe", readers={"pe": op_pe[TT - 1]})
    AB.free(f"{tag}_hcb", readers={"pe": ht_pe[TT - 1]})
    moe_in_ready = {"dv": he_dv[TT - 1]}

    if upto == "M":
        dbg_dump(P, "hnT", AB.ap(f"{tag}_hnT"), moe_in_ready)
        dbg_dump(P, "xmid2", X[:, :], moe_in_ready, dt=F32)
        return

    AB.alloc(f"{tag}_wgu", 2 * 8 * 512)
    wguv = AB.ap(f"{tag}_wgu").rearrange("p (w c n) -> p w c n", w=2, n=512)
    AB.alloc(f"{tag}_wd", 4 * 1024)
    wdv = v3(AB.ap(f"{tag}_wd"), 1024)
    AB.alloc(f"{tag}_gub", 3 * 4 * 512)
    gub = AB.ap(f"{tag}_gub").rearrange("p (k i n) -> p k i n", k=3, n=512)
    AB.alloc(f"{tag}_wgate", 8 * 8)
    wgv = v3(AB.ap(f"{tag}_wgate"), 8)
    combw = SC[:, 0:8 * TT].rearrange("p (t e) -> p t e", e=8)
    mx8 = SC[:, 8 * TT:8 * TT + 8]
    rt8 = SC[:, 8 * TT + 8:8 * TT + 24]
    iota8 = SC[:, 8 * TT + 24:8 * TT + 32]
    lg8 = SC[:, 8 * TT + 32:8 * TT + 40]
    ebc = P.AF[:, 2048:2056]

    lgt = P.res("lg", 1, 16)[0]
    gpi = P.res("gp", 1)[0]
    lde = P.res("ld", 1, 16)[0]

    @P.block.sync
    def _(sy):
        sy.dma_start(out=ebc, in_=ebias_d.broadcast_to([128, 8])).then_inc(P.sems["ld"], 16)

    @P.block.gpsimd
    def _(g):
        apply_pending(P, g, f"{tag}_wgate")
        g.dma_start(out=wgv, in_=wgate_d.rearrange("(c p) n -> p c n", p=128)
                    ).then_inc(P.sems["lg"], 16)
        g.iota(iota8, [[1, 8]], base=0, channel_multiplier=0,
               allow_small_or_imprecise_dtypes=True).then_inc(P.sems["gp"], 1)

    rt_pe = P.res("pe", TT)
    rt_dv = P.res("dv", 7 * TT)
    rt_ac = P.res("ac", TT)

    @P.block.tensor
    def _(t):
        t.wait_ge(P.sems["lg"], lgt)
        for tt in range(TT):
            bank = b[tt % 2]
            if tt >= 2:
                t.wait_ge(P.sems["dv"], rt_dv[7 * (tt - 2) + 2])
            t.wait_ge(P.sems["dv"], he_dv[tt])
            for c in range(8):
                mm = t.matmul(bank[:, 0:8], hnT[:, c, tt * 128:(tt + 1) * 128],
                              wgv[:, c, :], start=(c == 0), stop=(c == 7))
            mm.then_inc(P.sems["pe"], 1)

    @P.block.vector
    def _(vv):
        vv.wait_ge(P.sems["gp"], gpi)
        for tt in range(TT):
            bank = b[tt % 2]
            vv.wait_ge(P.sems["pe"], rt_pe[tt])
            if tt == 0:
                vv.wait_ge(P.sems["ld"], lde)
            if tt >= 1:
                vv.wait_ge(P.sems["dv"], rt_dv[7 * (tt - 1) + 6])
            vv.tensor_add(lg8, bank[:, 0:8], ebc).then_inc(P.sems["dv"], 1)
            vv.wait_ge(P.sems["dv"], rt_dv[7 * tt])
            vv.max(mx8, lg8).then_inc(P.sems["dv"], 1)
            vv.wait_ge(P.sems["dv"], rt_dv[7 * tt + 1])
            vv.max_index(P.mi8[:], mx8, lg8)
            vv.tensor_sub(rt8[:, 0:1], mx8[:, 1:2], mx8[:, 0:1]).then_inc(P.sems["dv"], 1)
            vv.wait_ge(P.sems["ac"], rt_ac[tt])
            vv.tensor_scalar(rt8[:, 2:3], rt8[:, 1:2], -1.0, 1.0, ALU.mult, ALU.add)
            vv.tensor_copy(rt8[:, 3:4], P.mi8[:, 0:1])
            vv.tensor_copy(rt8[:, 4:5], P.mi8[:, 1:2]).then_inc(P.sems["dv"], 1)
            vv.wait_ge(P.sems["dv"], rt_dv[7 * tt + 3])
            vv.tensor_scalar(combw[:, tt, :], iota8, rt8[:, 3:4], None, ALU.is_equal)
            vv.tensor_scalar(rt8[:, 8:16], iota8, rt8[:, 4:5], None, ALU.is_equal
                             ).then_inc(P.sems["dv"], 1)
            vv.wait_ge(P.sems["dv"], rt_dv[7 * tt + 4])
            vv.tensor_scalar_mul(combw[:, tt, :], combw[:, tt, :], rt8[:, 2:3]
                                 ).then_inc(P.sems["dv"], 1)
            vv.wait_ge(P.sems["dv"], rt_dv[7 * tt + 5])
            vv.scalar_tensor_tensor(combw[:, tt, :], rt8[:, 8:16], rt8[:, 1:2],
                                    combw[:, tt, :], ALU.mult, ALU.add
                                    ).then_inc(P.sems["dv"], 1)

    @P.block.scalar
    def _(sa):
        for tt in range(TT):
            sa.wait_ge(P.sems["dv"], rt_dv[7 * tt + 2])
            sa.activation(rt8[:, 1:2], rt8[:, 0:1], AF.Sigmoid).then_inc(P.sems["ac"], 1)
    cw_dv = P.cnt["dv"]

    lgw = P.res("lg", 3 * 9, 16)
    prev_gu_last = None
    prev_dn_last = None
    prev_dn_dv = None
    for e in range(9):
        @P.block.gpsimd
        def _(g, e=e, prev_gu_last=prev_gu_last, prev_dn_last=prev_dn_last):
            if e == 0:
                apply_pending(P, g, f"{tag}_wgu", f"{tag}_wd", f"{tag}_gub")
            if prev_gu_last is not None:
                g.wait_ge(P.sems["pe"], prev_gu_last)
            g.dma_start(out=wguv[:, 0, :, :], in_=wg_d[e].rearrange("(c p) n -> p c n", p=128)
                        ).then_inc(P.sems["lg"], 16)
            g.dma_start(out=wguv[:, 1, :, :], in_=wu_d[e].rearrange("(c p) n -> p c n", p=128)
                        ).then_inc(P.sems["lg"], 16)
            if prev_dn_last is not None:
                g.wait_ge(P.sems["pe"], prev_dn_last)
            g.dma_start(out=wdv, in_=wd_d[e].rearrange("(c p) n -> p c n", p=128)
                        ).then_inc(P.sems["lg"], 16)

        gu_ac = P.res("ac", 4 * NQ)
        gu_dv = P.res("dv", 2 * 4 * NQ)
        dn_dv = P.res("dv", 4 * NQ)
        P.cnt["dv"] -= 3 * 4 * NQ
        dvo = P.res("dv", 3 * 4 * NQ)
        gu_dv, dn_dv = {}, {}
        k_ = 0
        for tq in range(NQ):
            for it in range(4):
                gu_dv[2 * (tq * 4 + it)] = dvo[k_]; k_ += 1
                gu_dv[2 * (tq * 4 + it) + 1] = dvo[k_]; k_ += 1
            for j in range(4):
                dn_dv[tq * 4 + j] = dvo[k_]; k_ += 1
        pev = P.res("pe", 8 * NQ)
        gu_pe, dn_pe = {}, {}
        k_ = 0
        for tq in range(NQ):
            for it in range(4):
                gu_pe[tq * 4 + it] = pev[k_]; k_ += 1
            for j in range(4):
                dn_pe[tq * 4 + j] = pev[k_]; k_ += 1

        @P.block.tensor
        def _(t, e=e, gu_pe=gu_pe, dn_pe=dn_pe, gu_ac=gu_ac, gu_dv=gu_dv, dn_dv=dn_dv,
              prev_dn_dv=prev_dn_dv):
            t.wait_ge(P.sems["lg"], lgw[3 * e + 1])
            if e == 0:
                t.wait_ge(P.sems["dv"], cw_dv)
            for tq in range(NQ):
                for it in range(4):
                    i = tq * 4 + it
                    bank, bank2 = (b[2], b[3]) if i % 2 == 0 else (b[4], b[5])
                    if i >= 2:
                        t.wait_ge(P.sems["ac"], gu_ac[i - 2])
                        t.wait_ge(P.sems["dv"], gu_dv[2 * (i - 2)])
                    for c in range(8):
                        t.matmul(bank[:], wguv[:, 0, c, it * 128:(it + 1) * 128],
                                 hnT[:, c, tq * 512:(tq + 1) * 512],
                                 start=(c == 0), stop=(c == 7))
                    for c in range(8):
                        mm = t.matmul(bank2[:], wguv[:, 1, c, it * 128:(it + 1) * 128],
                                      hnT[:, c, tq * 512:(tq + 1) * 512],
                                      start=(c == 0), stop=(c == 7))
                    mm.then_inc(P.sems["pe"], 1)
                t.wait_ge(P.sems["lg"], lgw[3 * e + 2])
                t.wait_ge(P.sems["dv"], gu_dv[2 * (tq * 4 + 3) + 1])
                for j in range(4):
                    tt = tq * 4 + j
                    if tt >= 1:
                        t.wait_ge(P.sems["dv"], dn_dv[tt - 1])
                    elif prev_dn_dv is not None:
                        t.wait_ge(P.sems["dv"], prev_dn_dv)
                    for i2 in range(4):
                        t.matmul(b[0][:], gub[:, 2, i2, j * 128:(j + 1) * 128],
                                 wdv[:, i2, 0:512], start=(i2 == 0), stop=(i2 == 3))
                    for i2 in range(4):
                        mm = t.matmul(b[1][:], gub[:, 2, i2, j * 128:(j + 1) * 128],
                                      wdv[:, i2, 512:1024], start=(i2 == 0), stop=(i2 == 3))
                    mm.then_inc(P.sems["pe"], 1)

        @P.block.scalar
        def _(sa, e=e, gu_pe=gu_pe, dn_pe=dn_pe, prev_dn_last=prev_dn_last):
            if e == 0:
                apply_pending(P, sa, f"{tag}_gub")
            for tq in range(NQ):
                for it in range(4):
                    i = tq * 4 + it
                    sa.wait_ge(P.sems["pe"], gu_pe[i])
                    if tq >= 1:
                        sa.wait_ge(P.sems["pe"], dn_pe[tq * 4 - 1])
                    elif prev_dn_last is not None:
                        sa.wait_ge(P.sems["pe"], prev_dn_last)
                    bank = b[2] if i % 2 == 0 else b[4]
                    sa.activation(gub[:, 0, it, :], bank[:], AF.Silu).then_inc(P.sems["ac"], 1)

        @P.block.vector
        def _(vv, e=e, gu_pe=gu_pe, dn_pe=dn_pe, gu_ac=gu_ac, gu_dv=gu_dv, dn_dv=dn_dv,
              prev_dn_last=prev_dn_last):
            if e == 0:
                apply_pending(P, vv, f"{tag}_gub")
            for tq in range(NQ):
                for it in range(4):
                    i = tq * 4 + it
                    vv.wait_ge(P.sems["pe"], gu_pe[i])
                    if tq == 0 and prev_dn_last is not None:
                        vv.wait_ge(P.sems["pe"], prev_dn_last)
                    if tq >= 1:
                        vv.wait_ge(P.sems["pe"], dn_pe[tq * 4 - 1])
                    bank2 = b[3] if i % 2 == 0 else b[5]
                    vv.tensor_copy(gub[:, 1, it, :], bank2[:]).then_inc(P.sems["dv"], 1)
                    vv.wait_ge(P.sems["dv"], gu_dv[2 * i])
                    vv.wait_ge(P.sems["ac"], gu_ac[i])
                    vv.tensor_mul(gub[:, 2, it, :], gub[:, 0, it, :], gub[:, 1, it, :]
                                  ).then_inc(P.sems["dv"], 1)
                for j in range(4):
                    tt = tq * 4 + j
                    vv.wait_ge(P.sems["pe"], dn_pe[tt])
                    x_t = X[:, tt * 1024:(tt + 1) * 1024]
                    cwc = 1.0 if e == 0 else combw[:, tt, e - 1:e]
                    vv.scalar_tensor_tensor(x_t[:, 0:512], b[0][:], cwc, x_t[:, 0:512],
                                            ALU.mult, ALU.add)
                    vv.scalar_tensor_tensor(x_t[:, 512:1024], b[1][:], cwc, x_t[:, 512:1024],
                                            ALU.mult, ALU.add).then_inc(P.sems["dv"], 1)

        prev_gu_last = gu_pe[4 * NQ - 1]
        prev_dn_last = dn_pe[4 * NQ - 1]
        prev_dn_dv = dn_dv[4 * NQ - 1]

    AB.free(f"{tag}_hnT", f"{tag}_wgu", f"{tag}_wd", f"{tag}_gub",
            readers={"pe": P.cnt["pe"]})
    AB.free(f"{tag}_wgate", readers={"pe": rt_pe[TT - 1]})
    P.ready = emit_postln(P, tag, {"dv": prev_dn_dv}, refresh_xt=refresh_out)




def pack_l1_attn(Wq, bq, Wk, bk, Wv, bv, Wo, bo):
    c = np.ascontiguousarray
    return {"a1_wq": c(Wq.T, np.float32), "a1_wk": c(Wk.T, np.float32),
            "a1_wv": c(Wv.T, np.float32), "a1_wo": c(Wo.T, np.float32),
            "a1_bq": c(bq.reshape(8, 128).T, np.float32),
            "a1_brow": c(np.stack([bk, bv, bo]), np.float32)}


def emit_l1_attn(P, s, upto=None):
    T, TT, NQ = P.T, P.TT, P.NQ
    AB, SC, X, b, bt = P.AB, P.SC, P.X, P.banks, P.bt
    tag = f"a1s{s}"

    wq_d = P.din("a1_wq", [1024, 1024])
    wk_d = P.din("a1_wk", [1024, 1024])
    wv_d = P.din("a1_wv", [1024, 1024])
    wo_d = P.din("a1_wo", [1024, 1024])
    bq_d = P.din("a1_bq", [128, 8])
    brow_d = P.din("a1_brow", [3, 1024])
    sc_d = P.din("c_sincos", [128, 2 * TT])

    xready = dict(P.ready)
    xT = v3(AB.ap("xT"), T)

    scc = P.AF[:, 2056:2056 + 2 * TT]
    bqc = SC[:, 3968:3976]
    AB.alloc(f"{tag}_brow", 3072)
    brow_ap = AB.ap(f"{tag}_brow")
    brow = {i: brow_ap[0:1, i * 1024:(i + 1) * 1024] for i in range(3)}
    lc = P.res("ld", 2, 16)
    lbr = P.res("lg", 3, 16)[-1]

    @P.block.sync
    def _(sy):
        for snm, v_ in xready.items():
            sy.wait_ge(P.sems[snm], v_)
        sy.dma_start(out=scc, in_=sc_d).then_inc(P.sems["ld"], 16)
        sy.dma_start(out=bqc, in_=bq_d).then_inc(P.sems["ld"], 16)

    @P.block.gpsimd
    def _(g):
        apply_pending(P, g, f"{tag}_brow")
        g.dma_start(out=brow[0], in_=brow_d[0:1, :]).then_inc(P.sems["lg"], 16)
        g.dma_start(out=brow[1], in_=brow_d[1:2, :]).then_inc(P.sems["lg"], 16)
        g.dma_start(out=brow[2], in_=brow_d[2:3, :]).then_inc(P.sems["lg"], 16)

    AB.alloc(f"{tag}_qT", 8 * T)
    qT = v3(AB.ap(f"{tag}_qT"), T)
    AB.alloc(f"{tag}_wq", 8 * 1024)
    wqv = v3(AB.ap(f"{tag}_wq"), 1024)
    lwq = P.res("lg", 1, 16)[0]

    @P.block.gpsimd
    def _(g):
        apply_pending(P, g, f"{tag}_wq")
        g.dma_start(out=wqv, in_=wq_d.rearrange("(c p) n -> p c n", p=128)
                    ).then_inc(P.sems["lg"], 16)

    qp_pe = P.res("pe", 8 * NQ)
    qp_ac = P.res("ac", 8 * NQ)

    @P.block.tensor
    def _(t):
        t.wait_ge(P.sems["lg"], lwq)
        for snm, v_ in xready.items():
            t.wait_ge(P.sems[snm], v_)
        i = 0
        for ct in range(8):
            for tq in range(NQ):
                bank = b[i % 2]
                if i >= 2:
                    t.wait_ge(P.sems["ac"], qp_ac[i - 2])
                for c in range(8):
                    mm = t.matmul(bank[:], wqv[:, c, ct * 128:(ct + 1) * 128],
                                  xT[:, c, tq * 512:(tq + 1) * 512],
                                  start=(c == 0), stop=(c == 7))
                mm.then_inc(P.sems["pe"], 1)
                i += 1

    @P.block.scalar
    def _(sa):
        apply_pending(P, sa, f"{tag}_qT")
        sa.wait_ge(P.sems["ld"], lc[1])
        i = 0
        for ct in range(8):
            for tq in range(NQ):
                sa.wait_ge(P.sems["pe"], qp_pe[i])
                sa.activation(qT[:, ct, tq * 512:(tq + 1) * 512], b[i % 2][:],
                              AF.Relu, bias=bqc[:, ct:ct + 1]).then_inc(P.sems["ac"], 1)
                i += 1
    AB.free(f"{tag}_wq", readers={"ac": qp_ac[-1]})
    if upto == "Q":
        dbg_dump(P, "a1qT", AB.ap(f"{tag}_qT"), {"ac": qp_ac[-1]})
        return

    AB.alloc(f"{tag}_wkv", 2 * 8 * 1024)
    wkv = AB.ap(f"{tag}_wkv").rearrange("p (w c n) -> p w c n", w=2, n=1024)
    AB.alloc(f"{tag}_ks", 16 * 128)
    ksb0 = AB.ap(f"{tag}_ks").rearrange("p (h n) -> p h n", n=128)
    AB.alloc(f"{tag}_vv", 2 * 1024)
    vvb = v3(AB.ap(f"{tag}_vv"), 1024)
    lkv = P.res("lg", 2, 16)

    @P.block.gpsimd
    def _(g):
        apply_pending(P, g, f"{tag}_wkv", f"{tag}_ks", f"{tag}_vv")
        g.dma_start(out=wkv[:, 0, :, :], in_=wk_d.rearrange("(c p) n -> p c n", p=128)
                    ).then_inc(P.sems["lg"], 16)
        g.dma_start(out=wkv[:, 1, :, :], in_=wv_d.rearrange("(c p) n -> p c n", p=128)
                    ).then_inc(P.sems["lg"], 16)

    pev = P.res("pe", 3 * TT)
    kp_pe = {tt: pev[3 * tt] for tt in range(TT)}
    vp_pe = {tt: pev[3 * tt + 1] for tt in range(TT)}
    kv_pe = {tt: pev[3 * tt + 2] for tt in range(TT)}
    dvv = P.res("dv", 3 * TT)
    ke1 = {tt: dvv[3 * tt] for tt in range(TT)}
    ke2 = {tt: dvv[3 * tt + 1] for tt in range(TT)}
    vev = {tt: dvv[3 * tt + 2] for tt in range(TT)}
    kv_banks = (b[2], b[3])
    ks_bank = b[6]
    scr = [SC[:, 0:1024], SC[:, 1024:2048]]

    @P.block.tensor
    def _(t):
        t.wait_ge(P.sems["lg"], lkv[-1])
        t.wait_ge(P.sems["lg"], lbr)
        t.wait_ge(P.sems["ld"], P.const_ready)
        t.wait_ge(P.sems["ac"], qp_ac[-1])
        for tt in range(TT):
            kb = (b[0], b[1]) if tt % 2 == 0 else (b[4], b[5])
            if tt >= 1:
                t.wait_ge(P.sems["dv"], ke1[tt - 1])
                t.wait_ge(P.sems["dv"], vev[tt - 1])
            for nj in range(2):
                for c in range(8):
                    t.matmul(kb[nj][:], xT[:, c, tt * 128:(tt + 1) * 128],
                             wkv[:, 0, c, nj * 512:(nj + 1) * 512],
                             start=(c == 0), stop=False)
                mm = t.matmul(kb[nj][:], P.onesb[0:1, 0:128],
                              brow[0][:, nj * 512:(nj + 1) * 512], start=False, stop=True)
            mm.then_inc(P.sems["pe"], 1)
            vb = (b[4], b[5]) if tt % 2 == 0 else (b[0], b[1])
            for nj in range(2):
                for c in range(8):
                    t.matmul(vb[nj][:], xT[:, c, tt * 128:(tt + 1) * 128],
                             wkv[:, 1, c, nj * 512:(nj + 1) * 512],
                             start=(c == 0), stop=False)
                mm = t.matmul(vb[nj][:], P.onesb[0:1, 0:128],
                              brow[1][:, nj * 512:(nj + 1) * 512], start=False, stop=True)
            mm.then_inc(P.sems["pe"], 1)
            t.wait_ge(P.sems["dv"], ke2[tt])
            t.wait_ge(P.sems["dv"], vev[tt])
            for h in range(16):
                t.matmul(kv_banks[h // 8][:, (h % 8) * 64:(h % 8) * 64 + 64],
                         ksb0[:, h, :], vvb[:, tt % 2, h * 64:(h + 1) * 64],
                         start=(tt == 0 and h % 8 == 0), stop=(tt == TT - 1 and h % 8 == 7))
            for h in range(16):
                mm = t.matmul(ks_bank[:, h:h + 1], ksb0[:, h, :], P.onesb[:, 0:1],
                              start=(tt == 0 and h == 0), stop=(tt == TT - 1 and h == 15))
            mm.then_inc(P.sems["pe"], 1)

    @P.block.vector
    def _(vv):
        apply_pending(P, vv, f"{tag}_ks", f"{tag}_vv")
        vv.wait_ge(P.sems["ld"], lc[0])
        for tt in range(TT):
            kb = (b[0], b[1]) if tt % 2 == 0 else (b[4], b[5])
            vb = (b[4], b[5]) if tt % 2 == 0 else (b[0], b[1])
            vv.wait_ge(P.sems["pe"], kp_pe[tt])
            if tt >= 1:
                vv.wait_ge(P.sems["pe"], kv_pe[tt - 1])
            vv.tensor_scalar_max(scr[tt % 2][:, 0:512], kb[0][:], 0.0)
            vv.tensor_scalar_max(scr[tt % 2][:, 512:1024], kb[1][:], 0.0
                                 ).then_inc(P.sems["dv"], 1)
            vv.wait_ge(P.sems["dv"], ke1[tt])
            sview = scr[tt % 2].rearrange("p (h n) -> p h n", n=64)
            vv.tensor_scalar_mul(ksb0[:, :, 0:64], sview, scc[:, tt:tt + 1])
            vv.tensor_scalar_mul(ksb0[:, :, 64:128], sview, scc[:, TT + tt:TT + tt + 1]
                                 ).then_inc(P.sems["dv"], 1)
            vv.wait_ge(P.sems["pe"], vp_pe[tt])
            vv.tensor_copy(vvb[:, tt % 2, 0:512], vb[0][:])
            vv.tensor_copy(vvb[:, tt % 2, 512:1024], vb[1][:]).then_inc(P.sems["dv"], 1)

    AB.alloc(f"{tag}_kvr", 16 * 136)
    kvr = AB.ap(f"{tag}_kvr").rearrange("p (h n) -> p h n", n=136)
    kve = P.res("dv", 1)[0]

    @P.block.vector
    def _(vv):
        apply_pending(P, vv, f"{tag}_kvr")
        vv.wait_ge(P.sems["pe"], kv_pe[TT - 1])
        for h in range(16):
            bank = kv_banks[h // 8]
            sl = (h % 8) * 64
            vv.tensor_copy(kvr[0:64, h, 0:64], bank[0:64, sl:sl + 64])
            vv.tensor_copy(kvr[64:128, h, 0:64], bank[0:64, sl:sl + 64])
            vv.tensor_copy(kvr[0:64, h, 68:132], bank[64:128, sl:sl + 64])
            vv.tensor_copy(kvr[64:128, h, 68:132], bank[64:128, sl:sl + 64])
            vv.tensor_copy(kvr[0:64, h, 64:65], ks_bank[0:64, h:h + 1])
            vv.tensor_copy(kvr[64:128, h, 64:65], ks_bank[0:64, h:h + 1])
            vv.tensor_copy(kvr[0:64, h, 132:133], ks_bank[64:128, h:h + 1])
            ins = vv.tensor_copy(kvr[64:128, h, 132:133], ks_bank[64:128, h:h + 1])
        ins.then_inc(P.sems["dv"], 1)
    AB.free(f"{tag}_wkv", f"{tag}_ks", f"{tag}_vv", readers={"pe": kv_pe[TT - 1], "dv": kve})
    AB.free("xT", readers={"pe": kv_pe[TT - 1]})
    if upto == "KV":
        dbg_dump(P, "a1kvr", AB.ap(f"{tag}_kvr"), {"dv": kve})
        return

    AB.alloc(f"{tag}_y", TT * 1024)
    y = v3(AB.ap(f"{tag}_y"), 1024)
    GRP = [(0, 2, 4, 6), (8, 10, 12, 14), (1, 3, 5, 7), (9, 11, 13, 15)]
    at_pe = P.res("pe", TT * len(GRP))
    at_dv = P.res("dv", TT * len(GRP) * 6)

    @P.block.tensor
    def _(t):
        t.wait_ge(P.sems["dv"], kve)
        t.wait_ge(P.sems["ac"], qp_ac[-1])
        for tt in range(TT):
            for g_, heads in enumerate(GRP):
                pr = (b[0], b[1]) if (tt * len(GRP) + g_) % 2 == 0 else (b[4], b[5])
                idx = tt * len(GRP) + g_
                nh = len(heads)
                base = (heads[0] % 2) * 64
                if idx >= 2:
                    t.wait_ge(P.sems["dv"], at_dv[6 * (idx - 2)])
                for j, h in enumerate(heads):
                    t.matmul(pr[0][:, j * 128:j * 128 + 65],
                             qT[base:base + 64, h // 2, tt * 128:(tt + 1) * 128],
                             kvr[base:base + 64, h, 0:65], start=(j == 0), stop=(j == nh - 1))
                    mm = t.matmul(pr[1][:, j * 128:j * 128 + 65],
                                  qT[base:base + 64, h // 2, tt * 128:(tt + 1) * 128],
                                  kvr[base:base + 64, h, 68:133], start=(j == 0), stop=(j == nh - 1))
                mm.then_inc(P.sems["pe"], 1)

    @P.block.vector
    def _(vv):
        apply_pending(P, vv, f"{tag}_y")
        tps = SC[:, 2048:2048 + 512]
        bts_ = SC[:, 2560:2560 + 512]
        att = SC[:, 3072:3072 + 512]
        zz = SC[:, 3584:3584 + 16]
        for tt in range(TT):
            for g_, heads in enumerate(GRP):
                idx = tt * len(GRP) + g_
                nh = len(heads)
                pr = (b[0], b[1]) if idx % 2 == 0 else (b[4], b[5])
                vv.wait_ge(P.sems["pe"], at_pe[idx])
                vv.tensor_copy(tps[:, 0:nh * 128], pr[0][:, 0:nh * 128])
                vv.tensor_copy(bts_[:, 0:nh * 128], pr[1][:, 0:nh * 128]).then_inc(P.sems["dv"], 1)
                vv.wait_ge(P.sems["dv"], at_dv[6 * idx])
                attv = att[:, 0:nh * 128].rearrange("p (j n) -> p j n", n=128)[:, :, 0:65]
                topv = tps[:, 0:nh * 128].rearrange("p (j n) -> p j n", n=128)[:, :, 0:65]
                botv = bts_[:, 0:nh * 128].rearrange("p (j n) -> p j n", n=128)[:, :, 0:65]
                vv.tensor_scalar_mul(attv, topv, scc[:, tt:tt + 1]).then_inc(P.sems["dv"], 1)
                vv.wait_ge(P.sems["dv"], at_dv[6 * idx + 1])
                vv.scalar_tensor_tensor(attv, botv, scc[:, TT + tt:TT + tt + 1], attv,
                                        ALU.mult, ALU.add).then_inc(P.sems["dv"], 1)
                vv.wait_ge(P.sems["dv"], at_dv[6 * idx + 2])
                dview = att[:, 0:nh * 128].rearrange("p (j n) -> p j n", n=128)[:, :, 64:65]
                vv.tensor_scalar_max(zz[:, 0:nh], dview, 1e-4).then_inc(P.sems["dv"], 1)
                vv.wait_ge(P.sems["dv"], at_dv[6 * idx + 3])
                vv.reciprocal(zz[:, 8:8 + nh], zz[:, 0:nh]).then_inc(P.sems["dv"], 1)
                vv.wait_ge(P.sems["dv"], at_dv[6 * idx + 4])
                for j, h in enumerate(heads):
                    ins = vv.tensor_scalar_mul(y[:, tt, h * 64:(h + 1) * 64],
                                               att[:, j * 128:j * 128 + 64],
                                               zz[:, 8 + j:9 + j])
                ins.then_inc(P.sems["dv"], 1)
    y_done = P.cnt["dv"]
    if upto == "AT":
        dbg_dump(P, "a1y", AB.ap(f"{tag}_y"), {"dv": y_done})
        return
    AB.free(f"{tag}_qT", f"{tag}_kvr", readers={"pe": at_pe[-1]})

    AB.alloc(f"{tag}_yT", 8 * T)
    yT = v3(AB.ap(f"{tag}_yT"), T)
    AB.alloc(f"{tag}_wo", 8 * 1024)
    wov = v3(AB.ap(f"{tag}_wo"), 1024)
    lwo = P.res("lg", 1, 16)[0]

    @P.block.gpsimd
    def _(g):
        apply_pending(P, g, f"{tag}_wo")
        g.dma_start(out=wov, in_=wo_d.rearrange("(c p) n -> p c n", p=128)
                    ).then_inc(P.sems["lg"], 16)

    yt_pe = P.res("pe", TT)
    yt_dv = P.res("dv", TT)

    @P.block.tensor
    def _(t):
        t.wait_ge(P.sems["dv"], y_done)
        for tt in range(TT):
            if tt >= 1:
                t.wait_ge(P.sems["dv"], yt_dv[tt - 1])
            for c in range(8):
                mm = t.transpose(bt[:, c * 128:(c + 1) * 128], y[:, tt, c * 128:(c + 1) * 128],
                                 P.ident[:])
            mm.then_inc(P.sems["pe"], 1)

    @P.block.vector
    def _(vv):
        apply_pending(P, vv, f"{tag}_yT")
        for tt in range(TT):
            vv.wait_ge(P.sems["pe"], yt_pe[tt])
            vv.tensor_copy(yT[:, :, tt * 128:(tt + 1) * 128],
                           bt[:].rearrange("p (c t) -> p c t", t=128)).then_inc(P.sems["dv"], 1)

    op_pe = P.res("pe", TT)
    op_dv = P.res("dv", TT)

    @P.block.tensor
    def _(t):
        t.wait_ge(P.sems["lg"], lwo)
        t.wait_ge(P.sems["dv"], yt_dv[TT - 1])
        for tt in range(TT):
            pr = (b[2], b[3]) if tt % 2 == 0 else (b[4], b[5])
            if tt >= 2:
                t.wait_ge(P.sems["dv"], op_dv[tt - 2])
            for nj in range(2):
                for c in range(8):
                    t.matmul(pr[nj][:], yT[:, c, tt * 128:(tt + 1) * 128],
                             wov[:, c, nj * 512:(nj + 1) * 512],
                             start=(c == 0), stop=False)
                mm = t.matmul(pr[nj][:], P.onesb[0:1, 0:128],
                              brow[2][:, nj * 512:(nj + 1) * 512], start=False, stop=True)
            mm.then_inc(P.sems["pe"], 1)

    @P.block.vector
    def _(vv):
        for tt in range(TT):
            pr = (b[2], b[3]) if tt % 2 == 0 else (b[4], b[5])
            vv.wait_ge(P.sems["pe"], op_pe[tt])
            vv.tensor_add(X[:, tt * 1024:tt * 1024 + 512], pr[0][:],
                          X[:, tt * 1024:tt * 1024 + 512])
            vv.tensor_add(X[:, tt * 1024 + 512:(tt + 1) * 1024], pr[1][:],
                          X[:, tt * 1024 + 512:(tt + 1) * 1024]).then_inc(P.sems["dv"], 1)

    AB.free(f"{tag}_y", readers={"pe": yt_pe[TT - 1]})
    AB.free(f"{tag}_yT", f"{tag}_wo", f"{tag}_brow", readers={"pe": op_pe[TT - 1]})
    P.ready = emit_postln(P, tag, {"dv": op_dv[TT - 1]})



def pack_l1_ffn(router_w, w1, b1, w2, b2):
    c = np.ascontiguousarray
    return {"f1_rw": c(router_w, np.float32), "f1_w1": c(w1, np.float32),
            "f1_b1": c(b1.reshape(14, 16, 128).transpose(0, 2, 1), np.float32),
            "f1_w2": c(w2, np.float32), "f1_b2": c(b2, np.float32)}


def emit_l1_ffn(P, s, upto=None, guard=()):
    T, TT, NQ = P.T, P.TT, P.NQ
    AB, SC, X, b, bt = P.AB, P.SC, P.X, P.banks, P.bt
    tag = f"f1s{s}"

    rw_d = P.din("f1_rw", [1024, 14])
    w1_d = P.din("f1_w1", [14, 1024, 2048])
    b1_d = P.din("f1_b1", [14, 128, 16])
    w2_d = P.din("f1_w2", [14, 2048, 1024])
    b2_d = P.din("f1_b2", [14, 1024])

    xready = dict(P.ready)
    xT = v3(AB.ap("xT"), T)

    AB.alloc(f"{tag}_rw", 8 * 14)
    rwv = v3(AB.ap(f"{tag}_rw"), 14)
    AB.alloc(f"{tag}_rwe", 8 * 14)
    rwe = v3(AB.ap(f"{tag}_rwe"), 14)
    AB.alloc(f"{tag}_ecb", 2 * 1024)
    ecb_off = AB.blocks[f"{tag}_ecb"][0]
    ecb = [AB.t[:, ecb_off + i * 1024:ecb_off + (i + 1) * 1024] for i in range(2)]
    AB.alloc(f"{tag}_eT", 2 * 1024)
    eT_off = AB.blocks[f"{tag}_eT"][0]
    eT = [v3(AB.t[:, eT_off + i * 1024:eT_off + (i + 1) * 1024], 128) for i in range(2)]
    combw = SC[:, 0:14 * TT].rearrange("p (t e) -> p t e", e=14)
    mx8 = SC[:, 14 * TT + 0:14 * TT + 8]
    rt = SC[:, 14 * TT + 8:14 * TT + 24]
    iota14 = SC[:, 14 * TT + 24:14 * TT + 40]
    exps = SC[:, 14 * TT + 40:14 * TT + 56]
    rwf = SC[:, 3856:3968].rearrange("p (c n) -> p c n", n=14)

    lrw = P.res("ld", 1, 16)[0]
    gpi = P.res("gp", 1)[0]
    rp_dv = P.res("dv", 2)

    @P.block.sync
    def _(sy):
        for snm, v_ in xready.items():
            sy.wait_ge(P.sems[snm], v_)
        sy.dma_start(out=rwf, in_=rw_d.rearrange("(c p) n -> p c n", p=128)
                     ).then_inc(P.sems["ld"], 16)

    @P.block.gpsimd
    def _(g):
        g.iota(iota14[:, 0:14], [[1, 14]], base=0, channel_multiplier=0,
               allow_small_or_imprecise_dtypes=True).then_inc(P.sems["gp"], 1)

    @P.block.vector
    def _(vv):
        apply_pending(P, vv, f"{tag}_rw", f"{tag}_rwe", f"{tag}_ecb", f"{tag}_eT")
        vv.wait_ge(P.sems["ld"], lrw)
        vv.tensor_copy(rwv.rearrange("p c n -> p (c n)"),
                       rwf.rearrange("p c n -> p (c n)")).then_inc(P.sems["dv"], 1)
        vv.wait_ge(P.sems["dv"], rp_dv[0])
        vv.tensor_sub(rwe.rearrange("p c n -> p (c n)"), rwf.rearrange("p c n -> p (c n)"),
                      rwv.rearrange("p c n -> p (c n)")).then_inc(P.sems["dv"], 1)

    rt_pe = P.res("pe", 2 * TT)
    rt_dv = P.res("dv", 9 * TT)
    rt_ac = P.res("ac", TT)

    @P.block.tensor
    def _(t):
        t.wait_ge(P.sems["dv"], rp_dv[1])
        for snm, v_ in xready.items():
            t.wait_ge(P.sems[snm], v_)
        for tt in range(TT):
            bank = b[tt % 2]
            t.wait_ge(P.sems["dv"], rt_dv[9 * tt + 1])
            if tt >= 1:
                t.wait_ge(P.sems["dv"], rt_dv[9 * (tt - 1) + 2])
            for c in range(8):
                mm = t.transpose(bt[:, c * 128:(c + 1) * 128],
                                 ecb[tt % 2][:, c * 128:(c + 1) * 128], P.ident[:])
            mm.then_inc(P.sems["pe"], 1)
            if tt >= 2:
                t.wait_ge(P.sems["dv"], rt_dv[9 * (tt - 2) + 4])
            t.wait_ge(P.sems["dv"], rt_dv[9 * tt + 2])
            for c in range(8):
                t.matmul(bank[:, 0:14], xT[:, c, tt * 128:(tt + 1) * 128],
                         rwv[:, c, :], start=(c == 0), stop=False)
            for c in range(8):
                t.matmul(bank[:, 0:14], eT[tt % 2][:, c, :], rwv[:, c, :],
                         start=False, stop=False)
            for c in range(8):
                mm = t.matmul(bank[:, 0:14], xT[:, c, tt * 128:(tt + 1) * 128],
                              rwe[:, c, :], start=False, stop=(c == 7))
            mm.then_inc(P.sems["pe"], 1)

    @P.block.vector
    def _(vv):
        vv.wait_ge(P.sems["gp"], gpi)
        for snm, v_ in xready.items():
            vv.wait_ge(P.sems[snm], v_)
        for tt in range(TT):
            bank = b[tt % 2]
            if tt >= 2:
                vv.wait_ge(P.sems["pe"], rt_pe[2 * (tt - 2)])
            vv.tensor_copy(ecb[tt % 2], X[:, tt * 1024:(tt + 1) * 1024]
                           ).then_inc(P.sems["dv"], 1)
            vv.wait_ge(P.sems["dv"], rt_dv[9 * tt])
            vv.tensor_sub(ecb[tt % 2], X[:, tt * 1024:(tt + 1) * 1024], ecb[tt % 2]
                          ).then_inc(P.sems["dv"], 1)
            vv.wait_ge(P.sems["pe"], rt_pe[2 * tt])
            vv.tensor_copy(eT[tt % 2].rearrange("p c n -> p (c n)"), bt[:]
                           ).then_inc(P.sems["dv"], 1)
            vv.wait_ge(P.sems["pe"], rt_pe[2 * tt + 1])
            if tt >= 1:
                vv.wait_ge(P.sems["dv"], rt_dv[9 * (tt - 1) + 8])
            vv.max(mx8, bank[:, 0:14]).then_inc(P.sems["dv"], 1)
            vv.wait_ge(P.sems["dv"], rt_dv[9 * tt + 3])
            vv.max_index(P.mi8[:], mx8, bank[:, 0:14])
            vv.tensor_scalar_mul(rt[:, 0:1], mx8[:, 0:1], -1.0).then_inc(P.sems["dv"], 1)
            vv.wait_ge(P.sems["ac"], rt_ac[tt])
            vv.reciprocal(rt[:, 2:3], rt[:, 1:2])
            vv.tensor_copy(rt[:, 3:4], P.mi8[:, 0:1]).then_inc(P.sems["dv"], 1)
            vv.wait_ge(P.sems["dv"], rt_dv[9 * tt + 5])
            vv.tensor_scalar(combw[:, tt, :], iota14[:, 0:14], rt[:, 3:4], None,
                             ALU.is_equal).then_inc(P.sems["dv"], 1)
            vv.wait_ge(P.sems["dv"], rt_dv[9 * tt + 6])
            vv.tensor_scalar_mul(combw[:, tt, :], combw[:, tt, :], rt[:, 2:3]
                                 ).then_inc(P.sems["dv"], 1)
            vv.engine_nop().then_inc(P.sems["dv"], 1)

    @P.block.scalar
    def _(sa):
        for tt in range(TT):
            bank = b[tt % 2]
            sa.wait_ge(P.sems["dv"], rt_dv[9 * tt + 4])
            sa.activation(exps[:, 0:14], bank[:, 0:14], AF.Exp, bias=rt[:, 0:1],
                          accum_out=rt[:, 1:2]).then_inc(P.sems["ac"], 1)
    cw_dv = P.cnt["dv"]
    AB.free(f"{tag}_rw", f"{tag}_rwe", f"{tag}_ecb", f"{tag}_eT",
            readers={"pe": rt_pe[-1]})
    if "rb" in guard:
        emit_seq_barrier(P)

    AB.alloc(f"{tag}_w1", 8 * 2048)
    w1v = v3(AB.ap(f"{tag}_w1"), 2048)
    AB.alloc(f"{tag}_w2", 16 * 1024)
    w2v = v3(AB.ap(f"{tag}_w2"), 1024)
    AB.alloc(f"{tag}_hid", 16 * 512)
    hid = v3(AB.ap(f"{tag}_hid"), 512)
    b1s = SC[:, 3840:3856]
    AB.alloc(f"{tag}_b2", 1024)
    b2r = AB.ap(f"{tag}_b2")

    lge = P.res("lg", 3 * 14, 16)
    lb1 = P.res("ld", 14, 16)
    prev_h_last = None
    prev_hac_last = None
    prev_dn_last = None
    prev_dn_dv = None
    for e in range(14):
        if "eb" in guard and e >= 1:
            emit_seq_barrier(P)

        @P.block.gpsimd
        def _(g, e=e, prev_h_last=prev_h_last, prev_dn_last=prev_dn_last):
            if e == 0:
                apply_pending(P, g, f"{tag}_w1", f"{tag}_w2", f"{tag}_hid", f"{tag}_b2")
            if prev_h_last is not None:
                g.wait_ge(P.sems["pe"], prev_h_last)
            g.dma_start(out=w1v, in_=w1_d[e].rearrange("(c p) n -> p c n", p=128)
                        ).then_inc(P.sems["lg"], 16)
            if prev_dn_last is not None:
                g.wait_ge(P.sems["pe"], prev_dn_last)
            g.dma_start(out=w2v, in_=w2_d[e].rearrange("(c p) n -> p c n", p=128)
                        ).then_inc(P.sems["lg"], 16)
            g.dma_start(out=b2r[0:1, :], in_=b2_d[e:e + 1, :]).then_inc(P.sems["lg"], 16)

        @P.block.sync
        def _(sy, e=e, prev_hac_last=(None if e == 0 else prev_hac_last)):
            if prev_hac_last is not None:
                sy.wait_ge(P.sems["ac"], prev_hac_last)
            sy.dma_start(out=b1s, in_=b1_d[e]).then_inc(P.sems["ld"], 16)

        pev = P.res("pe", (16 + 4) * NQ)
        h_pe, dn_pe = {}, {}
        k_ = 0
        for tq in range(NQ):
            for it in range(16):
                h_pe[tq * 16 + it] = pev[k_]; k_ += 1
            for j in range(4):
                dn_pe[tq * 4 + j] = pev[k_]; k_ += 1
        h_ac = P.res("ac", 16 * NQ)
        dn_dv = P.res("dv", 4 * NQ)

        @P.block.tensor
        def _(t, e=e, h_pe=h_pe, dn_pe=dn_pe, h_ac=h_ac, dn_dv=dn_dv,
              prev_dn_dv=prev_dn_dv, lge=lge, lb1=lb1):
            t.wait_ge(P.sems["lg"], lge[3 * e])
            if e == 0:
                t.wait_ge(P.sems["dv"], cw_dv)
            for tq in range(NQ):
                for it in range(16):
                    i = tq * 16 + it
                    bank = b[2 + i % 4]
                    if i >= 4:
                        t.wait_ge(P.sems["ac"], h_ac[i - 4])
                    for c in range(8):
                        mm = t.matmul(bank[:], w1v[:, c, it * 128:(it + 1) * 128],
                                      xT[:, c, tq * 512:(tq + 1) * 512],
                                      start=(c == 0), stop=(c == 7))
                    mm.then_inc(P.sems["pe"], 1)
                t.wait_ge(P.sems["lg"], lge[3 * e + 2])
                t.wait_ge(P.sems["ac"], h_ac[tq * 16 + 15])
                for j in range(4):
                    tt = tq * 4 + j
                    if tt >= 1:
                        t.wait_ge(P.sems["dv"], dn_dv[tt - 1])
                    elif prev_dn_dv is not None:
                        t.wait_ge(P.sems["dv"], prev_dn_dv)
                    for nj in range(2):
                        for i2 in range(16):
                            t.matmul(b[0 if nj == 0 else 1][:],
                                     hid[:, i2, j * 128:(j + 1) * 128],
                                     w2v[:, i2, nj * 512:(nj + 1) * 512],
                                     start=(i2 == 0), stop=False)
                        mm = t.matmul(b[0 if nj == 0 else 1][:], P.onesb[0:1, 0:128],
                                      b2r[0:1, nj * 512:(nj + 1) * 512],
                                      start=False, stop=True)
                    mm.then_inc(P.sems["pe"], 1)

        @P.block.scalar
        def _(sa, e=e, h_pe=h_pe, dn_pe=dn_pe, prev_dn_last=prev_dn_last, lb1=lb1):
            if e == 0:
                apply_pending(P, sa, f"{tag}_hid")
            sa.wait_ge(P.sems["ld"], lb1[e])
            for tq in range(NQ):
                if tq >= 1:
                    sa.wait_ge(P.sems["pe"], dn_pe[tq * 4 - 1])
                elif prev_dn_last is not None:
                    sa.wait_ge(P.sems["pe"], prev_dn_last)
                for it in range(16):
                    i = tq * 16 + it
                    sa.wait_ge(P.sems["pe"], h_pe[i])
                    sa.activation(hid[:, it, :], b[2 + i % 4][:], AF.Gelu,
                                  bias=b1s[:, it:it + 1]).then_inc(P.sems["ac"], 1)

        @P.block.vector
        def _(vv, e=e, dn_pe=dn_pe, dn_dv=dn_dv):
            for tq in range(NQ):
                for j in range(4):
                    tt = tq * 4 + j
                    vv.wait_ge(P.sems["pe"], dn_pe[tt])
                    x_t = X[:, tt * 1024:(tt + 1) * 1024]
                    cwc = combw[:, tt, e:e + 1]
                    vv.scalar_tensor_tensor(x_t[:, 0:512], b[0][:], cwc, x_t[:, 0:512],
                                            ALU.mult, ALU.add)
                    vv.scalar_tensor_tensor(x_t[:, 512:1024], b[1][:], cwc, x_t[:, 512:1024],
                                            ALU.mult, ALU.add).then_inc(P.sems["dv"], 1)

        prev_h_last = h_pe[16 * NQ - 1]
        prev_hac_last = h_ac[16 * NQ - 1]
        prev_dn_last = dn_pe[4 * NQ - 1]
        prev_dn_dv = dn_dv[4 * NQ - 1]

    AB.free(f"{tag}_w1", f"{tag}_w2", f"{tag}_hid", f"{tag}_b2",
            readers={"pe": P.cnt["pe"]})
    if "pb" in guard:
        emit_seq_barrier(P)
    P.ready = emit_postln(P, tag, {"dv": prev_dn_dv}, refresh_xt=False)


def emit_l1_ffn_sp(P, s, guard=("pb",)):
    T, TT, NQ = P.T, P.TT, P.NQ
    AB, SC, X, b, bt = P.AB, P.SC, P.X, P.banks, P.bt
    tag = f"f1s{s}"

    rw_d = P.din("f1_rw", [1024, 14])
    w1_d = P.din("f1_w1", [14, 1024, 2048])
    b1_d = P.din("f1_b1", [14, 128, 16])
    w2_d = P.din("f1_w2", [14, 2048, 1024])
    b2_d = P.din("f1_b2", [14, 1024])

    xready = dict(P.ready)
    xT = v3(AB.ap("xT"), T)

    AB.alloc(f"{tag}_rw", 8 * 14)
    rwv = v3(AB.ap(f"{tag}_rw"), 14)
    AB.alloc(f"{tag}_rwe", 8 * 14)
    rwe = v3(AB.ap(f"{tag}_rwe"), 14)
    AB.alloc(f"{tag}_ecb", 2 * 1024)
    ecb_off = AB.blocks[f"{tag}_ecb"][0]
    ecb = [AB.t[:, ecb_off + i * 1024:ecb_off + (i + 1) * 1024] for i in range(2)]
    AB.alloc(f"{tag}_eT", 2 * 1024)
    eT_off = AB.blocks[f"{tag}_eT"][0]
    eT = [v3(AB.t[:, eT_off + i * 1024:eT_off + (i + 1) * 1024], 128) for i in range(2)]
    AB.alloc(f"{tag}_ohb", TT * 14)
    ohb = v3(AB.ap(f"{tag}_ohb"), 14)

    mx8 = SC[:, 0:8]
    rt = SC[:, 8:24]
    iota14 = SC[:, 24:40]
    exps = SC[:, 40:56]
    oh0 = 56
    oneh = SC[:, oh0:oh0 + 14 * TT].rearrange("p (t e) -> p t e", e=14)
    i0 = oh0 + 14 * TT
    iota128 = SC[:, i0:i0 + 128]
    pr0 = i0 + 128
    prefix = SC[:, pr0:pr0 + 14 * TT]
    pm0 = pr0 + 14 * TT
    posm = [SC[:, pm0 + k * 14 * TT:pm0 + (k + 1) * 14 * TT] for k in range(2)]
    wg0 = pm0 + 2 * 14 * TT
    assert wg0 + TT <= 3840
    rwf = SC[:, 3856:3968].rearrange("p (c n) -> p c n", n=14)
    b1s = SC[:, 3840:3856]

    lrw = P.res("ld", 1, 16)[0]
    gpi = P.res("gp", 1)[0]
    rp_dv = P.res("dv", 2)

    @P.block.sync
    def _(sy):
        for snm, v_ in xready.items():
            sy.wait_ge(P.sems[snm], v_)
        sy.dma_start(out=rwf, in_=rw_d.rearrange("(c p) n -> p c n", p=128)
                     ).then_inc(P.sems["ld"], 16)

    @P.block.gpsimd
    def _(g):
        g.iota(iota14[:, 0:14], [[1, 14]], base=0, channel_multiplier=0,
               allow_small_or_imprecise_dtypes=True)
        g.iota(iota128, [[1, 128]], base=0, channel_multiplier=0,
               allow_small_or_imprecise_dtypes=True).then_inc(P.sems["gp"], 1)

    @P.block.vector
    def _(vv):
        apply_pending(P, vv, f"{tag}_rw", f"{tag}_rwe", f"{tag}_ecb", f"{tag}_eT",
                      f"{tag}_ohb")
        vv.wait_ge(P.sems["ld"], lrw)
        vv.tensor_copy(rwv.rearrange("p c n -> p (c n)"),
                       rwf.rearrange("p c n -> p (c n)")).then_inc(P.sems["dv"], 1)
        vv.wait_ge(P.sems["dv"], rp_dv[0])
        vv.tensor_sub(rwe.rearrange("p c n -> p (c n)"), rwf.rearrange("p c n -> p (c n)"),
                      rwv.rearrange("p c n -> p (c n)")).then_inc(P.sems["dv"], 1)

    rt_pe = P.res("pe", 2 * TT)
    rt_dv = P.res("dv", 9 * TT)
    rt_ac = P.res("ac", TT)

    @P.block.tensor
    def _(t):
        t.wait_ge(P.sems["dv"], rp_dv[1])
        for snm, v_ in xready.items():
            t.wait_ge(P.sems[snm], v_)
        for tt in range(TT):
            bank = b[tt % 2]
            t.wait_ge(P.sems["dv"], rt_dv[9 * tt + 1])
            if tt >= 1:
                t.wait_ge(P.sems["dv"], rt_dv[9 * (tt - 1) + 2])
            for c in range(8):
                mm = t.transpose(bt[:, c * 128:(c + 1) * 128],
                                 ecb[tt % 2][:, c * 128:(c + 1) * 128], P.ident[:])
            mm.then_inc(P.sems["pe"], 1)
            if tt >= 2:
                t.wait_ge(P.sems["dv"], rt_dv[9 * (tt - 2) + 4])
            t.wait_ge(P.sems["dv"], rt_dv[9 * tt + 2])
            for c in range(8):
                t.matmul(bank[:, 0:14], xT[:, c, tt * 128:(tt + 1) * 128],
                         rwv[:, c, :], start=(c == 0), stop=False)
            for c in range(8):
                t.matmul(bank[:, 0:14], eT[tt % 2][:, c, :], rwv[:, c, :],
                         start=False, stop=False)
            for c in range(8):
                mm = t.matmul(bank[:, 0:14], xT[:, c, tt * 128:(tt + 1) * 128],
                              rwe[:, c, :], start=False, stop=(c == 7))
            mm.then_inc(P.sems["pe"], 1)

    @P.block.vector
    def _(vv):
        vv.wait_ge(P.sems["gp"], gpi)
        for snm, v_ in xready.items():
            vv.wait_ge(P.sems[snm], v_)
        for tt in range(TT):
            bank = b[tt % 2]
            if tt >= 2:
                vv.wait_ge(P.sems["pe"], rt_pe[2 * (tt - 2)])
            vv.tensor_copy(ecb[tt % 2], X[:, tt * 1024:(tt + 1) * 1024]
                           ).then_inc(P.sems["dv"], 1)
            vv.wait_ge(P.sems["dv"], rt_dv[9 * tt])
            vv.tensor_sub(ecb[tt % 2], X[:, tt * 1024:(tt + 1) * 1024], ecb[tt % 2]
                          ).then_inc(P.sems["dv"], 1)
            vv.wait_ge(P.sems["pe"], rt_pe[2 * tt])
            vv.tensor_copy(eT[tt % 2].rearrange("p c n -> p (c n)"), bt[:]
                           ).then_inc(P.sems["dv"], 1)
            vv.wait_ge(P.sems["pe"], rt_pe[2 * tt + 1])
            if tt >= 1:
                vv.wait_ge(P.sems["dv"], rt_dv[9 * (tt - 1) + 8])
            vv.max(mx8, bank[:, 0:14]).then_inc(P.sems["dv"], 1)
            vv.wait_ge(P.sems["dv"], rt_dv[9 * tt + 3])
            vv.max_index(P.mi8[:], mx8, bank[:, 0:14])
            vv.tensor_scalar_mul(rt[:, 0:1], mx8[:, 0:1], -1.0).then_inc(P.sems["dv"], 1)
            vv.wait_ge(P.sems["ac"], rt_ac[tt])
            vv.reciprocal(rt[:, 2:3], rt[:, 1:2])
            vv.tensor_copy(rt[:, 3:4], P.mi8[:, 0:1]).then_inc(P.sems["dv"], 1)
            vv.wait_ge(P.sems["dv"], rt_dv[9 * tt + 5])
            vv.tensor_scalar(oneh[:, tt, :], iota14[:, 0:14], rt[:, 3:4], None,
                             ALU.is_equal).then_inc(P.sems["dv"], 1)
            vv.wait_ge(P.sems["dv"], rt_dv[9 * tt + 6])
            vv.tensor_copy(ohb[:, tt, :], oneh[:, tt, :])
            vv.tensor_copy(SC[:, wg0 + tt:wg0 + tt + 1], rt[:, 2:3]
                           ).then_inc(P.sems["dv"], 1)
            vv.engine_nop().then_inc(P.sems["dv"], 1)

    @P.block.scalar
    def _(sa):
        for tt in range(TT):
            bank = b[tt % 2]
            sa.wait_ge(P.sems["dv"], rt_dv[9 * tt + 4])
            sa.activation(exps[:, 0:14], bank[:, 0:14], AF.Exp, bias=rt[:, 0:1],
                          accum_out=rt[:, 1:2]).then_inc(P.sems["ac"], 1)

    AB.free(f"{tag}_rw", f"{tag}_rwe", f"{tag}_ecb", f"{tag}_eT", "xT",
            readers={"pe": rt_pe[-1]})
    emit_seq_barrier(P)

    pf_pe = P.res("pe", 1)
    pf_dv = P.res("dv", 6)

    @P.block.tensor
    def _(t):
        t.wait_ge(P.sems["ld"], P.const_ready)
        n = 0
        for tt in range(TT):
            for tt2 in range(tt + 1):
                n += 1
                mm = t.matmul(b[6][:, tt * 14:(tt + 1) * 14],
                              (P.ltri if tt2 == tt else P.onesb)[:],
                              ohb[:, tt2, :], start=(tt == 0 and tt2 == 0),
                              stop=(tt == TT - 1 and tt2 == tt))
        mm.then_inc(P.sems["pe"], 1)

    AB.alloc(f"{tag}_Xb", TT * 1024)
    Xbv = v3(AB.ap(f"{tag}_Xb"), 1024)

    @P.block.vector
    def _(vv):
        apply_pending(P, vv, f"{tag}_Xb")
        vv.wait_ge(P.sems["pe"], pf_pe[0])
        vv.tensor_copy(prefix, b[6][:, 0:14 * TT]).then_inc(P.sems["dv"], 1)
        vv.wait_ge(P.sems["dv"], pf_dv[0])
        vv.tensor_scalar_add(posm[0], prefix, 1.0).then_inc(P.sems["dv"], 1)
        vv.wait_ge(P.sems["dv"], pf_dv[1])
        vv.tensor_mul(posm[0], posm[0], SC[:, oh0:oh0 + 14 * TT]
                      ).then_inc(P.sems["dv"], 1)
        vv.wait_ge(P.sems["dv"], pf_dv[2])
        vv.tensor_scalar_add(posm[0], posm[0], -1.0).then_inc(P.sems["dv"], 1)
        vv.wait_ge(P.sems["dv"], pf_dv[3])
        vv.tensor_scalar_add(posm[1], posm[0], -128.0).then_inc(P.sems["dv"], 1)
        for tt in range(TT):
            ins = vv.tensor_copy(Xbv[:, tt, :], X[:, tt * 1024:(tt + 1) * 1024])
        ins.then_inc(P.sems["dv"], 1)
    AB.free(f"{tag}_ohb", readers={"pe": pf_pe[0]})

    AB.alloc(f"{tag}_w1", 8 * 2048)
    w1v = v3(AB.ap(f"{tag}_w1"), 2048)
    AB.alloc(f"{tag}_w2", 8 * 1024)
    w2v = v3(AB.ap(f"{tag}_w2"), 1024)
    AB.alloc(f"{tag}_b2", 1024)
    b2r = AB.ap(f"{tag}_b2")
    AB.alloc(f"{tag}_Sg", TT * 128)
    Sgv = v3(AB.ap(f"{tag}_Sg"), 128)
    AB.alloc(f"{tag}_St", 2 * TT * 128)
    Stv = AB.ap(f"{tag}_St").rearrange("p (k t s) -> p k t s", k=2, s=128)
    AB.alloc(f"{tag}_xgsb", 1024)
    xgsb = AB.ap(f"{tag}_xgsb")
    AB.alloc(f"{tag}_xgT", 8 * 256)
    xgT = v3(AB.ap(f"{tag}_xgT"), 256)
    AB.alloc(f"{tag}_hid", 16 * 256)
    hid = v3(AB.ap(f"{tag}_hid"), 256)
    AB.alloc(f"{tag}_yo", 2 * 1024)
    yov = v3(AB.ap(f"{tag}_yo"), 1024)

    rounds = (TT + 7) // 8
    lge = P.res("lg", 4 * 14, 16)
    lb1 = P.res("ld", 14, 16)
    prev = None
    for e in range(14):
        peL = P.res("pe", 2 * (2 + rounds) + 16 + 1 + 4 + 2 * TT)
        g_pe = [peL[k * (2 + rounds)] for k in range(2)]
        xgt_pe = [peL[k * (2 + rounds) + 1] for k in range(2)]
        st_pe = [peL[(kr // rounds) * (2 + rounds) + 2 + kr % rounds]
                 for kr in range(2 * rounds)]
        o_ = 2 * (2 + rounds)
        h_pe = peL[o_:o_ + 16]
        dnh0_pe = peL[o_ + 16]
        dn_pe = peL[o_ + 17:o_ + 21]
        sc_pe = peL[o_ + 21:o_ + 21 + 2 * TT]
        dvL = P.res("dv", 2 * (3 + rounds) + 4 + 2 * TT)
        sg_dv = [dvL[k * (3 + rounds)] for k in range(2)]
        xgs_dv = [dvL[k * (3 + rounds) + 1] for k in range(2)]
        xgt_dv = [dvL[k * (3 + rounds) + 2] for k in range(2)]
        st_dv = [dvL[(kr // rounds) * (3 + rounds) + 3 + kr % rounds]
                 for kr in range(2 * rounds)]
        o_ = 2 * (3 + rounds)
        yo_dv = dvL[o_:o_ + 4]
        xa_dv = dvL[o_ + 4:o_ + 4 + 2 * TT]
        h_ac = P.res("ac", 16)

        @P.block.gpsimd
        def _(g, e=e, prev=prev, dnh0_pe=dnh0_pe):
            if e == 0:
                apply_pending(P, g, f"{tag}_w1", f"{tag}_w2", f"{tag}_b2")
            if prev is not None:
                g.wait_ge(P.sems["pe"], prev["h"])
            g.dma_start(out=w1v, in_=w1_d[e].rearrange("(c p) n -> p c n", p=128)
                        ).then_inc(P.sems["lg"], 16)
            if prev is not None:
                g.wait_ge(P.sems["pe"], prev["dn"])
            g.dma_start(out=w2v, in_=w2_d[e, 0:1024].rearrange("(c p) n -> p c n", p=128)
                        ).then_inc(P.sems["lg"], 16)
            g.dma_start(out=b2r[0:1, :], in_=b2_d[e:e + 1, :]).then_inc(P.sems["lg"], 16)
            g.wait_ge(P.sems["pe"], dnh0_pe)
            g.dma_start(out=w2v, in_=w2_d[e, 1024:2048].rearrange("(c p) n -> p c n", p=128)
                        ).then_inc(P.sems["lg"], 16)

        @P.block.sync
        def _(sy, e=e, prev=prev):
            if prev is not None:
                sy.wait_ge(P.sems["ac"], prev["gelu"])
            sy.dma_start(out=b1s, in_=b1_d[e]).then_inc(P.sems["ld"], 16)

        @P.block.vector
        def _(vv, e=e, prev=prev, sg_dv=sg_dv, g_pe=g_pe, xgs_dv=xgs_dv,
              xgt_pe=xgt_pe, xgt_dv=xgt_dv, st_pe=st_pe, st_dv=st_dv,
              dn_pe=dn_pe, yo_dv=yo_dv):
            if e == 0:
                apply_pending(P, vv, f"{tag}_Sg", f"{tag}_St", f"{tag}_xgsb",
                              f"{tag}_xgT", f"{tag}_yo")
                vv.wait_ge(P.sems["dv"], pf_dv[4])
            if prev is not None:
                vv.wait_ge(P.sems["pe"], prev["sc"])
            for k in range(2):
                if k == 1:
                    vv.wait_ge(P.sems["pe"], g_pe[0])
                    vv.wait_ge(P.sems["pe"], st_pe[rounds - 1])
                for tt in range(TT):
                    ins = vv.tensor_scalar(Sgv[:, tt, :], iota128,
                                           posm[k][:, tt * 14 + e:tt * 14 + e + 1],
                                           None, ALU.is_equal)
                ins.then_inc(P.sems["dv"], 1)
                vv.wait_ge(P.sems["pe"], g_pe[k])
                vv.tensor_copy(xgsb[:, 0:512], b[0][:])
                vv.tensor_copy(xgsb[:, 512:1024], b[1][:]).then_inc(P.sems["dv"], 1)
                vv.wait_ge(P.sems["pe"], xgt_pe[k])
                vv.tensor_copy(xgT[:, :, k * 128:(k + 1) * 128],
                               bt[:].rearrange("p (c t) -> p c t", t=128)
                               ).then_inc(P.sems["dv"], 1)
                for r in range(rounds):
                    nt = min(8, TT - r * 8)
                    vv.wait_ge(P.sems["pe"], st_pe[k * rounds + r])
                    vv.tensor_copy(Stv[:, k, r * 8:r * 8 + nt, :],
                                   bt[:, 0:nt * 128].rearrange("p (c t) -> p c t", t=128)
                                   ).then_inc(P.sems["dv"], 1)
            for k in range(2):
                for nj in range(2):
                    vv.wait_ge(P.sems["pe"], dn_pe[2 * k + nj])
                    vv.tensor_copy(yov[:, k, nj * 512:(nj + 1) * 512],
                                   b[2 * k + nj][:]).then_inc(P.sems["dv"], 1)

        @P.block.tensor
        def _(t, e=e, prev=prev, sg_dv=sg_dv, g_pe=g_pe, xgs_dv=xgs_dv,
              xgt_pe=xgt_pe, xgt_dv=xgt_dv, st_pe=st_pe, st_dv=st_dv,
              h_pe=h_pe, h_ac=h_ac, dn_pe=dn_pe, yo_dv=yo_dv, sc_pe=sc_pe,
              xa_dv=xa_dv):
            bt_free = prev["bt"] if prev is not None else None
            for k in range(2):
                t.wait_ge(P.sems["dv"], sg_dv[k])
                if e == 0 and k == 0:
                    t.wait_ge(P.sems["dv"], pf_dv[5])
                if prev is not None and k == 0:
                    t.wait_ge(P.sems["dv"], prev["yo"])
                if k == 1:
                    t.wait_ge(P.sems["dv"], xgs_dv[0])
                for tt in range(TT):
                    t.matmul(b[0][:], Sgv[:, tt, :], Xbv[:, tt, 0:512],
                             start=(tt == 0), stop=(tt == TT - 1))
                    mm = t.matmul(b[1][:], Sgv[:, tt, :], Xbv[:, tt, 512:1024],
                                  start=(tt == 0), stop=(tt == TT - 1))
                mm.then_inc(P.sems["pe"], 1)
                t.wait_ge(P.sems["dv"], xgs_dv[k])
                if bt_free is not None:
                    t.wait_ge(P.sems["dv"], bt_free)
                for c in range(8):
                    mm = t.transpose(bt[:, c * 128:(c + 1) * 128],
                                     xgsb[:, c * 128:(c + 1) * 128], P.ident[:])
                mm.then_inc(P.sems["pe"], 1)
                prev_evac = xgt_dv[k]
                for r in range(rounds):
                    nt = min(8, TT - r * 8)
                    t.wait_ge(P.sems["dv"], prev_evac)
                    for i in range(nt):
                        mm = t.transpose(bt[:, i * 128:(i + 1) * 128],
                                         Sgv[:, r * 8 + i, :], P.ident[:])
                    mm.then_inc(P.sems["pe"], 1)
                    prev_evac = st_dv[k * rounds + r]
                bt_free = prev_evac
            t.wait_ge(P.sems["lg"], lge[4 * e])
            t.wait_ge(P.sems["dv"], xgt_dv[1])
            if prev is not None:
                t.wait_ge(P.sems["dv"], prev["xa"])
            for it in range(16):
                if it >= 4:
                    t.wait_ge(P.sems["ac"], h_ac[it - 4])
                for c in range(8):
                    mm = t.matmul(b[2 + it % 4][:, 0:256], w1v[:, c, it * 128:(it + 1) * 128],
                                  xgT[:, c, :], start=(c == 0), stop=(c == 7))
                mm.then_inc(P.sems["pe"], 1)
            t.wait_ge(P.sems["lg"], lge[4 * e + 2])
            t.wait_ge(P.sems["ac"], h_ac[15])
            for k in range(2):
                for nj in range(2):
                    bk = b[2 * k + nj]
                    for i2 in range(8):
                        mm = t.matmul(bk[:], hid[:, i2, k * 128:(k + 1) * 128],
                                      w2v[:, i2, nj * 512:(nj + 1) * 512],
                                      start=(i2 == 0), stop=False)
            mm.then_inc(P.sems["pe"], 1)
            t.wait_ge(P.sems["lg"], lge[4 * e + 3])
            for k in range(2):
                for nj in range(2):
                    bk = b[2 * k + nj]
                    for i2 in range(8):
                        t.matmul(bk[:], hid[:, 8 + i2, k * 128:(k + 1) * 128],
                                 w2v[:, i2, nj * 512:(nj + 1) * 512],
                                 start=False, stop=False)
                    mm = t.matmul(bk[:], P.onesb[0:1, 0:128],
                                  b2r[0:1, nj * 512:(nj + 1) * 512],
                                  start=False, stop=True)
                    mm.then_inc(P.sems["pe"], 1)
            t.wait_ge(P.sems["dv"], yo_dv[3])
            t.wait_ge(P.sems["dv"], st_dv[2 * rounds - 1])
            g2 = 0
            for tt in range(TT):
                for nj in range(2):
                    bk = b[4 + g2 % 3]
                    if g2 >= 3:
                        t.wait_ge(P.sems["dv"], xa_dv[g2 - 3])
                    for k in range(2):
                        mm = t.matmul(bk[:], Stv[:, k, tt, :],
                                      yov[:, k, nj * 512:(nj + 1) * 512],
                                      start=(k == 0), stop=(k == 1))
                    mm.then_inc(P.sems["pe"], 1)
                    g2 += 1

        @P.block.scalar
        def _(sa, e=e, prev=prev, h_pe=h_pe, h_ac=h_ac):
            if e == 0:
                apply_pending(P, sa, f"{tag}_hid")
            sa.wait_ge(P.sems["ld"], lb1[e])
            if prev is not None:
                sa.wait_ge(P.sems["pe"], prev["dn"])
            for it in range(16):
                sa.wait_ge(P.sems["pe"], h_pe[it])
                sa.activation(hid[:, it, :], b[2 + it % 4][:, 0:256], AF.Gelu,
                              bias=b1s[:, it:it + 1]).then_inc(P.sems["ac"], 1)

        @P.block.vector
        def _(vv, e=e, sc_pe=sc_pe, xa_dv=xa_dv):
            g2 = 0
            for tt in range(TT):
                for nj in range(2):
                    bk = b[4 + g2 % 3]
                    vv.wait_ge(P.sems["pe"], sc_pe[g2])
                    x_t = X[:, tt * 1024 + nj * 512:tt * 1024 + (nj + 1) * 512]
                    vv.scalar_tensor_tensor(x_t, bk[:], SC[:, wg0 + tt:wg0 + tt + 1],
                                            x_t, ALU.mult, ALU.add
                                            ).then_inc(P.sems["dv"], 1)
                    g2 += 1

        prev = dict(h=h_pe[15], dn=dn_pe[3], gelu=h_ac[15], sc=sc_pe[-1],
                    xa=xa_dv[-1], yo=yo_dv[3], bt=st_dv[2 * rounds - 1])

    AB.free(f"{tag}_Xb", f"{tag}_w1", f"{tag}_w2", f"{tag}_b2", f"{tag}_Sg",
            f"{tag}_St", f"{tag}_xgsb", f"{tag}_xgT", f"{tag}_hid", f"{tag}_yo",
            readers={"pe": P.cnt["pe"]})
    if "pb" in guard:
        emit_seq_barrier(P)
    P.ready = emit_postln(P, tag, {"dv": prev["xa"]}, refresh_xt=False)


def build_a0(T=2048, n_seq=4, debug=False, hold=None, upto=None):
    P = Prog(T, n_seq)
    P.debug = debug or (upto is not None)
    P.setup()
    P.AB.hold = debug if hold is None else hold
    with P.es, P.nc.Block() as block:
        P.block = block
        P.dout("xout", [n_seq, T, 1024])
        emit_consts_load(P)
        for s in range(n_seq):
            if s:
                emit_seq_barrier(P)
            emit_load_x(P, s)
            emit_l0_attn(P, s, upto=upto)
            if upto is None:
                emit_store_x(P, s, final=(s == n_seq - 1))
            else:
                @P.block.sync
                def _(sy):
                    sy.wait_ge(P.sems["st"], P.cnt["st"])
                break
    return P


def build_f0(T=2048, n_seq=4, debug=False, hold=None, upto=None):
    def em(P, s, upto=None):
        return emit_l0_ffn(P, s, upto=upto, refresh_out=False)
    return _build_one(em, T, n_seq, debug=debug, hold=hold, upto=upto, refresh=False)


def build_f0_old(T=2048, n_seq=4, debug=False, hold=None, upto=None):
    P = Prog(T, n_seq)
    P.debug = debug or (upto is not None)
    P.setup()
    P.AB.hold = debug if hold is None else hold
    with P.es, P.nc.Block() as block:
        P.block = block
        P.dout("xout", [n_seq, T, 1024])
        emit_consts_load(P)
        for s in range(n_seq):
            emit_load_x(P, s, refresh=False)
            emit_l0_ffn(P, s, upto=upto)
            if upto is None:
                emit_store_x(P, s, final=(s == n_seq - 1))
            else:
                @P.block.sync
                def _(sy):
                    sy.wait_ge(P.sems["st"], P.cnt["st"])
                break
    return P


def _build_one(emitter, T=2048, n_seq=4, debug=False, hold=None, upto=None,
               refresh=True):
    P = Prog(T, n_seq)
    P.debug = debug or (upto is not None)
    P.setup()
    P.AB.hold = debug if hold is None else hold
    with P.es, P.nc.Block() as block:
        P.block = block
        P.dout("xout", [n_seq, T, 1024])
        emit_consts_load(P)
        for s in range(n_seq):
            if s:
                emit_seq_barrier(P)
            emit_load_x(P, s, refresh=refresh)
            emitter(P, s, upto=upto)
            if upto is None:
                emit_store_x(P, s, final=(s == n_seq - 1))
            else:
                @P.block.sync
                def _(sy):
                    sy.wait_ge(P.sems["st"], P.cnt["st"])
                break
    return P


def build_a1(T=2048, n_seq=4, **kw):
    return _build_one(emit_l1_attn, T, n_seq, **kw)


def build_f1(T=2048, n_seq=4, **kw):
    return _build_one(emit_l1_ffn, T, n_seq, **kw)


def build_f1sp(T=2048, n_seq=4, **kw):
    def em(P, s, upto=None):
        return emit_l1_ffn_sp(P, s)
    return _build_one(em, T, n_seq, **kw)


def build_full(T=2048, n_seq=4, debug=False, stop=None, f1_guard=("rb", "pb"),
               barriers="full"):
    P = Prog(T, n_seq)
    P.debug = debug
    P.f1_guard = f1_guard
    P.setup()
    bar = (lambda: emit_seq_barrier(P)) if barriers == "full" else (lambda: None)
    with P.es, P.nc.Block() as block:
        P.block = block
        P.dout("xout", [n_seq, T, 1024])
        emit_consts_load(P)
        for s in range(n_seq):
            if s:
                bar()
            nb = 4 if stop is None or s < stop[0] else stop[1]
            emit_load_x(P, s, refresh=True)
            emit_l0_attn(P, s, refresh_out=False)
            if debug:
                dbg_dump(P, f"x0_{s}", P.X[:, :], dict(P.ready), dt=F32)
            if nb >= 2:
                bar()
                emit_l0_ffn(P, s, refresh_out=True)
                if debug:
                    dbg_dump(P, f"x1_{s}", P.X[:, :], dict(P.ready), dt=F32)
            if nb >= 3:
                bar()
                emit_l1_attn(P, s)
                if debug:
                    dbg_dump(P, f"x2_{s}", P.X[:, :], dict(P.ready), dt=F32)
            if nb >= 4:
                bar()
                if getattr(P, "f1_sparse", True):
                    emit_l1_ffn_sp(P, s)
                else:
                    emit_l1_ffn(P, s, guard=getattr(P, "f1_guard", ()))
            last = (s == n_seq - 1) or (stop is not None and s == stop[0])
            emit_store_x(P, s, final=last)
            if last and stop is not None and s == stop[0]:
                break
    return P


_PCACHE = {}


def _progs():
    if "p" not in _PCACHE:
        _PCACHE["p"] = [build_a0(2048, 4), build_f0(2048, 4),
                        build_a1(2048, 4), build_f1(2048, 4)]
    return _PCACHE["p"]


def _prog_full():
    if "f" not in _PCACHE:
        _PCACHE["f"] = build_full(2048, 4)
    return _PCACHE["f"]


def kernel(**inputs):
    T, NS = 2048, 4
    x = np.ascontiguousarray(inputs["L0_attn_x"], np.float32)
    consts = host_consts(T)
    packs = {}
    packs.update(pack_l0_attn(inputs["L0_attn_w_qkv"], inputs["L0_attn_w_o"]))
    packs.update(pack_l0_ffn(inputs["L0_ffn_Wq"], inputs["L0_ffn_Wkva"], inputs["L0_ffn_Wkvb"],
                             inputs["L0_ffn_Wo"], inputs["L0_ffn_Wg"], inputs["L0_ffn_Wu"],
                             inputs["L0_ffn_Wd"], inputs["L0_ffn_sWg"], inputs["L0_ffn_sWu"],
                             inputs["L0_ffn_sWd"], inputs["L0_ffn_Wgate"],
                             inputs["L0_ffn_expert_bias"], inputs["L0_ffn_w_ln1"],
                             inputs["L0_ffn_w_ln2"]))
    packs.update(pack_l1_attn(inputs["L1_attn_Wq"], inputs["L1_attn_bq"], inputs["L1_attn_Wk"],
                              inputs["L1_attn_bk"], inputs["L1_attn_Wv"], inputs["L1_attn_bv"],
                              inputs["L1_attn_Wo"], inputs["L1_attn_bo"]))
    packs.update(pack_l1_ffn(inputs["L1_ffn_router_w"], inputs["L1_ffn_w1"], inputs["L1_ffn_b1"],
                             inputs["L1_ffn_w2"], inputs["L1_ffn_b2"]))
    allin = {**consts, **packs}

    P = _prog_full()
    names = set(P.dram.keys())
    base = {k: np.ascontiguousarray(v) for k, v in allin.items() if k in names}
    in_maps = [{**base, "xin": np.ascontiguousarray(x[4 * c:4 * c + 4])} for c in range(8)]
    res = run_bass_kernel_spmd(P.nc, in_maps, core_ids=list(range(8)))
    return np.concatenate([np.asarray(res.results[c]["xout"], np.float32)
                           for c in range(8)], axis=0)


# version 09
# speedup vs baseline: 1.6371x; 1.0027x over previous
import numpy as np
import ml_dtypes
from contextlib import ExitStack
import concourse.bass as bass
import concourse.mybir as mybir
from concourse.bass_utils import run_bass_kernel_spmd

F32 = mybir.dt.float32
BF16 = mybir.dt.bfloat16
U32 = mybir.dt.uint32
AF = mybir.ActivationFunctionType
ALU = mybir.AluOpType
AX = mybir.AxisListType
BFNP = ml_dtypes.bfloat16

SWAP16 = [(i + 16) % 32 for i in range(32)]
AB_COLS = 60416


def _l0_perm():
    return [*range(0, 16), *range(32, 48), *range(16, 32), *range(48, 64)]

def _mla_perm():
    return [*range(32), *[32 + p for p in (*range(0, 32, 2), *range(1, 32, 2))]]

def host_consts(T):
    t = np.arange(T, dtype=np.float64)
    invf = 1.0 / (10000.0 ** (np.arange(32) / 32.0))
    ang = t[None, :] * invf[:, None]
    Tc0 = np.zeros((128, T)); Ts0 = np.zeros((128, T))
    for r in range(128):
        qd, idx = r // 32, r % 32
        f = (qd % 2) * 16 + (idx % 16)
        Tc0[r] = np.cos(ang[f])
        Ts0[r] = np.sin(ang[f]) * (1.0 if idx < 16 else -1.0)
    invf2 = 1.0 / (100000.0 ** (np.arange(0, 32, 2) / 32.0))
    ang2 = t[None, :] * invf2[:, None]
    TcM = np.zeros((128, T)); TsM = np.zeros((128, T))
    for r in range(128):
        qd, idx = r // 32, r % 32
        if qd % 2 == 0:
            TcM[r] = 1.0
        else:
            f = idx % 16
            TcM[r] = np.cos(ang2[f])
            TsM[r] = np.sin(ang2[f]) * (1.0 if idx < 16 else -1.0)
    masks = np.zeros((128, 4, 512))
    for v_, o in enumerate((0, 128, 256, 384)):
        masks[:, v_, :] = (np.arange(128)[:, None] + o <= np.arange(512)[None, :])
    idx = np.pi / 2 * (t + 1) / T
    TT = T // 128
    sc = np.zeros((128, 2 * TT), np.float32)
    for tt in range(TT):
        sc[:, tt] = np.sin(idx[tt * 128:(tt + 1) * 128])
        sc[:, TT + tt] = np.cos(idx[tt * 128:(tt + 1) * 128])
    return {
        "c_ident": np.eye(128).astype(BFNP),
        "c_onesb": np.ones((128, 128), BFNP),
        "c_ltri": np.triu(np.ones((128, 128)), 1).astype(BFNP),
        "c_tc0": Tc0.astype(BFNP), "c_ts0": Ts0.astype(BFNP),
        "c_tcm": TcM.astype(BFNP), "c_tsm": TsM.astype(BFNP),
        "c_masks": masks.astype(BFNP).reshape(128, 4 * 512),
        "c_sincos": sc,
    }

def pack_l0_attn(w_qkv, w_o):
    p = np.asarray(_l0_perm())
    q = w_qkv[:, :1024].reshape(1024, 16, 64)[:, :, p].reshape(1024, 1024)
    k = w_qkv[:, 1024:1280].reshape(1024, 4, 64)[:, :, p].reshape(1024, 256)
    v = w_qkv[:, 1280:1536]
    return {"a0_wqkv": np.ascontiguousarray(np.concatenate([q, k, v], axis=1), np.float32),
            "a0_wo": np.ascontiguousarray(w_o, np.float32)}

def pack_l0_ffn(Wq, Wkva, Wkvb, Wo, Wg, Wu, Wd, sWg, sWu, sWd, Wgate, expert_bias,
                w_ln1, w_ln2):
    p64 = np.asarray(_mla_perm())
    p32 = np.asarray([*range(0, 32, 2), *range(1, 32, 2)])
    wq = Wq.reshape(1024, 16, 64)[:, :, p64].reshape(1024, 1024)
    wkva = np.concatenate([Wkva[:, :128], Wkva[:, 128:][:, p32]], axis=1)
    c = np.ascontiguousarray
    woe = Wo.reshape(16, 64, 1024)[:, :32, :].reshape(512, 1024)
    return {"f0_wq": c(wq, np.float32), "f0_wkva": c(wkva, np.float32),
            "f0_wkvb": c(Wkvb, np.float32), "f0_woe": c(woe, np.float32),
            "f0_wg": c(np.concatenate([sWg, Wg], 0), np.float32),
            "f0_wu": c(np.concatenate([sWu, Wu], 0), np.float32),
            "f0_wd": c(np.concatenate([sWd, Wd], 0), np.float32),
            "f0_wgate": c(Wgate, np.float32),
            "f0_ebias": c(expert_bias.reshape(1, 8), np.float32),
            "f0_g1": c(w_ln1.reshape(1, 1024), np.float32),
            "f0_g2": c(w_ln2.reshape(1, 1024), np.float32)}



class Arena:
    def __init__(self, tensor, total):
        self.t, self.total, self.blocks = tensor, total, {}
        self.hold = False
        self.freed = []
        self.pending = {}

    def alloc(self, name, size):
        size = int(size)
        off = 0
        for (o, s) in sorted(self.blocks.values()):
            if off + size <= o:
                break
            off = max(off, o + s)
        assert off + size <= self.total, \
            f"arena overflow: {name} sz={size} off={off} held={sorted(self.blocks.items(), key=lambda kv: kv[1])}"
        self.blocks[name] = (off, size)
        waits = {}
        keep = []
        for (o, s, rd) in self.freed:
            if o < off + size and off < o + s:
                for k, v in rd.items():
                    waits[k] = max(waits.get(k, 0), v)
                if o < off:
                    keep.append((o, off - o, rd))
                if o + s > off + size:
                    keep.append((off + size, o + s - (off + size), rd))
            else:
                keep.append((o, s, rd))
        self.freed = keep
        self.pending[name] = waits
        return off

    def free(self, *names, readers=None):
        if self.hold:
            return
        for n in names:
            off, size = self.blocks[n]
            self.freed.append((off, size, dict(readers or {})))
            del self.blocks[n]

    def ap(self, name):
        off, size = self.blocks[name]
        return self.t[:, off:off + size]


def apply_pending(P, eng, *names):
    for n in names:
        for snm, v_ in P.AB.pending.get(n, {}).items():
            eng.wait_ge(P.sems[snm], v_)


def v3(ap, inner):
    return ap.rearrange("p (c t) -> p c t", t=inner)


class Prog:
    def __init__(self, T=2048, n_seq=4):
        self.T, self.TT, self.NQ = T, T // 128, T // 512
        self.n_seq = n_seq
        self.nc = bass.Bass("TRN2", target_bir_lowering=False, detect_race_conditions=False)
        self.es = ExitStack()
        self.cnt, self.sems = {}, {}
        self.dram = {}
        self.ready = {}

    def sem(self, name):
        if name not in self.sems:
            self.sems[name] = self.es.enter_context(self.nc.semaphore(name))
            self.cnt[name] = 0
        return self.sems[name]

    def res(self, name, n, step=1):
        self.sem(name)
        base = self.cnt[name]
        self.cnt[name] += n * step
        return [base + (i + 1) * step for i in range(n)]

    def setup(self):
        nc = self.nc
        for s in ("ld", "lg", "pe", "dv", "ac", "gp", "st"):
            self.sem(s)
        self.X = self.es.enter_context(nc.sbuf_tensor("X", [128, self.TT * 1024], F32))
        ab_t = self.es.enter_context(nc.sbuf_tensor("ABt", [128, AB_COLS], BF16))
        self.AB = Arena(ab_t, AB_COLS)
        self.SC = self.es.enter_context(nc.sbuf_tensor("SCt", [128, 4096], F32))
        self.AF = self.es.enter_context(nc.sbuf_tensor("AFt", [128, 2304], F32))
        self.ident = self.es.enter_context(nc.sbuf_tensor("ident", [128, 128], BF16))
        self.banks = [self.es.enter_context(nc.psum_tensor(f"pb{i}", [128, 512], F32))
                      for i in range(7)]
        self.bt = self.es.enter_context(nc.psum_tensor("pbt", [128, 1024], BF16))
        self.onesb = self.es.enter_context(nc.sbuf_tensor("onesb", [128, 128], BF16))
        self.ltri = self.es.enter_context(nc.sbuf_tensor("ltri", [128, 128], BF16))
        self.mi8 = self.es.enter_context(nc.sbuf_tensor("mi8", [128, 8], U32))

    def din(self, name, shape, dt=F32):
        if name not in self.dram:
            self.dram[name] = self.nc.dram_tensor(name, shape, dt, kind="ExternalInput").ap()
        return self.dram[name]

    def dout(self, name, shape, dt=F32):
        self.dram[name] = self.nc.dram_tensor(name, shape, dt, kind="ExternalOutput").ap()
        return self.dram[name]



def emit_consts_load(P):
    idt = P.din("c_ident", [128, 128], BF16)
    ob = P.din("c_onesb", [128, 128], BF16)
    lt = P.din("c_ltri", [128, 128], BF16)
    ldv = P.res("ld", 3, 16)

    @P.block.sync
    def _(sy):
        sy.dma_start(out=P.ident[:], in_=idt).then_inc(P.sems["ld"], 16)
        sy.dma_start(out=P.onesb[:], in_=ob).then_inc(P.sems["ld"], 16)
        sy.dma_start(out=P.ltri[:], in_=lt).then_inc(P.sems["ld"], 16)
    P.const_ready = ldv[-1]


def dbg_dump(P, name, ap2d, wait, dt=BF16):
    if not getattr(P, "debug", False):
        return
    shp = [128, int(np.prod(ap2d.shape[1:]))]
    full = "dbg_" + name
    while full in P.dram:
        full += "_"
    d = P.dout(full, shp, dt)
    P.res("st", 1, 16)

    @P.block.sync
    def _(sy, ap2d=ap2d, wait=dict(wait), d=d):
        for snm, v_ in wait.items():
            sy.wait_ge(P.sems[snm], v_)
        sy.dma_start(out=d, in_=ap2d).then_inc(P.sems["st"], 16)


def emit_toT(P, tag, wait=None, dst="xT", cast_emit=None, pe_extra_wait=None,
             tile_wait=None):
    T, TT = P.T, P.TT
    X = P.X
    if dst not in P.AB.blocks:
        P.AB.alloc(dst, 8 * T)
    dsv = v3(P.AB.ap(dst), T)
    cast_off = P.AB.alloc(f"{tag}_cb", 2048)
    cb = [P.AB.t[:, cast_off:cast_off + 1024], P.AB.t[:, cast_off + 1024:cast_off + 2048]]
    if cast_emit is None:
        def cast_emit(vv, tt, out_ap):
            return vv.tensor_copy(out_ap, X[:, tt * 1024:(tt + 1) * 1024])

    t_v = P.res("pe", TT)
    order = [("cast", 0)]
    for tt in range(1, TT):
        order += [("cast", tt), ("evac", tt - 1)]
    order.append(("evac", TT - 1))
    dvv = P.res("dv", len(order))
    c_v, e_v = {}, {}
    for i, (kind, tt) in enumerate(order):
        (c_v if kind == "cast" else e_v)[tt] = dvv[i]

    @P.block.vector
    def _(vv):
        if wait:
            for snm, v_ in wait.items():
                vv.wait_ge(P.sems[snm], v_)
        apply_pending(P, vv, dst, f"{tag}_cb")
        for kind, tt in order:
            if kind == "cast":
                if tt >= 2:
                    vv.wait_ge(P.sems["pe"], t_v[tt - 2])
                if tile_wait is not None:
                    for snm, v_ in tile_wait(tt).items():
                        vv.wait_ge(P.sems[snm], v_)
                cast_emit(vv, tt, cb[tt % 2]).then_inc(P.sems["dv"], 1)
            else:
                vv.wait_ge(P.sems["pe"], t_v[tt])
                vv.tensor_copy(dsv[:, :, tt * 128:(tt + 1) * 128],
                               P.bt[:].rearrange("p (c t) -> p c t", t=128)
                               ).then_inc(P.sems["dv"], 1)

    @P.block.tensor
    def _(t):
        t.wait_ge(P.sems["ld"], P.const_ready)
        if pe_extra_wait:
            for snm, v_ in pe_extra_wait.items():
                t.wait_ge(P.sems[snm], v_)
        for tt in range(TT):
            t.wait_ge(P.sems["dv"], c_v[tt])
            if tt >= 1:
                t.wait_ge(P.sems["dv"], e_v[tt - 1])
            for c in range(8):
                mm = t.transpose(P.bt[:, c * 128:(c + 1) * 128],
                                 cb[tt % 2][:, c * 128:(c + 1) * 128], P.ident[:])
            mm.then_inc(P.sems["pe"], 1)
    P.AB.free(f"{tag}_cb", readers={"pe": t_v[-1]})
    P.xt_tiles = dict(e_v)
    return {"dv": dvv[-1], "pe": t_v[-1]}


def emit_xT_refresh(P, tag, wait=None, dst="xT", tile_wait=None):
    return emit_toT(P, tag, wait=wait, dst=dst, tile_wait=tile_wait)


def emit_seq_barrier(P):
    snap = {k: v for k, v in P.cnt.items() if v > 0}

    def mk(eng):
        for k, v in snap.items():
            eng.wait_ge(P.sems[k], v)
    P.block.tensor(mk)
    P.block.vector(mk)
    P.block.scalar(mk)
    P.block.gpsimd(mk)
    P.block.sync(mk)


def emit_load_x(P, s, refresh=True):
    TT = P.TT
    ldv = P.res("ld", 4, 16)
    xin = P.din("xin", [P.n_seq, P.T, 1024])
    st_prev = P.cnt.get("st", 0)
    q4 = TT // 4

    @P.block.sync
    def _(sy):
        if st_prev:
            sy.wait_ge(P.sems["st"], st_prev)
        for qq in range(4):
            sy.dma_start(
                out=P.X[:, qq * q4 * 1024:(qq + 1) * q4 * 1024]
                    .rearrange("p (tt c) -> p tt c", c=1024),
                in_=xin[s, qq * q4 * 128:(qq + 1) * q4 * 128]
                    .rearrange("(tt p) c -> p tt c", p=128)
                ).then_inc(P.sems["ld"], 16)
    if refresh:
        P.ready = emit_xT_refresh(P, f"lx{s}",
                                  tile_wait=lambda tt: {"ld": ldv[tt // q4]})
        P.ready["ld"] = ldv[-1]
    else:
        P.ready = {"ld": ldv[-1]}


def emit_store_x(P, s, final=False):
    P.res("st", 1, 16)
    xout = P.dram["xout"]

    @P.block.sync
    def _(sy, ready=dict(P.ready)):
        for snm, v_ in ready.items():
            sy.wait_ge(P.sems[snm], v_)
        sy.dma_start(out=xout[s].rearrange("(tt p) c -> p tt c", p=128),
                     in_=P.X[:, :].rearrange("p (tt c) -> p tt c", c=1024)
                     ).then_inc(P.sems["st"], 16)
        if final:
            sy.wait_ge(P.sems["st"], P.cnt["st"])


def emit_postln(P, tag, xready, refresh_xt=True):
    TT = P.TT
    SC = P.SC
    stats = SC[:, 0:12 * TT].rearrange("p (t k) -> p t k", k=12)
    agg = SC[:, 12 * TT:16 * TT].rearrange("p (t k) -> p t k", k=4)
    aux = SC[:, 16 * TT:20 * TT].rearrange("p (t k) -> p t k", k=4)

    g1, g2, g3, g4 = P.res("dv", 4)
    g5 = P.res("ac", 1)[0]
    g6, g7 = P.res("dv", 2)
    g8 = P.res("ac", TT)

    @P.block.vector
    def _(vv):
        for snm, v_ in xready.items():
            vv.wait_ge(P.sems[snm], v_)
        for tt in range(TT):
            vv.bn_stats(stats[:, tt, 0:6], P.X[:, tt * 1024:tt * 1024 + 512])
            ins = vv.bn_stats(stats[:, tt, 6:12], P.X[:, tt * 1024 + 512:(tt + 1) * 1024])
        ins.then_inc(P.sems["dv"], 1)
        vv.wait_ge(P.sems["dv"], g1)
        for tt in range(TT):
            ins = vv.bn_aggr(agg[:, tt, 0:2], stats[:, tt, :])
        ins.then_inc(P.sems["dv"], 1)
        vv.wait_ge(P.sems["dv"], g2)
        vv.tensor_scalar_add(aux[:, :, 0], agg[:, :, 1], 1e-5).then_inc(P.sems["dv"], 1)
        vv.wait_ge(P.sems["dv"], g3)
        vv.reciprocal(aux[:, :, 1], aux[:, :, 0]).then_inc(P.sems["dv"], 1)

    @P.block.scalar
    def _(sa):
        sa.wait_ge(P.sems["dv"], g4)
        sa.activation(aux[:, :, 1], aux[:, :, 1], AF.Sqrt).then_inc(P.sems["ac"], 1)

    @P.block.vector
    def _(vv):
        vv.wait_ge(P.sems["ac"], g5)
        vv.tensor_mul(aux[:, :, 2], agg[:, :, 0], aux[:, :, 1]).then_inc(P.sems["dv"], 1)
        vv.wait_ge(P.sems["dv"], g6)
        vv.tensor_scalar_mul(aux[:, :, 2], aux[:, :, 2], -1.0).then_inc(P.sems["dv"], 1)

    @P.block.scalar
    def _(sa):
        sa.wait_ge(P.sems["dv"], g7)
        for tt in range(TT):
            sa.activation(P.X[:, tt * 1024:(tt + 1) * 1024], P.X[:, tt * 1024:(tt + 1) * 1024],
                          AF.Identity, bias=aux[:, tt, 2:3], scale=aux[:, tt, 1:2]
                          ).then_inc(P.sems["ac"], 1)

    ready = {"ac": g8[-1]}
    if refresh_xt:
        ready = emit_xT_refresh(P, f"{tag}_rx",
                                tile_wait=lambda tt: {"ac": g8[tt]})
        ready["ac"] = g8[-1]
    return ready


def emit_rope(P, tile_ap, rsc, tbl, done_sems, tbl_ready, rsc_name=None):
    r1 = P.res("dv", 1)[0]
    r2 = P.res("gp", 1)[0]
    r3 = P.res("dv", 1)[0]
    r4 = P.res("gp", 1)[0]

    @P.block.vector
    def _(vv):
        for snm, v_ in done_sems.items():
            vv.wait_ge(P.sems[snm], v_)
        if rsc_name:
            apply_pending(P, vv, rsc_name)
        vv.wait_ge(P.sems["ld"], tbl_ready)
        vv.tensor_mul(rsc[:, 0, :], tile_ap, tbl[:, 0, :]).then_inc(P.sems["dv"], 1)

    @P.block.gpsimd
    def _(g):
        for snm, v_ in done_sems.items():
            g.wait_ge(P.sems[snm], v_)
        if rsc_name:
            apply_pending(P, g, rsc_name)
        g.wait_ge(P.sems["ld"], tbl_ready)
        g.tensor_mul(rsc[:, 1, :], tile_ap, tbl[:, 1, :]).then_inc(P.sems["gp"], 1)

    @P.block.vector
    def _(vv):
        vv.wait_ge(P.sems["gp"], r2)
        vv.stream_shuffle(rsc[:, 1, :], rsc[:, 1, :], SWAP16).then_inc(P.sems["dv"], 1)

    @P.block.gpsimd
    def _(g):
        g.wait_ge(P.sems["dv"], r1)
        g.wait_ge(P.sems["dv"], r3)
        g.tensor_add(tile_ap, rsc[:, 0, :], rsc[:, 1, :]).then_inc(P.sems["gp"], 1)
    return r4



def emit_l0_attn(P, s, upto=None, refresh_out=True):
    T, TT, NQ = P.T, P.TT, P.NQ
    AB, SC, X, b, bt = P.AB, P.SC, P.X, P.banks, P.bt
    tag = f"a0s{s}"
    nc = P.nc

    wqkv_d = P.din("a0_wqkv", [1024, 1536])
    wo_d = P.din("a0_wo", [1024, 1024])
    tc0_d = P.din("c_tc0", [128, T], BF16)
    ts0_d = P.din("c_ts0", [128, T], BF16)
    masks_d = P.din("c_masks", [128, 4 * 512], BF16)
    xT = v3(AB.ap("xT"), T)

    AB.alloc(f"{tag}_wqkv", 8 * 1536)
    AB.alloc(f"{tag}_qn", TT * 1024)
    AB.alloc(f"{tag}_kn", TT * 256)
    AB.alloc(f"{tag}_v", TT * 4 * 66)
    wq = v3(AB.ap(f"{tag}_wqkv"), 1536)
    qn = v3(AB.ap(f"{tag}_qn"), 1024)
    kn = v3(AB.ap(f"{tag}_kn"), 256)
    vv_ = AB.ap(f"{tag}_v").rearrange("p (t g e) -> p t g e", g=4, e=66)
    SSQ = SC[:, 0:20 * TT].rearrange("p (t h) -> p t h", h=20)
    SQS = SC[:, 20 * TT:20 * TT + 1280]

    lw = P.res("lg", 1, 16)[0]
    gpo = P.res("gp", 1)[0]
    xready = dict(P.ready)

    @P.block.gpsimd
    def _(g):
        apply_pending(P, g, f"{tag}_wqkv", f"{tag}_v", f"{tag}_qn", f"{tag}_kn")
        g.dma_start(out=wq, in_=wqkv_d.rearrange("(c p) n -> p c n", p=128)
                    ).then_inc(P.sems["lg"], 16)
        g.memset(vv_[:, :, :, 64:66], 1.0).then_inc(P.sems["gp"], 1)

    pe_a = P.res("pe", TT)
    ac_a = P.res("ac", TT)
    dv_a = P.res("dv", TT)

    @P.block.tensor
    def _(t):
        t.wait_ge(P.sems["lg"], lw)
        for snm, v_ in xready.items():
            t.wait_ge(P.sems[snm], v_)
        for tt in range(TT):
            trio = (0, 1, 2) if tt % 2 == 0 else (3, 4, 5)
            if tt >= 2:
                t.wait_ge(P.sems["dv"], dv_a[tt - 2])
            for nj in range(3):
                for c in range(8):
                    mm = t.matmul(b[trio[nj]][:], xT[:, c, tt * 128:(tt + 1) * 128],
                                  wq[:, c, nj * 512:(nj + 1) * 512],
                                  start=(c == 0), stop=(c == 7))
            mm.then_inc(P.sems["pe"], 1)

    @P.block.scalar
    def _(sa):
        for tt in range(TT):
            trio = (0, 1, 2) if tt % 2 == 0 else (3, 4, 5)
            sa.wait_ge(P.sems["pe"], pe_a[tt])
            if tt >= 1:
                sa.wait_ge(P.sems["dv"], dv_a[tt - 1])
            sa.activation(SQS[:, 0:512], b[trio[0]][:], AF.Square)
            sa.activation(SQS[:, 512:1024], b[trio[1]][:], AF.Square)
            sa.activation(SQS[:, 1024:1280], b[trio[2]][:, 0:256], AF.Square
                          ).then_inc(P.sems["ac"], 1)

    @P.block.vector
    def _(vv):
        vv.wait_ge(P.sems["gp"], gpo)
        apply_pending(P, vv, f"{tag}_qn", f"{tag}_kn", f"{tag}_v")
        for tt in range(TT):
            trio = (0, 1, 2) if tt % 2 == 0 else (3, 4, 5)
            vv.wait_ge(P.sems["ac"], ac_a[tt])
            vv.tensor_copy(qn[:, tt, 0:512], b[trio[0]][:])
            vv.tensor_copy(qn[:, tt, 512:1024], b[trio[1]][:])
            vv.tensor_copy(kn[:, tt, :], b[trio[2]][:, 0:256])
            for g_ in range(4):
                vv.tensor_copy(vv_[:, tt, g_, 0:64], b[trio[2]][:, 256 + g_ * 64:320 + g_ * 64])
            vv.tensor_reduce(SSQ[:, tt, 0:16], SQS[:, 0:1024].rearrange("p (h e) -> p h e", e=64),
                             AX.X, ALU.add)
            vv.tensor_reduce(SSQ[:, tt, 16:20], SQS[:, 1024:1280].rearrange("p (h e) -> p h e", e=64),
                             AX.X, ALU.add).then_inc(P.sems["dv"], 1)

    AB.free(f"{tag}_wqkv", "xT", readers={"pe": pe_a[TT - 1]})

    n1 = P.res("ac", 1)[0]
    n2, n3, n4 = P.res("dv", 3)
    n5 = P.res("gp", 1)[0]
    SSQf = SSQ.rearrange("p t h -> p (t h)")

    @P.block.scalar
    def _(sa):
        sa.wait_ge(P.sems["dv"], dv_a[TT - 1])
        sa.activation(SSQf, SSQf, AF.Sqrt).then_inc(P.sems["ac"], 1)

    @P.block.vector
    def _(vv):
        vv.wait_ge(P.sems["ac"], n1)
        vv.tensor_scalar_add(SSQf, SSQf, 1e-10).then_inc(P.sems["dv"], 1)
        vv.wait_ge(P.sems["dv"], n2)
        vv.reciprocal(SSQf, SSQf).then_inc(P.sems["dv"], 1)
        vv.wait_ge(P.sems["dv"], n3)
        for tt in range(TT):
            for h in range(0, 16, 2):
                ins = vv.tensor_scalar_mul(qn[:, tt, h * 64:(h + 1) * 64],
                                           qn[:, tt, h * 64:(h + 1) * 64], SSQ[:, tt, h:h + 1])
        ins.then_inc(P.sems["dv"], 1)

    @P.block.gpsimd
    def _(g):
        g.wait_ge(P.sems["dv"], n3)
        for tt in range(TT):
            for h in range(1, 16, 2):
                g.tensor_scalar_mul(qn[:, tt, h * 64:(h + 1) * 64],
                                    qn[:, tt, h * 64:(h + 1) * 64], SSQ[:, tt, h:h + 1])
            for g_ in range(4):
                ins = g.tensor_scalar_mul(kn[:, tt, g_ * 64:(g_ + 1) * 64],
                                          kn[:, tt, g_ * 64:(g_ + 1) * 64],
                                          SSQ[:, tt, 16 + g_:17 + g_])
        ins.then_inc(P.sems["gp"], 1)

    if upto == "A":
        dbg_dump(P, "qn", AB.ap(f"{tag}_qn"), {"dv": n4, "gp": n5})
        dbg_dump(P, "kn", AB.ap(f"{tag}_kn"), {"dv": n4, "gp": n5})
        dbg_dump(P, "vv", AB.ap(f"{tag}_v"), {"dv": dv_a[TT - 1], "gp": gpo})
        return

    AB.alloc(f"{tag}_qT", 8 * T)
    qT = v3(AB.ap(f"{tag}_qT"), T)
    AB.alloc(f"{tag}_tbl", 2 * T)
    tbl = v3(AB.ap(f"{tag}_tbl"), T)
    AB.alloc(f"{tag}_kts", 2 * T)
    kts = v3(AB.ap(f"{tag}_kts"), T)
    AB.alloc(f"{tag}_rsc", 2 * T)
    rsc = v3(AB.ap(f"{tag}_rsc"), T)

    lt1, lt2 = P.res("ld", 2, 16)

    @P.block.sync
    def _(sy):
        apply_pending(P, sy, f"{tag}_tbl")
        sy.dma_start(out=tbl[:, 0, :], in_=tc0_d).then_inc(P.sems["ld"], 16)
        sy.dma_start(out=tbl[:, 1, :], in_=ts0_d).then_inc(P.sems["ld"], 16)

    kt_pe = P.res("pe", TT)
    kt_dv = P.res("dv", TT)

    @P.block.tensor
    def _(t):
        t.wait_ge(P.sems["gp"], n5)
        t.wait_ge(P.sems["dv"], n4)
        for tt in range(TT):
            if tt >= 1:
                t.wait_ge(P.sems["dv"], kt_dv[tt - 1])
            t.transpose(bt[:, 0:128], kn[:, tt, 0:128], P.ident[:])
            t.transpose(bt[:, 128:256], kn[:, tt, 128:256], P.ident[:]).then_inc(P.sems["pe"], 1)

    @P.block.vector
    def _(vv):
        apply_pending(P, vv, f"{tag}_kts")
        for tt in range(TT):
            vv.wait_ge(P.sems["pe"], kt_pe[tt])
            vv.tensor_copy(kts[:, 0, tt * 128:(tt + 1) * 128], bt[:, 0:128])
            vv.tensor_copy(kts[:, 1, tt * 128:(tt + 1) * 128], bt[:, 128:256]
                           ).then_inc(P.sems["dv"], 1)
    AB.free(f"{tag}_kn", readers={"pe": kt_pe[TT - 1]})
    AB.alloc(f"{tag}_ktr01", 2 * T)
    AB.alloc(f"{tag}_ktr23", 2 * T)
    ktr = [v3(AB.ap(f"{tag}_ktr01"), T), v3(AB.ap(f"{tag}_ktr23"), T)]

    kr0 = emit_rope(P, kts[:, 0, :], rsc, tbl, {"dv": kt_dv[TT - 1]}, lt2,
                    rsc_name=f"{tag}_rsc")
    kr1 = emit_rope(P, kts[:, 1, :], rsc, tbl, {"gp": kr0}, lt2)
    krep = P.res("dv", 1)[0]

    @P.block.vector
    def _(vv):
        apply_pending(P, vv, f"{tag}_ktr01", f"{tag}_ktr23")
        vv.wait_ge(P.sems["gp"], kr1)
        for g_ in range(4):
            src = kts[:, g_ // 2, :]
            half = (g_ % 2) * 64
            vv.tensor_copy(ktr[g_ // 2][0:64, g_ % 2, :], src[half:half + 64, :])
            ins = vv.tensor_copy(ktr[g_ // 2][64:128, g_ % 2, :], src[half:half + 64, :])
        ins.then_inc(P.sems["dv"], 1)

    qt_pe = P.res("pe", TT)
    qt_dv = P.res("dv", TT)

    @P.block.tensor
    def _(t):
        t.wait_ge(P.sems["dv"], kt_dv[TT - 1])
        for tt in range(TT):
            if tt >= 1:
                t.wait_ge(P.sems["dv"], qt_dv[tt - 1])
            for c in range(8):
                mm = t.transpose(bt[:, c * 128:(c + 1) * 128], qn[:, tt, c * 128:(c + 1) * 128],
                                 P.ident[:])
            mm.then_inc(P.sems["pe"], 1)

    @P.block.vector
    def _(vv):
        apply_pending(P, vv, f"{tag}_qT")
        vv.wait_ge(P.sems["dv"], krep)
        for tt in range(TT):
            vv.wait_ge(P.sems["pe"], qt_pe[tt])
            vv.tensor_copy(qT[:, :, tt * 128:(tt + 1) * 128],
                           bt[:].rearrange("p (c t) -> p c t", t=128)).then_inc(P.sems["dv"], 1)

    prev = {"dv": qt_dv[TT - 1]}
    for c in range(8):
        r = emit_rope(P, qT[:, c, :], rsc, tbl, prev, lt2)
        prev = {"gp": r}
    rope_done = r
    if upto == "B":
        dbg_dump(P, "qT", AB.ap(f"{tag}_qT"), {"gp": rope_done})
        dbg_dump(P, "ktr01", AB.ap(f"{tag}_ktr01"), {"dv": krep})
        dbg_dump(P, "ktr23", AB.ap(f"{tag}_ktr23"), {"dv": krep})
        return
    AB.free(f"{tag}_qn", readers={"pe": qt_pe[TT - 1]})
    AB.free(f"{tag}_kts", readers={"dv": krep})
    AB.free(f"{tag}_rsc", f"{tag}_tbl", readers={"gp": rope_done})

    AB.alloc(f"{tag}_y", TT * 1024)
    y = v3(AB.ap(f"{tag}_y"), 1024)
    AB.alloc(f"{tag}_wo", 8 * 1024)
    wo = v3(AB.ap(f"{tag}_wo"), 1024)
    AB.alloc(f"{tag}_mask", 4 * 512)
    msk = v3(AB.ap(f"{tag}_mask"), 512)
    AB.alloc(f"{tag}_strip", 4 * 512)
    strip = v3(AB.ap(f"{tag}_strip"), 512)

    lm = P.res("ld", 1, 16)[0]
    lwo = P.res("lg", 1, 16)[0]

    @P.block.sync
    def _(sy):
        apply_pending(P, sy, f"{tag}_mask")
        sy.dma_start(out=AB.ap(f"{tag}_mask"), in_=masks_d).then_inc(P.sems["ld"], 16)

    @P.block.gpsimd
    def _(g):
        apply_pending(P, g, f"{tag}_wo")
        g.dma_start(out=wo, in_=wo_d.rearrange("(c p) n -> p c n", p=128)
                    ).then_inc(P.sems["lg"], 16)

    rd = SC[:, 1664:1672]
    prev_ep = None
    for p_ in range(8):
        g_ = p_ // 2
        for qj in range(NQ):
            kmax = min(TT, 4 * (qj + 1))
            diag = sorted(set(range(4 * qj, kmax)))
            pe_v = P.res("pe", 2 * kmax)
            s_idx, a_idx = {}, {}
            n_ = 0
            for ki in range(kmax):
                s_idx[ki] = pe_v[n_]; n_ += 1
                if ki >= 1:
                    a_idx[ki - 1] = pe_v[n_]; n_ += 1
            a_idx[kmax - 1] = pe_v[n_]
            ac_v = P.res("ac", kmax)
            dvl = P.res("dv", len(diag) + 2)
            dv_m = {ki: dvl[i] for i, ki in enumerate(diag)}
            ep1, ep2 = dvl[-2], dvl[-1]

            def emit_av(t, ki, kmax, diag, ac_v, dv_m, first_out):
                t.wait_ge(P.sems["ac"], ac_v[ki])
                if ki in diag:
                    t.wait_ge(P.sems["dv"], dv_m[ki])
                sidx = (ki % 2) * 2
                g_l = g_
                for j in range(4):
                    t.matmul(b[0][:, j * 128:j * 128 + 65],
                             strip[:, sidx, j * 128:(j + 1) * 128],
                             vv_[:, ki, g_l, 0:65], start=first_out,
                             stop=(ki == kmax - 1 and j == 3))
                    mm = t.matmul(b[1][:, j * 128:j * 128 + 65],
                                  strip[:, sidx + 1, j * 128:(j + 1) * 128],
                                  vv_[:, ki, g_l, 0:65], start=first_out,
                                  stop=(ki == kmax - 1 and j == 3))
                    first_out = False
                mm.then_inc(P.sems["pe"], 1)

            @P.block.tensor
            def _(t, p_=p_, qj=qj, kmax=kmax, diag=diag, ac_v=ac_v,
                  dv_m=dv_m, prev_ep=prev_ep, g_=g_):
                if p_ == 0 and qj == 0:
                    t.wait_ge(P.sems["dv"], krep)
                    t.wait_ge(P.sems["gp"], rope_done)
                    t.wait_ge(P.sems["ld"], lm)
                if prev_ep is not None:
                    t.wait_ge(P.sems["dv"], prev_ep)
                for ki in range(kmax):
                    sb_, sb2 = (b[2], b[3]) if ki % 2 == 0 else (b[4], b[5])
                    if ki >= 2:
                        t.wait_ge(P.sems["ac"], ac_v[ki - 2])
                    kt_t = ktr[g_ // 2]
                    gg = g_ % 2
                    t.matmul(sb_[:], kt_t[0:64, gg, ki * 128:(ki + 1) * 128],
                             qT[0:64, p_, qj * 512:(qj + 1) * 512], start=True, stop=True)
                    t.matmul(sb2[:], kt_t[64:128, gg, ki * 128:(ki + 1) * 128],
                             qT[64:128, p_, qj * 512:(qj + 1) * 512], start=True, stop=True
                             ).then_inc(P.sems["pe"], 1)
                    if ki >= 1:
                        emit_av(t, ki - 1, kmax, diag, ac_v, dv_m, ki == 1)
                emit_av(t, kmax - 1, kmax, diag, ac_v, dv_m, kmax == 1)

            @P.block.scalar
            def _(sa, kmax=kmax, s_idx=s_idx, a_idx=a_idx, p_=p_, qj=qj):
                if p_ == 0 and qj == 0:
                    apply_pending(P, sa, f"{tag}_strip")
                for ki in range(kmax):
                    sidx = (ki % 2) * 2
                    sa.wait_ge(P.sems["pe"], s_idx[ki])
                    if ki >= 2:
                        sa.wait_ge(P.sems["pe"], a_idx[ki - 2])
                    sb_, sb2 = (b[2], b[3]) if ki % 2 == 0 else (b[4], b[5])
                    sa.activation(strip[:, sidx, :], sb_[:], AF.Exp, scale=0.125)
                    sa.activation(strip[:, sidx + 1, :], sb2[:], AF.Exp, scale=0.125
                                  ).then_inc(P.sems["ac"], 1)

            @P.block.vector
            def _(vv, kmax=kmax, qj=qj, diag=diag, ac_v=ac_v, a_idx=a_idx,
                  ep1=ep1, ep2=ep2, p_=p_):
                if p_ == 0 and qj == 0:
                    apply_pending(P, vv, f"{tag}_y")
                    vv.wait_ge(P.sems["ld"], lm)
                for ki in diag:
                    sidx = (ki % 2) * 2
                    o_var = ki - 4 * qj
                    vv.wait_ge(P.sems["ac"], ac_v[ki])
                    vv.tensor_mul(strip[:, sidx, :], strip[:, sidx, :], msk[:, o_var, :])
                    vv.tensor_mul(strip[:, sidx + 1, :], strip[:, sidx + 1, :],
                                  msk[:, o_var, :]).then_inc(P.sems["dv"], 1)
                vv.wait_ge(P.sems["pe"], a_idx[kmax - 1])
                vv.reciprocal(rd[:, 0:4], bass.AP(b[0], 64, [[512, 128], [128, 4]]))
                vv.reciprocal(rd[:, 4:8], bass.AP(b[1], 64, [[512, 128], [128, 4]])
                              ).then_inc(P.sems["dv"], 1)
                vv.wait_ge(P.sems["dv"], ep1)
                for j in range(4):
                    vv.tensor_scalar_mul(y[:, qj * 4 + j, (2 * p_) * 64:(2 * p_ + 1) * 64],
                                         b[0][:, j * 128:j * 128 + 64], rd[:, j:j + 1])
                    ins = vv.tensor_scalar_mul(y[:, qj * 4 + j, (2 * p_ + 1) * 64:(2 * p_ + 2) * 64],
                                               b[1][:, j * 128:j * 128 + 64], rd[:, 4 + j:5 + j])
                ins.then_inc(P.sems["dv"], 1)
            prev_ep = ep2

    AB.free(f"{tag}_qT", f"{tag}_ktr01", f"{tag}_ktr23", f"{tag}_strip", f"{tag}_v",
            readers={"pe": P.cnt["pe"]})
    AB.free(f"{tag}_mask", readers={"dv": prev_ep})

    if upto == "C":
        dbg_dump(P, "y", AB.ap(f"{tag}_y"), {"dv": prev_ep})
        return

    AB.alloc(f"{tag}_yT", 8 * T)
    yT = v3(AB.ap(f"{tag}_yT"), T)
    yt_pe = P.res("pe", TT)
    yt_dv = P.res("dv", TT)

    @P.block.tensor
    def _(t):
        t.wait_ge(P.sems["dv"], prev_ep)
        for tt in range(TT):
            if tt >= 1:
                t.wait_ge(P.sems["dv"], yt_dv[tt - 1])
            for c in range(8):
                mm = t.transpose(bt[:, c * 128:(c + 1) * 128], y[:, tt, c * 128:(c + 1) * 128],
                                 P.ident[:])
            mm.then_inc(P.sems["pe"], 1)

    @P.block.vector
    def _(vv):
        apply_pending(P, vv, f"{tag}_yT")
        for tt in range(TT):
            vv.wait_ge(P.sems["pe"], yt_pe[tt])
            vv.tensor_copy(yT[:, :, tt * 128:(tt + 1) * 128],
                           bt[:].rearrange("p (c t) -> p c t", t=128)).then_inc(P.sems["dv"], 1)

    op_pe = P.res("pe", TT)
    op_dv = P.res("dv", TT)

    @P.block.tensor
    def _(t):
        t.wait_ge(P.sems["lg"], lwo)
        for tt in range(TT):
            pr = (b[2], b[3]) if tt % 2 == 0 else (b[4], b[5])
            if tt >= 2:
                t.wait_ge(P.sems["dv"], op_dv[tt - 2])
            for nj in range(2):
                for c in range(8):
                    mm = t.matmul(pr[nj][:], yT[:, c, tt * 128:(tt + 1) * 128],
                                  wo[:, c, nj * 512:(nj + 1) * 512],
                                  start=(c == 0), stop=(c == 7))
            mm.then_inc(P.sems["pe"], 1)

    @P.block.vector
    def _(vv):
        for tt in range(TT):
            pr = (b[2], b[3]) if tt % 2 == 0 else (b[4], b[5])
            vv.wait_ge(P.sems["pe"], op_pe[tt])
            vv.tensor_add(X[:, tt * 1024:tt * 1024 + 512], pr[0][:],
                          X[:, tt * 1024:tt * 1024 + 512])
            vv.tensor_add(X[:, tt * 1024 + 512:(tt + 1) * 1024], pr[1][:],
                          X[:, tt * 1024 + 512:(tt + 1) * 1024]).then_inc(P.sems["dv"], 1)

    dbg_dump(P, "xpre", P.X[:, :], {"dv": op_dv[TT - 1]}, dt=F32)
    AB.free(f"{tag}_y", readers={"pe": yt_pe[TT - 1]})
    AB.free(f"{tag}_yT", f"{tag}_wo", readers={"pe": op_pe[TT - 1]})
    P.ready = emit_postln(P, tag, {"dv": op_dv[TT - 1]}, refresh_xt=refresh_out)



def emit_rms_toT(P, tag, gain_ap, dst, xready):
    TT = P.TT
    SC = P.SC
    stats = SC[:, 2048:2048 + 12 * TT].rearrange("p (t k) -> p t k", k=12)
    agg = SC[:, 2048 + 12 * TT:2048 + 16 * TT].rearrange("p (t k) -> p t k", k=4)
    g1, g2, g3, g3b, g3c, g4 = P.res("dv", 6)
    g5 = P.res("ac", 1)[0]

    @P.block.vector
    def _(vv):
        for snm, v_ in xready.items():
            vv.wait_ge(P.sems[snm], v_)
        for tt in range(TT):
            vv.bn_stats(stats[:, tt, 0:6], P.X[:, tt * 1024:tt * 1024 + 512])
            ins = vv.bn_stats(stats[:, tt, 6:12], P.X[:, tt * 1024 + 512:(tt + 1) * 1024])
        ins.then_inc(P.sems["dv"], 1)
        vv.wait_ge(P.sems["dv"], g1)
        for tt in range(TT):
            ins = vv.bn_aggr(agg[:, tt, 0:2], stats[:, tt, :])
        ins.then_inc(P.sems["dv"], 1)
        vv.wait_ge(P.sems["dv"], g2)
        vv.tensor_mul(agg[:, :, 2], agg[:, :, 0], agg[:, :, 0]).then_inc(P.sems["dv"], 1)
        vv.wait_ge(P.sems["dv"], g3)
        vv.tensor_add(agg[:, :, 2], agg[:, :, 2], agg[:, :, 1]).then_inc(P.sems["dv"], 1)
        vv.wait_ge(P.sems["dv"], g3b)
        vv.tensor_scalar_add(agg[:, :, 2], agg[:, :, 2], 1e-5).then_inc(P.sems["dv"], 1)
        vv.wait_ge(P.sems["dv"], g3c)
        vv.reciprocal(agg[:, :, 3], agg[:, :, 2]).then_inc(P.sems["dv"], 1)

    @P.block.scalar
    def _(sa):
        sa.wait_ge(P.sems["dv"], g4)
        sa.activation(agg[:, :, 3], agg[:, :, 3], AF.Sqrt).then_inc(P.sems["ac"], 1)

    def cast(vv, tt, out_ap):
        return vv.scalar_tensor_tensor(out_ap, P.X[:, tt * 1024:(tt + 1) * 1024],
                                       agg[:, tt, 3:4], gain_ap, ALU.mult, ALU.mult)
    return emit_toT(P, f"{tag}_h", wait={"ac": g5}, dst=dst, cast_emit=cast)


def emit_l0_ffn(P, s, upto=None, refresh_out=True):
    T, TT, NQ = P.T, P.TT, P.NQ
    AB, SC, X, b, bt = P.AB, P.SC, P.X, P.banks, P.bt
    nc = P.nc
    tag = f"f0s{s}"

    wq_d = P.din("f0_wq", [1024, 1024])
    wkva_d = P.din("f0_wkva", [1024, 160])
    wkvb_d = P.din("f0_wkvb", [128, 512])
    woe_d = P.din("f0_woe", [512, 1024])
    wg_d = P.din("f0_wg", [9, 1024, 512])
    wu_d = P.din("f0_wu", [9, 1024, 512])
    wd_d = P.din("f0_wd", [9, 512, 1024])
    wgate_d = P.din("f0_wgate", [1024, 8])
    ebias_d = P.din("f0_ebias", [1, 8])
    tcm_d = P.din("c_tcm", [128, T], BF16)
    tsm_d = P.din("c_tsm", [128, T], BF16)
    masks_d = P.din("c_masks", [128, 4 * 512], BF16)

    if not getattr(P, "f0_gains", False):
        P.f0_gains = True
        g1_d = P.din("f0_g1", [1, 1024])
        g2_d = P.din("f0_g2", [1, 1024])
        gl = P.res("ld", 2, 16)

        @P.block.sync
        def _(sy):
            sy.dma_start(out=P.AF[:, 0:1024], in_=g1_d.broadcast_to([128, 1024])
                         ).then_inc(P.sems["ld"], 16)
            sy.dma_start(out=P.AF[:, 1024:2048], in_=g2_d.broadcast_to([128, 1024])
                         ).then_inc(P.sems["ld"], 16)
        P.f0_gl = gl[-1]

        P.f0_ones = None

    xready = dict(P.ready)

    AB.alloc(f"{tag}_hT", 8 * T)
    hT = v3(AB.ap(f"{tag}_hT"), T)
    hr = emit_rms_toT(P, tag, P.AF[:, 0:1024], f"{tag}_hT",
                      {**xready, "ld": P.f0_gl})

    AB.alloc(f"{tag}_qT", 8 * T)
    qT = v3(AB.ap(f"{tag}_qT"), T)
    AB.alloc(f"{tag}_wq", 8 * 1024)
    wqv = v3(AB.ap(f"{tag}_wq"), 1024)
    AB.alloc(f"{tag}_tbl", 2 * T)
    tbl = v3(AB.ap(f"{tag}_tbl"), T)

    lw = P.res("lg", 1, 16)[0]

    @P.block.gpsimd
    def _(g):
        apply_pending(P, g, f"{tag}_wq")
        g.dma_start(out=wqv, in_=wq_d.rearrange("(c p) n -> p c n", p=128)
                    ).then_inc(P.sems["lg"], 16)

    ltl = P.res("ld", 2, 16)
    lt = ltl[-1]

    @P.block.sync
    def _(sy):
        apply_pending(P, sy, f"{tag}_tbl")
        sy.dma_start(out=tbl[:, 0, :], in_=tcm_d).then_inc(P.sems["ld"], 16)
        sy.dma_start(out=tbl[:, 1, :], in_=tsm_d).then_inc(P.sems["ld"], 16)

    qp_pe = P.res("pe", 8 * NQ)
    qp_dv = P.res("dv", 8 * NQ)

    @P.block.tensor
    def _(t):
        t.wait_ge(P.sems["lg"], lw)
        t.wait_ge(P.sems["dv"], hr["dv"])
        i = 0
        for pt in range(8):
            for tq in range(NQ):
                bank = b[i % 2]
                if i >= 2:
                    t.wait_ge(P.sems["dv"], qp_dv[i - 2])
                for c in range(8):
                    mm = t.matmul(bank[:], wqv[:, c, pt * 128:(pt + 1) * 128],
                                  hT[:, c, tq * 512:(tq + 1) * 512],
                                  start=(c == 0), stop=(c == 7))
                mm.then_inc(P.sems["pe"], 1)
                i += 1

    @P.block.vector
    def _(vv):
        apply_pending(P, vv, f"{tag}_qT")
        i = 0
        for pt in range(8):
            for tq in range(NQ):
                vv.wait_ge(P.sems["pe"], qp_pe[i])
                vv.tensor_copy(qT[:, pt, tq * 512:(tq + 1) * 512], b[i % 2][:]
                               ).then_inc(P.sems["dv"], 1)
                i += 1

    AB.free(f"{tag}_wq", readers={"pe": qp_pe[-1]})

    AB.alloc(f"{tag}_rsc", 2 * T)
    rsc = v3(AB.ap(f"{tag}_rsc"), T)
    prev = {"dv": qp_dv[-1]}
    first = True
    for c in range(8):
        r = emit_rope(P, qT[:, c, :], rsc, tbl, prev, lt,
                      rsc_name=(f"{tag}_rsc" if first else None))
        prev = {"gp": r}
        first = False
    q_rope_done = r

    AB.alloc(f"{tag}_kvl", T)
    kvl = v3(AB.ap(f"{tag}_kvl"), T)
    AB.alloc(f"{tag}_krs", T)
    KRS = AB.ap(f"{tag}_krs")
    AB.alloc(f"{tag}_wkva", 8 * 160)
    wkvav = v3(AB.ap(f"{tag}_wkva"), 160)
    AB.alloc(f"{tag}_wkvb", 512)
    wkvbv = AB.ap(f"{tag}_wkvb")

    lkvl = P.res("lg", 2, 16)
    lkv = lkvl[-1]

    @P.block.gpsimd
    def _(g):
        apply_pending(P, g, f"{tag}_wkva", f"{tag}_wkvb")
        g.dma_start(out=wkvav, in_=wkva_d.rearrange("(c p) n -> p c n", p=128)
                    ).then_inc(P.sems["lg"], 16)
        g.dma_start(out=wkvbv, in_=wkvb_d).then_inc(P.sems["lg"], 16)

    ck_pe = P.res("pe", NQ + NQ)
    ck_dv = P.res("dv", NQ + NQ + 1)

    @P.block.tensor
    def _(t):
        t.wait_ge(P.sems["lg"], lkv)
        for tq in range(NQ):
            bank = b[2 + tq % 2]
            if tq >= 2:
                t.wait_ge(P.sems["dv"], ck_dv[tq - 2])
            for c in range(8):
                mm = t.matmul(bank[:], wkvav[:, c, 0:128], hT[:, c, tq * 512:(tq + 1) * 512],
                              start=(c == 0), stop=(c == 7))
            mm.then_inc(P.sems["pe"], 1)
        for tq in range(NQ):
            bank = b[4 + tq % 2]
            if tq >= 2:
                t.wait_ge(P.sems["dv"], ck_dv[NQ + tq - 2])
            for c in range(8):
                mm = t.matmul(bank[0:32, :], wkvav[:, c, 128:160], hT[:, c, tq * 512:(tq + 1) * 512],
                              start=(c == 0), stop=(c == 7))
            mm.then_inc(P.sems["pe"], 1)

    @P.block.vector
    def _(vv):
        apply_pending(P, vv, f"{tag}_kvl", f"{tag}_krs")
        for tq in range(NQ):
            vv.wait_ge(P.sems["pe"], ck_pe[tq])
            vv.tensor_copy(kvl[:, 0, tq * 512:(tq + 1) * 512], b[2 + tq % 2][:]
                           ).then_inc(P.sems["dv"], 1)
        for tq in range(NQ):
            vv.wait_ge(P.sems["pe"], ck_pe[NQ + tq])
            vv.tensor_copy(KRS[32:64, tq * 512:(tq + 1) * 512], b[4 + tq % 2][0:32, :]
                           ).then_inc(P.sems["dv"], 1)
        vv.memset(KRS[0:32, :], 0.0)
        vv.memset(KRS[64:128, :], 0.0).then_inc(P.sems["dv"], 1)

    kro = emit_rope(P, KRS, rsc, tbl, {"dv": ck_dv[-1], "gp": q_rope_done}, lt)
    AB.free(f"{tag}_rsc", f"{tag}_tbl", readers={"gp": kro})
    AB.free(f"{tag}_wkva", readers={"pe": ck_pe[-1]})

    AB.alloc(f"{tag}_kva", 4 * T)
    kva = v3(AB.ap(f"{tag}_kva"), T)
    TTH = TT // 2
    AB.alloc(f"{tag}_v0", TTH * 512)
    AB.alloc(f"{tag}_v1", TTH * 512)
    _vh = [v3(AB.ap(f"{tag}_v0"), 512), v3(AB.ap(f"{tag}_v1"), 512)]
    def vsl(ki, sl):
        return _vh[ki // TTH][:, ki % TTH, sl]

    kb_pe = P.res("pe", 4 * NQ)
    kb_dv = P.res("dv", 4 * NQ)

    @P.block.tensor
    def _(t):
        t.wait_ge(P.sems["dv"], ck_dv[NQ - 1])
        t.wait_ge(P.sems["dv"], qp_dv[-1])
        i = 0
        for st_ in range(4):
            for tq in range(NQ):
                bank = b[i % 2]
                if i >= 2:
                    t.wait_ge(P.sems["dv"], kb_dv[i - 2])
                t.matmul(bank[:], wkvbv[:, st_ * 128:(st_ + 1) * 128],
                         kvl[:, 0, tq * 512:(tq + 1) * 512], start=True, stop=True
                         ).then_inc(P.sems["pe"], 1)
                i += 1

    @P.block.vector
    def _(vv):
        apply_pending(P, vv, f"{tag}_kva")
        i = 0
        for st_ in range(4):
            for tq in range(NQ):
                vv.wait_ge(P.sems["pe"], kb_pe[i])
                vv.tensor_copy(kva[:, st_, tq * 512:(tq + 1) * 512], b[i % 2][:]
                               ).then_inc(P.sems["dv"], 1)
                i += 1

    vt_pe = P.res("pe", TT)
    vt_dv = P.res("dv", TT)

    @P.block.tensor
    def _(t):
        t.wait_ge(P.sems["dv"], ck_dv[-1])
        for tt in range(TT):
            bank = b[2 + tt % 2]
            if tt >= 2:
                t.wait_ge(P.sems["dv"], vt_dv[tt - 2])
            t.matmul(bank[:], kvl[:, 0, tt * 128:(tt + 1) * 128], wkvbv[:],
                     start=True, stop=True).then_inc(P.sems["pe"], 1)

    @P.block.vector
    def _(vv):
        apply_pending(P, vv, f"{tag}_v0", f"{tag}_v1")
        for tt in range(TT):
            vv.wait_ge(P.sems["pe"], vt_pe[tt])
            vv.tensor_copy(vsl(tt, slice(0, 512)), b[2 + tt % 2][:]).then_inc(P.sems["dv"], 1)

    AB.free(f"{tag}_hT", readers={"pe": vt_pe[-1]})
    AB.free(f"{tag}_kvl", f"{tag}_wkvb", readers={"pe": vt_pe[-1]})

    AB.alloc(f"{tag}_y", TT * 512)
    y = v3(AB.ap(f"{tag}_y"), 512)
    AB.alloc(f"{tag}_woe", 4 * 1024)
    woev = v3(AB.ap(f"{tag}_woe"), 1024)
    AB.alloc(f"{tag}_mask", 4 * 512)
    msk = v3(AB.ap(f"{tag}_mask"), 512)
    AB.alloc(f"{tag}_strip", 4 * 512)
    strip = v3(AB.ap(f"{tag}_strip"), 512)
    AB.alloc(f"{tag}_ktp", T)
    ktp = AB.ap(f"{tag}_ktp")

    lm = P.res("ld", 1, 16)[0]
    lwo = P.res("lg", 1, 16)[0]

    @P.block.sync
    def _(sy):
        apply_pending(P, sy, f"{tag}_mask")
        sy.dma_start(out=AB.ap(f"{tag}_mask"), in_=masks_d).then_inc(P.sems["ld"], 16)

    @P.block.gpsimd
    def _(g):
        apply_pending(P, g, f"{tag}_woe")
        g.dma_start(out=woev, in_=woe_d.rearrange("(c p) n -> p c n", p=128)
                    ).then_inc(P.sems["lg"], 16)

    rd = SC[:, 1664:1672]
    prev_ep = None
    prev_sc = None
    for p_ in range(8):
        stA, rA = p_ // 2, (2 * p_ % 4) * 32
        stB, rB = p_ // 2, ((2 * p_ + 1) % 4) * 32
        ktb = P.res("dv", 1)[0]

        @P.block.vector
        def _(vv, p_=p_, stA=stA, rA=rA, stB=stB, rB=rB, prev_sc=prev_sc):
            if p_ == 0:
                apply_pending(P, vv, f"{tag}_ktp")
                vv.wait_ge(P.sems["dv"], kb_dv[-1])
                vv.wait_ge(P.sems["gp"], kro)
            if prev_sc is not None:
                vv.wait_ge(P.sems["pe"], prev_sc)
            vv.tensor_copy(ktp[0:32, :], kva[rA:rA + 32, stA, :])
            vv.tensor_copy(ktp[32:64, :], KRS[32:64, :])
            vv.tensor_copy(ktp[64:96, :], kva[rB:rB + 32, stB, :])
            vv.tensor_copy(ktp[96:128, :], KRS[32:64, :]).then_inc(P.sems["dv"], 1)

        for qj in range(NQ):
            kmax = min(TT, 4 * (qj + 1))
            diag = sorted(set(range(4 * qj, kmax)))
            pe_v = P.res("pe", 2 * kmax)
            s_idx, a_idx = {}, {}
            n_ = 0
            for ki in range(kmax):
                s_idx[ki] = pe_v[n_]; n_ += 1
                if ki >= 1:
                    a_idx[ki - 1] = pe_v[n_]; n_ += 1
            a_idx[kmax - 1] = pe_v[n_]
            ac_v = P.res("ac", kmax)
            dvl = P.res("dv", len(diag) + 2)
            dv_m = {ki: dvl[i] for i, ki in enumerate(diag)}
            ep1, ep2 = dvl[-2], dvl[-1]

            def emit_avm(t, ki, kmax, diag, ac_v, dv_m, first_out, p_l):
                t.wait_ge(P.sems["ac"], ac_v[ki])
                if ki in diag:
                    t.wait_ge(P.sems["dv"], dv_m[ki])
                sidx = (ki % 2) * 2
                for j in range(4):
                    t.matmul(b[0][:, j * 64:j * 64 + 32],
                             strip[:, sidx, j * 128:(j + 1) * 128],
                             vsl(ki, slice(2 * p_l * 32, (2 * p_l + 1) * 32)),
                             start=first_out, stop=False)
                    t.matmul(b[0][:, j * 64 + 32:j * 64 + 33],
                             strip[:, sidx, j * 128:(j + 1) * 128],
                             P.onesb[:, 0:1], start=False,
                             stop=(ki == kmax - 1 and j == 3))
                    t.matmul(b[1][:, j * 64:j * 64 + 32],
                             strip[:, sidx + 1, j * 128:(j + 1) * 128],
                             vsl(ki, slice((2 * p_l + 1) * 32, (2 * p_l + 2) * 32)),
                             start=first_out, stop=False)
                    mm = t.matmul(b[1][:, j * 64 + 32:j * 64 + 33],
                                  strip[:, sidx + 1, j * 128:(j + 1) * 128],
                                  P.onesb[:, 0:1], start=False,
                                  stop=(ki == kmax - 1 and j == 3))
                    first_out = False
                mm.then_inc(P.sems["pe"], 1)

            @P.block.tensor
            def _(t, p_=p_, qj=qj, kmax=kmax, diag=diag, ac_v=ac_v,
                  dv_m=dv_m, prev_ep=prev_ep, ktb=ktb):
                if p_ == 0 and qj == 0:
                    t.wait_ge(P.sems["gp"], q_rope_done)
                    t.wait_ge(P.sems["ld"], lm)
                    t.wait_ge(P.sems["ld"], P.const_ready)
                    t.wait_ge(P.sems["dv"], vt_dv[TT - 1])
                if qj == 0:
                    t.wait_ge(P.sems["dv"], ktb)
                if prev_ep is not None:
                    t.wait_ge(P.sems["dv"], prev_ep)
                for ki in range(kmax):
                    sb_, sb2 = (b[2], b[3]) if ki % 2 == 0 else (b[4], b[5])
                    if ki >= 2:
                        t.wait_ge(P.sems["ac"], ac_v[ki - 2])
                    t.matmul(sb_[:], ktp[0:64, ki * 128:(ki + 1) * 128],
                             qT[0:64, p_, qj * 512:(qj + 1) * 512], start=True, stop=True)
                    t.matmul(sb2[:], ktp[64:128, ki * 128:(ki + 1) * 128],
                             qT[64:128, p_, qj * 512:(qj + 1) * 512], start=True, stop=True
                             ).then_inc(P.sems["pe"], 1)
                    if ki >= 1:
                        emit_avm(t, ki - 1, kmax, diag, ac_v, dv_m, ki == 1, p_)
                emit_avm(t, kmax - 1, kmax, diag, ac_v, dv_m, kmax == 1, p_)

            @P.block.scalar
            def _(sa, kmax=kmax, s_idx=s_idx, a_idx=a_idx, p_=p_, qj=qj):
                if p_ == 0 and qj == 0:
                    apply_pending(P, sa, f"{tag}_strip")
                for ki in range(kmax):
                    sidx = (ki % 2) * 2
                    sa.wait_ge(P.sems["pe"], s_idx[ki])
                    if ki >= 2:
                        sa.wait_ge(P.sems["pe"], a_idx[ki - 2])
                    sb_, sb2 = (b[2], b[3]) if ki % 2 == 0 else (b[4], b[5])
                    sa.activation(strip[:, sidx, :], sb_[:], AF.Exp, scale=0.125)
                    sa.activation(strip[:, sidx + 1, :], sb2[:], AF.Exp, scale=0.125
                                  ).then_inc(P.sems["ac"], 1)

            @P.block.vector
            def _(vv, kmax=kmax, qj=qj, diag=diag, ac_v=ac_v, a_idx=a_idx,
                  ep1=ep1, ep2=ep2, p_=p_):
                if p_ == 0 and qj == 0:
                    apply_pending(P, vv, f"{tag}_y")
                    vv.wait_ge(P.sems["ld"], lm)
                for ki in diag:
                    sidx = (ki % 2) * 2
                    o_var = ki - 4 * qj
                    vv.wait_ge(P.sems["ac"], ac_v[ki])
                    vv.tensor_mul(strip[:, sidx, :], strip[:, sidx, :], msk[:, o_var, :])
                    vv.tensor_mul(strip[:, sidx + 1, :], strip[:, sidx + 1, :],
                                  msk[:, o_var, :]).then_inc(P.sems["dv"], 1)
                vv.wait_ge(P.sems["pe"], a_idx[kmax - 1])
                vv.reciprocal(rd[:, 0:4], bass.AP(b[0], 32, [[512, 128], [64, 4]]))
                vv.reciprocal(rd[:, 4:8], bass.AP(b[1], 32, [[512, 128], [64, 4]])
                              ).then_inc(P.sems["dv"], 1)
                vv.wait_ge(P.sems["dv"], ep1)
                for j in range(4):
                    vv.tensor_scalar_mul(y[:, qj * 4 + j, (2 * p_) * 32:(2 * p_ + 1) * 32],
                                         b[0][:, j * 64:j * 64 + 32], rd[:, j:j + 1])
                    ins = vv.tensor_scalar_mul(y[:, qj * 4 + j, (2 * p_ + 1) * 32:(2 * p_ + 2) * 32],
                                               b[1][:, j * 64:j * 64 + 32], rd[:, 4 + j:5 + j])
                ins.then_inc(P.sems["dv"], 1)
            prev_ep = ep2
        prev_sc = s_idx[kmax - 1]

    AB.free(f"{tag}_qT", f"{tag}_strip", f"{tag}_ktp", f"{tag}_kva", f"{tag}_krs",
            f"{tag}_v0", f"{tag}_v1", readers={"pe": P.cnt["pe"]})
    AB.free(f"{tag}_mask", readers={"dv": prev_ep})

    if upto == "C2":
        dbg_dump(P, "ymla", AB.ap(f"{tag}_y"), {"dv": prev_ep})
        dbg_dump(P, "qTm", AB.ap(f"{tag}_qT"), {"gp": q_rope_done})
        dbg_dump(P, "krsm", AB.ap(f"{tag}_krs"), {"gp": kro})
        dbg_dump(P, "kvam", AB.ap(f"{tag}_kva"), {"dv": kb_dv[-1]})
        return

    AB.alloc(f"{tag}_ymT", 4 * T)
    ymT = v3(AB.ap(f"{tag}_ymT"), T)
    yt_pe = P.res("pe", TT)
    yt_dv = P.res("dv", TT)

    @P.block.tensor
    def _(t):
        t.wait_ge(P.sems["dv"], prev_ep)
        for tt in range(TT):
            if tt >= 1:
                t.wait_ge(P.sems["dv"], yt_dv[tt - 1])
            for c in range(4):
                mm = t.transpose(bt[:, c * 128:(c + 1) * 128], y[:, tt, c * 128:(c + 1) * 128],
                                 P.ident[:])
            mm.then_inc(P.sems["pe"], 1)

    @P.block.vector
    def _(vv):
        apply_pending(P, vv, f"{tag}_ymT")
        for tt in range(TT):
            vv.wait_ge(P.sems["pe"], yt_pe[tt])
            vv.tensor_copy(ymT[:, :, tt * 128:(tt + 1) * 128],
                           bt[:, 0:512].rearrange("p (c t) -> p c t", t=128)
                           ).then_inc(P.sems["dv"], 1)

    AB.alloc(f"{tag}_hnT", 8 * T)
    hnT = v3(AB.ap(f"{tag}_hnT"), T)
    AB.alloc(f"{tag}_hcb", 2048)
    hcb_off = AB.blocks[f"{tag}_hcb"][0]
    hcb = [AB.t[:, hcb_off:hcb_off + 1024], AB.t[:, hcb_off + 1024:hcb_off + 2048]]
    xm = [SC[:, 2048:3072], SC[:, 3072:4096]]
    est = SC[:, 1672:1672 + 16 * TT].rearrange("p (t k) -> p t k", k=16)

    opv = P.res("pe", 2 * TT)
    op_pe = {tt: opv[2 * tt] for tt in range(TT)}
    ht_pe = {tt: opv[2 * tt + 1] for tt in range(TT)}
    dvv = P.res("dv", 8 * TT)
    e_v = {(tt, k): dvv[8 * tt + k] for tt in range(TT) for k in range(8)}
    xm_dv = {tt: e_v[(tt, 0)] for tt in range(TT)}
    ms_dv = {tt: e_v[(tt, 5)] for tt in range(TT)}
    hn_dv = {tt: e_v[(tt, 6)] for tt in range(TT)}
    he_dv = {tt: e_v[(tt, 7)] for tt in range(TT)}
    rr_ac = P.res("ac", TT)
    sq_ac = {tt: rr_ac[tt] for tt in range(TT)}

    @P.block.tensor
    def _(t):
        t.wait_ge(P.sems["lg"], lwo)
        t.wait_ge(P.sems["dv"], yt_dv[TT - 1])
        for tt in range(TT):
            pr = (b[2], b[3]) if tt % 2 == 0 else (b[4], b[5])
            if tt >= 2:
                t.wait_ge(P.sems["dv"], xm_dv[tt - 2])
            for nj in range(2):
                for c in range(4):
                    mm = t.matmul(pr[nj][:], ymT[:, c, tt * 128:(tt + 1) * 128],
                                  woev[:, c, nj * 512:(nj + 1) * 512],
                                  start=(c == 0), stop=(c == 3))
            mm.then_inc(P.sems["pe"], 1)
            t.wait_ge(P.sems["dv"], hn_dv[tt])
            if tt >= 1:
                t.wait_ge(P.sems["dv"], he_dv[tt - 1])
            for c in range(8):
                mm = t.transpose(bt[:, c * 128:(c + 1) * 128],
                                 hcb[tt % 2][:, c * 128:(c + 1) * 128], P.ident[:])
            mm.then_inc(P.sems["pe"], 1)

    @P.block.vector
    def _(vv):
        apply_pending(P, vv, f"{tag}_hnT", f"{tag}_hcb")
        for tt in range(TT):
            pr = (b[2], b[3]) if tt % 2 == 0 else (b[4], b[5])
            vv.wait_ge(P.sems["pe"], op_pe[tt])
            x_t = X[:, tt * 1024:(tt + 1) * 1024]
            vv.tensor_add(xm[tt % 2][:, 0:512], pr[0][:], x_t[:, 0:512])
            vv.tensor_add(xm[tt % 2][:, 512:1024], pr[1][:], x_t[:, 512:1024]
                          ).then_inc(P.sems["dv"], 1)
            vv.wait_ge(P.sems["dv"], e_v[(tt, 0)])
            vv.bn_stats(est[:, tt, 0:6], xm[tt % 2][:, 0:512])
            vv.bn_stats(est[:, tt, 6:12], xm[tt % 2][:, 512:1024]).then_inc(P.sems["dv"], 1)
            vv.wait_ge(P.sems["dv"], e_v[(tt, 1)])
            vv.bn_aggr(est[:, tt, 12:14], est[:, tt, 0:12]).then_inc(P.sems["dv"], 1)
            vv.wait_ge(P.sems["dv"], e_v[(tt, 2)])
            vv.scalar_tensor_tensor(est[:, tt, 14:15], est[:, tt, 12:13], est[:, tt, 12:13],
                                    est[:, tt, 13:14], ALU.mult, ALU.add).then_inc(P.sems["dv"], 1)
            vv.wait_ge(P.sems["dv"], e_v[(tt, 3)])
            vv.tensor_scalar_add(est[:, tt, 14:15], est[:, tt, 14:15], 1e-5
                                 ).then_inc(P.sems["dv"], 1)
            vv.wait_ge(P.sems["dv"], e_v[(tt, 4)])
            vv.reciprocal(est[:, tt, 15:16], est[:, tt, 14:15]).then_inc(P.sems["dv"], 1)
            vv.wait_ge(P.sems["ac"], sq_ac[tt])
            vv.scalar_tensor_tensor(hcb[tt % 2], xm[tt % 2], est[:, tt, 15:16],
                                    P.AF[:, 1024:2048], ALU.mult, ALU.mult)
            vv.tensor_add(X[:, tt * 1024:(tt + 1) * 1024],
                          xm[tt % 2], X[:, tt * 1024:(tt + 1) * 1024]).then_inc(P.sems["dv"], 1)
            vv.wait_ge(P.sems["pe"], ht_pe[tt])
            vv.tensor_copy(hnT[:, :, tt * 128:(tt + 1) * 128],
                           bt[:].rearrange("p (c t) -> p c t", t=128)).then_inc(P.sems["dv"], 1)

    @P.block.scalar
    def _(sa):
        for tt in range(TT):
            sa.wait_ge(P.sems["dv"], ms_dv[tt])
            sa.activation(est[:, tt, 15:16], est[:, tt, 15:16], AF.Sqrt
                          ).then_inc(P.sems["ac"], 1)

    AB.free(f"{tag}_y", readers={"pe": yt_pe[TT - 1]})
    AB.free(f"{tag}_ymT", f"{tag}_woe", readers={"pe": op_pe[TT - 1]})
    AB.free(f"{tag}_hcb", readers={"pe": ht_pe[TT - 1]})
    moe_in_ready = {"dv": he_dv[TT - 1]}

    if upto == "M":
        dbg_dump(P, "hnT", AB.ap(f"{tag}_hnT"), moe_in_ready)
        dbg_dump(P, "xmid2", X[:, :], moe_in_ready, dt=F32)
        return

    AB.alloc(f"{tag}_wgu", 2 * 8 * 512)
    wguv = AB.ap(f"{tag}_wgu").rearrange("p (w c n) -> p w c n", w=2, n=512)
    AB.alloc(f"{tag}_wd", 4 * 1024)
    wdv = v3(AB.ap(f"{tag}_wd"), 1024)
    AB.alloc(f"{tag}_gub", 3 * 4 * 512)
    gub = AB.ap(f"{tag}_gub").rearrange("p (k i n) -> p k i n", k=3, n=512)
    AB.alloc(f"{tag}_wgate", 8 * 8)
    wgv = v3(AB.ap(f"{tag}_wgate"), 8)
    combw = SC[:, 0:8 * TT].rearrange("p (t e) -> p t e", e=8)
    mx8 = SC[:, 8 * TT:8 * TT + 8]
    rt8 = SC[:, 8 * TT + 8:8 * TT + 24]
    iota8 = SC[:, 8 * TT + 24:8 * TT + 32]
    lg8 = SC[:, 8 * TT + 32:8 * TT + 40]
    ebc = P.AF[:, 2048:2056]

    lgt = P.res("lg", 1, 16)[0]
    gpi = P.res("gp", 1)[0]
    lde = P.res("ld", 1, 16)[0]

    @P.block.sync
    def _(sy):
        sy.dma_start(out=ebc, in_=ebias_d.broadcast_to([128, 8])).then_inc(P.sems["ld"], 16)

    @P.block.gpsimd
    def _(g):
        apply_pending(P, g, f"{tag}_wgate")
        g.dma_start(out=wgv, in_=wgate_d.rearrange("(c p) n -> p c n", p=128)
                    ).then_inc(P.sems["lg"], 16)
        g.iota(iota8, [[1, 8]], base=0, channel_multiplier=0,
               allow_small_or_imprecise_dtypes=True).then_inc(P.sems["gp"], 1)

    rt_pe = P.res("pe", TT)
    rt_dv = P.res("dv", 7 * TT)
    rt_ac = P.res("ac", TT)

    @P.block.tensor
    def _(t):
        t.wait_ge(P.sems["lg"], lgt)
        for tt in range(TT):
            bank = b[tt % 2]
            if tt >= 2:
                t.wait_ge(P.sems["dv"], rt_dv[7 * (tt - 2) + 2])
            t.wait_ge(P.sems["dv"], he_dv[tt])
            for c in range(8):
                mm = t.matmul(bank[:, 0:8], hnT[:, c, tt * 128:(tt + 1) * 128],
                              wgv[:, c, :], start=(c == 0), stop=(c == 7))
            mm.then_inc(P.sems["pe"], 1)

    @P.block.vector
    def _(vv):
        vv.wait_ge(P.sems["gp"], gpi)
        for tt in range(TT):
            bank = b[tt % 2]
            vv.wait_ge(P.sems["pe"], rt_pe[tt])
            if tt == 0:
                vv.wait_ge(P.sems["ld"], lde)
            if tt >= 1:
                vv.wait_ge(P.sems["dv"], rt_dv[7 * (tt - 1) + 6])
            vv.tensor_add(lg8, bank[:, 0:8], ebc).then_inc(P.sems["dv"], 1)
            vv.wait_ge(P.sems["dv"], rt_dv[7 * tt])
            vv.max(mx8, lg8).then_inc(P.sems["dv"], 1)
            vv.wait_ge(P.sems["dv"], rt_dv[7 * tt + 1])
            vv.max_index(P.mi8[:], mx8, lg8)
            vv.tensor_sub(rt8[:, 0:1], mx8[:, 1:2], mx8[:, 0:1]).then_inc(P.sems["dv"], 1)
            vv.wait_ge(P.sems["ac"], rt_ac[tt])
            vv.tensor_scalar(rt8[:, 2:3], rt8[:, 1:2], -1.0, 1.0, ALU.mult, ALU.add)
            vv.tensor_copy(rt8[:, 3:4], P.mi8[:, 0:1])
            vv.tensor_copy(rt8[:, 4:5], P.mi8[:, 1:2]).then_inc(P.sems["dv"], 1)
            vv.wait_ge(P.sems["dv"], rt_dv[7 * tt + 3])
            vv.tensor_scalar(combw[:, tt, :], iota8, rt8[:, 3:4], None, ALU.is_equal)
            vv.tensor_scalar(rt8[:, 8:16], iota8, rt8[:, 4:5], None, ALU.is_equal
                             ).then_inc(P.sems["dv"], 1)
            vv.wait_ge(P.sems["dv"], rt_dv[7 * tt + 4])
            vv.tensor_scalar_mul(combw[:, tt, :], combw[:, tt, :], rt8[:, 2:3]
                                 ).then_inc(P.sems["dv"], 1)
            vv.wait_ge(P.sems["dv"], rt_dv[7 * tt + 5])
            vv.scalar_tensor_tensor(combw[:, tt, :], rt8[:, 8:16], rt8[:, 1:2],
                                    combw[:, tt, :], ALU.mult, ALU.add
                                    ).then_inc(P.sems["dv"], 1)

    @P.block.scalar
    def _(sa):
        for tt in range(TT):
            sa.wait_ge(P.sems["dv"], rt_dv[7 * tt + 2])
            sa.activation(rt8[:, 1:2], rt8[:, 0:1], AF.Sigmoid).then_inc(P.sems["ac"], 1)
    cw_dv = P.cnt["dv"]

    lgw = P.res("lg", 3 * 9, 16)
    prev_gu_last = None
    prev_dn_last = None
    prev_dn_dv = None
    for e in range(9):
        @P.block.gpsimd
        def _(g, e=e, prev_gu_last=prev_gu_last, prev_dn_last=prev_dn_last):
            if e == 0:
                apply_pending(P, g, f"{tag}_wgu", f"{tag}_wd", f"{tag}_gub")
            if prev_gu_last is not None:
                g.wait_ge(P.sems["pe"], prev_gu_last)
            g.dma_start(out=wguv[:, 0, :, :], in_=wg_d[e].rearrange("(c p) n -> p c n", p=128)
                        ).then_inc(P.sems["lg"], 16)
            g.dma_start(out=wguv[:, 1, :, :], in_=wu_d[e].rearrange("(c p) n -> p c n", p=128)
                        ).then_inc(P.sems["lg"], 16)
            if prev_dn_last is not None:
                g.wait_ge(P.sems["pe"], prev_dn_last)
            g.dma_start(out=wdv, in_=wd_d[e].rearrange("(c p) n -> p c n", p=128)
                        ).then_inc(P.sems["lg"], 16)

        gu_ac = P.res("ac", 4 * NQ)
        gu_dv = P.res("dv", 2 * 4 * NQ)
        dn_dv = P.res("dv", 4 * NQ)
        P.cnt["dv"] -= 3 * 4 * NQ
        dvo = P.res("dv", 3 * 4 * NQ)
        gu_dv, dn_dv = {}, {}
        k_ = 0
        for tq in range(NQ):
            for it in range(4):
                gu_dv[2 * (tq * 4 + it)] = dvo[k_]; k_ += 1
                gu_dv[2 * (tq * 4 + it) + 1] = dvo[k_]; k_ += 1
            for j in range(4):
                dn_dv[tq * 4 + j] = dvo[k_]; k_ += 1
        pev = P.res("pe", 8 * NQ)
        gu_pe, dn_pe = {}, {}
        k_ = 0
        for tq in range(NQ):
            for it in range(4):
                gu_pe[tq * 4 + it] = pev[k_]; k_ += 1
            for j in range(4):
                dn_pe[tq * 4 + j] = pev[k_]; k_ += 1

        @P.block.tensor
        def _(t, e=e, gu_pe=gu_pe, dn_pe=dn_pe, gu_ac=gu_ac, gu_dv=gu_dv, dn_dv=dn_dv,
              prev_dn_dv=prev_dn_dv):
            t.wait_ge(P.sems["lg"], lgw[3 * e + 1])
            if e == 0:
                t.wait_ge(P.sems["dv"], cw_dv)
            for tq in range(NQ):
                for it in range(4):
                    i = tq * 4 + it
                    bank, bank2 = (b[2], b[3]) if i % 2 == 0 else (b[4], b[5])
                    if i >= 2:
                        t.wait_ge(P.sems["ac"], gu_ac[i - 2])
                        t.wait_ge(P.sems["dv"], gu_dv[2 * (i - 2)])
                    for c in range(8):
                        t.matmul(bank[:], wguv[:, 0, c, it * 128:(it + 1) * 128],
                                 hnT[:, c, tq * 512:(tq + 1) * 512],
                                 start=(c == 0), stop=(c == 7))
                    for c in range(8):
                        mm = t.matmul(bank2[:], wguv[:, 1, c, it * 128:(it + 1) * 128],
                                      hnT[:, c, tq * 512:(tq + 1) * 512],
                                      start=(c == 0), stop=(c == 7))
                    mm.then_inc(P.sems["pe"], 1)
                t.wait_ge(P.sems["lg"], lgw[3 * e + 2])
                t.wait_ge(P.sems["dv"], gu_dv[2 * (tq * 4 + 3) + 1])
                for j in range(4):
                    tt = tq * 4 + j
                    if tt >= 1:
                        t.wait_ge(P.sems["dv"], dn_dv[tt - 1])
                    elif prev_dn_dv is not None:
                        t.wait_ge(P.sems["dv"], prev_dn_dv)
                    for i2 in range(4):
                        t.matmul(b[0][:], gub[:, 2, i2, j * 128:(j + 1) * 128],
                                 wdv[:, i2, 0:512], start=(i2 == 0), stop=(i2 == 3))
                    for i2 in range(4):
                        mm = t.matmul(b[1][:], gub[:, 2, i2, j * 128:(j + 1) * 128],
                                      wdv[:, i2, 512:1024], start=(i2 == 0), stop=(i2 == 3))
                    mm.then_inc(P.sems["pe"], 1)

        @P.block.scalar
        def _(sa, e=e, gu_pe=gu_pe, dn_pe=dn_pe, prev_dn_last=prev_dn_last):
            if e == 0:
                apply_pending(P, sa, f"{tag}_gub")
            for tq in range(NQ):
                for it in range(4):
                    i = tq * 4 + it
                    sa.wait_ge(P.sems["pe"], gu_pe[i])
                    if tq >= 1:
                        sa.wait_ge(P.sems["pe"], dn_pe[tq * 4 - 1])
                    elif prev_dn_last is not None:
                        sa.wait_ge(P.sems["pe"], prev_dn_last)
                    bank = b[2] if i % 2 == 0 else b[4]
                    sa.activation(gub[:, 0, it, :], bank[:], AF.Silu).then_inc(P.sems["ac"], 1)

        @P.block.vector
        def _(vv, e=e, gu_pe=gu_pe, dn_pe=dn_pe, gu_ac=gu_ac, gu_dv=gu_dv, dn_dv=dn_dv,
              prev_dn_last=prev_dn_last):
            if e == 0:
                apply_pending(P, vv, f"{tag}_gub")
            for tq in range(NQ):
                for it in range(4):
                    i = tq * 4 + it
                    vv.wait_ge(P.sems["pe"], gu_pe[i])
                    if tq == 0 and prev_dn_last is not None:
                        vv.wait_ge(P.sems["pe"], prev_dn_last)
                    if tq >= 1:
                        vv.wait_ge(P.sems["pe"], dn_pe[tq * 4 - 1])
                    bank2 = b[3] if i % 2 == 0 else b[5]
                    vv.tensor_copy(gub[:, 1, it, :], bank2[:]).then_inc(P.sems["dv"], 1)
                    vv.wait_ge(P.sems["dv"], gu_dv[2 * i])
                    vv.wait_ge(P.sems["ac"], gu_ac[i])
                    vv.tensor_mul(gub[:, 2, it, :], gub[:, 0, it, :], gub[:, 1, it, :]
                                  ).then_inc(P.sems["dv"], 1)
                for j in range(4):
                    tt = tq * 4 + j
                    vv.wait_ge(P.sems["pe"], dn_pe[tt])
                    x_t = X[:, tt * 1024:(tt + 1) * 1024]
                    cwc = 1.0 if e == 0 else combw[:, tt, e - 1:e]
                    vv.scalar_tensor_tensor(x_t[:, 0:512], b[0][:], cwc, x_t[:, 0:512],
                                            ALU.mult, ALU.add)
                    vv.scalar_tensor_tensor(x_t[:, 512:1024], b[1][:], cwc, x_t[:, 512:1024],
                                            ALU.mult, ALU.add).then_inc(P.sems["dv"], 1)

        prev_gu_last = gu_pe[4 * NQ - 1]
        prev_dn_last = dn_pe[4 * NQ - 1]
        prev_dn_dv = dn_dv[4 * NQ - 1]

    AB.free(f"{tag}_hnT", f"{tag}_wgu", f"{tag}_wd", f"{tag}_gub",
            readers={"pe": P.cnt["pe"]})
    AB.free(f"{tag}_wgate", readers={"pe": rt_pe[TT - 1]})
    P.ready = emit_postln(P, tag, {"dv": prev_dn_dv}, refresh_xt=refresh_out)




def pack_l1_attn(Wq, bq, Wk, bk, Wv, bv, Wo, bo):
    c = np.ascontiguousarray
    return {"a1_wq": c(Wq.T, np.float32), "a1_wk": c(Wk.T, np.float32),
            "a1_wv": c(Wv.T, np.float32), "a1_wo": c(Wo.T, np.float32),
            "a1_bq": c(bq.reshape(8, 128).T, np.float32),
            "a1_brow": c(np.stack([bk, bv, bo]), np.float32)}


def emit_l1_attn(P, s, upto=None):
    T, TT, NQ = P.T, P.TT, P.NQ
    AB, SC, X, b, bt = P.AB, P.SC, P.X, P.banks, P.bt
    tag = f"a1s{s}"

    wq_d = P.din("a1_wq", [1024, 1024])
    wk_d = P.din("a1_wk", [1024, 1024])
    wv_d = P.din("a1_wv", [1024, 1024])
    wo_d = P.din("a1_wo", [1024, 1024])
    bq_d = P.din("a1_bq", [128, 8])
    brow_d = P.din("a1_brow", [3, 1024])
    sc_d = P.din("c_sincos", [128, 2 * TT])

    xready = dict(P.ready)
    xT = v3(AB.ap("xT"), T)

    scc = P.AF[:, 2056:2056 + 2 * TT]
    bqc = SC[:, 3968:3976]
    AB.alloc(f"{tag}_brow", 3072)
    brow_ap = AB.ap(f"{tag}_brow")
    brow = {i: brow_ap[0:1, i * 1024:(i + 1) * 1024] for i in range(3)}
    lc = P.res("ld", 2, 16)
    lbr = P.res("lg", 3, 16)[-1]

    @P.block.sync
    def _(sy):
        for snm, v_ in xready.items():
            sy.wait_ge(P.sems[snm], v_)
        sy.dma_start(out=scc, in_=sc_d).then_inc(P.sems["ld"], 16)
        sy.dma_start(out=bqc, in_=bq_d).then_inc(P.sems["ld"], 16)

    @P.block.gpsimd
    def _(g):
        apply_pending(P, g, f"{tag}_brow")
        g.dma_start(out=brow[0], in_=brow_d[0:1, :]).then_inc(P.sems["lg"], 16)
        g.dma_start(out=brow[1], in_=brow_d[1:2, :]).then_inc(P.sems["lg"], 16)
        g.dma_start(out=brow[2], in_=brow_d[2:3, :]).then_inc(P.sems["lg"], 16)

    AB.alloc(f"{tag}_qT", 8 * T)
    qT = v3(AB.ap(f"{tag}_qT"), T)
    AB.alloc(f"{tag}_wq", 8 * 1024)
    wqv = v3(AB.ap(f"{tag}_wq"), 1024)
    lwq = P.res("lg", 1, 16)[0]

    @P.block.gpsimd
    def _(g):
        apply_pending(P, g, f"{tag}_wq")
        g.dma_start(out=wqv, in_=wq_d.rearrange("(c p) n -> p c n", p=128)
                    ).then_inc(P.sems["lg"], 16)

    qp_pe = P.res("pe", 8 * NQ)
    qp_ac = P.res("ac", 8 * NQ)

    xt_tiles = getattr(P, "xt_tiles", None)

    @P.block.tensor
    def _(t):
        t.wait_ge(P.sems["lg"], lwq)
        i = 0
        for tq in range(NQ):
            if xt_tiles is not None:
                t.wait_ge(P.sems["dv"], xt_tiles[4 * tq + 3])
            else:
                for snm, v_ in xready.items():
                    t.wait_ge(P.sems[snm], v_)
            for ct in range(8):
                bank = b[i % 2]
                if i >= 2:
                    t.wait_ge(P.sems["ac"], qp_ac[i - 2])
                for c in range(8):
                    mm = t.matmul(bank[:], wqv[:, c, ct * 128:(ct + 1) * 128],
                                  xT[:, c, tq * 512:(tq + 1) * 512],
                                  start=(c == 0), stop=(c == 7))
                mm.then_inc(P.sems["pe"], 1)
                i += 1

    @P.block.scalar
    def _(sa):
        apply_pending(P, sa, f"{tag}_qT")
        sa.wait_ge(P.sems["ld"], lc[1])
        i = 0
        for tq in range(NQ):
            for ct in range(8):
                sa.wait_ge(P.sems["pe"], qp_pe[i])
                sa.activation(qT[:, ct, tq * 512:(tq + 1) * 512], b[i % 2][:],
                              AF.Relu, bias=bqc[:, ct:ct + 1]).then_inc(P.sems["ac"], 1)
                i += 1
    AB.free(f"{tag}_wq", readers={"ac": qp_ac[-1]})
    if upto == "Q":
        dbg_dump(P, "a1qT", AB.ap(f"{tag}_qT"), {"ac": qp_ac[-1]})
        return

    AB.alloc(f"{tag}_wkv", 2 * 8 * 1024)
    wkv = AB.ap(f"{tag}_wkv").rearrange("p (w c n) -> p w c n", w=2, n=1024)
    AB.alloc(f"{tag}_ks", 16 * 128)
    ksb0 = AB.ap(f"{tag}_ks").rearrange("p (h n) -> p h n", n=128)
    AB.alloc(f"{tag}_vv", 2 * 1024)
    vvb = v3(AB.ap(f"{tag}_vv"), 1024)
    lkv = P.res("lg", 2, 16)

    @P.block.gpsimd
    def _(g):
        apply_pending(P, g, f"{tag}_wkv", f"{tag}_ks", f"{tag}_vv")
        g.dma_start(out=wkv[:, 0, :, :], in_=wk_d.rearrange("(c p) n -> p c n", p=128)
                    ).then_inc(P.sems["lg"], 16)
        g.dma_start(out=wkv[:, 1, :, :], in_=wv_d.rearrange("(c p) n -> p c n", p=128)
                    ).then_inc(P.sems["lg"], 16)

    pev = P.res("pe", 3 * TT)
    kp_pe = {tt: pev[3 * tt] for tt in range(TT)}
    vp_pe = {tt: pev[3 * tt + 1] for tt in range(TT)}
    kv_pe = {tt: pev[3 * tt + 2] for tt in range(TT)}
    dvv = P.res("dv", 3 * TT)
    ke1 = {tt: dvv[3 * tt] for tt in range(TT)}
    ke2 = {tt: dvv[3 * tt + 1] for tt in range(TT)}
    vev = {tt: dvv[3 * tt + 2] for tt in range(TT)}
    kv_banks = (b[2], b[3])
    ks_bank = b[6]
    scr = [SC[:, 0:1024], SC[:, 1024:2048]]

    @P.block.tensor
    def _(t):
        t.wait_ge(P.sems["lg"], lkv[-1])
        t.wait_ge(P.sems["lg"], lbr)
        t.wait_ge(P.sems["ld"], P.const_ready)
        t.wait_ge(P.sems["ac"], qp_ac[-1])
        for tt in range(TT):
            kb = (b[0], b[1]) if tt % 2 == 0 else (b[4], b[5])
            if tt >= 1:
                t.wait_ge(P.sems["dv"], ke1[tt - 1])
                t.wait_ge(P.sems["dv"], vev[tt - 1])
            for nj in range(2):
                for c in range(8):
                    t.matmul(kb[nj][:], xT[:, c, tt * 128:(tt + 1) * 128],
                             wkv[:, 0, c, nj * 512:(nj + 1) * 512],
                             start=(c == 0), stop=False)
                mm = t.matmul(kb[nj][:], P.onesb[0:1, 0:128],
                              brow[0][:, nj * 512:(nj + 1) * 512], start=False, stop=True)
            mm.then_inc(P.sems["pe"], 1)
            vb = (b[4], b[5]) if tt % 2 == 0 else (b[0], b[1])
            for nj in range(2):
                for c in range(8):
                    t.matmul(vb[nj][:], xT[:, c, tt * 128:(tt + 1) * 128],
                             wkv[:, 1, c, nj * 512:(nj + 1) * 512],
                             start=(c == 0), stop=False)
                mm = t.matmul(vb[nj][:], P.onesb[0:1, 0:128],
                              brow[1][:, nj * 512:(nj + 1) * 512], start=False, stop=True)
            mm.then_inc(P.sems["pe"], 1)
            t.wait_ge(P.sems["dv"], ke2[tt])
            t.wait_ge(P.sems["dv"], vev[tt])
            for h in range(16):
                t.matmul(kv_banks[h // 8][:, (h % 8) * 64:(h % 8) * 64 + 64],
                         ksb0[:, h, :], vvb[:, tt % 2, h * 64:(h + 1) * 64],
                         start=(tt == 0 and h % 8 == 0), stop=(tt == TT - 1 and h % 8 == 7))
            for h in range(16):
                mm = t.matmul(ks_bank[:, h:h + 1], ksb0[:, h, :], P.onesb[:, 0:1],
                              start=(tt == 0 and h == 0), stop=(tt == TT - 1 and h == 15))
            mm.then_inc(P.sems["pe"], 1)

    @P.block.vector
    def _(vv):
        apply_pending(P, vv, f"{tag}_ks", f"{tag}_vv")
        vv.wait_ge(P.sems["ld"], lc[0])
        for tt in range(TT):
            kb = (b[0], b[1]) if tt % 2 == 0 else (b[4], b[5])
            vb = (b[4], b[5]) if tt % 2 == 0 else (b[0], b[1])
            vv.wait_ge(P.sems["pe"], kp_pe[tt])
            if tt >= 1:
                vv.wait_ge(P.sems["pe"], kv_pe[tt - 1])
            vv.tensor_scalar_max(scr[tt % 2][:, 0:512], kb[0][:], 0.0)
            vv.tensor_scalar_max(scr[tt % 2][:, 512:1024], kb[1][:], 0.0
                                 ).then_inc(P.sems["dv"], 1)
            vv.wait_ge(P.sems["dv"], ke1[tt])
            sview = scr[tt % 2].rearrange("p (h n) -> p h n", n=64)
            vv.tensor_scalar_mul(ksb0[:, :, 0:64], sview, scc[:, tt:tt + 1])
            vv.tensor_scalar_mul(ksb0[:, :, 64:128], sview, scc[:, TT + tt:TT + tt + 1]
                                 ).then_inc(P.sems["dv"], 1)
            vv.wait_ge(P.sems["pe"], vp_pe[tt])
            vv.tensor_copy(vvb[:, tt % 2, 0:512], vb[0][:])
            vv.tensor_copy(vvb[:, tt % 2, 512:1024], vb[1][:]).then_inc(P.sems["dv"], 1)

    AB.alloc(f"{tag}_kvr", 16 * 136)
    kvr = AB.ap(f"{tag}_kvr").rearrange("p (h n) -> p h n", n=136)
    kve = P.res("dv", 1)[0]

    @P.block.vector
    def _(vv):
        apply_pending(P, vv, f"{tag}_kvr")
        vv.wait_ge(P.sems["pe"], kv_pe[TT - 1])
        for h in range(16):
            bank = kv_banks[h // 8]
            sl = (h % 8) * 64
            vv.tensor_copy(kvr[0:64, h, 0:64], bank[0:64, sl:sl + 64])
            vv.tensor_copy(kvr[64:128, h, 0:64], bank[0:64, sl:sl + 64])
            vv.tensor_copy(kvr[0:64, h, 68:132], bank[64:128, sl:sl + 64])
            vv.tensor_copy(kvr[64:128, h, 68:132], bank[64:128, sl:sl + 64])
            vv.tensor_copy(kvr[0:64, h, 64:65], ks_bank[0:64, h:h + 1])
            vv.tensor_copy(kvr[64:128, h, 64:65], ks_bank[0:64, h:h + 1])
            vv.tensor_copy(kvr[0:64, h, 132:133], ks_bank[64:128, h:h + 1])
            ins = vv.tensor_copy(kvr[64:128, h, 132:133], ks_bank[64:128, h:h + 1])
        ins.then_inc(P.sems["dv"], 1)
    AB.free(f"{tag}_wkv", f"{tag}_ks", f"{tag}_vv", readers={"pe": kv_pe[TT - 1], "dv": kve})
    AB.free("xT", readers={"pe": kv_pe[TT - 1]})
    if upto == "KV":
        dbg_dump(P, "a1kvr", AB.ap(f"{tag}_kvr"), {"dv": kve})
        return

    AB.alloc(f"{tag}_y", TT * 1024)
    y = v3(AB.ap(f"{tag}_y"), 1024)
    GRP = [(0, 2, 4, 6), (8, 10, 12, 14), (1, 3, 5, 7), (9, 11, 13, 15)]
    at_pe = P.res("pe", TT * len(GRP))
    at_dv = P.res("dv", TT * len(GRP) * 6)

    @P.block.tensor
    def _(t):
        t.wait_ge(P.sems["dv"], kve)
        t.wait_ge(P.sems["ac"], qp_ac[-1])
        for tt in range(TT):
            for g_, heads in enumerate(GRP):
                pr = (b[0], b[1]) if (tt * len(GRP) + g_) % 2 == 0 else (b[4], b[5])
                idx = tt * len(GRP) + g_
                nh = len(heads)
                base = (heads[0] % 2) * 64
                if idx >= 2:
                    t.wait_ge(P.sems["dv"], at_dv[6 * (idx - 2)])
                for j, h in enumerate(heads):
                    t.matmul(pr[0][:, j * 128:j * 128 + 65],
                             qT[base:base + 64, h // 2, tt * 128:(tt + 1) * 128],
                             kvr[base:base + 64, h, 0:65], start=(j == 0), stop=(j == nh - 1))
                    mm = t.matmul(pr[1][:, j * 128:j * 128 + 65],
                                  qT[base:base + 64, h // 2, tt * 128:(tt + 1) * 128],
                                  kvr[base:base + 64, h, 68:133], start=(j == 0), stop=(j == nh - 1))
                mm.then_inc(P.sems["pe"], 1)

    @P.block.vector
    def _(vv):
        apply_pending(P, vv, f"{tag}_y")
        tps = SC[:, 2048:2048 + 512]
        bts_ = SC[:, 2560:2560 + 512]
        att = SC[:, 3072:3072 + 512]
        zz = SC[:, 3584:3584 + 16]
        for tt in range(TT):
            for g_, heads in enumerate(GRP):
                idx = tt * len(GRP) + g_
                nh = len(heads)
                pr = (b[0], b[1]) if idx % 2 == 0 else (b[4], b[5])
                vv.wait_ge(P.sems["pe"], at_pe[idx])
                vv.tensor_copy(tps[:, 0:nh * 128], pr[0][:, 0:nh * 128])
                vv.tensor_copy(bts_[:, 0:nh * 128], pr[1][:, 0:nh * 128]).then_inc(P.sems["dv"], 1)
                vv.wait_ge(P.sems["dv"], at_dv[6 * idx])
                attv = att[:, 0:nh * 128].rearrange("p (j n) -> p j n", n=128)[:, :, 0:65]
                topv = tps[:, 0:nh * 128].rearrange("p (j n) -> p j n", n=128)[:, :, 0:65]
                botv = bts_[:, 0:nh * 128].rearrange("p (j n) -> p j n", n=128)[:, :, 0:65]
                vv.tensor_scalar_mul(attv, topv, scc[:, tt:tt + 1]).then_inc(P.sems["dv"], 1)
                vv.wait_ge(P.sems["dv"], at_dv[6 * idx + 1])
                vv.scalar_tensor_tensor(attv, botv, scc[:, TT + tt:TT + tt + 1], attv,
                                        ALU.mult, ALU.add).then_inc(P.sems["dv"], 1)
                vv.wait_ge(P.sems["dv"], at_dv[6 * idx + 2])
                dview = att[:, 0:nh * 128].rearrange("p (j n) -> p j n", n=128)[:, :, 64:65]
                vv.tensor_scalar_max(zz[:, 0:nh], dview, 1e-4).then_inc(P.sems["dv"], 1)
                vv.wait_ge(P.sems["dv"], at_dv[6 * idx + 3])
                vv.reciprocal(zz[:, 8:8 + nh], zz[:, 0:nh]).then_inc(P.sems["dv"], 1)
                vv.wait_ge(P.sems["dv"], at_dv[6 * idx + 4])
                for j, h in enumerate(heads):
                    ins = vv.tensor_scalar_mul(y[:, tt, h * 64:(h + 1) * 64],
                                               att[:, j * 128:j * 128 + 64],
                                               zz[:, 8 + j:9 + j])
                ins.then_inc(P.sems["dv"], 1)
    y_done = P.cnt["dv"]
    if upto == "AT":
        dbg_dump(P, "a1y", AB.ap(f"{tag}_y"), {"dv": y_done})
        return
    AB.free(f"{tag}_qT", f"{tag}_kvr", readers={"pe": at_pe[-1]})

    AB.alloc(f"{tag}_yT", 8 * T)
    yT = v3(AB.ap(f"{tag}_yT"), T)
    AB.alloc(f"{tag}_wo", 8 * 1024)
    wov = v3(AB.ap(f"{tag}_wo"), 1024)
    lwo = P.res("lg", 1, 16)[0]

    @P.block.gpsimd
    def _(g):
        apply_pending(P, g, f"{tag}_wo")
        g.dma_start(out=wov, in_=wo_d.rearrange("(c p) n -> p c n", p=128)
                    ).then_inc(P.sems["lg"], 16)

    yt_pe = P.res("pe", TT)
    yt_dv = P.res("dv", TT)

    @P.block.tensor
    def _(t):
        t.wait_ge(P.sems["dv"], y_done)
        for tt in range(TT):
            if tt >= 1:
                t.wait_ge(P.sems["dv"], yt_dv[tt - 1])
            for c in range(8):
                mm = t.transpose(bt[:, c * 128:(c + 1) * 128], y[:, tt, c * 128:(c + 1) * 128],
                                 P.ident[:])
            mm.then_inc(P.sems["pe"], 1)

    @P.block.vector
    def _(vv):
        apply_pending(P, vv, f"{tag}_yT")
        for tt in range(TT):
            vv.wait_ge(P.sems["pe"], yt_pe[tt])
            vv.tensor_copy(yT[:, :, tt * 128:(tt + 1) * 128],
                           bt[:].rearrange("p (c t) -> p c t", t=128)).then_inc(P.sems["dv"], 1)

    op_pe = P.res("pe", TT)
    op_dv = P.res("dv", TT)

    @P.block.tensor
    def _(t):
        t.wait_ge(P.sems["lg"], lwo)
        t.wait_ge(P.sems["dv"], yt_dv[TT - 1])
        for tt in range(TT):
            pr = (b[2], b[3]) if tt % 2 == 0 else (b[4], b[5])
            if tt >= 2:
                t.wait_ge(P.sems["dv"], op_dv[tt - 2])
            for nj in range(2):
                for c in range(8):
                    t.matmul(pr[nj][:], yT[:, c, tt * 128:(tt + 1) * 128],
                             wov[:, c, nj * 512:(nj + 1) * 512],
                             start=(c == 0), stop=False)
                mm = t.matmul(pr[nj][:], P.onesb[0:1, 0:128],
                              brow[2][:, nj * 512:(nj + 1) * 512], start=False, stop=True)
            mm.then_inc(P.sems["pe"], 1)

    @P.block.vector
    def _(vv):
        for tt in range(TT):
            pr = (b[2], b[3]) if tt % 2 == 0 else (b[4], b[5])
            vv.wait_ge(P.sems["pe"], op_pe[tt])
            vv.tensor_add(X[:, tt * 1024:tt * 1024 + 512], pr[0][:],
                          X[:, tt * 1024:tt * 1024 + 512])
            vv.tensor_add(X[:, tt * 1024 + 512:(tt + 1) * 1024], pr[1][:],
                          X[:, tt * 1024 + 512:(tt + 1) * 1024]).then_inc(P.sems["dv"], 1)

    AB.free(f"{tag}_y", readers={"pe": yt_pe[TT - 1]})
    AB.free(f"{tag}_yT", f"{tag}_wo", f"{tag}_brow", readers={"pe": op_pe[TT - 1]})
    P.ready = emit_postln(P, tag, {"dv": op_dv[TT - 1]})



def pack_l1_ffn(router_w, w1, b1, w2, b2):
    c = np.ascontiguousarray
    return {"f1_rw": c(router_w, np.float32), "f1_w1": c(w1, np.float32),
            "f1_b1": c(b1.reshape(14, 16, 128).transpose(0, 2, 1), np.float32),
            "f1_w2": c(w2, np.float32), "f1_b2": c(b2, np.float32)}


def emit_l1_ffn(P, s, upto=None, guard=()):
    T, TT, NQ = P.T, P.TT, P.NQ
    AB, SC, X, b, bt = P.AB, P.SC, P.X, P.banks, P.bt
    tag = f"f1s{s}"

    rw_d = P.din("f1_rw", [1024, 14])
    w1_d = P.din("f1_w1", [14, 1024, 2048])
    b1_d = P.din("f1_b1", [14, 128, 16])
    w2_d = P.din("f1_w2", [14, 2048, 1024])
    b2_d = P.din("f1_b2", [14, 1024])

    xready = dict(P.ready)
    xT = v3(AB.ap("xT"), T)

    AB.alloc(f"{tag}_rw", 8 * 14)
    rwv = v3(AB.ap(f"{tag}_rw"), 14)
    AB.alloc(f"{tag}_rwe", 8 * 14)
    rwe = v3(AB.ap(f"{tag}_rwe"), 14)
    AB.alloc(f"{tag}_ecb", 2 * 1024)
    ecb_off = AB.blocks[f"{tag}_ecb"][0]
    ecb = [AB.t[:, ecb_off + i * 1024:ecb_off + (i + 1) * 1024] for i in range(2)]
    AB.alloc(f"{tag}_eT", 2 * 1024)
    eT_off = AB.blocks[f"{tag}_eT"][0]
    eT = [v3(AB.t[:, eT_off + i * 1024:eT_off + (i + 1) * 1024], 128) for i in range(2)]
    combw = SC[:, 0:14 * TT].rearrange("p (t e) -> p t e", e=14)
    mx8 = SC[:, 14 * TT + 0:14 * TT + 8]
    rt = SC[:, 14 * TT + 8:14 * TT + 24]
    iota14 = SC[:, 14 * TT + 24:14 * TT + 40]
    exps = SC[:, 14 * TT + 40:14 * TT + 56]
    rwf = SC[:, 3856:3968].rearrange("p (c n) -> p c n", n=14)

    lrw = P.res("ld", 1, 16)[0]
    gpi = P.res("gp", 1)[0]
    rp_dv = P.res("dv", 2)

    @P.block.sync
    def _(sy):
        for snm, v_ in xready.items():
            sy.wait_ge(P.sems[snm], v_)
        sy.dma_start(out=rwf, in_=rw_d.rearrange("(c p) n -> p c n", p=128)
                     ).then_inc(P.sems["ld"], 16)

    @P.block.gpsimd
    def _(g):
        g.iota(iota14[:, 0:14], [[1, 14]], base=0, channel_multiplier=0,
               allow_small_or_imprecise_dtypes=True).then_inc(P.sems["gp"], 1)

    @P.block.vector
    def _(vv):
        apply_pending(P, vv, f"{tag}_rw", f"{tag}_rwe", f"{tag}_ecb", f"{tag}_eT")
        vv.wait_ge(P.sems["ld"], lrw)
        vv.tensor_copy(rwv.rearrange("p c n -> p (c n)"),
                       rwf.rearrange("p c n -> p (c n)")).then_inc(P.sems["dv"], 1)
        vv.wait_ge(P.sems["dv"], rp_dv[0])
        vv.tensor_sub(rwe.rearrange("p c n -> p (c n)"), rwf.rearrange("p c n -> p (c n)"),
                      rwv.rearrange("p c n -> p (c n)")).then_inc(P.sems["dv"], 1)

    rt_pe = P.res("pe", 2 * TT)
    rt_dv = P.res("dv", 9 * TT)
    rt_ac = P.res("ac", TT)

    @P.block.tensor
    def _(t):
        t.wait_ge(P.sems["dv"], rp_dv[1])
        for snm, v_ in xready.items():
            t.wait_ge(P.sems[snm], v_)
        for tt in range(TT):
            bank = b[tt % 2]
            t.wait_ge(P.sems["dv"], rt_dv[9 * tt + 1])
            if tt >= 1:
                t.wait_ge(P.sems["dv"], rt_dv[9 * (tt - 1) + 2])
            for c in range(8):
                mm = t.transpose(bt[:, c * 128:(c + 1) * 128],
                                 ecb[tt % 2][:, c * 128:(c + 1) * 128], P.ident[:])
            mm.then_inc(P.sems["pe"], 1)
            if tt >= 2:
                t.wait_ge(P.sems["dv"], rt_dv[9 * (tt - 2) + 4])
            t.wait_ge(P.sems["dv"], rt_dv[9 * tt + 2])
            for c in range(8):
                t.matmul(bank[:, 0:14], xT[:, c, tt * 128:(tt + 1) * 128],
                         rwv[:, c, :], start=(c == 0), stop=False)
            for c in range(8):
                t.matmul(bank[:, 0:14], eT[tt % 2][:, c, :], rwv[:, c, :],
                         start=False, stop=False)
            for c in range(8):
                mm = t.matmul(bank[:, 0:14], xT[:, c, tt * 128:(tt + 1) * 128],
                              rwe[:, c, :], start=False, stop=(c == 7))
            mm.then_inc(P.sems["pe"], 1)

    @P.block.vector
    def _(vv):
        vv.wait_ge(P.sems["gp"], gpi)
        for snm, v_ in xready.items():
            vv.wait_ge(P.sems[snm], v_)
        for tt in range(TT):
            bank = b[tt % 2]
            if tt >= 2:
                vv.wait_ge(P.sems["pe"], rt_pe[2 * (tt - 2)])
            vv.tensor_copy(ecb[tt % 2], X[:, tt * 1024:(tt + 1) * 1024]
                           ).then_inc(P.sems["dv"], 1)
            vv.wait_ge(P.sems["dv"], rt_dv[9 * tt])
            vv.tensor_sub(ecb[tt % 2], X[:, tt * 1024:(tt + 1) * 1024], ecb[tt % 2]
                          ).then_inc(P.sems["dv"], 1)
            vv.wait_ge(P.sems["pe"], rt_pe[2 * tt])
            vv.tensor_copy(eT[tt % 2].rearrange("p c n -> p (c n)"), bt[:]
                           ).then_inc(P.sems["dv"], 1)
            vv.wait_ge(P.sems["pe"], rt_pe[2 * tt + 1])
            if tt >= 1:
                vv.wait_ge(P.sems["dv"], rt_dv[9 * (tt - 1) + 8])
            vv.max(mx8, bank[:, 0:14]).then_inc(P.sems["dv"], 1)
            vv.wait_ge(P.sems["dv"], rt_dv[9 * tt + 3])
            vv.max_index(P.mi8[:], mx8, bank[:, 0:14])
            vv.tensor_scalar_mul(rt[:, 0:1], mx8[:, 0:1], -1.0).then_inc(P.sems["dv"], 1)
            vv.wait_ge(P.sems["ac"], rt_ac[tt])
            vv.reciprocal(rt[:, 2:3], rt[:, 1:2])
            vv.tensor_copy(rt[:, 3:4], P.mi8[:, 0:1]).then_inc(P.sems["dv"], 1)
            vv.wait_ge(P.sems["dv"], rt_dv[9 * tt + 5])
            vv.tensor_scalar(combw[:, tt, :], iota14[:, 0:14], rt[:, 3:4], None,
                             ALU.is_equal).then_inc(P.sems["dv"], 1)
            vv.wait_ge(P.sems["dv"], rt_dv[9 * tt + 6])
            vv.tensor_scalar_mul(combw[:, tt, :], combw[:, tt, :], rt[:, 2:3]
                                 ).then_inc(P.sems["dv"], 1)
            vv.engine_nop().then_inc(P.sems["dv"], 1)

    @P.block.scalar
    def _(sa):
        for tt in range(TT):
            bank = b[tt % 2]
            sa.wait_ge(P.sems["dv"], rt_dv[9 * tt + 4])
            sa.activation(exps[:, 0:14], bank[:, 0:14], AF.Exp, bias=rt[:, 0:1],
                          accum_out=rt[:, 1:2]).then_inc(P.sems["ac"], 1)
    cw_dv = P.cnt["dv"]
    AB.free(f"{tag}_rw", f"{tag}_rwe", f"{tag}_ecb", f"{tag}_eT",
            readers={"pe": rt_pe[-1]})
    if "rb" in guard:
        emit_seq_barrier(P)

    AB.alloc(f"{tag}_w1", 8 * 2048)
    w1v = v3(AB.ap(f"{tag}_w1"), 2048)
    AB.alloc(f"{tag}_w2", 16 * 1024)
    w2v = v3(AB.ap(f"{tag}_w2"), 1024)
    AB.alloc(f"{tag}_hid", 16 * 512)
    hid = v3(AB.ap(f"{tag}_hid"), 512)
    b1s = SC[:, 3840:3856]
    AB.alloc(f"{tag}_b2", 1024)
    b2r = AB.ap(f"{tag}_b2")

    lge = P.res("lg", 3 * 14, 16)
    lb1 = P.res("ld", 14, 16)
    prev_h_last = None
    prev_hac_last = None
    prev_dn_last = None
    prev_dn_dv = None
    for e in range(14):
        if "eb" in guard and e >= 1:
            emit_seq_barrier(P)

        @P.block.gpsimd
        def _(g, e=e, prev_h_last=prev_h_last, prev_dn_last=prev_dn_last):
            if e == 0:
                apply_pending(P, g, f"{tag}_w1", f"{tag}_w2", f"{tag}_hid", f"{tag}_b2")
            if prev_h_last is not None:
                g.wait_ge(P.sems["pe"], prev_h_last)
            g.dma_start(out=w1v, in_=w1_d[e].rearrange("(c p) n -> p c n", p=128)
                        ).then_inc(P.sems["lg"], 16)
            if prev_dn_last is not None:
                g.wait_ge(P.sems["pe"], prev_dn_last)
            g.dma_start(out=w2v, in_=w2_d[e].rearrange("(c p) n -> p c n", p=128)
                        ).then_inc(P.sems["lg"], 16)
            g.dma_start(out=b2r[0:1, :], in_=b2_d[e:e + 1, :]).then_inc(P.sems["lg"], 16)

        @P.block.sync
        def _(sy, e=e, prev_hac_last=(None if e == 0 else prev_hac_last)):
            if prev_hac_last is not None:
                sy.wait_ge(P.sems["ac"], prev_hac_last)
            sy.dma_start(out=b1s, in_=b1_d[e]).then_inc(P.sems["ld"], 16)

        pev = P.res("pe", (16 + 4) * NQ)
        h_pe, dn_pe = {}, {}
        k_ = 0
        for tq in range(NQ):
            for it in range(16):
                h_pe[tq * 16 + it] = pev[k_]; k_ += 1
            for j in range(4):
                dn_pe[tq * 4 + j] = pev[k_]; k_ += 1
        h_ac = P.res("ac", 16 * NQ)
        dn_dv = P.res("dv", 4 * NQ)

        @P.block.tensor
        def _(t, e=e, h_pe=h_pe, dn_pe=dn_pe, h_ac=h_ac, dn_dv=dn_dv,
              prev_dn_dv=prev_dn_dv, lge=lge, lb1=lb1):
            t.wait_ge(P.sems["lg"], lge[3 * e])
            if e == 0:
                t.wait_ge(P.sems["dv"], cw_dv)
            for tq in range(NQ):
                for it in range(16):
                    i = tq * 16 + it
                    bank = b[2 + i % 4]
                    if i >= 4:
                        t.wait_ge(P.sems["ac"], h_ac[i - 4])
                    for c in range(8):
                        mm = t.matmul(bank[:], w1v[:, c, it * 128:(it + 1) * 128],
                                      xT[:, c, tq * 512:(tq + 1) * 512],
                                      start=(c == 0), stop=(c == 7))
                    mm.then_inc(P.sems["pe"], 1)
                t.wait_ge(P.sems["lg"], lge[3 * e + 2])
                t.wait_ge(P.sems["ac"], h_ac[tq * 16 + 15])
                for j in range(4):
                    tt = tq * 4 + j
                    if tt >= 1:
                        t.wait_ge(P.sems["dv"], dn_dv[tt - 1])
                    elif prev_dn_dv is not None:
                        t.wait_ge(P.sems["dv"], prev_dn_dv)
                    for nj in range(2):
                        for i2 in range(16):
                            t.matmul(b[0 if nj == 0 else 1][:],
                                     hid[:, i2, j * 128:(j + 1) * 128],
                                     w2v[:, i2, nj * 512:(nj + 1) * 512],
                                     start=(i2 == 0), stop=False)
                        mm = t.matmul(b[0 if nj == 0 else 1][:], P.onesb[0:1, 0:128],
                                      b2r[0:1, nj * 512:(nj + 1) * 512],
                                      start=False, stop=True)
                    mm.then_inc(P.sems["pe"], 1)

        @P.block.scalar
        def _(sa, e=e, h_pe=h_pe, dn_pe=dn_pe, prev_dn_last=prev_dn_last, lb1=lb1):
            if e == 0:
                apply_pending(P, sa, f"{tag}_hid")
            sa.wait_ge(P.sems["ld"], lb1[e])
            for tq in range(NQ):
                if tq >= 1:
                    sa.wait_ge(P.sems["pe"], dn_pe[tq * 4 - 1])
                elif prev_dn_last is not None:
                    sa.wait_ge(P.sems["pe"], prev_dn_last)
                for it in range(16):
                    i = tq * 16 + it
                    sa.wait_ge(P.sems["pe"], h_pe[i])
                    sa.activation(hid[:, it, :], b[2 + i % 4][:], AF.Gelu,
                                  bias=b1s[:, it:it + 1]).then_inc(P.sems["ac"], 1)

        @P.block.vector
        def _(vv, e=e, dn_pe=dn_pe, dn_dv=dn_dv):
            for tq in range(NQ):
                for j in range(4):
                    tt = tq * 4 + j
                    vv.wait_ge(P.sems["pe"], dn_pe[tt])
                    x_t = X[:, tt * 1024:(tt + 1) * 1024]
                    cwc = combw[:, tt, e:e + 1]
                    vv.scalar_tensor_tensor(x_t[:, 0:512], b[0][:], cwc, x_t[:, 0:512],
                                            ALU.mult, ALU.add)
                    vv.scalar_tensor_tensor(x_t[:, 512:1024], b[1][:], cwc, x_t[:, 512:1024],
                                            ALU.mult, ALU.add).then_inc(P.sems["dv"], 1)

        prev_h_last = h_pe[16 * NQ - 1]
        prev_hac_last = h_ac[16 * NQ - 1]
        prev_dn_last = dn_pe[4 * NQ - 1]
        prev_dn_dv = dn_dv[4 * NQ - 1]

    AB.free(f"{tag}_w1", f"{tag}_w2", f"{tag}_hid", f"{tag}_b2",
            readers={"pe": P.cnt["pe"]})
    if "pb" in guard:
        emit_seq_barrier(P)
    P.ready = emit_postln(P, tag, {"dv": prev_dn_dv}, refresh_xt=False)


def emit_l1_ffn_sp(P, s, guard=("pb",)):
    T, TT, NQ = P.T, P.TT, P.NQ
    AB, SC, X, b, bt = P.AB, P.SC, P.X, P.banks, P.bt
    tag = f"f1s{s}"

    rw_d = P.din("f1_rw", [1024, 14])
    w1_d = P.din("f1_w1", [14, 1024, 2048])
    b1_d = P.din("f1_b1", [14, 128, 16])
    w2_d = P.din("f1_w2", [14, 2048, 1024])
    b2_d = P.din("f1_b2", [14, 1024])

    xready = dict(P.ready)
    xT = v3(AB.ap("xT"), T)

    AB.alloc(f"{tag}_rw", 8 * 14)
    rwv = v3(AB.ap(f"{tag}_rw"), 14)
    AB.alloc(f"{tag}_rwe", 8 * 14)
    rwe = v3(AB.ap(f"{tag}_rwe"), 14)
    AB.alloc(f"{tag}_ecb", 2 * 1024)
    ecb_off = AB.blocks[f"{tag}_ecb"][0]
    ecb = [AB.t[:, ecb_off + i * 1024:ecb_off + (i + 1) * 1024] for i in range(2)]
    AB.alloc(f"{tag}_eT", 2 * 1024)
    eT_off = AB.blocks[f"{tag}_eT"][0]
    eT = [v3(AB.t[:, eT_off + i * 1024:eT_off + (i + 1) * 1024], 128) for i in range(2)]
    AB.alloc(f"{tag}_ohb", TT * 14)
    ohb = v3(AB.ap(f"{tag}_ohb"), 14)

    mx8 = SC[:, 0:8]
    rt = SC[:, 8:24]
    iota14 = SC[:, 24:40]
    exps = SC[:, 40:56]
    oh0 = 56
    oneh = SC[:, oh0:oh0 + 14 * TT].rearrange("p (t e) -> p t e", e=14)
    i0 = oh0 + 14 * TT
    iota128 = SC[:, i0:i0 + 128]
    pr0 = i0 + 128
    prefix = SC[:, pr0:pr0 + 14 * TT]
    pm0 = pr0 + 14 * TT
    posm = [SC[:, pm0 + k * 14 * TT:pm0 + (k + 1) * 14 * TT] for k in range(2)]
    wg0 = pm0 + 2 * 14 * TT
    assert wg0 + TT <= 3840
    rwf = SC[:, 3856:3968].rearrange("p (c n) -> p c n", n=14)
    b1s = SC[:, 3840:3856]

    lrw = P.res("ld", 1, 16)[0]
    gpi = P.res("gp", 1)[0]
    rp_dv = P.res("dv", 2)

    @P.block.sync
    def _(sy):
        for snm, v_ in xready.items():
            sy.wait_ge(P.sems[snm], v_)
        sy.dma_start(out=rwf, in_=rw_d.rearrange("(c p) n -> p c n", p=128)
                     ).then_inc(P.sems["ld"], 16)

    @P.block.gpsimd
    def _(g):
        g.iota(iota14[:, 0:14], [[1, 14]], base=0, channel_multiplier=0,
               allow_small_or_imprecise_dtypes=True)
        g.iota(iota128, [[1, 128]], base=0, channel_multiplier=0,
               allow_small_or_imprecise_dtypes=True).then_inc(P.sems["gp"], 1)

    @P.block.vector
    def _(vv):
        apply_pending(P, vv, f"{tag}_rw", f"{tag}_rwe", f"{tag}_ecb", f"{tag}_eT",
                      f"{tag}_ohb")
        vv.wait_ge(P.sems["ld"], lrw)
        vv.tensor_copy(rwv.rearrange("p c n -> p (c n)"),
                       rwf.rearrange("p c n -> p (c n)")).then_inc(P.sems["dv"], 1)
        vv.wait_ge(P.sems["dv"], rp_dv[0])
        vv.tensor_sub(rwe.rearrange("p c n -> p (c n)"), rwf.rearrange("p c n -> p (c n)"),
                      rwv.rearrange("p c n -> p (c n)")).then_inc(P.sems["dv"], 1)

    rt_pe = P.res("pe", 2 * TT)
    rt_dv = P.res("dv", 9 * TT)
    rt_ac = P.res("ac", TT)

    @P.block.tensor
    def _(t):
        t.wait_ge(P.sems["dv"], rp_dv[1])
        for snm, v_ in xready.items():
            t.wait_ge(P.sems[snm], v_)
        for tt in range(TT):
            bank = b[tt % 2]
            t.wait_ge(P.sems["dv"], rt_dv[9 * tt + 1])
            if tt >= 1:
                t.wait_ge(P.sems["dv"], rt_dv[9 * (tt - 1) + 2])
            for c in range(8):
                mm = t.transpose(bt[:, c * 128:(c + 1) * 128],
                                 ecb[tt % 2][:, c * 128:(c + 1) * 128], P.ident[:])
            mm.then_inc(P.sems["pe"], 1)
            if tt >= 2:
                t.wait_ge(P.sems["dv"], rt_dv[9 * (tt - 2) + 4])
            t.wait_ge(P.sems["dv"], rt_dv[9 * tt + 2])
            for c in range(8):
                t.matmul(bank[:, 0:14], xT[:, c, tt * 128:(tt + 1) * 128],
                         rwv[:, c, :], start=(c == 0), stop=False)
            for c in range(8):
                t.matmul(bank[:, 0:14], eT[tt % 2][:, c, :], rwv[:, c, :],
                         start=False, stop=False)
            for c in range(8):
                mm = t.matmul(bank[:, 0:14], xT[:, c, tt * 128:(tt + 1) * 128],
                              rwe[:, c, :], start=False, stop=(c == 7))
            mm.then_inc(P.sems["pe"], 1)

    @P.block.vector
    def _(vv):
        vv.wait_ge(P.sems["gp"], gpi)
        for snm, v_ in xready.items():
            vv.wait_ge(P.sems[snm], v_)
        for tt in range(TT):
            bank = b[tt % 2]
            if tt >= 2:
                vv.wait_ge(P.sems["pe"], rt_pe[2 * (tt - 2)])
            vv.tensor_copy(ecb[tt % 2], X[:, tt * 1024:(tt + 1) * 1024]
                           ).then_inc(P.sems["dv"], 1)
            vv.wait_ge(P.sems["dv"], rt_dv[9 * tt])
            vv.tensor_sub(ecb[tt % 2], X[:, tt * 1024:(tt + 1) * 1024], ecb[tt % 2]
                          ).then_inc(P.sems["dv"], 1)
            vv.wait_ge(P.sems["pe"], rt_pe[2 * tt])
            vv.tensor_copy(eT[tt % 2].rearrange("p c n -> p (c n)"), bt[:]
                           ).then_inc(P.sems["dv"], 1)
            vv.wait_ge(P.sems["pe"], rt_pe[2 * tt + 1])
            if tt >= 1:
                vv.wait_ge(P.sems["dv"], rt_dv[9 * (tt - 1) + 8])
            vv.max(mx8, bank[:, 0:14]).then_inc(P.sems["dv"], 1)
            vv.wait_ge(P.sems["dv"], rt_dv[9 * tt + 3])
            vv.max_index(P.mi8[:], mx8, bank[:, 0:14])
            vv.tensor_scalar_mul(rt[:, 0:1], mx8[:, 0:1], -1.0).then_inc(P.sems["dv"], 1)
            vv.wait_ge(P.sems["ac"], rt_ac[tt])
            vv.reciprocal(rt[:, 2:3], rt[:, 1:2])
            vv.tensor_copy(rt[:, 3:4], P.mi8[:, 0:1]).then_inc(P.sems["dv"], 1)
            vv.wait_ge(P.sems["dv"], rt_dv[9 * tt + 5])
            vv.tensor_scalar(oneh[:, tt, :], iota14[:, 0:14], rt[:, 3:4], None,
                             ALU.is_equal).then_inc(P.sems["dv"], 1)
            vv.wait_ge(P.sems["dv"], rt_dv[9 * tt + 6])
            vv.tensor_copy(ohb[:, tt, :], oneh[:, tt, :])
            vv.tensor_copy(SC[:, wg0 + tt:wg0 + tt + 1], rt[:, 2:3]
                           ).then_inc(P.sems["dv"], 1)
            vv.engine_nop().then_inc(P.sems["dv"], 1)

    @P.block.scalar
    def _(sa):
        for tt in range(TT):
            bank = b[tt % 2]
            sa.wait_ge(P.sems["dv"], rt_dv[9 * tt + 4])
            sa.activation(exps[:, 0:14], bank[:, 0:14], AF.Exp, bias=rt[:, 0:1],
                          accum_out=rt[:, 1:2]).then_inc(P.sems["ac"], 1)

    AB.free(f"{tag}_rw", f"{tag}_rwe", f"{tag}_ecb", f"{tag}_eT", "xT",
            readers={"pe": rt_pe[-1]})
    emit_seq_barrier(P)

    pf_pe = P.res("pe", 1)
    pf_dv = P.res("dv", 6)

    @P.block.tensor
    def _(t):
        t.wait_ge(P.sems["ld"], P.const_ready)
        n = 0
        for tt in range(TT):
            for tt2 in range(tt + 1):
                n += 1
                mm = t.matmul(b[6][:, tt * 14:(tt + 1) * 14],
                              (P.ltri if tt2 == tt else P.onesb)[:],
                              ohb[:, tt2, :], start=(tt == 0 and tt2 == 0),
                              stop=(tt == TT - 1 and tt2 == tt))
        mm.then_inc(P.sems["pe"], 1)

    AB.alloc(f"{tag}_Xb", TT * 1024)
    Xbv = v3(AB.ap(f"{tag}_Xb"), 1024)

    @P.block.vector
    def _(vv):
        apply_pending(P, vv, f"{tag}_Xb")
        vv.wait_ge(P.sems["pe"], pf_pe[0])
        vv.tensor_copy(prefix, b[6][:, 0:14 * TT]).then_inc(P.sems["dv"], 1)
        vv.wait_ge(P.sems["dv"], pf_dv[0])
        vv.tensor_scalar_add(posm[0], prefix, 1.0).then_inc(P.sems["dv"], 1)
        vv.wait_ge(P.sems["dv"], pf_dv[1])
        vv.tensor_mul(posm[0], posm[0], SC[:, oh0:oh0 + 14 * TT]
                      ).then_inc(P.sems["dv"], 1)
        vv.wait_ge(P.sems["dv"], pf_dv[2])
        vv.tensor_scalar_add(posm[0], posm[0], -1.0).then_inc(P.sems["dv"], 1)
        vv.wait_ge(P.sems["dv"], pf_dv[3])
        vv.tensor_scalar_add(posm[1], posm[0], -128.0).then_inc(P.sems["dv"], 1)
        for tt in range(TT):
            ins = vv.tensor_copy(Xbv[:, tt, :], X[:, tt * 1024:(tt + 1) * 1024])
        ins.then_inc(P.sems["dv"], 1)
    AB.free(f"{tag}_ohb", readers={"pe": pf_pe[0]})

    AB.alloc(f"{tag}_w1", 8 * 2048)
    w1v = v3(AB.ap(f"{tag}_w1"), 2048)
    AB.alloc(f"{tag}_w2", 8 * 1024)
    w2v = v3(AB.ap(f"{tag}_w2"), 1024)
    AB.alloc(f"{tag}_b2", 1024)
    b2r = AB.ap(f"{tag}_b2")
    AB.alloc(f"{tag}_Sg", TT * 128)
    Sgv = v3(AB.ap(f"{tag}_Sg"), 128)
    AB.alloc(f"{tag}_St", 2 * TT * 128)
    Stv = AB.ap(f"{tag}_St").rearrange("p (k t s) -> p k t s", k=2, s=128)
    AB.alloc(f"{tag}_xgsb", 1024)
    xgsb = AB.ap(f"{tag}_xgsb")
    AB.alloc(f"{tag}_xgT", 8 * 256)
    xgT = v3(AB.ap(f"{tag}_xgT"), 256)
    AB.alloc(f"{tag}_hid", 16 * 256)
    hid = v3(AB.ap(f"{tag}_hid"), 256)
    AB.alloc(f"{tag}_yo", 2 * 1024)
    yov = v3(AB.ap(f"{tag}_yo"), 1024)

    rounds = (TT + 7) // 8
    lge = P.res("lg", 4 * 14, 16)
    lb1 = P.res("ld", 14, 16)
    prev = None
    for e in range(14):
        peL = P.res("pe", 2 * (2 + rounds) + 16 + 1 + 4 + 2 * TT)
        g_pe = [peL[k * (2 + rounds)] for k in range(2)]
        xgt_pe = [peL[k * (2 + rounds) + 1] for k in range(2)]
        st_pe = [peL[(kr // rounds) * (2 + rounds) + 2 + kr % rounds]
                 for kr in range(2 * rounds)]
        o_ = 2 * (2 + rounds)
        h_pe = peL[o_:o_ + 16]
        dnh0_pe = peL[o_ + 16]
        dn_pe = peL[o_ + 17:o_ + 21]
        sc_pe = peL[o_ + 21:o_ + 21 + 2 * TT]
        dvL = P.res("dv", 2 * (3 + rounds) + 4 + 2 * TT)
        sg_dv = [dvL[k * (3 + rounds)] for k in range(2)]
        xgs_dv = [dvL[k * (3 + rounds) + 1] for k in range(2)]
        xgt_dv = [dvL[k * (3 + rounds) + 2] for k in range(2)]
        st_dv = [dvL[(kr // rounds) * (3 + rounds) + 3 + kr % rounds]
                 for kr in range(2 * rounds)]
        o_ = 2 * (3 + rounds)
        yo_dv = dvL[o_:o_ + 4]
        xa_dv = dvL[o_ + 4:o_ + 4 + 2 * TT]
        h_ac = P.res("ac", 16)

        @P.block.gpsimd
        def _(g, e=e, prev=prev, dnh0_pe=dnh0_pe):
            if e == 0:
                apply_pending(P, g, f"{tag}_w1", f"{tag}_w2", f"{tag}_b2")
            if prev is not None:
                g.wait_ge(P.sems["pe"], prev["h"])
            g.dma_start(out=w1v, in_=w1_d[e].rearrange("(c p) n -> p c n", p=128)
                        ).then_inc(P.sems["lg"], 16)
            if prev is not None:
                g.wait_ge(P.sems["pe"], prev["dn"])
            g.dma_start(out=w2v, in_=w2_d[e, 0:1024].rearrange("(c p) n -> p c n", p=128)
                        ).then_inc(P.sems["lg"], 16)
            g.dma_start(out=b2r[0:1, :], in_=b2_d[e:e + 1, :]).then_inc(P.sems["lg"], 16)
            g.wait_ge(P.sems["pe"], dnh0_pe)
            g.dma_start(out=w2v, in_=w2_d[e, 1024:2048].rearrange("(c p) n -> p c n", p=128)
                        ).then_inc(P.sems["lg"], 16)

        @P.block.sync
        def _(sy, e=e, prev=prev):
            if prev is not None:
                sy.wait_ge(P.sems["ac"], prev["gelu"])
            sy.dma_start(out=b1s, in_=b1_d[e]).then_inc(P.sems["ld"], 16)

        @P.block.vector
        def _(vv, e=e, prev=prev, sg_dv=sg_dv, g_pe=g_pe, xgs_dv=xgs_dv,
              xgt_pe=xgt_pe, xgt_dv=xgt_dv, st_pe=st_pe, st_dv=st_dv,
              dn_pe=dn_pe, yo_dv=yo_dv):
            if e == 0:
                apply_pending(P, vv, f"{tag}_Sg", f"{tag}_St", f"{tag}_xgsb",
                              f"{tag}_xgT", f"{tag}_yo")
                vv.wait_ge(P.sems["dv"], pf_dv[4])
            if prev is not None:
                vv.wait_ge(P.sems["pe"], prev["sc"])
            for k in range(2):
                if k == 1:
                    vv.wait_ge(P.sems["pe"], g_pe[0])
                    vv.wait_ge(P.sems["pe"], st_pe[rounds - 1])
                for tt in range(TT):
                    ins = vv.tensor_scalar(Sgv[:, tt, :], iota128,
                                           posm[k][:, tt * 14 + e:tt * 14 + e + 1],
                                           None, ALU.is_equal)
                ins.then_inc(P.sems["dv"], 1)
                vv.wait_ge(P.sems["pe"], g_pe[k])
                vv.tensor_copy(xgsb[:, 0:512], b[0][:])
                vv.tensor_copy(xgsb[:, 512:1024], b[1][:]).then_inc(P.sems["dv"], 1)
                vv.wait_ge(P.sems["pe"], xgt_pe[k])
                vv.tensor_copy(xgT[:, :, k * 128:(k + 1) * 128],
                               bt[:].rearrange("p (c t) -> p c t", t=128)
                               ).then_inc(P.sems["dv"], 1)
                for r in range(rounds):
                    nt = min(8, TT - r * 8)
                    vv.wait_ge(P.sems["pe"], st_pe[k * rounds + r])
                    vv.tensor_copy(Stv[:, k, r * 8:r * 8 + nt, :],
                                   bt[:, 0:nt * 128].rearrange("p (c t) -> p c t", t=128)
                                   ).then_inc(P.sems["dv"], 1)
            for k in range(2):
                for nj in range(2):
                    vv.wait_ge(P.sems["pe"], dn_pe[2 * k + nj])
                    vv.tensor_copy(yov[:, k, nj * 512:(nj + 1) * 512],
                                   b[2 * k + nj][:]).then_inc(P.sems["dv"], 1)

        @P.block.tensor
        def _(t, e=e, prev=prev, sg_dv=sg_dv, g_pe=g_pe, xgs_dv=xgs_dv,
              xgt_pe=xgt_pe, xgt_dv=xgt_dv, st_pe=st_pe, st_dv=st_dv,
              h_pe=h_pe, h_ac=h_ac, dn_pe=dn_pe, yo_dv=yo_dv, sc_pe=sc_pe,
              xa_dv=xa_dv):
            bt_free = prev["bt"] if prev is not None else None
            for k in range(2):
                t.wait_ge(P.sems["dv"], sg_dv[k])
                if e == 0 and k == 0:
                    t.wait_ge(P.sems["dv"], pf_dv[5])
                if prev is not None and k == 0:
                    t.wait_ge(P.sems["dv"], prev["yo"])
                if k == 1:
                    t.wait_ge(P.sems["dv"], xgs_dv[0])
                for tt in range(TT):
                    t.matmul(b[0][:], Sgv[:, tt, :], Xbv[:, tt, 0:512],
                             start=(tt == 0), stop=(tt == TT - 1))
                    mm = t.matmul(b[1][:], Sgv[:, tt, :], Xbv[:, tt, 512:1024],
                                  start=(tt == 0), stop=(tt == TT - 1))
                mm.then_inc(P.sems["pe"], 1)
                t.wait_ge(P.sems["dv"], xgs_dv[k])
                if bt_free is not None:
                    t.wait_ge(P.sems["dv"], bt_free)
                for c in range(8):
                    mm = t.transpose(bt[:, c * 128:(c + 1) * 128],
                                     xgsb[:, c * 128:(c + 1) * 128], P.ident[:])
                mm.then_inc(P.sems["pe"], 1)
                prev_evac = xgt_dv[k]
                for r in range(rounds):
                    nt = min(8, TT - r * 8)
                    t.wait_ge(P.sems["dv"], prev_evac)
                    for i in range(nt):
                        mm = t.transpose(bt[:, i * 128:(i + 1) * 128],
                                         Sgv[:, r * 8 + i, :], P.ident[:])
                    mm.then_inc(P.sems["pe"], 1)
                    prev_evac = st_dv[k * rounds + r]
                bt_free = prev_evac
            t.wait_ge(P.sems["lg"], lge[4 * e])
            t.wait_ge(P.sems["dv"], xgt_dv[1])
            if prev is not None:
                t.wait_ge(P.sems["dv"], prev["xa"])
            for it in range(16):
                if it >= 4:
                    t.wait_ge(P.sems["ac"], h_ac[it - 4])
                for c in range(8):
                    mm = t.matmul(b[2 + it % 4][:, 0:256], w1v[:, c, it * 128:(it + 1) * 128],
                                  xgT[:, c, :], start=(c == 0), stop=(c == 7))
                mm.then_inc(P.sems["pe"], 1)
            t.wait_ge(P.sems["lg"], lge[4 * e + 2])
            t.wait_ge(P.sems["ac"], h_ac[15])
            for k in range(2):
                for nj in range(2):
                    bk = b[2 * k + nj]
                    for i2 in range(8):
                        mm = t.matmul(bk[:], hid[:, i2, k * 128:(k + 1) * 128],
                                      w2v[:, i2, nj * 512:(nj + 1) * 512],
                                      start=(i2 == 0), stop=False)
            mm.then_inc(P.sems["pe"], 1)
            t.wait_ge(P.sems["lg"], lge[4 * e + 3])
            for k in range(2):
                for nj in range(2):
                    bk = b[2 * k + nj]
                    for i2 in range(8):
                        t.matmul(bk[:], hid[:, 8 + i2, k * 128:(k + 1) * 128],
                                 w2v[:, i2, nj * 512:(nj + 1) * 512],
                                 start=False, stop=False)
                    mm = t.matmul(bk[:], P.onesb[0:1, 0:128],
                                  b2r[0:1, nj * 512:(nj + 1) * 512],
                                  start=False, stop=True)
                    mm.then_inc(P.sems["pe"], 1)
            t.wait_ge(P.sems["dv"], yo_dv[3])
            t.wait_ge(P.sems["dv"], st_dv[2 * rounds - 1])
            g2 = 0
            for tt in range(TT):
                for nj in range(2):
                    bk = b[4 + g2 % 3]
                    if g2 >= 3:
                        t.wait_ge(P.sems["dv"], xa_dv[g2 - 3])
                    for k in range(2):
                        mm = t.matmul(bk[:], Stv[:, k, tt, :],
                                      yov[:, k, nj * 512:(nj + 1) * 512],
                                      start=(k == 0), stop=(k == 1))
                    mm.then_inc(P.sems["pe"], 1)
                    g2 += 1

        @P.block.scalar
        def _(sa, e=e, prev=prev, h_pe=h_pe, h_ac=h_ac):
            if e == 0:
                apply_pending(P, sa, f"{tag}_hid")
            sa.wait_ge(P.sems["ld"], lb1[e])
            if prev is not None:
                sa.wait_ge(P.sems["pe"], prev["dn"])
            for it in range(16):
                sa.wait_ge(P.sems["pe"], h_pe[it])
                sa.activation(hid[:, it, :], b[2 + it % 4][:, 0:256], AF.Gelu,
                              bias=b1s[:, it:it + 1]).then_inc(P.sems["ac"], 1)

        @P.block.vector
        def _(vv, e=e, sc_pe=sc_pe, xa_dv=xa_dv):
            g2 = 0
            for tt in range(TT):
                for nj in range(2):
                    bk = b[4 + g2 % 3]
                    vv.wait_ge(P.sems["pe"], sc_pe[g2])
                    x_t = X[:, tt * 1024 + nj * 512:tt * 1024 + (nj + 1) * 512]
                    vv.scalar_tensor_tensor(x_t, bk[:], SC[:, wg0 + tt:wg0 + tt + 1],
                                            x_t, ALU.mult, ALU.add
                                            ).then_inc(P.sems["dv"], 1)
                    g2 += 1

        prev = dict(h=h_pe[15], dn=dn_pe[3], gelu=h_ac[15], sc=sc_pe[-1],
                    xa=xa_dv[-1], yo=yo_dv[3], bt=st_dv[2 * rounds - 1])

    AB.free(f"{tag}_Xb", f"{tag}_w1", f"{tag}_w2", f"{tag}_b2", f"{tag}_Sg",
            f"{tag}_St", f"{tag}_xgsb", f"{tag}_xgT", f"{tag}_hid", f"{tag}_yo",
            readers={"pe": P.cnt["pe"]})
    if "pb" in guard:
        emit_seq_barrier(P)
    P.ready = emit_postln(P, tag, {"dv": prev["xa"]}, refresh_xt=False)


def build_a0(T=2048, n_seq=4, debug=False, hold=None, upto=None):
    P = Prog(T, n_seq)
    P.debug = debug or (upto is not None)
    P.setup()
    P.AB.hold = debug if hold is None else hold
    with P.es, P.nc.Block() as block:
        P.block = block
        P.dout("xout", [n_seq, T, 1024])
        emit_consts_load(P)
        for s in range(n_seq):
            if s:
                emit_seq_barrier(P)
            emit_load_x(P, s)
            emit_l0_attn(P, s, upto=upto)
            if upto is None:
                emit_store_x(P, s, final=(s == n_seq - 1))
            else:
                @P.block.sync
                def _(sy):
                    sy.wait_ge(P.sems["st"], P.cnt["st"])
                break
    return P


def build_f0(T=2048, n_seq=4, debug=False, hold=None, upto=None):
    def em(P, s, upto=None):
        return emit_l0_ffn(P, s, upto=upto, refresh_out=False)
    return _build_one(em, T, n_seq, debug=debug, hold=hold, upto=upto, refresh=False)


def build_f0_old(T=2048, n_seq=4, debug=False, hold=None, upto=None):
    P = Prog(T, n_seq)
    P.debug = debug or (upto is not None)
    P.setup()
    P.AB.hold = debug if hold is None else hold
    with P.es, P.nc.Block() as block:
        P.block = block
        P.dout("xout", [n_seq, T, 1024])
        emit_consts_load(P)
        for s in range(n_seq):
            emit_load_x(P, s, refresh=False)
            emit_l0_ffn(P, s, upto=upto)
            if upto is None:
                emit_store_x(P, s, final=(s == n_seq - 1))
            else:
                @P.block.sync
                def _(sy):
                    sy.wait_ge(P.sems["st"], P.cnt["st"])
                break
    return P


def _build_one(emitter, T=2048, n_seq=4, debug=False, hold=None, upto=None,
               refresh=True):
    P = Prog(T, n_seq)
    P.debug = debug or (upto is not None)
    P.setup()
    P.AB.hold = debug if hold is None else hold
    with P.es, P.nc.Block() as block:
        P.block = block
        P.dout("xout", [n_seq, T, 1024])
        emit_consts_load(P)
        for s in range(n_seq):
            if s:
                emit_seq_barrier(P)
            emit_load_x(P, s, refresh=refresh)
            emitter(P, s, upto=upto)
            if upto is None:
                emit_store_x(P, s, final=(s == n_seq - 1))
            else:
                @P.block.sync
                def _(sy):
                    sy.wait_ge(P.sems["st"], P.cnt["st"])
                break
    return P


def build_a1(T=2048, n_seq=4, **kw):
    return _build_one(emit_l1_attn, T, n_seq, **kw)


def build_f1(T=2048, n_seq=4, **kw):
    return _build_one(emit_l1_ffn, T, n_seq, **kw)


def build_f1sp(T=2048, n_seq=4, **kw):
    def em(P, s, upto=None):
        return emit_l1_ffn_sp(P, s)
    return _build_one(em, T, n_seq, **kw)


def build_full(T=2048, n_seq=4, debug=False, stop=None, f1_guard=("rb", "pb"),
               barriers="full"):
    P = Prog(T, n_seq)
    P.debug = debug
    P.f1_guard = f1_guard
    P.setup()
    bar = (lambda: emit_seq_barrier(P)) if barriers == "full" else (lambda: None)
    with P.es, P.nc.Block() as block:
        P.block = block
        P.dout("xout", [n_seq, T, 1024])
        emit_consts_load(P)
        for s in range(n_seq):
            if s:
                bar()
            nb = 4 if stop is None or s < stop[0] else stop[1]
            emit_load_x(P, s, refresh=True)
            emit_l0_attn(P, s, refresh_out=False)
            if debug:
                dbg_dump(P, f"x0_{s}", P.X[:, :], dict(P.ready), dt=F32)
            if nb >= 2:
                bar()
                emit_l0_ffn(P, s, refresh_out=True)
                if debug:
                    dbg_dump(P, f"x1_{s}", P.X[:, :], dict(P.ready), dt=F32)
            if nb >= 3:
                if barriers == "full":
                    pass
                emit_l1_attn(P, s)
                if debug:
                    dbg_dump(P, f"x2_{s}", P.X[:, :], dict(P.ready), dt=F32)
            if nb >= 4:
                bar()
                if getattr(P, "f1_sparse", True):
                    emit_l1_ffn_sp(P, s)
                else:
                    emit_l1_ffn(P, s, guard=getattr(P, "f1_guard", ()))
            last = (s == n_seq - 1) or (stop is not None and s == stop[0])
            emit_store_x(P, s, final=last)
            if last and stop is not None and s == stop[0]:
                break
    return P


_PCACHE = {}


def _progs():
    if "p" not in _PCACHE:
        _PCACHE["p"] = [build_a0(2048, 4), build_f0(2048, 4),
                        build_a1(2048, 4), build_f1(2048, 4)]
    return _PCACHE["p"]


def _prog_full():
    if "f" not in _PCACHE:
        _PCACHE["f"] = build_full(2048, 4)
    return _PCACHE["f"]


def kernel(**inputs):
    T, NS = 2048, 4
    x = np.ascontiguousarray(inputs["L0_attn_x"], np.float32)
    consts = host_consts(T)
    packs = {}
    packs.update(pack_l0_attn(inputs["L0_attn_w_qkv"], inputs["L0_attn_w_o"]))
    packs.update(pack_l0_ffn(inputs["L0_ffn_Wq"], inputs["L0_ffn_Wkva"], inputs["L0_ffn_Wkvb"],
                             inputs["L0_ffn_Wo"], inputs["L0_ffn_Wg"], inputs["L0_ffn_Wu"],
                             inputs["L0_ffn_Wd"], inputs["L0_ffn_sWg"], inputs["L0_ffn_sWu"],
                             inputs["L0_ffn_sWd"], inputs["L0_ffn_Wgate"],
                             inputs["L0_ffn_expert_bias"], inputs["L0_ffn_w_ln1"],
                             inputs["L0_ffn_w_ln2"]))
    packs.update(pack_l1_attn(inputs["L1_attn_Wq"], inputs["L1_attn_bq"], inputs["L1_attn_Wk"],
                              inputs["L1_attn_bk"], inputs["L1_attn_Wv"], inputs["L1_attn_bv"],
                              inputs["L1_attn_Wo"], inputs["L1_attn_bo"]))
    packs.update(pack_l1_ffn(inputs["L1_ffn_router_w"], inputs["L1_ffn_w1"], inputs["L1_ffn_b1"],
                             inputs["L1_ffn_w2"], inputs["L1_ffn_b2"]))
    allin = {**consts, **packs}

    P = _prog_full()
    names = set(P.dram.keys())
    base = {k: np.ascontiguousarray(v) for k, v in allin.items() if k in names}
    in_maps = [{**base, "xin": np.ascontiguousarray(x[4 * c:4 * c + 4])} for c in range(8)]
    res = run_bass_kernel_spmd(P.nc, in_maps, core_ids=list(range(8)))
    return np.concatenate([np.asarray(res.results[c]["xout"], np.float32)
                           for c in range(8)], axis=0)
